# version 06
# speedup vs baseline: 1.3003x; 1.1869x over previous
import contextlib
import numpy as np
import ml_dtypes
import concourse.bass as bass
import concourse.mybir as mybir
from concourse.bass_utils import run_bass_kernel_spmd

F32 = mybir.dt.float32
BF16 = mybir.dt.bfloat16
AF = mybir.ActivationFunctionType
ALU = mybir.AluOpType
BF = ml_dtypes.bfloat16

N_CORES = 8


ENGINES = ("PE", "ACT", "DVE", "POOL", "SP")
DMA_QUEUES = ("SPQ", "ACTQ", "POOLQ")
ISSUER = {"SPQ": "SP", "ACTQ": "ACT", "POOLQ": "POOL"}
ENG_ATTR = {"PE": "tensor", "ACT": "scalar", "DVE": "vector",
            "POOL": "gpsimd", "SP": "sync"}


class KB:
    def __init__(self, nc):
        self.nc = nc
        self.stack = contextlib.ExitStack()
        self.sems = {}
        self.dma_sems = set()
        for name in ENGINES:
            self.sems[name] = self.stack.enter_context(nc.semaphore("sem_" + name))
        for name in DMA_QUEUES:
            self._add_dma_sem(name)
        self.cnt = {name: 0 for name in self.sems}
        self.known = {e: {} for e in ENGINES}
        self.last_write = {}
        self.readers = {}
        self.thunks = {e: [] for e in ENGINES}
        self.tensors = {}

    def _add_dma_sem(self, name):
        self.sems[name] = self.stack.enter_context(
            self.nc.semaphore("sem_" + name.replace(".", "_")))
        self.dma_sems.add(name)

    def dma_queue(self, name):
        if name not in self.sems:
            self._add_dma_sem(name)
            self.cnt[name] = 0
        return name


    def begin_phase(self, name):
        if getattr(self, "phase", None) is not None:
            self.phase_barrier()
        self.phase = name
        if not hasattr(self, "phase_base"):
            self.phase_base = ((self.nc.sbuf_base + 31) // 32) * 32
        self.phase_off = self.phase_base
        self._arena_low = self.nc.sbuf_top

    def phase_barrier(self):
        for eng in ENGINES:
            self.barrier_on(eng)

    def _pname(self, name):
        ph = getattr(self, "phase", None)
        return f"{ph}_{name}" if ph else name

    def t(self, name):
        return self.tensors[self._pname(name)]

    def sbuf(self, name, shape, dtype):
        nbytes = shape[1] * {F32: 4, BF16: 2}.get(dtype, 4)
        name = self._pname(name)
        if getattr(self, "phase", None) is not None:
            off = self.phase_off
            self.phase_off = ((off + nbytes + 31) // 32) * 32
            assert self.phase_off <= self._arena_low, \
                f"phase {self.phase} sbuf overflow at {name}: {self.phase_off}"
            t = self.nc.alloc_sbuf_tensor_at(name, shape, dtype, offset=off)
        else:
            t = self.stack.enter_context(self.nc.sbuf_tensor(name, shape, dtype))
            assert self.nc.sbuf_base <= self.arena_low(), \
                f"sbuf bump ({self.nc.sbuf_base}) hit arena ({self.arena_low()})"
        self.tensors[name] = t
        return t

    def arena_low(self):
        return getattr(self, "_arena_low", self.nc.sbuf_top)

    def sbuf_at(self, name, shape, dtype, offset):
        name = self._pname(name)
        nbytes = shape[1] * {F32: 4, BF16: 2}.get(dtype, 4)
        assert offset + nbytes <= self.nc.sbuf_top, "arena over top"
        base = self.phase_off if getattr(self, "phase", None) is not None \
            else self.nc.sbuf_base
        assert offset >= base, f"arena under base: {offset} < {base}"
        self._arena_low = min(self.arena_low(), offset)
        t = self.nc.alloc_sbuf_tensor_at(name, shape, dtype, offset=offset)
        self.tensors[name] = t
        return t

    def psum(self, name, shape, dtype):
        t = self.stack.enter_context(self.nc.psum_tensor(name, shape, dtype))
        self.tensors[name] = t
        return t

    def dram_in(self, name, shape, dtype):
        return self.nc.dram_tensor(name, shape, dtype, kind="ExternalInput").ap()

    def dram_out(self, name, shape, dtype):
        return self.nc.dram_tensor(name, shape, dtype, kind="ExternalOutput").ap()

    def dram_tmp(self, name, shape, dtype):
        return self.nc.dram_tensor(self._pname(name), shape, dtype).ap()

    def constr(self, value):
        if not hasattr(self, "_constrow"):
            self._constrow = self.sbuf("constrow", [1, 16], F32)
            self._constrmap = {}
        if value not in self._constrmap:
            idx = len(self._constrmap)
            assert idx < 16
            self._constrmap[value] = idx
            ap = self._constrow[0:1, idx:idx + 1]
            self.group("POOL", lambda e, ap=ap, value=value: e.memset(ap, value),
                       writes=[self._constrow])
        idx = self._constrmap[value]
        return self._constrow[0:1, idx:idx + 1]

    def constv(self, value):
        if not hasattr(self, "_constpool"):
            self._constpool = self.sbuf("constpool", [128, 32], F32)
            self._constmap = {}
        if value not in self._constmap:
            idx = len(self._constmap)
            assert idx < 32
            self._constmap[value] = idx
            ap = self._constpool[:, idx:idx + 1]
            self.group("POOL", lambda e, ap=ap, value=value: e.memset(ap, value),
                       writes=[self._constpool])
        idx = self._constmap[value]
        return self._constpool[:, idx:idx + 1]

    @staticmethod
    def _key(x):
        if isinstance(x, (str, tuple)):
            return x
        if hasattr(x, "tensor"):
            return x.tensor.name
        return x.name

    def _wait(self, eng, sem_name, val):
        if self.known[eng].get(sem_name, 0) >= val:
            return
        self.known[eng][sem_name] = val
        sem = self.sems[sem_name]
        e = getattr(self.nc, ENG_ATTR[eng])
        self.thunks[eng].append(lambda e=e, sem=sem, val=val: e.wait_ge(sem, val))

    def group(self, eng, emit, reads=(), writes=(), dma=None):
        sem_name = dma if dma else eng
        step = 16 if dma else 1
        if dma:
            self._wait(eng, sem_name, self.cnt[sem_name])
        rkeys = [self._key(r) for r in reads]
        wkeys = [self._key(w) for w in writes]
        for k in rkeys:
            if k in self.last_write:
                s, c = self.last_write[k]
                self._wait(eng, s, c)
        for k in wkeys:
            if k in self.readers:
                for s, c in self.readers[k].items():
                    self._wait(eng, s, c)
            if k in self.last_write:
                s, c = self.last_write[k]
                self._wait(eng, s, c)
        self.cnt[sem_name] += step
        cnow = self.cnt[sem_name]
        sem = self.sems[sem_name]
        e = getattr(self.nc, ENG_ATTR[eng])

        def thunk(e=e, sem=sem, step=step, emit=emit):
            last = emit(e)
            assert last is not None, "emit must return the last instruction"
            last.then_inc(sem, step)
        self.thunks[eng].append(thunk)
        for k in wkeys:
            self.last_write[k] = (sem_name, cnow)
            self.readers[k] = {}
        for k in rkeys:
            self.readers.setdefault(k, {})
            prev = self.readers[k].get(sem_name, 0)
            self.readers[k][sem_name] = max(prev, cnow)

    def raw(self, eng, emit):
        e = getattr(self.nc, ENG_ATTR[eng])
        self.thunks[eng].append(lambda e=e: emit(e))

    def barrier_on(self, eng):
        for s in list(self.cnt):
            if self.cnt[s] > 0:
                self._wait(eng, s, self.cnt[s])

    def finalize(self):
        with self.nc.Block() as block:
            for eng in ENGINES:
                if not self.thunks[eng]:
                    continue
                dec = getattr(block, ENG_ATTR[eng])

                def body(e, eng=eng):
                    for th in self.thunks[eng]:
                        th()
                dec(body)
        self.stack.close()
        return self.nc



class Prog:

    def __init__(self, nc):
        self.nc = nc
        self.kb = KB(nc)
        kb = self.kb
        self.ident = kb.sbuf("identb", [128, 128], BF16)
        kb.group("POOL", lambda e: e.memset(self.ident[:, :], 0.0), writes=[self.ident])
        kb.group("POOL", lambda e: e.affine_select(
            out=self.ident[:, :], in_=self.ident[:, :], compare_op=ALU.not_equal,
            fill=1.0, base=0, pattern=[[-1, 128]], channel_multiplier=1),
            reads=[self.ident], writes=[self.ident])
        self.identf = kb.sbuf("identf", [128, 128], F32)
        kb.group("POOL", lambda e: e.memset(self.identf[:, :], 0.0), writes=[self.identf])
        kb.group("POOL", lambda e: e.affine_select(
            out=self.identf[:, :], in_=self.identf[:, :], compare_op=ALU.not_equal,
            fill=1.0, base=0, pattern=[[-1, 128]], channel_multiplier=1),
            reads=[self.identf], writes=[self.identf])
        self.ones_row = kb.sbuf("ones_row", [1, 512], BF16)
        kb.group("POOL", lambda e: e.memset(self.ones_row[:, :], 1.0),
                 writes=[self.ones_row])
        self.ones_col = kb.sbuf("ones_col", [128, 1], BF16)
        kb.group("POOL", lambda e: e.memset(self.ones_col[:, :], 1.0),
                 writes=[self.ones_col])
        self.onesf = kb.sbuf("onesf", [128, 128], F32)
        kb.group("POOL", lambda e: e.memset(self.onesf[:, :], 1.0),
                 writes=[self.onesf])
        self.pA = kb.psum("pA", [128, 1024], F32)
        self.pB = kb.psum("pB", [128, 1024], F32)
        self.pC = kb.psum("pC", [128, 512], F32)
        self.pD = kb.psum("pD", [128, 512], F32)
        self.pb0 = kb.psum("pb0", [128, 1024], BF16)
        self.pb1 = kb.psum("pb1", [128, 1024], BF16)
        kb.constv(0.0)
        kb.constr(0.0)
        self._uid = 0

    def uid(self):
        self._uid += 1
        return self._uid


def emit_ln_stats(pr, xsrc, ntiles, D, pfx):
    kb = pr.kb
    nchunk = (D + 511) // 512
    bns = kb.sbuf(f"{pfx}_bns", [128, 6 * nchunk], F32)
    mvb = kb.sbuf(f"{pfx}_mv", [128, 2 * ntiles], F32)
    for t in range(ntiles):
        src = xsrc(t)
        for c in range(nchunk):
            w = min(512, D - c * 512)
            kb.group("DVE", lambda e, src=src, c=c, w=w: e.bn_stats(
                bns[:, c * 6:(c + 1) * 6], src[:, c * 512:c * 512 + w]),
                reads=[src], writes=[bns])
        kb.group("DVE", lambda e, t=t: e.bn_aggr(mvb[:, 2 * t:2 * t + 2], bns[:, :]),
                 reads=[bns], writes=[mvb])
    return mvb


def emit_rstd(pr, var_view, n, eps, pfx, var_reads):
    kb = pr.kb
    rstd = kb.sbuf(f"{pfx}_rstd", [128, n], F32)
    epsap = kb.constv(eps)
    zero = kb.constv(0.0)
    kb.group("ACT", lambda e: e.activation(rstd[:, :], var_view, AF.Ln,
                                           bias=epsap, scale=1.0),
             reads=list(var_reads) + ["constpool"], writes=[rstd])
    kb.group("ACT", lambda e: e.activation(rstd[:, :], rstd[:, :], AF.Exp,
                                           bias=zero, scale=-0.5),
             reads=[rstd, "constpool"], writes=[rstd])
    return rstd


def emit_tm_to_fm(pr, src_fn, ntiles, D, dst, T_dst, col0, pfx,
                  dst_keys=None, src_reads=None):
    kb = pr.kb
    nchunk = D // 128
    assert nchunk <= 8
    pt = [pr.pb0, pr.pb1]
    for t in range(ntiles):
        src = src_fn(t)
        buf = pt[t % 2]
        def tr(e, src=src, buf=buf, nchunk=nchunk):
            last = None
            for c in range(nchunk):
                last = e.transpose(buf[:, c * 128:(c + 1) * 128],
                                   src[:, c * 128:(c + 1) * 128], pr.ident[:, :])
            return last
        rd = [src, pr.ident] + (src_reads(t) if src_reads else [])
        kb.group("PE", tr, reads=rd, writes=[buf])
        def cp(e, buf=buf, t=t, nchunk=nchunk):
            dap = bass.AP(dst, col0 + t * 128,
                          [[dst.shape[1], 128], [T_dst, nchunk], [1, 128]])
            return e.tensor_copy(dap, buf[:, 0:nchunk * 128].rearrange(
                "p (c n) -> p c n", c=nchunk))
        kb.group("DVE", cp, reads=[buf],
                 writes=[(dst.name, "fm", t) if dst_keys is None else dst_keys[t]])


def mm_group(kb, psum_ap, pieces, reads, writes, bias=None):
    def emit(e):
        last = None
        n = len(pieces) + (1 if bias is not None else 0)
        for i, (l, r) in enumerate(pieces):
            last = e.matmul(psum_ap, l, r, start=(i == 0), stop=(i == n - 1))
        if bias is not None:
            l, r = bias
            last = e.matmul(psum_ap, l, r, start=False, stop=True)
        return last
    kb.group("PE", emit, reads=reads, writes=writes)


def load_weight_chunked(kb, wb, dram, din, dout, queue, extra_reads=()):
    nchunk = din // 128
    kb.group("POOL", lambda e: e.dma_start(
        out=wb[:, 0:nchunk * dout].rearrange("p (c n) -> p c n", c=nchunk),
        in_=dram.rearrange("(c p) n -> p c n", p=128)),
        reads=list(extra_reads), writes=[wb], dma=queue)



def emit_attn_pair(pr, p, TQ, TK, qfm, kfm, vtm, ctxfm, scale, cb2, pfx,
                   causal=False, caus_masks=None):
    kb = pr.kb
    nkc = TK // 128
    nqc = TQ // 512
    pbuf = kb.t(pfx + "_pbuf")
    rb = kb.t(pfx + "_rb")
    den = rb
    DW = den.shape[1]
    RW = rb.shape[1]
    for qc in range(nqc):
        if causal:
            qend = (qc + 1) * 512
            kcs = [kc for kc in range(nkc) if kc * 128 < qend]
        else:
            kcs = list(range(nkc))
        n = len(kcs)

        def qk(i):
            kc = kcs[i]
            pX = pr.pA if i % 2 == 0 else pr.pB
            for h01 in range(2):
                hp = slice(h01 * 64, h01 * 64 + 64)
                kb.group("PE", lambda e, kc=kc, qc=qc, pX=pX, hp=hp, h01=h01:
                         e.matmul(pX[:, h01 * 512:(h01 + 1) * 512],
                                  kfm[hp, kc * 128:(kc + 1) * 128],
                                  qfm[hp, qc * 512:(qc + 1) * 512],
                                  start=True, stop=True),
                         reads=[(pfx, "kfm"), (pfx, "qfm")],
                         writes=[(pfx, "sc", i % 2, h01), (pX.name, h01)])

        def ex(i):
            kc = kcs[i]
            pX = pr.pA if i % 2 == 0 else pr.pB
            pt = pbuf[:, (i % 2) * 1024:(i % 2) * 1024 + 1024]
            kb.group("ACT", lambda e, pX=pX, pt=pt: e.activation(
                pt, pX[:, 0:1024], AF.Exp, bias=cb2[:, 2:3], scale=scale),
                reads=[(pfx, "sc", i % 2, 0), (pfx, "sc", i % 2, 1), (pfx, "cb")],
                writes=[(pfx, "pt", i % 2)])
            if causal:
                moff = kc * 128 - qc * 512
                if moff >= 0:
                    m = caus_masks[moff // 128]
                    kb.group("DVE", lambda e, pt=pt, m=m: e.tensor_tensor(
                        pt, pt, m[:, :], ALU.mult),
                        reads=[(pfx, "pt", i % 2), m], writes=[(pfx, "pt", i % 2)])

        def pv(i):
            kc = kcs[i]
            for h01 in range(2):
                pc = pr.pC if h01 == 0 else pr.pD
                kb.group("PE", lambda e, kc=kc, h01=h01, i=i, pc=pc, n=n: e.matmul(
                    pc[0:65, 0:512],
                    vtm[:, (kc * 2 + h01) * 65:(kc * 2 + h01) * 65 + 65],
                    pbuf[:, (i % 2) * 1024 + h01 * 512:
                         (i % 2) * 1024 + (h01 + 1) * 512],
                    start=(i == 0), stop=(i == n - 1)),
                    reads=[(pfx, "pt", i % 2), (pfx, "vtm")],
                    writes=[(pfx, "ctxp", h01)])

        for i in range(n):
            qk(i)
            ex(i)
            if i > 0:
                pv(i - 1)
        pv(n - 1)
        for h01 in range(2):
            pc = pr.pC if h01 == 0 else pr.pD
            kb.group("DVE", lambda e, pc=pc, h01=h01: e.tensor_copy(
                den[64:65, h01 * 512:(h01 + 1) * 512], pc[64:65, 0:512]),
                reads=[(pfx, "ctxp", h01)], writes=[(pfx, "den", h01)])
        kb.group("DVE", lambda e: e.reciprocal(den[64:65, 0:1024],
                                               den[64:65, 0:1024]),
                 reads=[(pfx, "den", 0), (pfx, "den", 1)],
                 writes=[(pfx, "den", 0), (pfx, "den", 1)])
        for h01 in range(2):
            kb.group("SP", lambda e, h01=h01: e.dma_start(
                out=bass.AP(rb, h01 * 512, [[RW, 64], [1, 512]]),
                in_=bass.AP(den, 64 * DW + h01 * 512, [[DW, 1], [0, 64], [1, 512]])),
                reads=[(pfx, "den", 0), (pfx, "den", 1)],
                writes=[(pfx, "rb", h01)], dma=kb.dma_queue(f"SPQ.rb{h01}"))
        for h01 in range(2):
            h = 2 * p + h01
            pc = pr.pC if h01 == 0 else pr.pD
            kb.group("DVE", lambda e, h=h, qc=qc, pc=pc, h01=h01: e.tensor_tensor(
                ctxfm[0:64, h * TQ + qc * 512: h * TQ + (qc + 1) * 512],
                pc[0:64, 0:512], rb[0:64, h01 * 512:(h01 + 1) * 512], ALU.mult),
                reads=[(pfx, "ctxp", h01), (pfx, "rb", h01)],
                writes=[(pfx, "ctx", h, qc)])


def emit_pair_bounds(pr, qfm, kfm, TQ, TK, scale, pfx):
    kb = pr.kb
    mx = kb.t(pfx + "_mx")
    mrow = kb.t(pfx + "_mrow")
    cb2 = kb.t(pfx + "_cb2")
    kb.group("DVE", lambda e: e.tensor_reduce(
        mx[:, 0:1], qfm[:, 0:TQ], axis=mybir.AxisListType.X,
        op=ALU.max, apply_absolute_value=True),
        reads=[(pfx, "qfm")], writes=[mx])
    kb.group("DVE", lambda e: e.tensor_reduce(
        mx[:, 1:2], kfm[:, 0:TK], axis=mybir.AxisListType.X,
        op=ALU.max, apply_absolute_value=True),
        reads=[(pfx, "kfm"), mx], writes=[mx])
    pdkey = (pfx, "ctxp", 1)
    for j in range(2):
        kb.group("PE", lambda e, j=j: e.transpose(
            pr.pD[0:1, j * 128:j * 128 + 128], mx[:, j:j + 1], pr.identf[:, :]),
            reads=[mx, pr.identf], writes=[pdkey])
    for h01 in range(2):
        for j in range(2):
            kb.group("DVE", lambda e, j=j, h01=h01: e.tensor_reduce(
                mrow[0:1, j:j + 1], pr.pD[0:1, j * 128 + h01 * 64:j * 128 + h01 * 64 + 64],
                axis=mybir.AxisListType.X, op=ALU.max),
                reads=[pdkey, mrow], writes=[mrow])
        kb.group("DVE", lambda e, h01=h01: e.tensor_tensor(
            mrow[0:1, 2 + h01:3 + h01], mrow[0:1, 0:1], mrow[0:1, 1:2], ALU.mult),
            reads=[mrow], writes=[mrow])
    kb.group("PE", lambda e: e.matmul(pr.pD[:, 0:2], pr.onesf[0:1, 0:128],
                                      mrow[0:1, 2:4], start=True, stop=True),
             reads=[mrow, pr.onesf], writes=[pdkey])
    kb.group("DVE", lambda e: e.tensor_scalar(
        cb2[:, 0:2], pr.pD[:, 0:2], -scale, None, ALU.mult),
        reads=[pdkey], writes=[(pfx, "cb")])
    kb.group("DVE", lambda e: e.tensor_tensor(
        cb2[:, 2:3], cb2[:, 0:1], cb2[:, 1:2], ALU.min),
        reads=[(pfx, "cb")], writes=[(pfx, "cb")])
    return cb2


def alloc_attn_scratch(pr, pfx):
    kb = pr.kb
    kb.sbuf(pfx + "_pbuf", [128, 2048], BF16)
    kb.sbuf(pfx + "_rb", [128, 1024], F32)
    kb.sbuf(pfx + "_mx", [128, 2], F32)
    kb.sbuf(pfx + "_mrow", [1, 4], F32)
    kb.sbuf(pfx + "_cb2", [128, 4], F32)



ATTN_CFGS = {
    "L0a": dict(npairs=8, wq="combined", qkv_bias=True, prenorm="lnln",
                qk_norm=None, scale=0.125, fixed_bound=None, causal=False,
                out_mode="proj", proj_bias=True, internal_residual=True,
                headsplit=False),
    "L3a": dict(npairs=8, wq="combined", qkv_bias=False, prenorm="ln",
                qk_norm="rms_w", scale=0.125, fixed_bound=None, causal=False,
                out_mode="proj", proj_bias=True, internal_residual=False,
                headsplit=False),
    "L2a": dict(npairs=4, wq="separate", qkv_bias=False, prenorm="ln",
                qk_norm="l2", scale=8.0, fixed_bound=8.0, causal=True,
                out_mode="dump_ctx", proj_bias=False, internal_residual=False,
                headsplit=True),
    "L0f_attn": dict(npairs=8, wq="separate", qkv_bias=False, prenorm="lnrms",
                     qk_norm=None, scale=0.125, fixed_bound=None, causal=False,
                     out_mode="proj", proj_bias=False, internal_residual=True,
                     headsplit=False, sandwich=False),
}


def attn_dram_decls(kb, which, TB=4096, weights_only=False):
    cfg = dict(ATTN_CFGS[which])
    TQ = TB if cfg["headsplit"] else TB // 2
    NP = cfg["npairs"]
    NH = 2 * NP
    D = 1024
    io = {}
    if not weights_only:
        io["xb"] = kb.dram_in("xb", [TB, D], F32)
    if cfg["wq"] == "combined":
        io["WqkvT"] = kb.dram_in("WqkvT", [D, 3 * D], F32)
    else:
        io["WqT"] = kb.dram_in("WqT", [D, NP * 128], F32)
        io["WkT"] = kb.dram_in("WkT", [D, NP * 128], F32)
        io["WvT"] = kb.dram_in("WvT", [D, NP * 128], F32)
    if cfg["qkv_bias"]:
        io["bqkv_cols"] = kb.dram_in("bqkv_cols", [128, 24], F32)
        io["bqkv_row"] = kb.dram_in("bqkv_row", [1, 3 * D], F32)
    if cfg["prenorm"] == "lnln":
        io["gammat"] = kb.dram_in("gammat", [128, D], F32)
        io["betat"] = kb.dram_in("betat", [128, D], F32)
    elif cfg["prenorm"] == "lnrms":
        io["normwt"] = kb.dram_in("normwt", [128, D], F32)
    if cfg["qk_norm"] == "rms_w":
        io["qnormt"] = kb.dram_in("qnormt", [128, 128], F32)
        io["knormt"] = kb.dram_in("knormt", [128, 128], F32)
    if cfg["out_mode"] == "proj":
        io["WoT"] = kb.dram_in("WoT", [D, D], F32)
        if cfg["proj_bias"]:
            io["bout_row"] = kb.dram_in("bout_row", [1, D], F32)
        if not weights_only:
            io["xo"] = kb.dram_out("xo", [TQ, D], F32)
    elif not weights_only:
        io["ctxo"] = kb.dram_out("ctxo", [64, NH * TQ], mybir.dt.bfloat16)
    return io


def emit_attn_atom(pr, io, which, TB=4096, dump_fm=False):
    cfg = dict(ATTN_CFGS[which])
    TQ = TB if cfg["headsplit"] else TB // 2
    NP = cfg["npairs"]
    NH = 2 * NP
    nc = pr.nc
    kb = pr.kb
    D = 1024
    NBT = TB // 128
    NQT = TQ // 128
    xb = io["xb"]
    if cfg["wq"] == "combined":
        WqkvT = io["WqkvT"]
        wsrc = lambda j: WqkvT
        woffs = (0, D, 2 * D)
    else:
        WqT, WkT, WvT = io["WqT"], io["WkT"], io["WvT"]
        wsrc = lambda j: (WqT, WkT, WvT)[j]
        woffs = (0, 0, 0)
    if cfg["qkv_bias"]:
        bqkv_cols = io["bqkv_cols"]
        bqkv_row = io["bqkv_row"]
    if cfg["prenorm"] == "lnln":
        gammat = io["gammat"]
        betat = io["betat"]
    elif cfg["prenorm"] == "lnrms":
        normwt = io["normwt"]
    if cfg["qk_norm"] == "rms_w":
        qnormt = io["qnormt"]
        knormt = io["knormt"]
    if cfg["out_mode"] == "proj":
        WoT = io["WoT"]
        if cfg["proj_bias"]:
            bout_row = io["bout_row"]
        xo = io["xo"]
    else:
        ctxo = io["ctxo"]

    alloc_attn_scratch(pr, "at")
    hT_off = nc.sbuf_top - max(16 * TB, 32 * 1024)
    ctx_off = hT_off - 2 * NH * TQ
    hT = kb.sbuf_at("hT", [128, 8 * TB], BF16, hT_off)
    ctxfm = kb.sbuf_at("ctxfm", [64, NH * TQ], BF16, ctx_off)
    qfm = kb.sbuf("qfm", [128, TQ], BF16)
    kfm = kb.sbuf("kfm", [128, TB], BF16)
    vtm = kb.sbuf("vtm", [128, (TB // 128) * 2 * 65], BF16)
    wpair = [kb.sbuf(f"wpair{i}", [128, 8 * 384], BF16) for i in range(2)]
    if cfg["out_mode"] == "proj":
        wo = kb.sbuf_at("wo", [64, 16 * 1024], BF16, hT_off)
        bor = kb.sbuf("bor", [1, D], BF16)
    if cfg["qkv_bias"]:
        bqc = kb.sbuf("bqc", [128, 24], F32)
        bqr = kb.sbuf("bqr", [1, D], BF16)
    gmt = kb.sbuf("gmt", [128, D], BF16)
    btt = kb.sbuf("btt", [128, D], BF16)
    qnt = kb.sbuf("qnt", [128, 128], BF16)
    knt = kb.sbuf("knt", [128, 128], BF16)
    xrb = [kb.sbuf(f"xrb{i}", [128, D], F32) for i in range(2)]
    xnb = [kb.sbuf(f"xnb{i}", [128, D], BF16) for i in range(2)]
    if cfg["out_mode"] == "proj":
        ybuf = kb.sbuf("ybuf", [128, D], F32)
        obuf = [kb.sbuf(f"obuf{i}", [128, D], F32) for i in range(2)]
    qtmb = kb.sbuf("qtmb", [128, 128], BF16)
    sqb = kb.sbuf("sqb", [128, 128], F32)
    nsum = kb.sbuf("nsum", [128, 4], F32)
    mvball = kb.sbuf("mvball", [128, 2 * NBT], F32)
    r1all = kb.sbuf("r1all", [128, NBT], F32)
    cball = kb.sbuf("cball", [128, NBT], F32)
    sc1 = kb.sbuf("sc1", [128, 4], F32)
    bns = kb.sbuf("bns", [128, 12], F32)
    mv2 = kb.sbuf("mv2", [128, 2], F32)
    cmasks = [kb.sbuf(f"cmask{m}", [128, 1024], BF16) for m in range(4)] \
        if cfg["causal"] else None

    if cfg["qkv_bias"]:
        kb.group("POOL", lambda e: e.dma_start(out=bqc[:, :], in_=bqkv_cols[:, :]),
                 writes=[bqc], dma="POOLQ")
        kb.group("POOL", lambda e: e.dma_start(out=bqr[:, :],
                                               in_=bqkv_row[0:1, 2 * D:3 * D]),
                 writes=[bqr], dma="POOLQ")
    if cfg["out_mode"] == "proj" and cfg["proj_bias"]:
        kb.group("POOL", lambda e: e.dma_start(out=bor[:, :], in_=bout_row[:, :]),
                 writes=[bor], dma="POOLQ")
    elif cfg["out_mode"] == "proj":
        kb.group("POOL", lambda e: e.memset(bor[:, :], 0.0), writes=[bor])
    if cfg["prenorm"] == "lnln":
        kb.group("POOL", lambda e: e.dma_start(out=gmt[:, :], in_=gammat[:, :]),
                 writes=[gmt], dma="POOLQ")
        kb.group("POOL", lambda e: e.dma_start(out=btt[:, :], in_=betat[:, :]),
                 writes=[btt], dma="POOLQ")
    elif cfg["prenorm"] == "lnrms":
        kb.group("POOL", lambda e: e.dma_start(out=gmt[:, :], in_=normwt[:, :]),
                 writes=[gmt], dma="POOLQ")
    if cfg["qk_norm"] == "rms_w":
        kb.group("POOL", lambda e: e.dma_start(out=qnt[:, :], in_=qnormt[:, :]),
                 writes=[qnt], dma="POOLQ")
        kb.group("POOL", lambda e: e.dma_start(out=knt[:, :], in_=knormt[:, :]),
                 writes=[knt], dma="POOLQ")
    kb.group("POOL", lambda e: e.memset(
        bass.AP(vtm, 64, [[vtm.shape[1], 128], [65, (TB // 128) * 2], [1, 1]]), 1.0),
        writes=["vtm_ones"])
    if cfg["causal"]:
        for m in range(4):
            kb.group("POOL", lambda e, m=m: e.memset(cmasks[m][:, :], 1.0),
                     writes=[cmasks[m]])
            for hf in range(2):
                kb.group("POOL", lambda e, m=m, hf=hf: e.affine_select(
                    out=cmasks[m][:, hf * 512:(hf + 1) * 512],
                    in_=cmasks[m][:, hf * 512:(hf + 1) * 512],
                    compare_op=ALU.is_ge,
                    fill=0.0, base=-128 * m, pattern=[[1, 512]],
                    channel_multiplier=-1),
                    reads=[cmasks[m]], writes=[cmasks[m]])
    if cfg["fixed_bound"] is not None:
        cb2t = kb.t("at_cb2")
        kb.group("POOL", lambda e, cb2t=cb2t: e.memset(
            cb2t[:, :], -cfg["fixed_bound"]),
            writes=[("at", "cb")])

    eps1 = kb.constv(1e-5)
    zero = kb.constv(0.0)

    def prep_tile(t):
        xt = xrb[t % 2]
        kb.group("SP", lambda e, t=t, xt=xt: e.dma_start(
            out=xt[:, :], in_=xb[t * 128:(t + 1) * 128, :]),
            writes=[xt], dma="SPQ")
        kb.group("DVE", lambda e, xt=xt: e.bn_stats(bns[:, 0:6], xt[:, 0:512]),
                 reads=[xt], writes=[bns])
        kb.group("DVE", lambda e, xt=xt: e.bn_stats(bns[:, 6:12], xt[:, 512:1024]),
                 reads=[xt, bns], writes=[bns])
        kb.group("DVE", lambda e, t=t: e.bn_aggr(mvball[:, 2 * t:2 * t + 2], bns[:, :]),
                 reads=[bns], writes=[mvball])
        kb.group("ACT", lambda e, t=t: e.activation(
            r1all[:, t:t + 1], mvball[:, 2 * t + 1:2 * t + 2], AF.Ln,
            bias=eps1, scale=1.0), reads=[mvball, "constpool"], writes=[r1all])
        kb.group("ACT", lambda e, t=t: e.activation(
            r1all[:, t:t + 1], r1all[:, t:t + 1], AF.Exp, bias=zero, scale=-0.5),
            reads=[r1all, "constpool"], writes=[r1all])
        if cfg["prenorm"] in ("lnln", "lnrms"):
            eps2 = eps1 if cfg["prenorm"] == "lnln" else kb.constv(1e-6)
            kb.group("DVE", lambda e, t=t: e.tensor_tensor(
                sc1[:, 0:1], mvball[:, 2 * t + 1:2 * t + 2], r1all[:, t:t + 1], ALU.mult),
                reads=[mvball, r1all], writes=[sc1])
            kb.group("DVE", lambda e, t=t: e.tensor_tensor(
                sc1[:, 1:2], sc1[:, 0:1], r1all[:, t:t + 1], ALU.mult),
                reads=[sc1, r1all], writes=[sc1])
            kb.group("ACT", lambda e, eps2=eps2: e.activation(
                sc1[:, 2:3], sc1[:, 1:2], AF.Ln, bias=eps2, scale=1.0),
                reads=[sc1, "constpool"], writes=[sc1])
            kb.group("ACT", lambda e: e.activation(sc1[:, 2:3], sc1[:, 2:3], AF.Exp,
                                                   bias=zero, scale=-0.5),
                     reads=[sc1, "constpool"], writes=[sc1])
            kb.group("DVE", lambda e, t=t: e.tensor_tensor(
                cball[:, t:t + 1], r1all[:, t:t + 1], sc1[:, 2:3], ALU.mult),
                reads=[sc1, r1all], writes=[cball])
            comb = cball
        else:
            comb = r1all
        kb.group("DVE", lambda e, t=t, xt=xt, comb=comb: e.tensor_scalar(
            xt[:, :], xt[:, :], mvball[:, 2 * t:2 * t + 1], comb[:, t:t + 1],
            ALU.subtract, ALU.mult), reads=[xt, mvball, comb], writes=[xt])
        xn = xnb[t % 2]
        if cfg["prenorm"] == "lnln":
            kb.group("DVE", lambda e, xt=xt, xn=xn: e.scalar_tensor_tensor(
                xn[:, :], xt[:, :], 1.0, gmt[:, :], ALU.mult, ALU.mult),
                reads=[xt, gmt], writes=[xn])
            kb.group("DVE", lambda e, xn=xn: e.tensor_tensor(
                xn[:, :], xn[:, :], btt[:, :], ALU.add), reads=[xn, btt], writes=[xn])
        elif cfg["prenorm"] == "lnrms":
            kb.group("DVE", lambda e, xt=xt, xn=xn: e.scalar_tensor_tensor(
                xn[:, :], xt[:, :], 1.0, gmt[:, :], ALU.mult, ALU.mult),
                reads=[xt, gmt], writes=[xn])
        else:
            kb.group("DVE", lambda e, xt=xt, xn=xn: e.tensor_copy(xn[:, :], xt[:, :]),
                     reads=[xt], writes=[xn])
        return xn[:, :]

    emit_tm_to_fm(pr, prep_tile, NBT, D, hT, TB, 0, "tf")

    hT_keys = [(hT.name, "fm", t) for t in range(NBT)]

    def qk_tm_norm(p, wp, woff, wbase, dst_fm, Ttok, wnt, fmkey):
        for tt in range(Ttok // 128):
            slot = tt % 2
            ps = pr.pB[:, slot * 512:slot * 512 + 128]
            pskey = (pr.pB.name, slot)
            mm_group(kb, ps,
                     [(hT[:, c * TB + tt * 128: c * TB + (tt + 1) * 128],
                       wp[:, c * 384 + wbase: c * 384 + wbase + 128])
                      for c in range(8)],
                     reads=[(wp.name, woff)] + hT_keys, writes=[pskey])
            kb.group("ACT", lambda e, ps=ps: e.activation(
                sqb[:, :], ps, AF.Square, bias=zero, scale=1.0),
                reads=[pskey, "constpool"], writes=[sqb])
            kb.group("DVE", lambda e: e.tensor_reduce(
                nsum[:, 0:2], sqb[:, :].rearrange("p (h d) -> p h d", h=2),
                axis=mybir.AxisListType.X, op=ALU.add),
                reads=[sqb], writes=[nsum])
            if cfg["qk_norm"] == "l2":
                kb.group("DVE", lambda e: e.tensor_scalar(
                    nsum[:, 0:2], nsum[:, 0:2], 1e-24, None, ALU.max),
                    reads=[nsum], writes=[nsum])
                lnscale, lnbias = 1.0, zero
            else:
                lnscale, lnbias = 1.0 / 64.0, kb.constv(1e-6)
            kb.group("ACT", lambda e, lnscale=lnscale, lnbias=lnbias: e.activation(
                nsum[:, 2:4], nsum[:, 0:2], AF.Ln, bias=lnbias, scale=lnscale),
                reads=[nsum, "constpool"], writes=[nsum])
            kb.group("ACT", lambda e: e.activation(
                nsum[:, 2:4], nsum[:, 2:4], AF.Exp, bias=zero, scale=-0.5),
                reads=[nsum, "constpool"], writes=[nsum])
            for h01 in range(2):
                kb.group("DVE", lambda e, ps=ps, h01=h01: e.tensor_scalar(
                    qtmb[:, h01 * 64:(h01 + 1) * 64], ps[:, h01 * 64:(h01 + 1) * 64],
                    nsum[:, 2 + h01:3 + h01], None, ALU.mult),
                    reads=[pskey, nsum, qtmb], writes=[qtmb])
            if cfg["qk_norm"] == "rms_w":
                kb.group("DVE", lambda e, wnt=wnt: e.tensor_tensor(
                    qtmb[:, :], qtmb[:, :], wnt[:, :], ALU.mult),
                    reads=[qtmb, wnt], writes=[qtmb])
            kb.group("PE", lambda e: e.transpose(pr.pb0[:, 0:128], qtmb[:, :],
                                                 pr.ident[:, :]),
                     reads=[qtmb, pr.ident], writes=[pr.pb0])
            kb.group("DVE", lambda e, tt=tt, dst_fm=dst_fm: e.tensor_copy(
                dst_fm[:, tt * 128:(tt + 1) * 128], pr.pb0[:, 0:128]),
                reads=[pr.pb0], writes=[fmkey])

    for p in range(NP):
        wp = wpair[p % 2]
        q = kb.dma_queue(f"POOLQ.w{p % 2}")
        for j in range(3):
            kb.group("POOL", lambda e, wp=wp, j=j, p=p: e.dma_start(
                out=bass.AP(wp, j * 128, [[wp.shape[1], 128], [384, 8], [1, 128]]),
                in_=wsrc(j)[:, woffs[j] + p * 128: woffs[j] + (p + 1) * 128].rearrange(
                    "(c p) n -> p c n", p=128)),
                writes=[(wp.name, j)], dma=q)
        if cfg["qk_norm"] is None:
            for qc in range(TQ // 512):
                ps = pr.pB[:, (qc % 2) * 512:(qc % 2) * 512 + 512]
                pskey = (pr.pB.name, qc % 2)
                mm_group(kb, ps,
                         [(wp[:, c * 384: c * 384 + 128],
                           hT[:, c * TB + qc * 512: c * TB + (qc + 1) * 512])
                          for c in range(8)],
                         reads=[(wp.name, 0)] + hT_keys, writes=[pskey])
                bias_ap = bqc[:, p:p + 1] if cfg["qkv_bias"] else zero
                kb.group("ACT", lambda e, qc=qc, ps=ps, bias_ap=bias_ap: e.activation(
                    qfm[:, qc * 512:(qc + 1) * 512], ps, AF.Identity,
                    bias=bias_ap, scale=1.0),
                    reads=[pskey, bqc if cfg["qkv_bias"] else "constpool"],
                    writes=[("at", "qfm")])
            for kc in range(TB // 512):
                ps = pr.pB[:, (kc % 2) * 512:(kc % 2) * 512 + 512]
                pskey = (pr.pB.name, kc % 2)
                mm_group(kb, ps,
                         [(wp[:, c * 384 + 128: c * 384 + 256],
                           hT[:, c * TB + kc * 512: c * TB + (kc + 1) * 512])
                          for c in range(8)],
                         reads=[(wp.name, 1)] + hT_keys, writes=[pskey])
                bias_ap = bqc[:, 8 + p:9 + p] if cfg["qkv_bias"] else zero
                kb.group("ACT", lambda e, kc=kc, ps=ps, bias_ap=bias_ap: e.activation(
                    kfm[:, kc * 512:(kc + 1) * 512], ps, AF.Identity,
                    bias=bias_ap, scale=1.0),
                    reads=[pskey, bqc if cfg["qkv_bias"] else "constpool"],
                    writes=[("at", "kfm")])
        else:
            qk_tm_norm(p, wp, 0, 0, qfm, TQ, qnt, ("at", "qfm"))
            qk_tm_norm(p, wp, 1, 128, kfm, TB, knt, ("at", "kfm"))
        for kt in range(TB // 128):
            slot = kt % 2
            ps = pr.pB[:, slot * 512:slot * 512 + 128]
            pskey = (pr.pB.name, slot)
            vbias = (pr.ones_row[0:1, 0:128],
                     bqr[0:1, p * 128: (p + 1) * 128]) \
                if cfg["qkv_bias"] else None
            vb_reads = [bqr, pr.ones_row] if cfg["qkv_bias"] else []
            mm_group(kb, ps,
                     [(hT[:, c * TB + kt * 128: c * TB + (kt + 1) * 128],
                       wp[:, c * 384 + 256: c * 384 + 384]) for c in range(8)],
                     reads=[(wp.name, 2)] + hT_keys + vb_reads,
                     writes=[pskey], bias=vbias)
            def vev(e, kt=kt, ps=ps):
                dap = bass.AP(vtm, kt * 2 * 65,
                              [[vtm.shape[1], 128], [65, 2], [1, 64]])
                return e.tensor_copy(dap, ps.rearrange("p (h n) -> p h n", h=2))
            kb.group("DVE", vev, reads=[pskey, "vtm_ones"], writes=[("at", "vtm")])
        if cfg["fixed_bound"] is None:
            cb2 = emit_pair_bounds(pr, qfm, kfm, TQ, TB, cfg["scale"], "at")
        else:
            cb2 = cb2t
        emit_attn_pair(pr, p, TQ, TB, qfm, kfm, vtm, ctxfm, cfg["scale"], cb2, "at",
                       causal=cfg["causal"], caus_masks=cmasks)

    ctx_keys = [("at", "ctx", h, qc) for h in range(NH) for qc in range(TQ // 512)]
    if cfg["out_mode"] == "dump_ctx":
        if not dump_fm:
            kb.group("SP", lambda e: e.dma_start(out=ctxo, in_=ctxfm[0:64, :]),
                     reads=ctx_keys, writes=["ctxo"], dma="SPQ")
        else:
            def cdma(e):
                ct = ctxo.tensor if hasattr(ctxo, "tensor") else ctxo
                dap = bass.AP(ct, 0, [[TQ, 64], [64 * TQ, NH], [1, TQ]])
                return e.dma_start(out=dap, in_=ctxfm[0:64, :].rearrange(
                    "p (h n) -> p h n", h=NH))
            kb.group("SP", cdma, reads=ctx_keys, writes=["ctxo"], dma="SPQ")
        return

    hT_dead = hT_keys + ["wo"]
    kb.group("POOL", lambda e: e.dma_start(
        out=wo[0:64, 0:NH * 1024].rearrange("p (c n) -> p c n", c=NH),
        in_=WoT[0:64 * NH, :].rearrange("(c p) n -> p c n", p=64)),
        reads=[], writes=hT_dead, dma="POOLQ")
    for t in range(NQT):
        ob = obuf[t % 2]
        for half in range(2):
            ps = pr.pA[:, half * 512:half * 512 + 512]
            pskey = (pr.pA.name, half)
            mm_group(kb, ps,
                     [(ctxfm[0:64, c * TQ + t * 128: c * TQ + (t + 1) * 128],
                       wo[0:64, c * 1024 + half * 512: c * 1024 + (half + 1) * 512])
                      for c in range(NH)],
                     reads=["wo", bor, pr.ones_row] + ctx_keys, writes=[pskey],
                     bias=(pr.ones_row[0:1, 0:128],
                           bor[0:1, half * 512:(half + 1) * 512]))
            kb.group("DVE", lambda e, half=half, ps=ps, ob=ob: e.tensor_copy(
                ob[:, half * 512:(half + 1) * 512], ps),
                reads=[pskey], writes=[ob])
        xt = xrb[t % 2]
        kb.group("SP", lambda e, t=t, xt=xt: e.dma_start(
            out=xt[:, :], in_=xb[t * 128:(t + 1) * 128, :]),
            writes=[xt], dma="SPQ")
        if cfg["internal_residual"]:
            kb.group("DVE", lambda e, t=t, xt=xt: e.tensor_scalar(
                ybuf[:, :], xt[:, :], mvball[:, 2 * t:2 * t + 1],
                r1all[:, t:t + 1], ALU.subtract, ALU.mult),
                reads=[xt, mvball, r1all], writes=[ybuf])
            kb.group("DVE", lambda e, ob=ob: e.tensor_tensor(
                ob[:, :], ob[:, :], ybuf[:, :], ALU.add),
                reads=[ob, ybuf], writes=[ob])
        if not cfg.get("sandwich", True):
            kb.group("SP", lambda e, t=t, ob=ob: e.dma_start(
                out=xo[t * 128:(t + 1) * 128, :], in_=ob[:, :]),
                reads=[ob], writes=[("xo", t)], dma="SPQ")
            continue
        for c in range(2):
            kb.group("DVE", lambda e, c=c, ob=ob: e.bn_stats(
                bns[:, c * 6:(c + 1) * 6], ob[:, c * 512:(c + 1) * 512]),
                reads=[ob, bns], writes=[bns])
        kb.group("DVE", lambda e: e.bn_aggr(mv2[:, :], bns[:, 0:12]),
                 reads=[bns], writes=[mv2])
        kb.group("ACT", lambda e: e.activation(mv2[:, 1:2], mv2[:, 1:2],
                                               AF.Ln, bias=eps1, scale=1.0),
                 reads=[mv2, "constpool"], writes=[mv2])
        kb.group("ACT", lambda e: e.activation(mv2[:, 1:2], mv2[:, 1:2],
                                               AF.Exp, bias=zero, scale=-0.5),
                 reads=[mv2, "constpool"], writes=[mv2])
        kb.group("DVE", lambda e, ob=ob: e.tensor_scalar(
            ob[:, :], ob[:, :], mv2[:, 0:1], mv2[:, 1:2],
            ALU.subtract, ALU.mult), reads=[ob, mv2], writes=[ob])
        kb.group("DVE", lambda e, ob=ob, xt=xt: e.tensor_tensor(
            ob[:, :], ob[:, :], xt[:, :], ALU.add),
            reads=[ob, xt], writes=[ob])
        kb.group("SP", lambda e, t=t, ob=ob: e.dma_start(
            out=xo[t * 128:(t + 1) * 128, :], in_=ob[:, :]),
            reads=[ob], writes=[("xo", t)], dma="SPQ")
    return


def build_attn_atom(which, TB=4096):
    nc = bass.Bass("TRN2", target_bir_lowering=False)
    pr = Prog(nc)
    io = attn_dram_decls(pr.kb, which, TB)
    emit_attn_atom(pr, io, which, TB)
    pr.kb.barrier_on("SP")
    pr.kb.finalize()
    return nc



def build_L0a(TB=4096, prenorm="lnln"):
    return build_attn_atom("L0a", TB=TB)


def emit_L2a_proj(pr, io, T=2048):
    nc = pr.nc
    kb = pr.kb
    D = 1024
    NT = T // 128
    x = io["x"]
    ctxin = io["ctxin"]
    WoD = io["WoD"]
    xo = io["xo"]
    cfm = kb.sbuf("cfm", [128, 8 * T], BF16)
    wob = kb.sbuf("wob", [128, 8 * D], BF16)
    xrb = [kb.sbuf(f"xrb{i}", [128, D], F32) for i in range(2)]
    obuf = [kb.sbuf(f"obuf{i}", [128, D], F32) for i in range(2)]
    bns = kb.sbuf("bns", [128, 12], F32)
    mv2 = kb.sbuf("mv2", [128, 2], F32)
    eps1 = kb.constv(1e-5)
    zero = kb.constv(0.0)
    kb.group("SP", lambda e: e.dma_start(
        out=cfm[:, :].rearrange("p (c n) -> p c n", c=8),
        in_=ctxin.rearrange("(c p) n -> p c n", p=128)), writes=[cfm], dma="SPQ")
    load_weight_chunked(kb, wob, WoD, D, D, "POOLQ")
    for t in range(NT):
        ob = obuf[t % 2]
        xt = xrb[t % 2]
        kb.group("SP", lambda e, t=t, xt=xt: e.dma_start(
            out=xt[:, :], in_=x[t * 128:(t + 1) * 128, :]),
            writes=[xt], dma="SPQ")
        for half in range(2):
            ps = pr.pA[:, half * 512:half * 512 + 512]
            pskey = (pr.pA.name, half)
            mm_group(kb, ps,
                     [(cfm[:, c * T + t * 128: c * T + (t + 1) * 128],
                       wob[:, c * D + half * 512: c * D + (half + 1) * 512])
                      for c in range(8)],
                     reads=[cfm, wob], writes=[pskey])
            kb.group("DVE", lambda e, half=half, ps=ps, ob=ob: e.tensor_copy(
                ob[:, half * 512:(half + 1) * 512], ps),
                reads=[pskey], writes=[ob])
        for c in range(2):
            kb.group("DVE", lambda e, c=c, ob=ob: e.bn_stats(
                bns[:, c * 6:(c + 1) * 6], ob[:, c * 512:(c + 1) * 512]),
                reads=[ob, bns], writes=[bns])
        kb.group("DVE", lambda e: e.bn_aggr(mv2[:, :], bns[:, 0:12]),
                 reads=[bns], writes=[mv2])
        kb.group("ACT", lambda e: e.activation(mv2[:, 1:2], mv2[:, 1:2],
                                               AF.Ln, bias=eps1, scale=1.0),
                 reads=[mv2, "constpool"], writes=[mv2])
        kb.group("ACT", lambda e: e.activation(mv2[:, 1:2], mv2[:, 1:2],
                                               AF.Exp, bias=zero, scale=-0.5),
                 reads=[mv2, "constpool"], writes=[mv2])
        kb.group("DVE", lambda e, ob=ob: e.tensor_scalar(
            ob[:, :], ob[:, :], mv2[:, 0:1], mv2[:, 1:2],
            ALU.subtract, ALU.mult), reads=[ob, mv2], writes=[ob])
        kb.group("DVE", lambda e, ob=ob, xt=xt: e.tensor_tensor(
            ob[:, :], ob[:, :], xt[:, :], ALU.add),
            reads=[ob, xt], writes=[ob])
        kb.group("SP", lambda e, t=t, ob=ob: e.dma_start(
            out=xo[t * 128:(t + 1) * 128, :], in_=ob[:, :]),
            reads=[ob], writes=[("xo", t)], dma="SPQ")
    return





def build_L2a_proj(T=2048):
    D = 1024
    nc = bass.Bass("TRN2", target_bir_lowering=False)
    pr = Prog(nc)
    kb = pr.kb
    io = {}
    io["x"] = kb.dram_in("x", [T, D], F32)
    io["ctxin"] = kb.dram_in("ctxin", [D, T], mybir.dt.bfloat16)
    io["WoD"] = kb.dram_in("WoD", [D, D], F32)
    io["xo"] = kb.dram_out("xo", [T, D], F32)
    emit_L2a_proj(pr, io, T=T)
    kb.barrier_on("SP")
    kb.finalize()
    return nc

def emit_L1a(pr, io, T=4096, fm_out=False):
    HK, HV, NHL = 256, 512, 4
    C = 128
    NCH = T // C
    NG = NCH // 8
    D = 1024
    nc = pr.nc
    kb = pr.kb
    xb = io["xb"]
    normwt = io["normwt"]
    qwT = io["qwT"]
    kwT = io["kwT"]
    vwT = io["vwT"]
    gwT = io["gwT"]
    bwT = io["bwT"]
    awT = io["awT"]
    qconvt = io["qconvt"]
    kconvt = io["kconvt"]
    vconvt = io["vconvt"]
    alog = io["alog"]
    dtb = io["dtb"]
    onormt = io["onormt"]
    og = io["og"]

    hT = kb.sbuf_at("hT", [128, 8 * T], BF16, nc.sbuf_top - 16 * T)
    rhsb2 = kb.sbuf_at("rhsb2", [128, 2 * HV], BF16, nc.sbuf_top - 16 * T - 2 * 2 * HV) \
        if False else None
    nwt = kb.sbuf("nwt", [128, D], BF16)
    onb = kb.sbuf("onb", [128, HV], F32)
    wqh = kb.sbuf("wqh", [128, 8 * HK], BF16)
    wkh = kb.sbuf("wkh", [128, 8 * HK], BF16)
    wvh = kb.sbuf("wvh", [128, 8 * HV], BF16)
    wgh = kb.sbuf("wgh", [128, 8 * HV], BF16)
    wba = kb.sbuf("wba", [128, 8 * 2 * NHL], BF16)
    cvt = kb.sbuf("cvt", [128, NHL * 16 * 4], F32)
    TRW = 9 * 128
    qraw = kb.sbuf("qraw", [128, 2 * TRW], BF16)
    kraw = kb.sbuf("kraw", [128, 2 * TRW], BF16)
    vraw = kb.sbuf("vraw", [128, 4 * TRW], BF16)
    TG = 1024
    qc = kb.sbuf("qcb", [128, 2 * TG], F32)
    kcb = qc
    vcb = kb.sbuf("vcb", [128, 2 * TG], F32)
    qcf = kb.sbuf("qcf", [128, 2 * TG], BF16)
    kcf = kb.sbuf("kcf", [128, 2 * TG], BF16)
    vcf = kb.sbuf("vcf", [128, 4 * TG], BF16)
    ktm = kb.sbuf("ktm", [128, 8 * HK], BF16)
    vtmg = kb.sbuf("vtmg", [128, 8 * HV], BF16)
    rrow = kb.sbuf("rrow", [1, TG], F32)
    g_all = kb.sbuf("g_all", [128, NHL * NCH], F32)
    beta_all = kb.sbuf("beta_all", [128, NHL * NCH], F32)
    G_all = kb.sbuf("G_all", [128, NHL * NCH], F32)
    egt_all = kb.sbuf("egt_all", [128, NHL * NCH], F32)
    nbeg_all = kb.sbuf("nbeg_all", [128, NHL * NCH], F32)
    egc_b = kb.sbuf("egc_b", [128, NCH], F32)
    gc_b = kb.sbuf("gc_b", [128, NCH], F32)
    growb = kb.sbuf("growb", [1, 2 * C], F32)
    dtl = kb.sbuf("dtl", [128, 8 * C], BF16)
    dst = kb.sbuf("dst", [128, 8 * C], BF16)
    qkd = kb.sbuf("qkd", [128, 8 * C], BF16)
    Xb = kb.sbuf("Xb", [128, 8 * C], BF16)
    Atb = kb.sbuf("Atb", [128, 8 * C], BF16)
    Wb2 = kb.sbuf("Wb2", [128, 8 * C], BF16)
    xtb = kb.sbuf("xtb", [128, C], BF16)
    Ttb = kb.sbuf("Ttb", [128, 8 * C], BF16)
    ident2 = kb.sbuf("ident2", [128, C], BF16)
    S32 = kb.sbuf("S32", [128, 2 * HV], F32)
    Sbf = kb.sbuf("Sbf", [128, 2 * HV], BF16)
    rhsb = kb.sbuf("rhsb", [128, HV], BF16)
    ubf = kb.sbuf("ubf", [128, HV], BF16)
    obuf = kb.sbuf("obuf", [128, HV], F32)
    o2b = kb.sbuf("o2b", [128, HV], F32)
    sgb = kb.sbuf("sgb", [128, HV], F32)
    ogb = kb.sbuf("ogb", [128, 2 * HV], BF16)
    ogfmb = kb.sbuf("ogfmb", [128, 512], BF16)
    kdec = kb.sbuf("kdec", [128, HK], BF16)
    scc = kb.sbuf("scc", [128, 8], F32)
    narow = kb.sbuf("narow", [1, 2 * NHL], F32)
    nab = kb.sbuf("nab", [128, NHL], F32)
    spb = [kb.sbuf(f"spb{i}", [128, NHL], F32) for i in range(3)]
    xrb = [kb.sbuf("xrb0", [128, D], F32)] * 2
    xnb = [kb.sbuf("xnb0", [128, D], BF16)] * 2
    mvball = kb.sbuf("mvball", [128, 2 * NCH], F32)
    r1all = kb.sbuf("r1all", [128, NCH], F32)
    cball = kb.sbuf("cball", [128, NCH], F32)
    sc1 = kb.sbuf("sc1", [128, 4], F32)
    bns = kb.sbuf("bns", [128, 12], F32)
    eps1 = kb.constv(1e-5)
    eps6 = kb.constv(1e-6)
    one1 = kb.constv(1.0)
    zero = kb.constv(0.0)

    kb.group("POOL", lambda e: e.dma_start(out=nwt[:, :], in_=normwt[:, :]),
             writes=[nwt], dma="POOLQ")
    kb.group("POOL", lambda e: e.dma_start(out=onb[:, :], in_=onormt[:, :]),
             writes=[onb], dma="POOLQ")
    kb.group("POOL", lambda e: e.dma_start(out=cvt[:, 0:32], in_=qconvt[:, :]),
             writes=["cv_q"], dma="POOLQ")
    kb.group("POOL", lambda e: e.dma_start(out=cvt[:, 32:64], in_=kconvt[:, :]),
             writes=["cv_k"], dma="POOLQ")
    kb.group("POOL", lambda e: e.dma_start(out=cvt[:, 64:128], in_=vconvt[:, :]),
             writes=["cv_v"], dma="POOLQ")
    kb.group("POOL", lambda e: e.dma_start(
        out=wba[:, 0:8 * NHL].rearrange("p (c n) -> p c n", c=8),
        in_=bwT.rearrange("(c p) n -> p c n", p=128)), writes=["wba_b"], dma="POOLQ")
    kb.group("POOL", lambda e: e.dma_start(
        out=wba[:, 8 * NHL:16 * NHL].rearrange("p (c n) -> p c n", c=8),
        in_=awT.rearrange("(c p) n -> p c n", p=128)), writes=["wba_a"], dma="POOLQ")
    kb.group("POOL", lambda e: e.dma_start(out=narow[0:1, 0:NHL], in_=alog[:, :]),
             writes=[narow], dma="POOLQ")
    kb.group("POOL", lambda e: e.dma_start(out=narow[0:1, NHL:2 * NHL], in_=dtb[:, :]),
             writes=[narow], dma="POOLQ")
    kb.group("POOL", lambda e: e.memset(ident2[:, :], 0.0), writes=[ident2])
    kb.group("POOL", lambda e: e.affine_select(
        out=ident2[:, :], in_=ident2[:, :], compare_op=ALU.not_equal, fill=2.0,
        base=0, pattern=[[-1, C]], channel_multiplier=1),
        reads=[ident2], writes=[ident2])
    zeror = kb.constr(0.0)
    kb.group("ACT", lambda e, zeror=zeror: e.activation(
        narow[0:1, 0:NHL], narow[0:1, 0:NHL], AF.Exp, bias=zeror, scale=1.0),
        reads=[narow, "constrow"], writes=[narow])
    kb.group("PE", lambda e: e.matmul(pr.pD[:, 0:2 * NHL], pr.onesf[0:1, 0:128],
                                      narow[0:1, 0:2 * NHL], start=True, stop=True),
             reads=[narow, pr.onesf], writes=[pr.pD])
    kb.group("DVE", lambda e: e.tensor_scalar(nab[:, :], pr.pD[:, 0:NHL], -1.0,
                                              None, ALU.mult),
             reads=[pr.pD], writes=[nab])
    kb.group("DVE", lambda e: e.tensor_copy(spb[2][:, :], pr.pD[:, NHL:2 * NHL]),
             reads=[pr.pD], writes=["dtb_b"])

    def prep_tile(t):
        xt = xrb[t % 2]
        kb.group("SP", lambda e, t=t, xt=xt: e.dma_start(
            out=xt[:, :], in_=xb[t * 128:(t + 1) * 128, :]),
            writes=[xt], dma="SPQ")
        kb.group("DVE", lambda e, xt=xt: e.bn_stats(bns[:, 0:6], xt[:, 0:512]),
                 reads=[xt], writes=[bns])
        kb.group("DVE", lambda e, xt=xt: e.bn_stats(bns[:, 6:12], xt[:, 512:1024]),
                 reads=[xt, bns], writes=[bns])
        kb.group("DVE", lambda e, t=t: e.bn_aggr(mvball[:, 2 * t:2 * t + 2], bns[:, :]),
                 reads=[bns], writes=[mvball])
        kb.group("ACT", lambda e, t=t: e.activation(
            r1all[:, t:t + 1], mvball[:, 2 * t + 1:2 * t + 2], AF.Ln,
            bias=eps1, scale=1.0), reads=[mvball, "constpool"], writes=[r1all])
        kb.group("ACT", lambda e, t=t: e.activation(
            r1all[:, t:t + 1], r1all[:, t:t + 1], AF.Exp, bias=zero, scale=-0.5),
            reads=[r1all, "constpool"], writes=[r1all])
        kb.group("DVE", lambda e, t=t: e.tensor_tensor(
            sc1[:, 0:1], mvball[:, 2 * t + 1:2 * t + 2], r1all[:, t:t + 1], ALU.mult),
            reads=[mvball, r1all], writes=[sc1])
        kb.group("DVE", lambda e, t=t: e.tensor_tensor(
            sc1[:, 1:2], sc1[:, 0:1], r1all[:, t:t + 1], ALU.mult),
            reads=[sc1, r1all], writes=[sc1])
        kb.group("ACT", lambda e, e6=eps6: e.activation(
            sc1[:, 2:3], sc1[:, 1:2], AF.Ln, bias=e6, scale=1.0),
            reads=[sc1, "constpool"], writes=[sc1])
        kb.group("ACT", lambda e: e.activation(sc1[:, 2:3], sc1[:, 2:3], AF.Exp,
                                               bias=zero, scale=-0.5),
                 reads=[sc1, "constpool"], writes=[sc1])
        kb.group("DVE", lambda e, t=t: e.tensor_tensor(
            cball[:, t:t + 1], r1all[:, t:t + 1], sc1[:, 2:3], ALU.mult),
            reads=[sc1, r1all], writes=[cball])
        kb.group("DVE", lambda e, t=t, xt=xt: e.tensor_scalar(
            xt[:, :], xt[:, :], mvball[:, 2 * t:2 * t + 1], cball[:, t:t + 1],
            ALU.subtract, ALU.mult), reads=[xt, mvball, cball], writes=[xt])
        xn = xnb[t % 2]
        kb.group("DVE", lambda e, xt=xt, xn=xn: e.scalar_tensor_tensor(
            xn[:, :], xt[:, :], 1.0, nwt[:, :], ALU.mult, ALU.mult),
            reads=[xt, nwt], writes=[xn])
        return xn[:, :]

    emit_tm_to_fm(pr, prep_tile, NCH, D, hT, T, 0, "tf")
    hT_keys = [(hT.name, "fm", t) for t in range(NCH)]

    for t in range(NCH):
        ps = pr.pC[:, 0:2 * NHL]
        mm_group(kb, pr.pC[:, 0:NHL],
                 [(hT[:, c * T + t * 128: c * T + (t + 1) * 128],
                   wba[:, c * NHL:(c + 1) * NHL]) for c in range(8)],
                 reads=["wba_b"] + hT_keys, writes=[pr.pC])
        mm_group(kb, pr.pC[:, NHL:2 * NHL],
                 [(hT[:, c * T + t * 128: c * T + (t + 1) * 128],
                   wba[:, 8 * NHL + c * NHL:8 * NHL + (c + 1) * NHL])
                  for c in range(8)],
                 reads=["wba_a"] + hT_keys, writes=[pr.pC])
        def bev(e, t=t, ps=ps):
            dap = bass.AP(beta_all, t, [[beta_all.shape[1], 128], [NCH, NHL]])
            return e.activation(dap, ps[:, 0:NHL], AF.Sigmoid, bias=zero, scale=1.0)
        kb.group("ACT", bev, reads=[pr.pC, "constpool"], writes=[beta_all])
        kb.group("DVE", lambda e, ps=ps: e.tensor_tensor(
            spb[0][:, :], ps[:, NHL:2 * NHL], spb[2][:, :], ALU.add),
            reads=[pr.pC, "dtb_b"], writes=[spb[0]])
        kb.group("ACT", lambda e: e.activation(spb[1][:, :], spb[0][:, :], AF.Abs,
                                               bias=zero, scale=1.0),
                 reads=[spb[0], "constpool"], writes=[spb[1]])
        kb.group("ACT", lambda e: e.activation(spb[1][:, :], spb[1][:, :], AF.Exp,
                                               bias=zero, scale=-1.0),
                 reads=[spb[1], "constpool"], writes=[spb[1]])
        kb.group("ACT", lambda e, one1=one1: e.activation(
            spb[1][:, :], spb[1][:, :], AF.Ln, bias=one1, scale=1.0),
            reads=[spb[1], "constpool"], writes=[spb[1]])
        kb.group("DVE", lambda e: e.tensor_scalar(spb[0][:, :], spb[0][:, :], 0.0,
                                                  None, ALU.max),
                 reads=[spb[0]], writes=[spb[0]])
        kb.group("DVE", lambda e: e.tensor_tensor(spb[0][:, :], spb[0][:, :],
                                                  spb[1][:, :], ALU.add),
                 reads=[spb[0], spb[1]], writes=[spb[0]])
        def gev(e, t=t):
            dap = bass.AP(g_all, t, [[g_all.shape[1], 128], [NCH, NHL]])
            return e.tensor_tensor(dap, spb[0][:, :], nab[:, :], ALU.mult)
        kb.group("DVE", gev, reads=[spb[0], nab], writes=[g_all])
    triu_f = kb.sbuf("triu_f", [128, C], F32)
    kb.group("POOL", lambda e: e.memset(triu_f[:, :], 1.0), writes=[triu_f])
    kb.group("POOL", lambda e: e.affine_select(
        out=triu_f[:, :], in_=triu_f[:, :], compare_op=ALU.is_ge, fill=0.0,
        base=0, pattern=[[1, C]], channel_multiplier=-1),
        reads=[triu_f], writes=[triu_f])
    for h in range(NHL):
        kb.group("PE", lambda e, h=h: e.matmul(
            pr.pA[:, 0:NCH], triu_f[:, :], g_all[:, h * NCH:(h + 1) * NCH],
            start=True, stop=True),
            reads=[triu_f, g_all], writes=[(pr.pA.name, 0)])
        kb.group("DVE", lambda e, h=h: e.tensor_copy(
            G_all[:, h * NCH:(h + 1) * NCH], pr.pA[:, 0:NCH]),
            reads=[(pr.pA.name, 0)], writes=[G_all])
    kb.group("ACT", lambda e: e.activation(egt_all[:, :], G_all[:, :], AF.Exp,
                                           bias=zero, scale=1.0),
             reads=[G_all, "constpool"], writes=[egt_all])
    kb.group("DVE", lambda e: e.tensor_scalar(nbeg_all[:, :], beta_all[:, :],
                                              -1.0, None, ALU.mult),
             reads=[beta_all], writes=[nbeg_all])
    kb.group("DVE", lambda e: e.tensor_tensor(nbeg_all[:, :], nbeg_all[:, :],
                                              egt_all[:, :], ALU.mult),
             reads=[nbeg_all, egt_all], writes=[nbeg_all])

    for h in range(NHL):
        q1 = kb.dma_queue("POOLQ.wh")
        kb.group("POOL", lambda e, h=h: e.dma_start(
            out=wqh[:, :].rearrange("p (c n) -> p c n", c=8),
            in_=qwT[:, h * HK:(h + 1) * HK].rearrange("(c p) n -> p c n", p=128)),
            writes=[wqh], dma=q1)
        kb.group("POOL", lambda e, h=h: e.dma_start(
            out=wkh[:, :].rearrange("p (c n) -> p c n", c=8),
            in_=kwT[:, h * HK:(h + 1) * HK].rearrange("(c p) n -> p c n", p=128)),
            writes=[wkh], dma=q1)
        kb.group("POOL", lambda e, h=h: e.dma_start(
            out=wvh[:, :].rearrange("p (c n) -> p c n", c=8),
            in_=vwT[:, h * HV:(h + 1) * HV].rearrange("(c p) n -> p c n", p=128)),
            writes=[wvh], dma=q1)
        kb.group("POOL", lambda e, h=h: e.dma_start(
            out=wgh[:, :].rearrange("p (c n) -> p c n", c=8),
            in_=gwT[:, h * HV:(h + 1) * HV].rearrange("(c p) n -> p c n", p=128)),
            writes=[wgh], dma=q1)
        kb.group("PE", lambda e, h=h: e.matmul(
            pr.pA[:, 0:NCH] if False else pr.pD[0:1, 0:NCH],
            pr.onesf[:, 0:1], g_all[:, h * NCH:(h + 1) * NCH],
            start=True, stop=True),
            reads=[g_all, pr.onesf], writes=[pr.pD])
        kb.group("DVE", lambda e: e.tensor_copy(growb[0:1, 0:NCH], pr.pD[0:1, 0:NCH]),
                 reads=[pr.pD], writes=[growb])
        kb.group("PE", lambda e: e.matmul(pr.pD[:, 0:NCH], pr.onesf[0:1, 0:128],
                                          growb[0:1, 0:NCH], start=True, stop=True),
                 reads=[growb, pr.onesf], writes=[pr.pD])
        kb.group("DVE", lambda e: e.tensor_copy(gc_b[:, :], pr.pD[:, 0:NCH]),
                 reads=[pr.pD], writes=[gc_b])
        kb.group("ACT", lambda e: e.activation(egc_b[:, :], gc_b[:, :], AF.Exp,
                                               bias=zero, scale=1.0),
                 reads=[gc_b, "constpool"], writes=[egc_b])
        kb.group("POOL", lambda e: e.memset(S32[:, :], 0.0), writes=[S32])
        kb.group("POOL", lambda e: e.memset(Sbf[:, :], 0.0), writes=[Sbf])

        for g in range(NG):
            t0 = g * 8
            for j, (wbuf, wname, nch, raw) in enumerate((
                    (wqh, "wqh", 2, qraw), (wkh, "wkh", 2, kraw),
                    (wvh, "wvh", 4, vraw))):
                if g == 0:
                    kb.group("POOL", lambda e, raw=raw, nch=nch: e.memset(
                        bass.AP(raw, 0,
                                [[raw.shape[1], 128], [TRW, nch], [1, 128]]), 0.0),
                        writes=[(raw.name, "z")])
                    blocks = [(128, 512), (640, 512)]
                else:
                    blocks = [(0, 512), (512, 512), (1024, 128)]
                for cc in range(nch):
                    for bi, (j0, w_) in enumerate(blocks):
                        slot = (cc + bi) % 2
                        ps = pr.pB[:, slot * 512:slot * 512 + w_]
                        pskey = (pr.pB.name, slot)
                        tok0 = t0 * 128 - 128 + j0
                        mm_group(kb, ps,
                                 [(wbuf[:, c * nch * 128 + cc * 128:
                                        c * nch * 128 + (cc + 1) * 128],
                                   hT[:, c * T + tok0: c * T + tok0 + w_])
                                  for c in range(8)],
                                 reads=[wbuf.name] + hT_keys, writes=[pskey])
                        kb.group("DVE", lambda e, raw=raw, cc=cc, j0=j0, w_=w_, ps=ps:
                                 e.tensor_copy(raw[:, cc * TRW + j0:
                                                   cc * TRW + j0 + w_], ps),
                                 reads=[pskey], writes=[(raw.name, cc, j0)])
            for (raw, cb, cf, nch, cvoff) in ((qraw, qc, qcf, 2, 0),
                                              (kraw, kcb, kcf, 2, 32),
                                              (vraw, vcb, vcf, 4, 64)):
                rawkeys = ([(raw.name, "z")] if g == 0 else []) + \
                    [(raw.name, cc, j0) for cc in range(nch)
                     for j0 in ([128, 640] if g == 0 else [0, 512, 1024])]
                for ccg in range(0, nch, 2):
                    for cc in (ccg, ccg + 1):
                        sl = cc % 2
                        for i in range(4):
                            tap = cvt[:, cvoff + (h * nch + cc) * 4 + i:
                                      cvoff + (h * nch + cc) * 4 + i + 1]
                            src2 = raw[:, cc * TRW + 125 + i: cc * TRW + 125 + i + TG]
                            dst = cb[:, sl * TG:(sl + 1) * TG]
                            if i == 0:
                                kb.group("DVE", lambda e, src2=src2, dst=dst, tap=tap:
                                         e.tensor_scalar(dst, src2, tap, None, ALU.mult),
                                         reads=rawkeys + ["cvt", cb.name], writes=[cb.name])
                            else:
                                kb.group("DVE", lambda e, src2=src2, dst=dst, tap=tap:
                                         e.scalar_tensor_tensor(dst, src2, tap, dst,
                                                                ALU.mult, ALU.add),
                                         reads=rawkeys + ["cvt", cb.name], writes=[cb.name])
                    for cc in (ccg, ccg + 1):
                        sl = cc % 2
                        kb.group("ACT", lambda e, cb=cb, sl=sl: e.activation(
                            cb[:, sl * TG:(sl + 1) * TG], cb[:, sl * TG:(sl + 1) * TG],
                            AF.Silu, bias=zero, scale=1.0),
                            reads=[cb.name, "constpool"], writes=[cb.name])
                    if nch == 4:
                        for cc in (ccg, ccg + 1):
                            sl = cc % 2
                            kb.group("DVE", lambda e, cb=cb, cf=cf, cc=cc, sl=sl:
                                     e.tensor_copy(cf[:, cc * TG:(cc + 1) * TG],
                                                   cb[:, sl * TG:(sl + 1) * TG]),
                                     reads=[cb.name], writes=[cf.name])
                if nch == 2:
                    for cc in range(2):
                        kb.group("ACT", lambda e, cb=cb, cc=cc: e.activation(
                            vcb[:, cc * TG:(cc + 1) * TG],
                            cb[:, cc * TG:(cc + 1) * TG],
                            AF.Square, bias=zero, scale=1.0),
                            reads=[cb.name, "constpool"], writes=[vcb.name])
                    for tc in range(TG // 512):
                        def sm(e, tc=tc):
                            last = None
                            for cc in range(2):
                                last = e.matmul(
                                    pr.pA[0:1, tc * 512:(tc + 1) * 512],
                                    pr.onesf[:, 0:1],
                                    vcb[:, cc * TG + tc * 512:
                                        cc * TG + (tc + 1) * 512],
                                    start=(cc == 0), stop=(cc == 1))
                            return last
                        kb.group("PE", sm, reads=[vcb.name, pr.onesf],
                                 writes=[(pr.pA.name, 0)])
                        kb.group("DVE", lambda e, tc=tc: e.tensor_copy(
                            rrow[0:1, tc * 512:(tc + 1) * 512],
                            pr.pA[0:1, tc * 512:(tc + 1) * 512]),
                            reads=[(pr.pA.name, 0)], writes=[rrow])
                    epsr = kb.constr(1e-6)
                    kb.group("ACT", lambda e, epsr=epsr: e.activation(
                        rrow[0:1, :], rrow[0:1, :], AF.Ln, bias=epsr,
                        scale=1.0), reads=[rrow, "constrow"], writes=[rrow])
                    qsc = kb.constr(-2.7725887222397811) if cb is qc and cf is qcf \
                        else kb.constr(0.0)
                    kb.group("ACT", lambda e, qsc=qsc: e.activation(
                        rrow[0:1, :], rrow[0:1, :], AF.Exp, bias=qsc, scale=-0.5),
                        reads=[rrow, "constrow"], writes=[rrow])
                    for tc in range(TG // 512):
                        kb.group("PE", lambda e, tc=tc: e.matmul(
                            pr.pD[:, 0:512], pr.onesf[0:1, 0:128],
                            rrow[0:1, tc * 512:(tc + 1) * 512],
                            start=True, stop=True),
                            reads=[rrow, pr.onesf], writes=[pr.pD])
                        for cc in range(2):
                            kb.group("DVE", lambda e, cb=cb, cf=cf, cc=cc, tc=tc:
                                     e.tensor_tensor(
                                         cf[:, cc * TG + tc * 512:cc * TG + (tc + 1) * 512],
                                         cb[:, cc * TG + tc * 512:cc * TG + (tc + 1) * 512],
                                         pr.pD[:, 0:512], ALU.mult),
                                     reads=[cb.name, pr.pD], writes=[cf.name])
            for c8 in range(8):
                def ktr(e, c8=c8):
                    last = None
                    for cc in range(2):
                        last = e.transpose(
                            pr.pb0[:, cc * 128:(cc + 1) * 128],
                            kcf[:, cc * TG + c8 * 128:cc * TG + (c8 + 1) * 128],
                            pr.ident[:, :])
                    return last
                kb.group("PE", ktr, reads=[kcf.name, pr.ident], writes=[pr.pb0])
                kb.group("DVE", lambda e, c8=c8: e.tensor_copy(
                    ktm[:, c8 * HK:(c8 + 1) * HK], pr.pb0[:, 0:HK]),
                    reads=[pr.pb0], writes=[ktm])
                def vtr(e, c8=c8):
                    last = None
                    for cc in range(4):
                        last = e.transpose(
                            pr.pb1[:, cc * 128:(cc + 1) * 128],
                            vcf[:, cc * TG + c8 * 128:cc * TG + (c8 + 1) * 128],
                            pr.ident[:, :])
                    return last
                kb.group("PE", vtr, reads=[vcf.name, pr.ident], writes=[pr.pb1])
                kb.group("DVE", lambda e, c8=c8: e.tensor_copy(
                    vtmg[:, c8 * HV:(c8 + 1) * HV], pr.pb1[:, 0:HV]),
                    reads=[pr.pb1], writes=[vtmg])
            for c8 in range(8):
                t = t0 + c8
                kb.group("PE", lambda e, t=t, h=h: e.transpose(
                    pr.pD[0:1, 0:128], G_all[:, h * NCH + t: h * NCH + t + 1],
                    pr.identf[:, :]), reads=[G_all, pr.identf], writes=[pr.pD])
                kb.group("DVE", lambda e: e.tensor_copy(growb[0:1, 0:C],
                                                        pr.pD[0:1, 0:128]),
                         reads=[pr.pD], writes=[growb])
                kb.group("DVE", lambda e: e.tensor_scalar(
                    growb[0:1, C:2 * C], growb[0:1, 0:C], -1.0, None, ALU.mult),
                    reads=[growb], writes=[growb])
                def dtm(e):
                    e.matmul(pr.pA[:, 0:C], pr.onesf[0:1, 0:128],
                             growb[0:1, 0:C], start=True, stop=False)
                    return e.matmul(pr.pA[:, 0:C], growb[0:1, C:2 * C],
                                    pr.onesf[0:1, 0:128], start=False, stop=True)
                kb.group("PE", dtm, reads=[growb, pr.onesf], writes=[(pr.pA.name, 0)])
                kb.group("DVE", lambda e: e.tensor_scalar(
                    pr.pA[:, 0:C], pr.pA[:, 0:C], 0.0, None, ALU.min),
                    reads=[(pr.pA.name, 0)], writes=[(pr.pA.name, 0)])
                kb.group("ACT", lambda e, c8=c8: e.activation(
                    dtl[:, c8 * C:(c8 + 1) * C], pr.pA[:, 0:C], AF.Exp,
                    bias=zero, scale=1.0),
                    reads=[(pr.pA.name, 0), "constpool"], writes=[(dtl.name, c8)])
                kb.group("POOL", lambda e, c8=c8: e.affine_select(
                    out=dtl[:, c8 * C:(c8 + 1) * C], in_=dtl[:, c8 * C:(c8 + 1) * C],
                    compare_op=ALU.is_ge, fill=0.0, base=0,
                    pattern=[[1, C]], channel_multiplier=-1),
                    reads=[(dtl.name, c8)], writes=[(dtl.name, c8)])
                def dsm(e):
                    e.matmul(pr.pA[:, 0:C], growb[0:1, 0:C],
                             pr.onesf[0:1, 0:128], start=True, stop=False)
                    return e.matmul(pr.pA[:, 0:C], pr.onesf[0:1, 0:128],
                                    growb[0:1, C:2 * C], start=False, stop=True)
                kb.group("PE", dsm, reads=[growb, pr.onesf], writes=[(pr.pA.name, 0)])
                kb.group("DVE", lambda e: e.tensor_scalar(
                    pr.pA[:, 0:C], pr.pA[:, 0:C], 0.0, None, ALU.min),
                    reads=[(pr.pA.name, 0)], writes=[(pr.pA.name, 0)])
                kb.group("ACT", lambda e, c8=c8: e.activation(
                    dst[:, c8 * C:(c8 + 1) * C], pr.pA[:, 0:C], AF.Exp,
                    bias=zero, scale=1.0),
                    reads=[(pr.pA.name, 0), "constpool"], writes=[(dst.name, c8)])
                kb.group("POOL", lambda e, c8=c8: e.affine_select(
                    out=dst[:, c8 * C:(c8 + 1) * C], in_=dst[:, c8 * C:(c8 + 1) * C],
                    compare_op=ALU.is_ge, fill=0.0, base=-1,
                    pattern=[[-1, C]], channel_multiplier=1),
                    reads=[(dst.name, c8)], writes=[(dst.name, c8)])
                def qkm(e, c8=c8):
                    last = None
                    for cc in range(2):
                        last = e.matmul(pr.pA[:, 0:C],
                                        kcf[:, cc * TG + c8 * 128:cc * TG + (c8 + 1) * 128],
                                        qcf[:, cc * TG + c8 * 128:cc * TG + (c8 + 1) * 128],
                                        start=(cc == 0), stop=(cc == 1))
                    return last
                kb.group("PE", qkm, reads=[kcf.name, qcf.name], writes=[(pr.pA.name, 0)])
                kb.group("DVE", lambda e, c8=c8: e.tensor_tensor(
                    qkd[:, c8 * C:(c8 + 1) * C], pr.pA[:, 0:C],
                    dtl[:, c8 * C:(c8 + 1) * C], ALU.mult),
                    reads=[(pr.pA.name, 0), (dtl.name, c8)], writes=[(qkd.name, c8)])
                def kkm(e, c8=c8):
                    last = None
                    for cc in range(2):
                        last = e.matmul(pr.pA[:, 0:C],
                                        kcf[:, cc * TG + c8 * 128:cc * TG + (c8 + 1) * 128],
                                        kcf[:, cc * TG + c8 * 128:cc * TG + (c8 + 1) * 128],
                                        start=(cc == 0), stop=(cc == 1))
                    return last
                kb.group("PE", kkm, reads=[kcf.name], writes=[(pr.pA.name, 0)])
                bcol = beta_all[:, h * NCH + t: h * NCH + t + 1]
                kb.group("DVE", lambda e, c8=c8, bcol=bcol: e.scalar_tensor_tensor(
                    Wb2[:, c8 * C:(c8 + 1) * C], pr.pA[:, 0:C], bcol,
                    dst[:, c8 * C:(c8 + 1) * C], ALU.mult, ALU.mult),
                    reads=[(pr.pA.name, 0), beta_all, (dst.name, c8)],
                    writes=[(Wb2.name, c8)])
                kb.group("DVE", lambda e, c8=c8: e.tensor_tensor(
                    Atb[:, c8 * C:(c8 + 1) * C], Wb2[:, c8 * C:(c8 + 1) * C],
                    pr.ident[:, :], ALU.add),
                    reads=[(Wb2.name, c8), pr.ident], writes=["preA", (Atb.name, c8)])
                kb.group("DVE", lambda e, c8=c8: e.tensor_tensor(
                    Xb[:, c8 * C:(c8 + 1) * C], pr.ident[:, :],
                    Wb2[:, c8 * C:(c8 + 1) * C], ALU.subtract),
                    reads=[(Wb2.name, c8), pr.ident], writes=[(Xb.name, c8)])
                kb.group("PE", lambda e, c8=c8: e.transpose(
                    pr.pb0[:, 0:C], Atb[:, c8 * C:(c8 + 1) * C], pr.ident[:, :]),
                    reads=[(Atb.name, c8), pr.ident], writes=[pr.pb0])
                kb.group("DVE", lambda e, c8=c8: e.tensor_copy(
                    Atb[:, c8 * C:(c8 + 1) * C], pr.pb0[:, 0:C]),
                    reads=[pr.pb0], writes=[(Atb.name, c8)])
            for it in range(6):
                for c8 in range(8):
                    kb.group("PE", lambda e, c8=c8: e.matmul(
                        pr.pA[:, 0:C], Atb[:, c8 * C:(c8 + 1) * C],
                        Xb[:, c8 * C:(c8 + 1) * C], start=True, stop=True),
                        reads=[(Atb.name, c8), (Xb.name, c8)], writes=[(pr.pA.name, 0)])
                    kb.group("DVE", lambda e, c8=c8: e.scalar_tensor_tensor(
                        Wb2[:, c8 * C:(c8 + 1) * C], pr.pA[:, 0:C], -1.0,
                        ident2[:, :], ALU.mult, ALU.add),
                        reads=[(pr.pA.name, 0), ident2], writes=[(Wb2.name, c8)])
                    kb.group("PE", lambda e, c8=c8: e.transpose(
                        pr.pb0[:, 0:C], Xb[:, c8 * C:(c8 + 1) * C], pr.ident[:, :]),
                        reads=[(Xb.name, c8), pr.ident], writes=[pr.pb0])
                    kb.group("DVE", lambda e: e.tensor_copy(xtb[:, :], pr.pb0[:, 0:C]),
                             reads=[pr.pb0], writes=[xtb])
                    kb.group("PE", lambda e, c8=c8: e.matmul(
                        pr.pB[:, 0:C], xtb[:, :], Wb2[:, c8 * C:(c8 + 1) * C],
                        start=True, stop=True),
                        reads=[xtb, (Wb2.name, c8)], writes=[(pr.pB.name, 0)])
                    kb.group("DVE", lambda e, c8=c8: e.tensor_copy(
                        Xb[:, c8 * C:(c8 + 1) * C], pr.pB[:, 0:C]),
                        reads=[(pr.pB.name, 0)], writes=[(Xb.name, c8)])
            for c8 in range(8):
                kb.group("PE", lambda e, c8=c8: e.transpose(
                    pr.pb0[:, 0:C], Xb[:, c8 * C:(c8 + 1) * C], pr.ident[:, :]),
                    reads=[(Xb.name, c8), pr.ident], writes=[pr.pb0])
                kb.group("DVE", lambda e, c8=c8: e.tensor_copy(
                    Ttb[:, c8 * C:(c8 + 1) * C], pr.pb0[:, 0:C]),
                    reads=[pr.pb0], writes=[(Ttb.name, c8)])
            for c8 in range(8):
                t = t0 + c8
                egtc = egt_all[:, h * NCH + t: h * NCH + t + 1]
                nbegc = nbeg_all[:, h * NCH + t: h * NCH + t + 1]
                bc = beta_all[:, h * NCH + t: h * NCH + t + 1]
                egcc = egc_b[:, t:t + 1]
                def ks0(e, c8=c8):
                    last = None
                    for cc in range(2):
                        last = e.matmul(pr.pA[:, 0:512],
                                        kcf[:, cc * TG + c8 * 128:cc * TG + (c8 + 1) * 128],
                                        Sbf[:, cc * HV:(cc + 1) * HV],
                                        start=(cc == 0), stop=(cc == 1))
                    return last
                kb.group("PE", ks0, reads=[kcf.name, Sbf], writes=[(pr.pA.name, 0)])
                kb.group("DVE", lambda e, c8=c8, bc=bc: e.tensor_scalar(
                    sgb[:, :], vtmg[:, c8 * HV:(c8 + 1) * HV], bc, None, ALU.mult),
                    reads=[vtmg, beta_all], writes=[sgb])
                kb.group("DVE", lambda e, nbegc=nbegc: e.scalar_tensor_tensor(
                    rhsb[:, :], pr.pA[:, 0:512], nbegc, sgb[:, :],
                    ALU.mult, ALU.add),
                    reads=[(pr.pA.name, 0), nbeg_all, sgb], writes=[rhsb])
                kb.group("PE", lambda e, c8=c8: e.matmul(
                    pr.pB[:, 0:512], Ttb[:, c8 * C:(c8 + 1) * C], rhsb[:, :],
                    start=True, stop=True),
                    reads=[(Ttb.name, c8), rhsb], writes=[(pr.pB.name, 0)])
                kb.group("DVE", lambda e: e.tensor_copy(ubf[:, :], pr.pB[:, 0:512]),
                         reads=[(pr.pB.name, 0)], writes=[ubf])
                def qs0(e, c8=c8):
                    last = None
                    for cc in range(2):
                        last = e.matmul(pr.pA[:, 0:512],
                                        qcf[:, cc * TG + c8 * 128:cc * TG + (c8 + 1) * 128],
                                        Sbf[:, cc * HV:(cc + 1) * HV],
                                        start=(cc == 0), stop=(cc == 1))
                    return last
                kb.group("PE", qs0, reads=[qcf.name, Sbf], writes=[(pr.pA.name, 0)])
                kb.group("PE", lambda e, c8=c8: e.matmul(
                    pr.pB[:, 0:512], qkd[:, c8 * C:(c8 + 1) * C], ubf[:, :],
                    start=True, stop=True),
                    reads=[(qkd.name, c8), ubf], writes=[(pr.pB.name, 0)])
                kb.group("DVE", lambda e: e.tensor_copy(o2b[:, :], pr.pB[:, 0:512]),
                         reads=[(pr.pB.name, 0)], writes=[o2b])
                kb.group("DVE", lambda e, egtc=egtc: e.scalar_tensor_tensor(
                    obuf[:, :], pr.pA[:, 0:512], egtc, o2b[:, :], ALU.mult, ALU.add),
                    reads=[(pr.pA.name, 0), egt_all, o2b], writes=[obuf])
                kb.group("DVE", lambda e, h=h, t=t: e.tensor_tensor(
                    scc[:, 2:3], gc_b[:, t:t + 1],
                    G_all[:, h * NCH + t:h * NCH + t + 1], ALU.subtract),
                    reads=[G_all, gc_b, scc], writes=[scc])
                kb.group("ACT", lambda e: e.activation(
                    scc[:, 2:3], scc[:, 2:3], AF.Exp, bias=zero, scale=1.0),
                    reads=[scc, "constpool"], writes=[scc])
                kb.group("DVE", lambda e, c8=c8: e.tensor_scalar(
                    kdec[:, :], ktm[:, c8 * HK:(c8 + 1) * HK],
                    scc[:, 2:3], None, ALU.mult),
                    reads=[ktm, scc], writes=[kdec])
                for cc in range(2):
                    kb.group("PE", lambda e, cc=cc: e.matmul(
                        pr.pA[:, 0:512], kdec[:, cc * 128:(cc + 1) * 128],
                        ubf[:, :], start=True, stop=True),
                        reads=[kdec, ubf], writes=[(pr.pA.name, 0)])
                    kb.group("DVE", lambda e, cc=cc, egcc=egcc: e.scalar_tensor_tensor(
                        S32[:, cc * HV:(cc + 1) * HV], S32[:, cc * HV:(cc + 1) * HV],
                        egcc, pr.pA[:, 0:512], ALU.mult, ALU.add),
                        reads=[S32, egc_b, (pr.pA.name, 0)], writes=[S32])
                kb.group("DVE", lambda e: e.tensor_copy(Sbf[:, :], S32[:, :]),
                         reads=[S32], writes=[Sbf])
                def gpm(e, t=t):
                    last = None
                    for c in range(8):
                        last = e.matmul(pr.pB[:, 0:512],
                                        hT[:, c * T + t * 128: c * T + (t + 1) * 128],
                                        wgh[:, c * HV:(c + 1) * HV],
                                        start=(c == 0), stop=(c == 7))
                    return last
                kb.group("PE", gpm, reads=[wgh.name] + hT_keys, writes=[(pr.pB.name, 0)])
                kb.group("ACT", lambda e: e.activation(sgb[:, :], pr.pB[:, 0:512],
                                                       AF.Silu, bias=zero, scale=1.0),
                         reads=[(pr.pB.name, 0), "constpool"], writes=[sgb])
                kb.group("DVE", lambda e: e.scalar_tensor_tensor(
                    o2b[:, :], obuf[:, :], 1.0, obuf[:, :], ALU.mult, ALU.mult,
                    accum_out=scc[:, 0:1]),
                    reads=[obuf], writes=[o2b, scc])
                kb.group("ACT", lambda e: e.activation(scc[:, 1:2], scc[:, 0:1],
                                                       AF.Ln, bias=eps1,
                                                       scale=1.0 / HV),
                         reads=[scc, "constpool"], writes=[scc])
                kb.group("ACT", lambda e: e.activation(scc[:, 1:2], scc[:, 1:2],
                                                       AF.Exp, bias=zero, scale=-0.5),
                         reads=[scc, "constpool"], writes=[scc])
                kb.group("DVE", lambda e: e.scalar_tensor_tensor(
                    o2b[:, :], obuf[:, :], scc[:, 1:2], sgb[:, :],
                    ALU.mult, ALU.mult), reads=[obuf, scc, sgb], writes=[o2b])
                kb.group("DVE", lambda e, c8=c8: e.tensor_tensor(
                    ogb[:, (c8 % 2) * HV:(c8 % 2 + 1) * HV], o2b[:, :],
                    onb[:, :], ALU.mult),
                    reads=[o2b, onb], writes=[(ogb.name, c8 % 2)])
                if not fm_out:
                    kb.group("SP", lambda e, g=g, h=h, c8=c8, t=t: e.dma_start(
                        out=og[t * 128:(t + 1) * 128, h * HV:(h + 1) * HV],
                        in_=ogb[:, (c8 % 2) * HV:(c8 % 2 + 1) * HV]),
                        reads=[(ogb.name, c8 % 2)], writes=["ogout"],
                        dma=kb.dma_queue(f"SPQ.og{c8 % 2}"))
                else:
                    def ogtr(e, c8=c8):
                        last = None
                        for cc in range(4):
                            last = e.transpose(
                                pr.pb0[:, cc * 128:(cc + 1) * 128],
                                ogb[:, (c8 % 2) * HV + cc * 128:
                                    (c8 % 2) * HV + (cc + 1) * 128],
                                pr.ident[:, :])
                        return last
                    kb.group("PE", ogtr, reads=[(ogb.name, c8 % 2), pr.ident],
                             writes=[pr.pb0])
                    kb.group("DVE", lambda e: e.tensor_copy(
                        ogfmb[:, :], pr.pb0[:, 0:512]),
                        reads=[pr.pb0], writes=["ogfmb"])
                    def ogdma(e, h=h, t=t):
                        ogt = og.tensor if hasattr(og, "tensor") else og
                        dap = bass.AP(ogt, (h * HV) * T + t * 128,
                                      [[T, 128], [128 * T, 4], [1, 128]])
                        return e.dma_start(
                            out=dap, in_=ogfmb[:, :].rearrange(
                                "p (c n) -> p c n", c=4))
                    kb.group("SP", ogdma, reads=["ogfmb"], writes=["ogout"],
                             dma=kb.dma_queue(f"SPQ.og{c8 % 2}"))
    return




def build_L1a(T=4096):
    D, HK, HV, NHL = 1024, 256, 512, 4
    nc = bass.Bass("TRN2", target_bir_lowering=False)
    pr = Prog(nc)
    kb = pr.kb
    io = {}
    io["xb"] = kb.dram_in("xb", [T, D], F32)
    io["normwt"] = kb.dram_in("normwt", [128, D], F32)
    io["qwT"] = kb.dram_in("qwT", [D, NHL * HK], F32)
    io["kwT"] = kb.dram_in("kwT", [D, NHL * HK], F32)
    io["vwT"] = kb.dram_in("vwT", [D, NHL * HV], F32)
    io["gwT"] = kb.dram_in("gwT", [D, NHL * HV], F32)
    io["bwT"] = kb.dram_in("bwT", [D, NHL], F32)
    io["awT"] = kb.dram_in("awT", [D, NHL], F32)
    io["qconvt"] = kb.dram_in("qconvt", [128, NHL * 2 * 4], F32)
    io["kconvt"] = kb.dram_in("kconvt", [128, NHL * 2 * 4], F32)
    io["vconvt"] = kb.dram_in("vconvt", [128, NHL * 4 * 4], F32)
    io["alog"] = kb.dram_in("alog", [1, NHL], F32)
    io["dtb"] = kb.dram_in("dtb", [1, NHL], F32)
    io["onormt"] = kb.dram_in("onormt", [128, HV], F32)
    io["og"] = kb.dram_out("og", [T, NHL * HV], mybir.dt.bfloat16)
    emit_L1a(pr, io, T=T)
    kb.barrier_on("SP")
    kb.finalize()
    return nc

def emit_L1a_proj(pr, io, T=2048, DV=4096):
    nc = pr.nc
    kb = pr.kb
    D = 1024
    NT = T // 128
    NPC = DV // 1024
    x = io["x"]
    ogf = io["ogf"]
    owT = io["owT"]
    xo = io["xo"]
    cfm = kb.sbuf("cfm", [128, 8 * T], BF16)
    wob = kb.sbuf("wob", [128, 8 * D], BF16)
    acc = kb.sbuf_at("acc", [128, NT * D * 4 // 4], F32, nc.sbuf_top - 4 * NT * D)
    xrb = [kb.sbuf(f"xrb{i}", [128, D], F32) for i in range(2)]
    obuf = [kb.sbuf(f"obuf{i}", [128, D], F32) for i in range(2)]
    ybuf = kb.sbuf("ybuf", [128, D], F32)
    mvball = kb.sbuf("mvball", [128, 2 * NT], F32)
    r1all = kb.sbuf("r1all", [128, NT], F32)
    bns = kb.sbuf("bns", [128, 12], F32)
    mv2 = kb.sbuf("mv2", [128, 2], F32)
    eps1 = kb.constv(1e-5)
    zero = kb.constv(0.0)
    for pc in range(NPC):
        kb.group("SP", lambda e, pc=pc: e.dma_start(
            out=cfm[:, :].rearrange("p (c n) -> p c n", c=8),
            in_=ogf[pc * 1024:(pc + 1) * 1024, :].rearrange("(c p) n -> p c n", p=128)),
            writes=[cfm], dma="SPQ")
        kb.group("POOL", lambda e, pc=pc: e.dma_start(
            out=wob[:, :].rearrange("p (c n) -> p c n", c=8),
            in_=owT[pc * 1024:(pc + 1) * 1024, :].rearrange("(c p) n -> p c n", p=128)),
            writes=[wob], dma="POOLQ")
        for t in range(NT):
            for half in range(2):
                ps = pr.pA[:, half * 512:half * 512 + 512]
                pskey = (pr.pA.name, half)
                mm_group(kb, ps,
                         [(cfm[:, c * T + t * 128: c * T + (t + 1) * 128],
                           wob[:, c * D + half * 512: c * D + (half + 1) * 512])
                          for c in range(8)],
                         reads=[cfm, wob], writes=[pskey])
                aslice = acc[:, t * D + half * 512: t * D + (half + 1) * 512]
                if pc == 0:
                    kb.group("DVE", lambda e, ps=ps, aslice=aslice: e.tensor_copy(
                        aslice, ps), reads=[pskey], writes=[(acc.name, t, half)])
                else:
                    kb.group("DVE", lambda e, ps=ps, aslice=aslice: e.tensor_tensor(
                        aslice, aslice, ps, ALU.add),
                        reads=[pskey, (acc.name, t, half)],
                        writes=[(acc.name, t, half)])
    for t in range(NT):
        xt = xrb[t % 2]
        ob = obuf[t % 2]
        kb.group("SP", lambda e, t=t, xt=xt: e.dma_start(
            out=xt[:, :], in_=x[t * 128:(t + 1) * 128, :]),
            writes=[xt], dma="SPQ")
        kb.group("DVE", lambda e, xt=xt: e.bn_stats(bns[:, 0:6], xt[:, 0:512]),
                 reads=[xt], writes=[bns])
        kb.group("DVE", lambda e, xt=xt: e.bn_stats(bns[:, 6:12], xt[:, 512:1024]),
                 reads=[xt, bns], writes=[bns])
        kb.group("DVE", lambda e: e.bn_aggr(mv2[:, :], bns[:, 0:12]),
                 reads=[bns], writes=[mv2])
        kb.group("ACT", lambda e: e.activation(mv2[:, 1:2], mv2[:, 1:2], AF.Ln,
                                               bias=eps1, scale=1.0),
                 reads=[mv2, "constpool"], writes=[mv2])
        kb.group("ACT", lambda e: e.activation(mv2[:, 1:2], mv2[:, 1:2], AF.Exp,
                                               bias=zero, scale=-0.5),
                 reads=[mv2, "constpool"], writes=[mv2])
        kb.group("DVE", lambda e, t=t, xt=xt: e.tensor_scalar(
            ybuf[:, :], xt[:, :], mv2[:, 0:1], mv2[:, 1:2], ALU.subtract, ALU.mult),
            reads=[xt, mv2], writes=[ybuf])
        kb.group("DVE", lambda e, t=t, ob=ob: e.tensor_tensor(
            ob[:, :], ybuf[:, :], acc[:, t * D:(t + 1) * D], ALU.add),
            reads=[ybuf, (acc.name, t, 0), (acc.name, t, 1)], writes=[ob])
        for c in range(2):
            kb.group("DVE", lambda e, c=c, ob=ob: e.bn_stats(
                bns[:, c * 6:(c + 1) * 6], ob[:, c * 512:(c + 1) * 512]),
                reads=[ob, bns], writes=[bns])
        kb.group("DVE", lambda e: e.bn_aggr(mv2[:, :], bns[:, 0:12]),
                 reads=[bns], writes=[mv2])
        kb.group("ACT", lambda e: e.activation(mv2[:, 1:2], mv2[:, 1:2], AF.Ln,
                                               bias=eps1, scale=1.0),
                 reads=[mv2, "constpool"], writes=[mv2])
        kb.group("ACT", lambda e: e.activation(mv2[:, 1:2], mv2[:, 1:2], AF.Exp,
                                               bias=zero, scale=-0.5),
                 reads=[mv2, "constpool"], writes=[mv2])
        kb.group("DVE", lambda e, ob=ob: e.tensor_scalar(
            ob[:, :], ob[:, :], mv2[:, 0:1], mv2[:, 1:2], ALU.subtract, ALU.mult),
            reads=[ob, mv2], writes=[ob])
        kb.group("DVE", lambda e, ob=ob, xt=xt: e.tensor_tensor(
            ob[:, :], ob[:, :], xt[:, :], ALU.add), reads=[ob, xt], writes=[ob])
        kb.group("SP", lambda e, t=t, ob=ob: e.dma_start(
            out=xo[t * 128:(t + 1) * 128, :], in_=ob[:, :]),
            reads=[ob], writes=[("xo", t)], dma="SPQ")
    return




def build_L1a_proj(T=2048, DV=4096):
    D = 1024
    nc = bass.Bass("TRN2", target_bir_lowering=False)
    pr = Prog(nc)
    kb = pr.kb
    io = {}
    io["x"] = kb.dram_in("x", [T, D], F32)
    io["ogf"] = kb.dram_in("ogf", [DV, T], mybir.dt.bfloat16)
    io["owT"] = kb.dram_in("owT", [DV, D], F32)
    io["xo"] = kb.dram_out("xo", [T, D], F32)
    emit_L1a_proj(pr, io, T=T, DV=DV)
    kb.barrier_on("SP")
    kb.finalize()
    return nc

def emit_sandwich_epilogue_tile(pr, t, ob, xt, bns, mv2, eps1, zero, xo,
                                extra_ln=None):
    kb = pr.kb
    if extra_ln is not None:
        lng, lnb = extra_ln
        for c in range(2):
            kb.group("DVE", lambda e, c=c, ob=ob: e.bn_stats(
                bns[:, c * 6:(c + 1) * 6], ob[:, c * 512:(c + 1) * 512]),
                reads=[ob, bns], writes=[bns])
        kb.group("DVE", lambda e: e.bn_aggr(mv2[:, :], bns[:, 0:12]),
                 reads=[bns], writes=[mv2])
        kb.group("ACT", lambda e: e.activation(mv2[:, 1:2], mv2[:, 1:2], AF.Ln,
                                               bias=eps1, scale=1.0),
                 reads=[mv2, "constpool"], writes=[mv2])
        kb.group("ACT", lambda e: e.activation(mv2[:, 1:2], mv2[:, 1:2], AF.Exp,
                                               bias=zero, scale=-0.5),
                 reads=[mv2, "constpool"], writes=[mv2])
        kb.group("DVE", lambda e, ob=ob: e.tensor_scalar(
            ob[:, :], ob[:, :], mv2[:, 0:1], mv2[:, 1:2], ALU.subtract, ALU.mult),
            reads=[ob, mv2], writes=[ob])
        kb.group("DVE", lambda e, ob=ob, lng=lng: e.tensor_tensor(
            ob[:, :], ob[:, :], lng[:, :], ALU.mult), reads=[ob, lng], writes=[ob])
        kb.group("DVE", lambda e, ob=ob, lnb=lnb: e.tensor_tensor(
            ob[:, :], ob[:, :], lnb[:, :], ALU.add), reads=[ob, lnb], writes=[ob])
    for c in range(2):
        kb.group("DVE", lambda e, c=c, ob=ob: e.bn_stats(
            bns[:, c * 6:(c + 1) * 6], ob[:, c * 512:(c + 1) * 512]),
            reads=[ob, bns], writes=[bns])
    kb.group("DVE", lambda e: e.bn_aggr(mv2[:, :], bns[:, 0:12]),
             reads=[bns], writes=[mv2])
    kb.group("ACT", lambda e: e.activation(mv2[:, 1:2], mv2[:, 1:2], AF.Ln,
                                           bias=eps1, scale=1.0),
             reads=[mv2, "constpool"], writes=[mv2])
    kb.group("ACT", lambda e: e.activation(mv2[:, 1:2], mv2[:, 1:2], AF.Exp,
                                           bias=zero, scale=-0.5),
             reads=[mv2, "constpool"], writes=[mv2])
    kb.group("DVE", lambda e, ob=ob: e.tensor_scalar(
        ob[:, :], ob[:, :], mv2[:, 0:1], mv2[:, 1:2], ALU.subtract, ALU.mult),
        reads=[ob, mv2], writes=[ob])
    kb.group("DVE", lambda e, ob=ob, xt=xt: e.tensor_tensor(
        ob[:, :], ob[:, :], xt[:, :], ALU.add), reads=[ob, xt], writes=[ob])
    kb.group("SP", lambda e, t=t, ob=ob: e.dma_start(
        out=xo[t * 128:(t + 1) * 128, :], in_=ob[:, :]),
        reads=[ob], writes=[("xo", t)], dma="SPQ")



def emit_L1f(pr, io, T=2048):
    E, H = 8, 2048
    D = 1024
    NT = T // 128
    nc = pr.nc
    kb = pr.kb
    x = io["x"]
    gateT = io["gateT"]
    w1 = io["w1"]
    b1r = io["b1r"]
    ln1gt = io["ln1gt"]
    ln1bt = io["ln1bt"]
    w2 = io["w2"]
    b2r = io["b2r"]
    lngt = io["lngt"]
    lnbt = io["lnbt"]
    xo = io["xo"]

    accd = kb.dram_tmp("accd", [T, D], F32)
    gbscr = kb.sbuf_at("gbscr", [128, H], F32, nc.sbuf_top - 4 * H)
    hnT = kb.sbuf_at("hnT", [128, 16 * 128], BF16,
                     nc.sbuf_top - 4 * H - 2 * 16 * 128)
    stg = kb.sbuf("stg", [128, D], F32)
    xT = kb.sbuf("xT", [128, 8 * T], BF16)
    w1b2 = [kb.sbuf(f"w1b{i}", [128, 8 * H], BF16) for i in range(2)]
    w2b = kb.sbuf("w2b", [128, 16 * D], BF16)
    gtb = kb.sbuf("gtb", [128, 8 * E], BF16)
    b1b = kb.sbuf("b1b", [1, H], BF16)
    b2b = kb.sbuf("b2b", [1, D], BF16)
    gbc = kb.sbuf("gbc", [128, H], BF16)
    bbc = kb.sbuf("bbc", [128, H], BF16)
    lngb = kb.sbuf("lngb", [128, D], BF16)
    lnbb = kb.sbuf("lnbb", [128, D], BF16)
    cmb = kb.sbuf("cmb", [128, NT * E], F32)
    hnb2 = [kb.sbuf(f"hnb{i}", [128, H], BF16) for i in range(2)]
    logit = kb.sbuf("logit", [128, E], F32)
    rtmp = kb.sbuf("rtmp", [128, E], F32)
    lmax = kb.sbuf("lmax", [128, 4], F32)
    xrb = [kb.sbuf("xrb0", [128, D], F32)] * 2
    xnb = [kb.sbuf("xnb0", [128, D], BF16)] * 2
    ybuf = kb.sbuf("ybuf", [128, D], F32)
    obuf = [kb.sbuf("obuf0", [128, D], F32)] * 2
    mvball = kb.sbuf("mvball", [128, 2 * NT], F32)
    r1all = kb.sbuf("r1all", [128, NT], F32)
    mvh = kb.sbuf("mvh", [128, 2], F32)
    bns = kb.sbuf("bns", [128, 24], F32)
    mv2 = kb.sbuf("mv2", [128, 2], F32)
    eps1 = kb.constv(1e-5)
    zero = kb.constv(0.0)

    load_weight_chunked(kb, gtb, gateT, D, E, "POOLQ")
    kb.group("POOL", lambda e: e.dma_start(out=lngb[:, :], in_=lngt[:, :]),
             writes=[lngb], dma="POOLQ")
    kb.group("POOL", lambda e: e.dma_start(out=lnbb[:, :], in_=lnbt[:, :]),
             writes=[lnbb], dma="POOLQ")

    def prep_tile(t):
        xt = xrb[t % 2]
        kb.group("SP", lambda e, t=t, xt=xt: e.dma_start(
            out=xt[:, :], in_=x[t * 128:(t + 1) * 128, :]),
            writes=[xt], dma="SPQ")
        kb.group("DVE", lambda e, xt=xt: e.bn_stats(bns[:, 0:6], xt[:, 0:512]),
                 reads=[xt], writes=[bns])
        kb.group("DVE", lambda e, xt=xt: e.bn_stats(bns[:, 6:12], xt[:, 512:1024]),
                 reads=[xt, bns], writes=[bns])
        kb.group("DVE", lambda e, t=t: e.bn_aggr(mvball[:, 2 * t:2 * t + 2],
                                                 bns[:, 0:12]),
                 reads=[bns], writes=[mvball])
        kb.group("ACT", lambda e, t=t: e.activation(
            r1all[:, t:t + 1], mvball[:, 2 * t + 1:2 * t + 2], AF.Ln,
            bias=eps1, scale=1.0), reads=[mvball, "constpool"], writes=[r1all])
        kb.group("ACT", lambda e, t=t: e.activation(
            r1all[:, t:t + 1], r1all[:, t:t + 1], AF.Exp, bias=zero, scale=-0.5),
            reads=[r1all, "constpool"], writes=[r1all])
        xn = xnb[t % 2]
        kb.group("DVE", lambda e, t=t, xt=xt, xn=xn: e.tensor_scalar(
            xn[:, :], xt[:, :], mvball[:, 2 * t:2 * t + 1], r1all[:, t:t + 1],
            ALU.subtract, ALU.mult), reads=[xt, mvball, r1all], writes=[xn])
        return xn[:, :]

    emit_tm_to_fm(pr, prep_tile, NT, D, xT, T, 0, "tf")
    xT_keys = [(xT.name, "fm", t) for t in range(NT)]

    for t in range(NT):
        mm_group(kb, pr.pC[:, 0:E],
                 [(xT[:, c * T + t * 128: c * T + (t + 1) * 128],
                   gtb[:, c * E:(c + 1) * E]) for c in range(8)],
                 reads=[gtb] + xT_keys, writes=[pr.pC])
        kb.group("DVE", lambda e: e.tensor_copy(logit[:, :], pr.pC[:, 0:E]),
                 reads=[pr.pC], writes=[logit])
        kb.group("DVE", lambda e: e.reduce_max(lmax[:, 0:1], logit[:, :],
                                               axis=mybir.AxisListType.X),
                 reads=[logit], writes=[lmax])
        kb.group("DVE", lambda e: e.tensor_scalar(rtmp[:, :], logit[:, :],
                                                  lmax[:, 0:1], None, ALU.is_ge),
                 reads=[logit, lmax], writes=[rtmp])
        kb.group("DVE", lambda e: e.scalar_tensor_tensor(
            rtmp[:, :], rtmp[:, :], -1e30, logit[:, :], ALU.mult, ALU.add),
            reads=[rtmp, logit], writes=[rtmp])
        kb.group("DVE", lambda e: e.reduce_max(lmax[:, 1:2], rtmp[:, :],
                                               axis=mybir.AxisListType.X),
                 reads=[rtmp], writes=[lmax])
        kb.group("DVE", lambda e: e.tensor_scalar(rtmp[:, :], logit[:, :],
                                                  lmax[:, 1:2], None, ALU.is_ge),
                 reads=[logit, lmax], writes=[rtmp])
        kb.group("DVE", lambda e: e.tensor_scalar(logit[:, :], logit[:, :],
                                                  lmax[:, 0:1], None, ALU.subtract),
                 reads=[logit, lmax], writes=[logit])
        kb.group("ACT", lambda e: e.activation(logit[:, :], logit[:, :], AF.Exp,
                                               bias=zero, scale=1.0),
                 reads=[logit, "constpool"], writes=[logit])
        kb.group("DVE", lambda e: e.tensor_tensor(logit[:, :], logit[:, :],
                                                  rtmp[:, :], ALU.mult),
                 reads=[logit, rtmp], writes=[logit])
        kb.group("DVE", lambda e: e.tensor_reduce(lmax[:, 2:3], logit[:, :],
                                                  axis=mybir.AxisListType.X,
                                                  op=ALU.add),
                 reads=[logit, lmax], writes=[lmax])
        kb.group("ACT", lambda e: e.activation(lmax[:, 3:4], lmax[:, 2:3], AF.Ln,
                                               bias=zero, scale=1.0),
                 reads=[lmax, "constpool"], writes=[lmax])
        kb.group("ACT", lambda e: e.activation(lmax[:, 3:4], lmax[:, 3:4], AF.Exp,
                                               bias=zero, scale=-1.0),
                 reads=[lmax, "constpool"], writes=[lmax])
        kb.group("DVE", lambda e, t=t: e.tensor_scalar(
            cmb[:, t * E:(t + 1) * E], logit[:, :], lmax[:, 3:4], None, ALU.mult),
            reads=[logit, lmax], writes=[(cmb.name, t)])

    pkeys = [(pr.pA.name, 0), (pr.pA.name, 1), (pr.pB.name, 0), (pr.pB.name, 1)]

    def emit_h(t, w1b):
        for hh in range(4):
            ps = (pr.pA if hh < 2 else pr.pB)[:, (hh % 2) * 512:(hh % 2) * 512 + 512]
            pk = (pr.pA.name if hh < 2 else pr.pB.name, hh % 2)
            mm_group(kb, ps,
                     [(xT[:, c * T + t * 128: c * T + (t + 1) * 128],
                       w1b[:, c * H + hh * 512: c * H + (hh + 1) * 512])
                      for c in range(8)],
                     reads=[w1b, b1b, pr.ones_row] + xT_keys, writes=[pk],
                     bias=(pr.ones_row[0:1, 0:128],
                           b1b[0:1, hh * 512:(hh + 1) * 512]))

    def emit_chain(t):
        hnb = hnb2[t % 2]
        for hh in range(4):
            psrc = (pr.pA if hh < 2 else pr.pB)[:, (hh % 2) * 512:(hh % 2) * 512 + 512]
            kb.group("DVE", lambda e, psrc=psrc, hh=hh: e.bn_stats(
                bns[:, hh * 6:(hh + 1) * 6], psrc),
                reads=pkeys + [bns], writes=[bns])
        kb.group("DVE", lambda e: e.bn_aggr(mvh[:, :], bns[:, :]),
                 reads=[bns], writes=[mvh])
        for half in range(2):
            psrc = (pr.pA if half == 0 else pr.pB)[:, 0:1024]
            kb.group("DVE", lambda e, psrc=psrc, half=half: e.tensor_copy(
                gbscr[:, half * 1024:(half + 1) * 1024], psrc),
                reads=pkeys, writes=[("gbscr", half)])
        kb.group("ACT", lambda e: e.activation(mvh[:, 1:2], mvh[:, 1:2], AF.Ln,
                                               bias=eps1, scale=1.0),
                 reads=[mvh, "constpool"], writes=[mvh])
        kb.group("ACT", lambda e: e.activation(mvh[:, 1:2], mvh[:, 1:2], AF.Exp,
                                               bias=zero, scale=-0.5),
                 reads=[mvh, "constpool"], writes=[mvh])
        for hh in range(4):
            sl = slice(hh * 512, (hh + 1) * 512)
            kb.group("DVE", lambda e, sl=sl: e.scalar_tensor_tensor(
                gbscr[:, sl], gbscr[:, sl],
                mvh[:, 0:1], gbc[:, sl], ALU.subtract, ALU.mult),
                reads=[("gbscr", 0), ("gbscr", 1), mvh, gbc],
                writes=[("gbscr", 0), ("gbscr", 1)])
            kb.group("DVE", lambda e, sl=sl: e.scalar_tensor_tensor(
                gbscr[:, sl], gbscr[:, sl],
                mvh[:, 1:2], bbc[:, sl], ALU.mult, ALU.add),
                reads=[("gbscr", 0), ("gbscr", 1), mvh, bbc],
                writes=[("gbscr", 0), ("gbscr", 1)])
            kb.group("ACT", lambda e, sl=sl, hnb=hnb: e.activation(
                hnb[:, sl], gbscr[:, sl], AF.Gelu,
                bias=zero, scale=1.0),
                reads=[("gbscr", 0), ("gbscr", 1), "constpool"],
                writes=[(hnb.name,)])

    def emit_ty(t, e_i):
        hnb = hnb2[t % 2]
        for g8 in range(2):
            buf = pr.pb0 if g8 == 0 else pr.pb1
            def tr(e, g8=g8, buf=buf, hnb=hnb):
                last = None
                for j in range(8):
                    last = e.transpose(buf[:, j * 128:(j + 1) * 128],
                                       hnb[:, (g8 * 8 + j) * 128:
                                           (g8 * 8 + j + 1) * 128],
                                       pr.ident[:, :])
                return last
            kb.group("PE", tr, reads=[(hnb.name,), pr.ident], writes=[buf])
            kb.group("DVE", lambda e, g8=g8, buf=buf: e.tensor_copy(
                hnT[:, g8 * 1024:(g8 + 1) * 1024], buf[:, 0:1024]),
                reads=[buf], writes=[(hnT.name, g8)])
        for half in range(2):
            ps = pr.pC[:, :] if half == 0 else pr.pD[:, :]
            pk = pr.pC if half == 0 else pr.pD
            mm_group(kb, ps,
                     [(hnT[:, c * 128:(c + 1) * 128],
                       w2b[:, c * D + half * 512: c * D + (half + 1) * 512])
                      for c in range(16)],
                     reads=[(hnT.name, 0), (hnT.name, 1), w2b, b2b, pr.ones_row],
                     writes=[pk],
                     bias=(pr.ones_row[0:1, 0:128],
                           b2b[0:1, half * 512:(half + 1) * 512]))
            sc = cmb[:, t * E + e_i: t * E + e_i + 1]
            kb.group("DVE", lambda e, ps=ps, sc=sc, half=half, t=t:
                     e.tensor_scalar(stg[:, half * 512:(half + 1) * 512],
                                     ps, sc, None, ALU.mult),
                     reads=[pk, (cmb.name, t)], writes=[(stg.name, half)])
        kb.group("POOL", lambda e, t=t, e_i=e_i: e.dma_start(
            out=accd[t * 128:(t + 1) * 128, :], in_=stg[:, :],
            accum_op=mybir.AluOpType.add)
            if e_i > 0 else
            e.dma_start(out=accd[t * 128:(t + 1) * 128, :], in_=stg[:, :]),
            reads=[(stg.name, 0), (stg.name, 1)], writes=[("accd", t)],
            dma=kb.dma_queue(f"POOLQ.acc{t % 2}"))

    pend = None
    for e_i in range(E):
        if pend is not None and pend[1] != e_i:
            emit_ty(*pend)
            pend = None
        w1b = w1b2[e_i % 2]
        kb.group("POOL", lambda e, e_i=e_i, w1b=w1b: e.dma_start(
            out=w1b[:, :].rearrange("p (c n) -> p c n", c=8),
            in_=w1[e_i].rearrange("(c p) n -> p c n", p=128)),
            writes=[w1b], dma=kb.dma_queue(f"POOLQ.w1{e_i % 2}"))
        kb.group("POOL", lambda e, e_i=e_i: e.dma_start(
            out=w2b[:, :].rearrange("p (c n) -> p c n", c=16),
            in_=w2[e_i].rearrange("(c p) n -> p c n", p=128)),
            writes=[w2b], dma=kb.dma_queue("POOLQ.w2"))
        kb.group("POOL", lambda e, e_i=e_i: e.dma_start(
            out=b1b[:, :], in_=b1r[e_i:e_i + 1, :]),
            writes=[b1b], dma=kb.dma_queue("POOLQ.w3"))
        kb.group("POOL", lambda e, e_i=e_i: e.dma_start(
            out=b2b[:, :], in_=b2r[e_i:e_i + 1, :]),
            writes=[b2b], dma=kb.dma_queue("POOLQ.w3"))
        kb.group("POOL", lambda e, e_i=e_i: e.dma_start(
            out=gbc[:, :], in_=ln1gt[e_i]), writes=[gbc],
            dma=kb.dma_queue("POOLQ.w4"))
        kb.group("POOL", lambda e, e_i=e_i: e.dma_start(
            out=bbc[:, :], in_=ln1bt[e_i]), writes=[bbc],
            dma=kb.dma_queue("POOLQ.w4"))
        for t in range(NT):
            emit_h(t, w1b)
            emit_chain(t)
            if pend is not None:
                emit_ty(*pend)
            pend = (t, e_i)
    emit_ty(*pend)
    for t in range(NT):
        xt = xrb[t % 2]
        ob = obuf[t % 2]
        kb.group("SP", lambda e, t=t, xt=xt: e.dma_start(
            out=xt[:, :], in_=x[t * 128:(t + 1) * 128, :]),
            writes=[xt], dma="SPQ")
        kb.group("SP", lambda e, t=t: e.dma_start(
            out=stg[:, :], in_=accd[t * 128:(t + 1) * 128, :]),
            reads=[("accd", t)], writes=[(stg.name, 0), (stg.name, 1)],
            dma=kb.dma_queue("SPQ.rd"))
        kb.group("DVE", lambda e, t=t, xt=xt: e.tensor_scalar(
            ybuf[:, :], xt[:, :], mvball[:, 2 * t:2 * t + 1], r1all[:, t:t + 1],
            ALU.subtract, ALU.mult), reads=[xt, mvball, r1all], writes=[ybuf])
        kb.group("DVE", lambda e, t=t, ob=ob: e.tensor_tensor(
            ob[:, :], ybuf[:, :], stg[:, :], ALU.add),
            reads=[ybuf, (stg.name, 0), (stg.name, 1)], writes=[ob])
        emit_sandwich_epilogue_tile(pr, t, ob, xt, bns, mv2, eps1, zero, xo,
                                    extra_ln=(lngb, lnbb))
    return





def build_L1f(T=2048):
    D, E, H = 1024, 8, 2048
    nc = bass.Bass("TRN2", target_bir_lowering=False)
    pr = Prog(nc)
    kb = pr.kb
    io = {}
    io["x"] = kb.dram_in("x", [T, D], F32)
    io["gateT"] = kb.dram_in("gateT", [D, E], F32)
    io["w1"] = kb.dram_in("w1", [E, D, H], F32)
    io["b1r"] = kb.dram_in("b1r", [E, H], F32)
    io["ln1gt"] = kb.dram_in("ln1gt", [E, 128, H], F32)
    io["ln1bt"] = kb.dram_in("ln1bt", [E, 128, H], F32)
    io["w2"] = kb.dram_in("w2", [E, H, D], F32)
    io["b2r"] = kb.dram_in("b2r", [E, D], F32)
    io["lngt"] = kb.dram_in("lngt", [128, D], F32)
    io["lnbt"] = kb.dram_in("lnbt", [128, D], F32)
    io["xo"] = kb.dram_out("xo", [T, D], F32)
    emit_L1f(pr, io, T=T)
    kb.barrier_on("SP")
    kb.finalize()
    return nc

def emit_L2f(pr, io, T=2048):
    E, I, G = 32, 512, 8
    D = 1024
    NT = T // 128
    nc = pr.nc
    kb = pr.kb
    x = io["x"]
    gateT = io["gateT"]
    gbias = io["gbias"]
    w_gateT = io["w_gateT"]
    w_upT = io["w_upT"]
    w_downT = io["w_downT"]
    sh_gateT = io["sh_gateT"]
    sh_upT = io["sh_upT"]
    sh_downT = io["sh_downT"]
    xo = io["xo"]

    accd = kb.dram_tmp("accd", [T, D], F32)
    stg = kb.sbuf("stg", [128, D], F32)
    xT = kb.sbuf("xT", [128, 8 * T], BF16)
    wgb2 = [kb.sbuf(f"wgb{i}", [128, 8 * 2 * I], BF16) for i in range(2)]
    wub2 = [kb.sbuf(f"wub{i}", [128, 8 * 2 * I], BF16) for i in range(2)]
    wdb2 = [kb.sbuf(f"wdb{i}", [128, 8 * D], BF16) for i in range(2)]
    gtb = kb.sbuf("gtb", [128, 8 * E], BF16)
    gbb = kb.sbuf("gbb", [128, E], F32)
    prodb2 = [kb.sbuf(f"prodb{i}", [128, 8 * 512], BF16) for i in range(2)]
    gsb2 = [kb.sbuf(f"gsb{i}", [128, 512], BF16) for i in range(2)]
    cmb = kb.sbuf("cmb", [128, NT * E], F32)
    logit = kb.sbuf("logit", [128, E], F32)
    rtmp = kb.sbuf("rtmp", [128, E], F32)
    rtm2 = kb.sbuf("rtm2", [128, E], F32)
    gsum = kb.sbuf("gsum", [128, G], F32)
    gsum2 = kb.sbuf("gsum2", [128, G], F32)
    gtmp = kb.sbuf("gtmp", [128, E], F32)
    mtmp = kb.sbuf("mtmp", [128, E], F32)
    lmax = kb.sbuf("lmax", [128, 4], F32)
    xrb = [kb.sbuf(f"xrb{i}", [128, D], F32) for i in range(2)]
    xnb = [kb.sbuf(f"xnb{i}", [128, D], BF16) for i in range(2)]
    obuf = [kb.sbuf(f"obuf{i}", [128, D], F32) for i in range(2)]
    mvball = kb.sbuf("mvball", [128, 2 * NT], F32)
    r1all = kb.sbuf("r1all", [128, NT], F32)
    bns = kb.sbuf("bns", [128, 12], F32)
    mv2 = kb.sbuf("mv2", [128, 2], F32)
    eps1 = kb.constv(1e-5)
    eps20 = kb.constv(1e-20)
    zero = kb.constv(0.0)

    load_weight_chunked(kb, gtb, gateT, D, E, "POOLQ")
    kb.group("POOL", lambda e: e.dma_start(out=gbb[:, :], in_=gbias[:, :]),
             writes=[gbb], dma="POOLQ")

    def prep_tile(t):
        xt = xrb[t % 2]
        kb.group("SP", lambda e, t=t, xt=xt: e.dma_start(
            out=xt[:, :], in_=x[t * 128:(t + 1) * 128, :]),
            writes=[xt], dma="SPQ")
        kb.group("DVE", lambda e, xt=xt: e.bn_stats(bns[:, 0:6], xt[:, 0:512]),
                 reads=[xt], writes=[bns])
        kb.group("DVE", lambda e, xt=xt: e.bn_stats(bns[:, 6:12], xt[:, 512:1024]),
                 reads=[xt, bns], writes=[bns])
        kb.group("DVE", lambda e, t=t: e.bn_aggr(mvball[:, 2 * t:2 * t + 2],
                                                 bns[:, 0:12]),
                 reads=[bns], writes=[mvball])
        kb.group("ACT", lambda e, t=t: e.activation(
            r1all[:, t:t + 1], mvball[:, 2 * t + 1:2 * t + 2], AF.Ln,
            bias=eps1, scale=1.0), reads=[mvball, "constpool"], writes=[r1all])
        kb.group("ACT", lambda e, t=t: e.activation(
            r1all[:, t:t + 1], r1all[:, t:t + 1], AF.Exp, bias=zero, scale=-0.5),
            reads=[r1all, "constpool"], writes=[r1all])
        xn = xnb[t % 2]
        kb.group("DVE", lambda e, t=t, xt=xt, xn=xn: e.tensor_scalar(
            xn[:, :], xt[:, :], mvball[:, 2 * t:2 * t + 1], r1all[:, t:t + 1],
            ALU.subtract, ALU.mult), reads=[xt, mvball, r1all], writes=[xn])
        return xn[:, :]

    emit_tm_to_fm(pr, prep_tile, NT, D, xT, T, 0, "tf")
    xT_keys = [(xT.name, "fm", t) for t in range(NT)]

    for t in range(NT):
        mm_group(kb, pr.pC[:, 0:E],
                 [(xT[:, c * T + t * 128: c * T + (t + 1) * 128],
                   gtb[:, c * E:(c + 1) * E]) for c in range(8)],
                 reads=[gtb] + xT_keys, writes=[pr.pC])
        kb.group("ACT", lambda e: e.activation(logit[:, :], pr.pC[:, 0:E],
                                               AF.Sigmoid, bias=zero, scale=1.0),
                 reads=[pr.pC, "constpool"], writes=[logit])
        kb.group("DVE", lambda e: e.tensor_tensor(rtmp[:, :], logit[:, :],
                                                  gbb[:, :], ALU.add),
                 reads=[logit, gbb], writes=[rtmp])
        kb.group("DVE", lambda e: e.tensor_reduce(
            gsum[:, :], rtmp[:, :].rearrange("p (g k) -> p g k", g=G),
            axis=mybir.AxisListType.X, op=ALU.max),
            reads=[rtmp], writes=[gsum])
        for gg in range(G):
            kb.group("DVE", lambda e, gg=gg: e.tensor_scalar(
                rtm2[:, gg * 4:(gg + 1) * 4], rtmp[:, gg * 4:(gg + 1) * 4],
                gsum[:, gg:gg + 1], None, ALU.is_ge),
                reads=[rtmp, gsum, rtm2], writes=[rtm2])
        kb.group("DVE", lambda e: e.scalar_tensor_tensor(
            rtm2[:, :], rtm2[:, :], -1e30, rtmp[:, :], ALU.mult, ALU.add),
            reads=[rtm2, rtmp], writes=[rtm2])
        kb.group("DVE", lambda e: e.tensor_reduce(
            gsum2[:, :], rtm2[:, :].rearrange("p (g k) -> p g k", g=G),
            axis=mybir.AxisListType.X, op=ALU.max),
            reads=[rtm2], writes=[gsum2])
        kb.group("DVE", lambda e: e.tensor_tensor(gsum[:, :], gsum[:, :],
                                                  gsum2[:, :], ALU.add),
                 reads=[gsum, gsum2], writes=[gsum])
        kb.group("DVE", lambda e: e.tensor_copy(gsum2[:, :], gsum[:, :]),
                 reads=[gsum], writes=[gsum2])
        for _ in range(3):
            kb.group("DVE", lambda e: e.reduce_max(lmax[:, 0:1], gsum2[:, :],
                                                   axis=mybir.AxisListType.X),
                     reads=[gsum2], writes=[lmax])
            kb.group("DVE", lambda e: e.tensor_scalar(
                gtmp[:, 0:G], gsum2[:, :], lmax[:, 0:1], None,
                ALU.is_ge), reads=[gsum2, lmax], writes=["gtmp"])
            kb.group("DVE", lambda e: e.scalar_tensor_tensor(
                gsum2[:, :], gtmp[:, 0:G], -1e30, gsum2[:, :],
                ALU.mult, ALU.add), reads=["gtmp", gsum2], writes=[gsum2])
        kb.group("DVE", lambda e: e.reduce_max(lmax[:, 0:1], gsum2[:, :],
                                               axis=mybir.AxisListType.X),
                 reads=[gsum2], writes=[lmax])
        kb.group("DVE", lambda e: e.tensor_scalar(gsum2[:, :], gsum[:, :],
                                                  lmax[:, 0:1], None, ALU.is_ge),
                 reads=[gsum, lmax], writes=[gsum2])
        for gg in range(G):
            kb.group("DVE", lambda e, gg=gg: e.tensor_scalar(
                rtm2[:, gg * 4:(gg + 1) * 4], rtmp[:, gg * 4:(gg + 1) * 4],
                gsum2[:, gg:gg + 1], None, ALU.mult),
                reads=[rtmp, gsum2, rtm2], writes=[rtm2])
        kb.group("DVE", lambda e: e.tensor_copy(mtmp[:, :],
                                                rtm2[:, :]),
                 reads=[rtm2], writes=["mtmp"])
        for _ in range(7):
            kb.group("DVE", lambda e: e.reduce_max(lmax[:, 1:2],
                                                   mtmp[:, :],
                                                   axis=mybir.AxisListType.X),
                     reads=["mtmp"], writes=[lmax])
            kb.group("DVE", lambda e: e.tensor_scalar(
                gtmp[:, 0:E], mtmp[:, :],
                lmax[:, 1:2], None, ALU.is_ge), reads=["mtmp", lmax],
                writes=["gtmp"])
            kb.group("DVE", lambda e: e.scalar_tensor_tensor(
                mtmp[:, :], gtmp[:, 0:E], -1e30,
                mtmp[:, :], ALU.mult, ALU.add),
                reads=["gtmp", "mtmp"], writes=["mtmp"])
        kb.group("DVE", lambda e: e.reduce_max(lmax[:, 1:2],
                                               mtmp[:, :],
                                               axis=mybir.AxisListType.X),
                 reads=["mtmp"], writes=[lmax])
        kb.group("DVE", lambda e: e.tensor_scalar(rtm2[:, :], rtm2[:, :],
                                                  lmax[:, 1:2], None, ALU.is_ge),
                 reads=[rtm2, lmax], writes=[rtm2])
        kb.group("DVE", lambda e: e.tensor_tensor(logit[:, :], logit[:, :],
                                                  rtm2[:, :], ALU.mult),
                 reads=[logit, rtm2], writes=[logit])
        kb.group("DVE", lambda e: e.tensor_reduce(lmax[:, 2:3], logit[:, :],
                                                  axis=mybir.AxisListType.X,
                                                  op=ALU.add),
                 reads=[logit, lmax], writes=[lmax])
        kb.group("ACT", lambda e: e.activation(lmax[:, 3:4], lmax[:, 2:3], AF.Ln,
                                               bias=eps20, scale=1.0),
                 reads=[lmax, "constpool"], writes=[lmax])
        kb.group("ACT", lambda e: e.activation(lmax[:, 3:4], lmax[:, 3:4], AF.Exp,
                                               bias=zero, scale=-1.0),
                 reads=[lmax, "constpool"], writes=[lmax])
        kb.group("DVE", lambda e, t=t: e.tensor_scalar(
            cmb[:, t * E:(t + 1) * E], logit[:, :], lmax[:, 3:4], 2.5,
            ALU.mult, ALU.mult),
            reads=[logit, lmax], writes=[(cmb.name, t)])

    def emit_down(e_i, tb, wdb, prodb, nffc):
        for tt4 in range(4):
            t = tb * 4 + tt4
            for half in range(2):
                ps = pr.pC[:, :] if half == 0 else pr.pD[:, :]
                pk = pr.pC if half == 0 else pr.pD
                mm_group(kb, ps,
                         [(prodb[:, fc * 512 + tt4 * 128: fc * 512 + tt4 * 128 + 128],
                           wdb[:, fc * D + half * 512: fc * D + (half + 1) * 512])
                          for fc in range(nffc)],
                         reads=[prodb, wdb], writes=[pk])
                if e_i < 0:
                    kb.group("DVE", lambda e, ps=ps, half=half: e.tensor_copy(
                        stg[:, half * 512:(half + 1) * 512], ps),
                        reads=[pk], writes=[(stg.name, half)])
                else:
                    sc = cmb[:, t * E + e_i: t * E + e_i + 1]
                    kb.group("DVE", lambda e, ps=ps, sc=sc, half=half:
                             e.tensor_scalar(stg[:, half * 512:(half + 1) * 512],
                                             ps, sc, None, ALU.mult),
                             reads=[pk, (cmb.name, t)], writes=[(stg.name, half)])
            kb.group("POOL", lambda e, t=t, e_i=e_i: e.dma_start(
                out=accd[t * 128:(t + 1) * 128, :], in_=stg[:, :],
                accum_op=mybir.AluOpType.add)
                if e_i >= 0 else
                e.dma_start(out=accd[t * 128:(t + 1) * 128, :], in_=stg[:, :]),
                reads=[(stg.name, 0), (stg.name, 1)], writes=[("accd", t)],
                dma=kb.dma_queue(f"POOLQ.acc{t % 2}"))

    pend = None
    blk = 0
    for e_i in range(-1, E):
        ff = 2 * I if e_i < 0 else I
        nffc = ff // 128
        wgb = wgb2[e_i % 2]
        wub = wub2[e_i % 2]
        wdb = wdb2[e_i % 2]
        if e_i < 0:
            kb.group("POOL", lambda e, ff=ff, wgb=wgb: e.dma_start(
                out=wgb[:, 0:8 * ff].rearrange("p (c n) -> p c n", c=8),
                in_=sh_gateT.rearrange("(c p) n -> p c n", p=128)),
                writes=[wgb], dma=kb.dma_queue(f"POOLQ.w1{e_i % 2}"))
            kb.group("POOL", lambda e, ff=ff, wub=wub: e.dma_start(
                out=wub[:, 0:8 * ff].rearrange("p (c n) -> p c n", c=8),
                in_=sh_upT.rearrange("(c p) n -> p c n", p=128)),
                writes=[wub], dma=kb.dma_queue(f"POOLQ.w2{e_i % 2}"))
            kb.group("POOL", lambda e, nffc=nffc, wdb=wdb: e.dma_start(
                out=wdb[:, 0:nffc * D].rearrange("p (c n) -> p c n", c=nffc),
                in_=sh_downT.rearrange("(c p) n -> p c n", p=128)),
                writes=[wdb], dma=kb.dma_queue(f"POOLQ.w3{e_i % 2}"))
        else:
            kb.group("POOL", lambda e, e_i=e_i, ff=ff, wgb=wgb: e.dma_start(
                out=wgb[:, 0:8 * ff].rearrange("p (c n) -> p c n", c=8),
                in_=w_gateT[e_i].rearrange("(c p) n -> p c n", p=128)),
                writes=[wgb], dma=kb.dma_queue(f"POOLQ.w1{e_i % 2}"))
            kb.group("POOL", lambda e, e_i=e_i, ff=ff, wub=wub: e.dma_start(
                out=wub[:, 0:8 * ff].rearrange("p (c n) -> p c n", c=8),
                in_=w_upT[e_i].rearrange("(c p) n -> p c n", p=128)),
                writes=[wub], dma=kb.dma_queue(f"POOLQ.w2{e_i % 2}"))
            kb.group("POOL", lambda e, e_i=e_i, nffc=nffc, wdb=wdb: e.dma_start(
                out=wdb[:, 0:nffc * D].rearrange("p (c n) -> p c n", c=nffc),
                in_=w_downT[e_i].rearrange("(c p) n -> p c n", p=128)),
                writes=[wdb], dma=kb.dma_queue(f"POOLQ.w3{e_i % 2}"))
        for tb in range(T // 512):
            prodb = prodb2[blk % 2]
            blk += 1
            for fc in range(nffc):
                psg = pr.pA[:, (fc % 2) * 512:(fc % 2) * 512 + 512]
                pkg = (pr.pA.name, fc % 2)
                mm_group(kb, psg,
                         [(wgb[:, c * ff + fc * 128: c * ff + (fc + 1) * 128],
                           xT[:, c * T + tb * 512: c * T + (tb + 1) * 512])
                          for c in range(8)],
                         reads=[wgb] + xT_keys, writes=[pkg])
                gsb = gsb2[fc % 2]
                kb.group("ACT", lambda e, psg=psg, gsb=gsb: e.activation(
                    gsb[:, :], psg, AF.Silu, bias=zero, scale=1.0),
                    reads=[pkg, "constpool"], writes=[gsb])
                psu = pr.pB[:, (fc % 2) * 512:(fc % 2) * 512 + 512]
                pku = (pr.pB.name, fc % 2)
                mm_group(kb, psu,
                         [(wub[:, c * ff + fc * 128: c * ff + (fc + 1) * 128],
                           xT[:, c * T + tb * 512: c * T + (tb + 1) * 512])
                          for c in range(8)],
                         reads=[wub] + xT_keys, writes=[pku])
                kb.group("DVE", lambda e, psu=psu, fc=fc, gsb=gsb, prodb=prodb:
                         e.tensor_tensor(
                    prodb[:, fc * 512:(fc + 1) * 512], gsb[:, :], psu, ALU.mult),
                    reads=[gsb, pku, prodb], writes=[prodb])
            if pend is not None:
                emit_down(*pend)
            pend = (e_i, tb, wdb, prodb, nffc)
    if pend is not None:
        emit_down(*pend)
    for t in range(NT):
        xt = xrb[t % 2]
        ob = obuf[t % 2]
        kb.group("SP", lambda e, t=t, xt=xt: e.dma_start(
            out=xt[:, :], in_=x[t * 128:(t + 1) * 128, :]),
            writes=[xt], dma="SPQ")
        kb.group("SP", lambda e, t=t, ob=ob: e.dma_start(
            out=ob[:, :], in_=accd[t * 128:(t + 1) * 128, :]),
            reads=[("accd", t)], writes=[ob], dma=kb.dma_queue("SPQ.rd"))
        emit_sandwich_epilogue_tile(pr, t, ob, xt, bns, mv2, eps1, zero, xo)
    return





def build_L2f(T=2048):
    D, E, I, G = 1024, 32, 512, 8
    nc = bass.Bass("TRN2", target_bir_lowering=False)
    pr = Prog(nc)
    kb = pr.kb
    io = {}
    io["x"] = kb.dram_in("x", [T, D], F32)
    io["gateT"] = kb.dram_in("gateT", [D, E], F32)
    io["gbias"] = kb.dram_in("gbias", [128, E], F32)
    io["w_gateT"] = kb.dram_in("w_gateT", [E, D, I], F32)
    io["w_upT"] = kb.dram_in("w_upT", [E, D, I], F32)
    io["w_downT"] = kb.dram_in("w_downT", [E, I, D], F32)
    io["sh_gateT"] = kb.dram_in("sh_gateT", [D, 2 * I], F32)
    io["sh_upT"] = kb.dram_in("sh_upT", [D, 2 * I], F32)
    io["sh_downT"] = kb.dram_in("sh_downT", [2 * I, D], F32)
    io["xo"] = kb.dram_out("xo", [T, D], F32)
    emit_L2f(pr, io, T=T)
    kb.barrier_on("SP")
    kb.finalize()
    return nc

def emit_L0f_moe(pr, io, T=2048):
    E, H = 8, 2048
    D = 1024
    NT = T // 128
    nc = pr.nc
    kb = pr.kb
    x = io["x"]
    x2 = io["x2"]
    norm2t = io["norm2t"]
    routerT = io["routerT"]
    sh_gate = io["sh_gate"]
    sh_up = io["sh_up"]
    sh_down = io["sh_down"]
    e_gate = io["e_gate"]
    e_up = io["e_up"]
    e_down = io["e_down"]
    xo = io["xo"]

    accd = kb.dram_tmp("accd", [T, D], F32)
    stg = kb.sbuf("stg", [128, D], F32)
    xT = kb.sbuf("xT", [128, 8 * T], BF16)
    wgb = kb.sbuf("wgb", [128, 8 * H], BF16)
    wub = kb.sbuf("wub", [128, 8 * H], BF16)
    wdb = kb.sbuf("wdb", [128, 16 * D], BF16)
    gtb = kb.sbuf("gtb", [128, 8 * E], BF16)
    prodb2 = [kb.sbuf(f"prodb{i}", [128, 16 * 512], BF16) for i in range(2)]
    gsb2 = [kb.sbuf(f"gsb{i}", [128, 512], BF16) for i in range(2)]
    nwt = kb.sbuf("nwt", [128, D], BF16)
    cmb = kb.sbuf("cmb", [128, NT * E], F32)
    logit = kb.sbuf("logit", [128, E], F32)
    rtmp = kb.sbuf("rtmp", [128, E], F32)
    lmax = kb.sbuf("lmax", [128, 4], F32)
    xrb = [kb.sbuf(f"xrb{i}", [128, D], F32) for i in range(2)]
    xnb = [kb.sbuf("xnb0", [128, D], BF16)] * 2
    obuf = [kb.sbuf(f"obuf{i}", [128, D], F32) for i in range(2)]
    mvball = kb.sbuf("mvball", [128, 2 * NT], F32)
    r1all = kb.sbuf("r1all", [128, NT], F32)
    sc1 = kb.sbuf("sc1", [128, 4], F32)
    bns = kb.sbuf("bns", [128, 12], F32)
    mv2 = kb.sbuf("mv2", [128, 2], F32)
    eps1 = kb.constv(1e-5)
    eps6 = kb.constv(1e-6)
    zero = kb.constv(0.0)

    load_weight_chunked(kb, gtb, routerT, D, E, "POOLQ")
    kb.group("POOL", lambda e: e.dma_start(out=nwt[:, :], in_=norm2t[:, :]),
             writes=[nwt], dma="POOLQ")

    def prep_tile(t):
        xt = xrb[t % 2]
        kb.group("SP", lambda e, t=t, xt=xt: e.dma_start(
            out=xt[:, :], in_=x2[t * 128:(t + 1) * 128, :]),
            writes=[xt], dma="SPQ")
        kb.group("DVE", lambda e, xt=xt: e.bn_stats(bns[:, 0:6], xt[:, 0:512]),
                 reads=[xt], writes=[bns])
        kb.group("DVE", lambda e, xt=xt: e.bn_stats(bns[:, 6:12], xt[:, 512:1024]),
                 reads=[xt, bns], writes=[bns])
        kb.group("DVE", lambda e, t=t: e.bn_aggr(mvball[:, 2 * t:2 * t + 2],
                                                 bns[:, 0:12]),
                 reads=[bns], writes=[mvball])
        kb.group("DVE", lambda e, t=t: e.tensor_tensor(
            sc1[:, 0:1], mvball[:, 2 * t:2 * t + 1], mvball[:, 2 * t:2 * t + 1],
            ALU.mult), reads=[mvball], writes=[sc1])
        kb.group("DVE", lambda e, t=t: e.tensor_tensor(
            sc1[:, 0:1], sc1[:, 0:1], mvball[:, 2 * t + 1:2 * t + 2], ALU.add),
            reads=[sc1, mvball], writes=[sc1])
        kb.group("ACT", lambda e: e.activation(sc1[:, 1:2], sc1[:, 0:1], AF.Ln,
                                               bias=eps6, scale=1.0),
                 reads=[sc1, "constpool"], writes=[sc1])
        kb.group("ACT", lambda e, t=t: e.activation(
            r1all[:, t:t + 1], sc1[:, 1:2], AF.Exp, bias=zero, scale=-0.5),
            reads=[sc1, "constpool"], writes=[r1all])
        xn = xnb[t % 2]
        kb.group("DVE", lambda e, t=t, xt=xt, xn=xn: e.tensor_scalar(
            xn[:, :], xt[:, :], r1all[:, t:t + 1], None, ALU.mult),
            reads=[xt, r1all], writes=[xn])
        kb.group("DVE", lambda e, xn=xn: e.tensor_tensor(
            xn[:, :], xn[:, :], nwt[:, :], ALU.mult), reads=[xn, nwt], writes=[xn])
        return xn[:, :]

    emit_tm_to_fm(pr, prep_tile, NT, D, xT, T, 0, "tf")
    xT_keys = [(xT.name, "fm", t) for t in range(NT)]

    for t in range(NT):
        mm_group(kb, pr.pC[:, 0:E],
                 [(xT[:, c * T + t * 128: c * T + (t + 1) * 128],
                   gtb[:, c * E:(c + 1) * E]) for c in range(8)],
                 reads=[gtb] + xT_keys, writes=[pr.pC])
        kb.group("DVE", lambda e: e.tensor_copy(logit[:, :], pr.pC[:, 0:E]),
                 reads=[pr.pC], writes=[logit])
        kb.group("DVE", lambda e: e.reduce_max(lmax[:, 0:1], logit[:, :],
                                               axis=mybir.AxisListType.X),
                 reads=[logit], writes=[lmax])
        kb.group("DVE", lambda e: e.tensor_scalar(logit[:, :], logit[:, :],
                                                  lmax[:, 0:1], None, ALU.subtract),
                 reads=[logit, lmax], writes=[logit])
        kb.group("ACT", lambda e: e.activation(logit[:, :], logit[:, :], AF.Exp,
                                               bias=zero, scale=1.0),
                 reads=[logit, "constpool"], writes=[logit])
        kb.group("DVE", lambda e: e.tensor_reduce(lmax[:, 1:2], logit[:, :],
                                                  axis=mybir.AxisListType.X,
                                                  op=ALU.add),
                 reads=[logit, lmax], writes=[lmax])
        kb.group("ACT", lambda e: e.activation(lmax[:, 1:2], lmax[:, 1:2], AF.Ln,
                                               bias=zero, scale=1.0),
                 reads=[lmax, "constpool"], writes=[lmax])
        kb.group("ACT", lambda e: e.activation(lmax[:, 1:2], lmax[:, 1:2], AF.Exp,
                                               bias=zero, scale=-1.0),
                 reads=[lmax, "constpool"], writes=[lmax])
        kb.group("DVE", lambda e: e.tensor_scalar(logit[:, :], logit[:, :],
                                                  lmax[:, 1:2], None, ALU.mult),
                 reads=[logit, lmax], writes=[logit])
        kb.group("DVE", lambda e: e.reduce_max(lmax[:, 0:1], logit[:, :],
                                               axis=mybir.AxisListType.X),
                 reads=[logit], writes=[lmax])
        kb.group("DVE", lambda e: e.tensor_scalar(rtmp[:, :], logit[:, :],
                                                  lmax[:, 0:1], None, ALU.is_ge),
                 reads=[logit, lmax], writes=[rtmp])
        kb.group("DVE", lambda e: e.scalar_tensor_tensor(
            rtmp[:, :], rtmp[:, :], -1e30, logit[:, :], ALU.mult, ALU.add),
            reads=[rtmp, logit], writes=[rtmp])
        kb.group("DVE", lambda e: e.reduce_max(lmax[:, 1:2], rtmp[:, :],
                                               axis=mybir.AxisListType.X),
                 reads=[rtmp], writes=[lmax])
        kb.group("DVE", lambda e: e.tensor_scalar(rtmp[:, :], logit[:, :],
                                                  lmax[:, 1:2], None, ALU.is_ge),
                 reads=[logit, lmax], writes=[rtmp])
        kb.group("ACT", lambda e: e.activation(logit[:, :], logit[:, :], AF.Exp,
                                               bias=zero, scale=1.0),
                 reads=[logit, "constpool"], writes=[logit])
        kb.group("DVE", lambda e: e.tensor_tensor(logit[:, :], logit[:, :],
                                                  rtmp[:, :], ALU.mult),
                 reads=[logit, rtmp], writes=[logit])
        kb.group("DVE", lambda e: e.tensor_reduce(lmax[:, 2:3], logit[:, :],
                                                  axis=mybir.AxisListType.X,
                                                  op=ALU.add),
                 reads=[logit, lmax], writes=[lmax])
        kb.group("ACT", lambda e: e.activation(lmax[:, 3:4], lmax[:, 2:3], AF.Ln,
                                               bias=zero, scale=1.0),
                 reads=[lmax, "constpool"], writes=[lmax])
        kb.group("ACT", lambda e: e.activation(lmax[:, 3:4], lmax[:, 3:4], AF.Exp,
                                               bias=zero, scale=-1.0),
                 reads=[lmax, "constpool"], writes=[lmax])
        kb.group("DVE", lambda e, t=t: e.tensor_scalar(
            cmb[:, t * E:(t + 1) * E], logit[:, :], lmax[:, 3:4], None, ALU.mult),
            reads=[logit, lmax], writes=[(cmb.name, t)])

    def emit_down(e_i, tb, prodb):
        for tt4 in range(4):
            t = tb * 4 + tt4
            for half in range(2):
                ps = pr.pC[:, :] if half == 0 else pr.pD[:, :]
                pk = pr.pC if half == 0 else pr.pD
                mm_group(kb, ps,
                         [(prodb[:, fc * 512 + tt4 * 128: fc * 512 + tt4 * 128 + 128],
                           wdb[:, fc * D + half * 512: fc * D + (half + 1) * 512])
                          for fc in range(16)],
                         reads=[prodb, wdb], writes=[pk])
                if e_i < 0:
                    kb.group("DVE", lambda e, ps=ps, half=half: e.tensor_copy(
                        stg[:, half * 512:(half + 1) * 512], ps),
                        reads=[pk], writes=[(stg.name, half)])
                else:
                    sc = cmb[:, t * E + e_i: t * E + e_i + 1]
                    kb.group("DVE", lambda e, ps=ps, sc=sc, half=half:
                             e.tensor_scalar(stg[:, half * 512:(half + 1) * 512],
                                             ps, sc, None, ALU.mult),
                             reads=[pk, (cmb.name, t)], writes=[(stg.name, half)])
            kb.group("POOL", lambda e, t=t, e_i=e_i: e.dma_start(
                out=accd[t * 128:(t + 1) * 128, :], in_=stg[:, :],
                accum_op=mybir.AluOpType.add)
                if e_i >= 0 else
                e.dma_start(out=accd[t * 128:(t + 1) * 128, :], in_=stg[:, :]),
                reads=[(stg.name, 0), (stg.name, 1)], writes=[("accd", t)],
                dma=kb.dma_queue(f"POOLQ.acc{t % 2}"))

    pend = None
    blk = 0
    for e_i in range(-1, E):
        if pend is not None:
            emit_down(*pend)
            pend = None
        srcs = (sh_gate, sh_up, sh_down) if e_i < 0 else \
            (e_gate[e_i], e_up[e_i], e_down[e_i])
        kb.group("POOL", lambda e, s=srcs[0]: e.dma_start(
            out=wgb[:, :].rearrange("p (c n) -> p c n", c=8),
            in_=s.rearrange("(c p) n -> p c n", p=128)),
            writes=[wgb], dma=kb.dma_queue("POOLQ.w1"))
        kb.group("POOL", lambda e, s=srcs[1]: e.dma_start(
            out=wub[:, :].rearrange("p (c n) -> p c n", c=8),
            in_=s.rearrange("(c p) n -> p c n", p=128)),
            writes=[wub], dma=kb.dma_queue("POOLQ.w2"))
        kb.group("POOL", lambda e, s=srcs[2]: e.dma_start(
            out=wdb[:, :].rearrange("p (c n) -> p c n", c=16),
            in_=s.rearrange("(c p) n -> p c n", p=128)),
            writes=[wdb], dma=kb.dma_queue("POOLQ.w3"))
        for tb in range(T // 512):
            prodb = prodb2[blk % 2]
            blk += 1
            for fc in range(16):
                psg = pr.pA[:, (fc % 2) * 512:(fc % 2) * 512 + 512]
                pkg = (pr.pA.name, fc % 2)
                mm_group(kb, psg,
                         [(wgb[:, c * H + fc * 128: c * H + (fc + 1) * 128],
                           xT[:, c * T + tb * 512: c * T + (tb + 1) * 512])
                          for c in range(8)],
                         reads=[wgb] + xT_keys, writes=[pkg])
                gsb = gsb2[fc % 2]
                kb.group("ACT", lambda e, psg=psg, gsb=gsb: e.activation(
                    gsb[:, :], psg, AF.Silu, bias=zero, scale=1.0),
                    reads=[pkg, "constpool"], writes=[gsb])
                psu = pr.pB[:, (fc % 2) * 512:(fc % 2) * 512 + 512]
                pku = (pr.pB.name, fc % 2)
                mm_group(kb, psu,
                         [(wub[:, c * H + fc * 128: c * H + (fc + 1) * 128],
                           xT[:, c * T + tb * 512: c * T + (tb + 1) * 512])
                          for c in range(8)],
                         reads=[wub] + xT_keys, writes=[pku])
                kb.group("DVE", lambda e, psu=psu, fc=fc, gsb=gsb, prodb=prodb:
                         e.tensor_tensor(
                    prodb[:, fc * 512:(fc + 1) * 512], gsb[:, :], psu, ALU.mult),
                    reads=[gsb, pku, prodb], writes=[prodb])
            if pend is not None:
                emit_down(*pend)
            pend = (e_i, tb, prodb)
    if pend is not None:
        emit_down(*pend)
    for t in range(NT):
        xt = xrb[t % 2]
        ob = obuf[t % 2]
        kb.group("SP", lambda e, t=t, xt=xt: e.dma_start(
            out=xt[:, :], in_=x[t * 128:(t + 1) * 128, :]),
            writes=[xt], dma="SPQ")
        kb.group("SP", lambda e, t=t, ob=ob: e.dma_start(
            out=ob[:, :], in_=x2[t * 128:(t + 1) * 128, :]),
            reads=[ob], writes=[ob], dma=kb.dma_queue("SPQ.x2"))
        kb.group("SP", lambda e, t=t: e.dma_start(
            out=stg[:, :], in_=accd[t * 128:(t + 1) * 128, :]),
            reads=[("accd", t)], writes=[(stg.name, 0), (stg.name, 1)],
            dma=kb.dma_queue("SPQ.rd"))
        kb.group("DVE", lambda e, ob=ob: e.tensor_tensor(
            ob[:, :], ob[:, :], stg[:, :], ALU.add),
            reads=[ob, (stg.name, 0), (stg.name, 1)], writes=[ob])
        emit_sandwich_epilogue_tile(pr, t, ob, xt, bns, mv2, eps1, zero, xo)
    return





def build_L0f_moe(T=2048):
    D, E, H = 1024, 8, 2048
    nc = bass.Bass("TRN2", target_bir_lowering=False)
    pr = Prog(nc)
    kb = pr.kb
    io = {}
    io["x"] = kb.dram_in("x", [T, D], F32)
    io["x2"] = kb.dram_in("x2", [T, D], F32)
    io["norm2t"] = kb.dram_in("norm2t", [128, D], F32)
    io["routerT"] = kb.dram_in("routerT", [D, E], F32)
    io["sh_gate"] = kb.dram_in("sh_gate", [D, H], F32)
    io["sh_up"] = kb.dram_in("sh_up", [D, H], F32)
    io["sh_down"] = kb.dram_in("sh_down", [H, D], F32)
    io["e_gate"] = kb.dram_in("e_gate", [E, D, H], F32)
    io["e_up"] = kb.dram_in("e_up", [E, D, H], F32)
    io["e_down"] = kb.dram_in("e_down", [E, H, D], F32)
    io["xo"] = kb.dram_out("xo", [T, D], F32)
    emit_L0f_moe(pr, io, T=T)
    kb.barrier_on("SP")
    kb.finalize()
    return nc

def emit_L3f(pr, io, T=2048):
    nc = pr.nc
    kb = pr.kb
    NT = T // 128
    D = 1024
    E = 8
    x = io["x"]
    routerT = io["routerT"]
    b_router = io["b_router"]
    expT = io["expT"]
    b_exp = io["b_exp"]
    xo = io["xo"]

    xr = kb.sbuf("xr", [128, NT * D], F32)
    xT = kb.sbuf("xT", [128, 8 * T], BF16)
    xnb = [kb.sbuf(f"xnb{i}", [128, D], BF16) for i in range(2)]
    routerTb = kb.sbuf("routerTb", [128, 8 * E], BF16)
    brow = kb.sbuf("brow", [1, E], BF16)
    bexpb = kb.sbuf("bexpb", [8, D], BF16)
    mselb = kb.sbuf("mselb", [128, NT * E], F32)
    mselT = kb.sbuf("mselT", [8, NT * 128], BF16)
    webuf = [kb.sbuf(f"webuf{i}", [128, 8 * D], BF16) for i in range(2)]
    acc = kb.sbuf("acc", [128, 8 * D], F32)
    logit = kb.sbuf("logit", [128, E], F32)
    rtmp = kb.sbuf("rtmp", [128, E], F32)
    lmax = kb.sbuf("lmax", [128, 2], F32)
    ytmp = kb.sbuf("ytmp", [128, D], F32)
    ps_r = pr.pC
    ps_t = pr.pD
    ps_e = pr.pA
    ps_b = pr.pB

    kb.group("SP", lambda e: e.dma_start(
        out=xr[:, :].rearrange("p (t n) -> p t n", t=NT),
        in_=x.rearrange("(t p) n -> p t n", p=128)), writes=[xr], dma="SPQ")
    load_weight_chunked(kb, routerTb, routerT, D, E, "POOLQ")
    kb.group("POOL", lambda e: e.dma_start(out=brow[:, :], in_=b_router[:, :]),
             writes=[brow], dma="POOLQ")
    kb.group("POOL", lambda e: e.dma_start(out=bexpb[:, :], in_=b_exp[:, :]),
             writes=[bexpb], dma="POOLQ")

    eps5c = kb.constv(1e-5)
    zeroc = kb.constv(0.0)
    mvb = emit_ln_stats(pr, lambda t: xr[:, t * D:(t + 1) * D], NT, D, "ln1")
    bns3 = kb.t("ln1_bns")
    rstd1 = emit_rstd(pr, mvb[:, 1::2], NT, 1e-5, "ln1", [mvb])

    def xn_src(t):
        buf = xnb[t % 2]
        kb.group("DVE", lambda e, t=t, buf=buf: e.tensor_scalar(
            buf[:, :], xr[:, t * D:(t + 1) * D], mvb[:, 2 * t:2 * t + 1],
            rstd1[:, t:t + 1], ALU.subtract, ALU.mult),
            reads=[xr, mvb, rstd1], writes=[buf])
        return buf[:, :]
    emit_tm_to_fm(pr, xn_src, NT, D, xT, T, 0, "t1")

    for t in range(NT):
        mm_group(kb, ps_r[:, 0:E],
                 [(xT[:, c * T + t * 128: c * T + (t + 1) * 128],
                   routerTb[:, c * E:(c + 1) * E]) for c in range(8)],
                 reads=[(xT.name, "fm", t), routerTb, pr.ones_row, brow],
                 writes=[ps_r],
                 bias=(pr.ones_row[0:1, 0:128], brow[0:1, :]))
        kb.group("DVE", lambda e: e.tensor_copy(logit[:, :], ps_r[:, 0:E]),
                 reads=[ps_r], writes=[logit])
        kb.group("DVE", lambda e: e.reduce_max(lmax[:, 0:1], logit[:, :], axis=mybir.AxisListType.X),
                 reads=[logit], writes=[lmax])
        kb.group("DVE", lambda e: e.tensor_scalar(rtmp[:, :], logit[:, :], lmax[:, 0:1],
                                                  None, ALU.is_ge),
                 reads=[logit, lmax], writes=[rtmp])
        kb.group("DVE", lambda e: e.scalar_tensor_tensor(rtmp[:, :], rtmp[:, :],
                                                         -1e30, logit[:, :],
                                                         ALU.mult, ALU.add),
                 reads=[rtmp, logit], writes=[rtmp])
        kb.group("DVE", lambda e: e.reduce_max(lmax[:, 1:2], rtmp[:, :], axis=mybir.AxisListType.X),
                 reads=[rtmp], writes=[lmax])
        kb.group("DVE", lambda e, t=t: e.tensor_scalar(mselb[:, t * E:(t + 1) * E],
                                                       logit[:, :], lmax[:, 1:2],
                                                       None, ALU.is_ge),
                 reads=[logit, lmax], writes=[(mselb.name, t)])
        kb.group("PE", lambda e, t=t: e.transpose(
            ps_t[0:8, 0:128], mselb[:, t * E:(t + 1) * E], pr.identf[:, :]),
            reads=[(mselb.name, t), pr.identf], writes=[ps_t])
        kb.group("DVE", lambda e, t=t: e.tensor_copy(
            mselT[0:8, t * 128:(t + 1) * 128], ps_t[0:8, 0:128]),
            reads=[ps_t], writes=[mselT])

    for tb in range((NT + 7) // 8):
        tiles = range(tb * 8, min(tb * 8 + 8, NT))
        for e_i in range(E):
            buf = webuf[e_i % 2]
            q = kb.dma_queue(f"POOLQ.w{e_i % 2}")
            load_weight_chunked(kb, buf, expT[e_i], D, D, q)
            for t in tiles:
                for half in range(2):
                    mm_group(kb, ps_e[:, half * 512:(half + 1) * 512],
                             [(xT[:, c * T + t * 128: c * T + (t + 1) * 128],
                               buf[:, c * D + half * 512: c * D + (half + 1) * 512])
                              for c in range(8)],
                             reads=[(xT.name, "fm", t), buf], writes=[(ps_e.name, half)])
                sc = mselb[:, t * E + e_i: t * E + e_i + 1]
                aslice = acc[:, (t - tb * 8) * D:(t - tb * 8 + 1) * D]
                if e_i == 0:
                    kb.group("DVE", lambda e, sc=sc, aslice=aslice: e.tensor_scalar(
                        aslice, ps_e[:, :], sc, None, ALU.mult),
                        reads=[(ps_e.name, 0), (ps_e.name, 1), (mselb.name, t)],
                        writes=[(acc.name, t)])
                else:
                    kb.group("DVE", lambda e, sc=sc, aslice=aslice: e.scalar_tensor_tensor(
                        aslice, ps_e[:, :], sc, aslice, ALU.mult, ALU.add),
                        reads=[(ps_e.name, 0), (ps_e.name, 1), (mselb.name, t), (acc.name, t)],
                        writes=[(acc.name, t)])
        for t in tiles:
            mm_group(kb, ps_b[:, 0:512],
                     [(mselT[0:8, t * 128:(t + 1) * 128], bexpb[0:8, 0:512])],
                     reads=[mselT, bexpb], writes=[(ps_b.name, 0)])
            mm_group(kb, ps_b[:, 512:1024],
                     [(mselT[0:8, t * 128:(t + 1) * 128], bexpb[0:8, 512:1024])],
                     reads=[mselT, bexpb], writes=[(ps_b.name, 1)])
            kb.group("DVE", lambda e, t=t, tb=tb: e.tensor_tensor(
                ytmp[:, :], acc[:, (t - tb * 8) * D:(t - tb * 8 + 1) * D],
                ps_b[:, :], ALU.add),
                reads=[(acc.name, t), (ps_b.name, 0), (ps_b.name, 1)], writes=[ytmp])
            for c in range(2):
                kb.group("DVE", lambda e, c=c: e.bn_stats(
                    bns3[:, c * 6:(c + 1) * 6], ytmp[:, c * 512:(c + 1) * 512]),
                    reads=[ytmp], writes=["ln1_bns"])
            kb.group("DVE", lambda e: e.bn_aggr(lmax[:, 0:2], bns3[:, 0:12]),
                     reads=["ln1_bns"], writes=[lmax])
            kb.group("ACT", lambda e: e.activation(lmax[:, 1:2], lmax[:, 1:2], AF.Ln,
                                                   bias=eps5c, scale=1.0),
                     reads=[lmax, "constpool"], writes=[lmax])
            kb.group("ACT", lambda e: e.activation(lmax[:, 1:2], lmax[:, 1:2], AF.Exp,
                                                   bias=zeroc, scale=-0.5),
                     reads=[lmax, "constpool"], writes=[lmax])
            kb.group("DVE", lambda e: e.tensor_scalar(ytmp[:, :], ytmp[:, :],
                                                      lmax[:, 0:1], lmax[:, 1:2],
                                                      ALU.subtract, ALU.mult),
                     reads=[ytmp, lmax], writes=[ytmp])
            kb.group("DVE", lambda e, t=t: e.tensor_tensor(
                xr[:, t * D:(t + 1) * D], xr[:, t * D:(t + 1) * D], ytmp[:, :], ALU.add),
                reads=[xr, ytmp], writes=[(xr.name, "out", t)])

    kb.group("SP", lambda e: e.dma_start(
        out=xo.rearrange("(t p) n -> p t n", p=128),
        in_=xr[:, :].rearrange("p (t n) -> p t n", t=NT)),
        reads=[(xr.name, "out", t) for t in range(NT)] + [xr], writes=["xo"], dma="SPQ")
    return




def build_L3f(T=2048):
    D, E = 1024, 8
    nc = bass.Bass("TRN2", target_bir_lowering=False)
    pr = Prog(nc)
    kb = pr.kb
    io = {}
    io["x"] = kb.dram_in("x", [T, D], F32)
    io["routerT"] = kb.dram_in("routerT", [D, E], F32)
    io["b_router"] = kb.dram_in("b_router", [1, E], F32)
    io["expT"] = kb.dram_in("expT", [E, D, D], F32)
    io["b_exp"] = kb.dram_in("b_exp", [E, D], F32)
    io["xo"] = kb.dram_out("xo", [T, D], F32)
    emit_L3f(pr, io, T=T)
    kb.barrier_on("SP")
    kb.finalize()
    return nc


def _shard_own(x):
    return [np.ascontiguousarray(x[c // 2, (c % 2) * 2048:(c % 2 + 1) * 2048])
            for c in range(8)]


def _unshard_own(shards):
    out = np.empty((4, 4096, 1024), dtype=shards[0].dtype)
    for c in range(8):
        out[c // 2, (c % 2) * 2048:(c % 2 + 1) * 2048] = shards[c]
    return out


def _shard_batch_reordered(x):
    outs = []
    for c in range(8):
        b, h = c // 2, c % 2
        outs.append(np.ascontiguousarray(np.concatenate(
            [x[b, h * 2048:(h + 1) * 2048], x[b, (1 - h) * 2048:(2 - h) * 2048]],
            axis=0)))
    return outs


def _shard_batch_natural(x):
    return [np.ascontiguousarray(x[c // 2]) for c in range(8)]


def _bt(a):
    return np.ascontiguousarray(np.broadcast_to(a, (128, a.shape[-1])))


_PROGRAMS = {}


def _prog(name, fn):
    if name not in _PROGRAMS:
        _PROGRAMS[name] = fn()
    return _PROGRAMS[name]


def _conv_cols(cw, rows):
    r = cw[rows].reshape(-1, 128, 4)
    out = np.zeros((128, r.shape[0] * 4), np.float32)
    for c in range(r.shape[0]):
        for i in range(4):
            out[:, c * 4 + i] = r[c, :, i]
    return out


def _launch(nc, in_maps):
    res = run_bass_kernel_spmd(nc, in_maps, core_ids=list(range(8)))
    return res.results


USE_FUSED = True


def kernel(**ins):
    if USE_FUSED:
        return kernel_fused(**ins)
    return kernel_multilaunch(**ins)


def kernel_multilaunch(**ins):
    f32 = np.float32
    x = np.asarray(ins["L0_attn_hidden_states"], dtype=f32)

    nc = _prog("L0a", lambda: build_attn_atom("L0a"))
    w = {
        "WqkvT": np.ascontiguousarray(ins["L0_attn_W_qkv"].T.astype(f32)),
        "bqkv_cols": np.ascontiguousarray(
            ins["L0_attn_b_qkv"].astype(f32).reshape(24, 128).T),
        "bqkv_row": ins["L0_attn_b_qkv"].astype(f32).reshape(1, 3072),
        "gammat": _bt(ins["L0_attn_ln_gamma"].astype(f32)),
        "betat": _bt(ins["L0_attn_ln_beta"].astype(f32)),
        "WoT": np.ascontiguousarray(ins["L0_attn_W_out"].T.astype(f32)),
        "bout_row": ins["L0_attn_b_out"].astype(f32).reshape(1, 1024),
    }
    shards = _shard_batch_reordered(x)
    res = _launch(nc, [dict(w, xb=s) for s in shards])
    x = _unshard_own([r["xo"] for r in res])

    nc = _prog("L0fa", lambda: build_attn_atom("L0f_attn"))
    w = {
        "WqT": ins["L0_ffn_wq"].astype(f32),
        "WkT": ins["L0_ffn_wk"].astype(f32),
        "WvT": ins["L0_ffn_wv"].astype(f32),
        "normwt": _bt(ins["L0_ffn_norm1_w"].astype(f32)),
        "WoT": ins["L0_ffn_wo"].astype(f32),
    }
    shards = _shard_batch_reordered(x)
    res = _launch(nc, [dict(w, xb=s) for s in shards])
    x2s = [r["xo"] for r in res]

    nc = _prog("L0fm", lambda: build_L0f_moe())
    w = {
        "norm2t": _bt(ins["L0_ffn_norm2_w"].astype(f32)),
        "routerT": ins["L0_ffn_router_w"].astype(f32),
        "sh_gate": ins["L0_ffn_sh_gate"].astype(f32),
        "sh_up": ins["L0_ffn_sh_up"].astype(f32),
        "sh_down": ins["L0_ffn_sh_down"].astype(f32),
        "e_gate": ins["L0_ffn_e_gate"].astype(f32),
        "e_up": ins["L0_ffn_e_up"].astype(f32),
        "e_down": ins["L0_ffn_e_down"].astype(f32),
    }
    xo = _shard_own(x)
    res = _launch(nc, [dict(w, x=xo[c], x2=x2s[c]) for c in range(8)])
    x = _unshard_own([r["xo"] for r in res])

    nc = _prog("L1a", lambda: build_L1a())
    q_w, k_w, v_w = (ins["L1_attn_q_w"].astype(f32), ins["L1_attn_k_w"].astype(f32),
                     ins["L1_attn_v_w"].astype(f32))
    g_w, b_w, a_w = (ins["L1_attn_g_w"].astype(f32), ins["L1_attn_b_w"].astype(f32),
                     ins["L1_attn_a_w"].astype(f32))
    in_maps = []
    shards = _shard_batch_natural(x)
    for c in range(8):
        h0 = (c % 2) * 4
        rqk = slice(h0 * 256, (h0 + 4) * 256)
        rv = slice(h0 * 512, (h0 + 4) * 512)
        in_maps.append({
            "xb": shards[c],
            "normwt": _bt(ins["L1_attn_norm_w"].astype(f32)),
            "qwT": np.ascontiguousarray(q_w[rqk].T),
            "kwT": np.ascontiguousarray(k_w[rqk].T),
            "vwT": np.ascontiguousarray(v_w[rv].T),
            "gwT": np.ascontiguousarray(g_w[rv].T),
            "bwT": np.ascontiguousarray(b_w[h0:h0 + 4].T),
            "awT": np.ascontiguousarray(a_w[h0:h0 + 4].T),
            "qconvt": _conv_cols(ins["L1_attn_q_conv"].astype(f32),
                                 np.arange(h0 * 256, (h0 + 4) * 256)),
            "kconvt": _conv_cols(ins["L1_attn_k_conv"].astype(f32),
                                 np.arange(h0 * 256, (h0 + 4) * 256)),
            "vconvt": _conv_cols(ins["L1_attn_v_conv"].astype(f32),
                                 np.arange(h0 * 512, (h0 + 4) * 512)),
            "alog": ins["L1_attn_A_log"].astype(f32)[h0:h0 + 4].reshape(1, 4),
            "dtb": ins["L1_attn_dt_bias"].astype(f32)[h0:h0 + 4].reshape(1, 4),
            "onormt": _bt(ins["L1_attn_o_norm_w"].astype(f32)),
        })
    res = _launch(nc, in_maps)
    ogs = [r["og"] for r in res]
    nc = _prog("L1ap", lambda: build_L1a_proj())
    owT = np.ascontiguousarray(ins["L1_attn_o_w"].astype(f32).T)
    in_maps = []
    xo = _shard_own(x)
    for c in range(8):
        b, hh = c // 2, c % 2
        og_full = np.concatenate([ogs[2 * b], ogs[2 * b + 1]], axis=1)
        ogf = np.ascontiguousarray(og_full[hh * 2048:(hh + 1) * 2048].T)
        in_maps.append({"x": xo[c], "ogf": ogf, "owT": owT})
    res = _launch(nc, in_maps)
    x = _unshard_own([r["xo"] for r in res])

    nc = _prog("L1f", lambda: build_L1f())
    E, H = 8, 2048
    w = {
        "gateT": ins["L1_ffn_gate_w"].astype(f32),
        "w1": ins["L1_ffn_w1"].astype(f32),
        "b1r": ins["L1_ffn_b1"].astype(f32),
        "ln1gt": np.ascontiguousarray(np.broadcast_to(
            ins["L1_ffn_ln1_g"].astype(f32)[:, None, :], (E, 128, H))),
        "ln1bt": np.ascontiguousarray(np.broadcast_to(
            ins["L1_ffn_ln1_b"].astype(f32)[:, None, :], (E, 128, H))),
        "w2": ins["L1_ffn_w2"].astype(f32),
        "b2r": ins["L1_ffn_b2"].astype(f32),
        "lngt": _bt(ins["L1_ffn_ln_g"].astype(f32)),
        "lnbt": _bt(ins["L1_ffn_ln_b"].astype(f32)),
    }
    res = _launch(nc, [dict(w, x=s) for s in _shard_own(x)])
    x = _unshard_own([r["xo"] for r in res])

    nc = _prog("L2a", lambda: build_attn_atom("L2a"))
    Wq, Wk, Wv = (ins["L2_attn_Wq"].astype(f32), ins["L2_attn_Wk"].astype(f32),
                  ins["L2_attn_Wv"].astype(f32))
    shards = _shard_batch_natural(x)
    in_maps = []
    for c in range(8):
        h0 = (c % 2) * 8
        sl = slice(h0 * 64, (h0 + 8) * 64)
        in_maps.append({
            "xb": shards[c],
            "WqT": np.ascontiguousarray(Wq[:, sl]),
            "WkT": np.ascontiguousarray(Wk[:, sl]),
            "WvT": np.ascontiguousarray(Wv[:, sl]),
        })
    res = _launch(nc, in_maps)
    ctxs = [r["ctxo"] for r in res]
    nc = _prog("L2ap", lambda: build_L2a_proj())
    in_maps = []
    xo = _shard_own(x)
    for c in range(8):
        b, hh = c // 2, c % 2
        cf = np.zeros((1024, 4096), dtype=ctxs[0].dtype)
        for i in range(2):
            cc = ctxs[2 * b + i]
            for hl in range(8):
                cf[(i * 8 + hl) * 64:(i * 8 + hl + 1) * 64] = \
                    cc[:, hl * 4096:(hl + 1) * 4096]
        ctxin = np.ascontiguousarray(cf[:, hh * 2048:(hh + 1) * 2048])
        in_maps.append({"x": xo[c], "ctxin": ctxin,
                        "WoD": ins["L2_attn_Wo"].astype(f32)})
    res = _launch(nc, in_maps)
    x = _unshard_own([r["xo"] for r in res])

    nc = _prog("L2f", lambda: build_L2f())
    E2 = 32
    w = {
        "gateT": np.ascontiguousarray(ins["L2_ffn_gate_w"].astype(f32).T),
        "gbias": np.ascontiguousarray(np.broadcast_to(
            ins["L2_ffn_gate_bias"].astype(f32), (128, E2))),
        "w_gateT": np.ascontiguousarray(
            ins["L2_ffn_w_gate"].astype(f32).transpose(0, 2, 1)),
        "w_upT": np.ascontiguousarray(
            ins["L2_ffn_w_up"].astype(f32).transpose(0, 2, 1)),
        "w_downT": np.ascontiguousarray(
            ins["L2_ffn_w_down"].astype(f32).transpose(0, 2, 1)),
        "sh_gateT": np.ascontiguousarray(ins["L2_ffn_sh_gate"].astype(f32).T),
        "sh_upT": np.ascontiguousarray(ins["L2_ffn_sh_up"].astype(f32).T),
        "sh_downT": np.ascontiguousarray(ins["L2_ffn_sh_down"].astype(f32).T),
    }
    res = _launch(nc, [dict(w, x=s) for s in _shard_own(x)])
    x = _unshard_own([r["xo"] for r in res])

    nc = _prog("L3a", lambda: build_attn_atom("L3a"))
    qn = ins["L3_attn_q_norm_w"].astype(f32)
    kn = ins["L3_attn_k_norm_w"].astype(f32)
    w = {
        "WqkvT": np.ascontiguousarray(ins["L3_attn_qkv_w"].astype(f32).T),
        "qnormt": np.ascontiguousarray(np.broadcast_to(np.tile(qn, 2), (128, 128))),
        "knormt": np.ascontiguousarray(np.broadcast_to(np.tile(kn, 2), (128, 128))),
        "WoT": np.ascontiguousarray(ins["L3_attn_proj_w"].astype(f32).T),
        "bout_row": ins["L3_attn_proj_b"].astype(f32).reshape(1, 1024),
    }
    shards = _shard_batch_reordered(x)
    res = _launch(nc, [dict(w, xb=s) for s in shards])
    x = _unshard_own([r["xo"] for r in res])

    nc = _prog("L3f", lambda: build_L3f())
    w = {
        "routerT": np.ascontiguousarray(ins["L3_ffn_w_router"].astype(f32).T),
        "b_router": ins["L3_ffn_b_router"].astype(f32).reshape(1, 8),
        "expT": np.ascontiguousarray(
            ins["L3_ffn_w_experts"].astype(f32).transpose(0, 2, 1)),
        "b_exp": ins["L3_ffn_b_experts"].astype(f32),
    }
    res = _launch(nc, [dict(w, x=s) for s in _shard_own(x)])
    x = _unshard_own([r["xo"] for r in res])
    return x



PAIRS = [[0, 1], [2, 3], [4, 5], [6, 7]]


def _emit_gather(pr, src_ap, dst_ap, tag):
    kb = pr.kb
    kb.group("POOL", lambda e: e.collective_compute(
        "AllGather", ALU.bypass, replica_groups=PAIRS,
        ins=[src_ap.opt()], outs=[dst_ap.opt()]),
        reads=[tag + "_src"], writes=[tag + "_dst"])


def _emit_reorder_batch(pr, xc_ap, xg_t, xbb_ap, tag, T=2048, D=1024):
    kb = pr.kb
    kb.group("POOL", lambda e: e.dma_start(out=xbb_ap[0:T, :], in_=xc_ap),
             reads=[tag + "_xc"], writes=[tag + "_own"],
             dma=kb.dma_queue("POOLQ.ro0"))

    def emit(e):
        pid = pr.nc.partition_id()
        off = (1 - (pid % 2)) * (T * D)
        return e.dma_start(out=xbb_ap[T:2 * T, :],
                           in_=bass.AP(xg_t, off, [[D, T], [1, D]]))
    kb.group("POOL", emit, reads=[tag + "_dst"], writes=[tag + "_peer"],
             dma=kb.dma_queue("POOLQ.ro1"))


def _emit_colslice(pr, src_t, dst_ap, R, TQ, tag):
    kb = pr.kb

    def emit(e):
        pid = pr.nc.partition_id()
        off = (pid % 2) * TQ
        return e.dma_start(out=dst_ap,
                           in_=bass.AP(src_t, off, [[2 * TQ, R], [1, TQ]]))
    kb.group("POOL", emit, reads=[tag + "_dst"], writes=[tag + "_sel"],
             dma=kb.dma_queue("POOLQ.ro0"))


def build_fused(upto=None):
    D, TQ, TB = 1024, 2048, 4096
    nc = bass.Bass("TRN2", target_bir_lowering=False)
    pr = Prog(nc)
    kb = pr.kb

    class NS:
        def __init__(self, p):
            self.p = p

        def dram_in(self, name, shape, dtype):
            return kb.dram_in(self.p + name, shape, dtype)

        def dram_out(self, name, shape, dtype):
            return nc.dram_tensor(self.p + name, shape, dtype).ap()

    xin = kb.dram_in("xin", [TQ, D], F32)
    xout = kb.dram_out("xout", [TQ, D], F32)
    xcA_t = nc.dram_tensor("xcA", [TQ, D], F32)
    xcB_t = nc.dram_tensor("xcB", [TQ, D], F32)
    xg_t = nc.dram_tensor("xg", [TB, D], F32)
    xbb_t = nc.dram_tensor("xbb", [TB, D], F32)
    x2d_t = nc.dram_tensor("x2d", [TQ, D], F32)
    ogfm_t = nc.dram_tensor("ogfm", [2048, TB], BF16)
    oggf_t = nc.dram_tensor("oggf", [4096, TB], BF16)
    ogsel_t = nc.dram_tensor("ogsel", [4096, TQ], BF16)
    ctxd_t = nc.dram_tensor("ctxd", [512, TB], BF16)
    ctxg_t = nc.dram_tensor("ctxg", [1024, TB], BF16)
    ctxsel_t = nc.dram_tensor("ctxsel", [1024, TQ], BF16)
    xcA, xcB = xcA_t.ap(), xcB_t.ap()
    xg, xbb = xg_t.ap(), xbb_t.ap()

    io_l0a = attn_dram_decls(NS("l0a."), "L0a", TB, weights_only=True)
    io_l0fa = attn_dram_decls(NS("l0fa."), "L0f_attn", TB, weights_only=True)
    ns = NS("l0fm.")
    io_l0fm = {
        "norm2t": ns.dram_in("norm2t", [128, D], F32),
        "routerT": ns.dram_in("routerT", [D, 8], F32),
        "sh_gate": ns.dram_in("sh_gate", [D, 2048], F32),
        "sh_up": ns.dram_in("sh_up", [D, 2048], F32),
        "sh_down": ns.dram_in("sh_down", [2048, D], F32),
        "e_gate": ns.dram_in("e_gate", [8, D, 2048], F32),
        "e_up": ns.dram_in("e_up", [8, D, 2048], F32),
        "e_down": ns.dram_in("e_down", [8, 2048, D], F32),
    }
    ns = NS("l1a.")
    io_l1a = {
        "normwt": ns.dram_in("normwt", [128, D], F32),
        "qwT": ns.dram_in("qwT", [D, 1024], F32),
        "kwT": ns.dram_in("kwT", [D, 1024], F32),
        "vwT": ns.dram_in("vwT", [D, 2048], F32),
        "gwT": ns.dram_in("gwT", [D, 2048], F32),
        "bwT": ns.dram_in("bwT", [D, 4], F32),
        "awT": ns.dram_in("awT", [D, 4], F32),
        "qconvt": ns.dram_in("qconvt", [128, 32], F32),
        "kconvt": ns.dram_in("kconvt", [128, 32], F32),
        "vconvt": ns.dram_in("vconvt", [128, 64], F32),
        "alog": ns.dram_in("alog", [1, 4], F32),
        "dtb": ns.dram_in("dtb", [1, 4], F32),
        "onormt": ns.dram_in("onormt", [128, 512], F32),
    }
    io_l1ap = {"owT": kb.dram_in("l1ap.owT", [4096, D], F32)}
    ns = NS("l1f.")
    io_l1f = {
        "gateT": ns.dram_in("gateT", [D, 8], F32),
        "w1": ns.dram_in("w1", [8, D, 2048], F32),
        "b1r": ns.dram_in("b1r", [8, 2048], F32),
        "ln1gt": ns.dram_in("ln1gt", [8, 128, 2048], F32),
        "ln1bt": ns.dram_in("ln1bt", [8, 128, 2048], F32),
        "w2": ns.dram_in("w2", [8, 2048, D], F32),
        "b2r": ns.dram_in("b2r", [8, D], F32),
        "lngt": ns.dram_in("lngt", [128, D], F32),
        "lnbt": ns.dram_in("lnbt", [128, D], F32),
    }
    io_l2a = attn_dram_decls(NS("l2a."), "L2a", TB, weights_only=True)
    io_l2ap = {"WoD": kb.dram_in("l2ap.WoD", [D, D], F32)}
    ns = NS("l2f.")
    io_l2f = {
        "gateT": ns.dram_in("gateT", [D, 32], F32),
        "gbias": ns.dram_in("gbias", [128, 32], F32),
        "w_gateT": ns.dram_in("w_gateT", [32, D, 512], F32),
        "w_upT": ns.dram_in("w_upT", [32, D, 512], F32),
        "w_downT": ns.dram_in("w_downT", [32, 512, D], F32),
        "sh_gateT": ns.dram_in("sh_gateT", [D, 1024], F32),
        "sh_upT": ns.dram_in("sh_upT", [D, 1024], F32),
        "sh_downT": ns.dram_in("sh_downT", [1024, D], F32),
    }
    io_l3a = attn_dram_decls(NS("l3a."), "L3a", TB, weights_only=True)
    ns = NS("l3f.")
    io_l3f = {
        "routerT": ns.dram_in("routerT", [D, 8], F32),
        "b_router": ns.dram_in("b_router", [1, 8], F32),
        "expT": ns.dram_in("expT", [8, D, D], F32),
        "b_exp": ns.dram_in("b_exp", [8, D], F32),
    }

    kb.begin_phase("pre")
    kb.group("SP", lambda e: e.dma_start(out=xcA, in_=xin),
             writes=["l0a_xc"], dma="SPQ")

    if upto == "l0fa_only":
        kb.begin_phase("l0fx0")
        _emit_gather(pr, xcA, xg, "gX")
        _emit_reorder_batch(pr, xcA, xg_t, xbb, "gX")
        kb.begin_phase("l0fa")
        io_l0fa["xb"] = xbb
        io_l0fa["xo"] = x2d_t.ap()
        emit_attn_atom(pr, io_l0fa, "L0f_attn", TB)
        kb.phase_barrier()
        kb.group("SP", lambda e: e.dma_start(out=xout, in_=x2d_t.ap()),
                 writes=["final"], dma="SPQ")
        kb.barrier_on("SP")
        kb.finalize()
        return nc
    kb.begin_phase("l0ax")
    _emit_gather(pr, xcA, xg, "g0")
    _emit_reorder_batch(pr, xcA, xg_t, xbb, "g0")
    kb.begin_phase("l0a")
    io_l0a["xb"] = xbb
    io_l0a["xo"] = xcB
    emit_attn_atom(pr, io_l0a, "L0a", TB)
    if upto == "l0a":
        kb.phase_barrier()
        kb.group("SP", lambda e: e.dma_start(out=xout, in_=xcB),
                 writes=["final"], dma="SPQ")
        kb.barrier_on("SP")
        kb.finalize()
        return nc

    kb.begin_phase("l0fx")
    _emit_gather(pr, xcB, xg, "g1")
    _emit_reorder_batch(pr, xcB, xg_t, xbb, "g1")
    kb.begin_phase("l0fa")
    if upto == "g1":
        kb.phase_barrier()
        kb.group("SP", lambda e: e.dma_start(out=xout, in_=xbb[TQ:2 * TQ, :]),
                 writes=["final"], dma="SPQ")
        kb.barrier_on("SP")
        kb.finalize()
        return nc
    io_l0fa["xb"] = xbb
    io_l0fa["xo"] = x2d_t.ap()
    emit_attn_atom(pr, io_l0fa, "L0f_attn", TB)
    if upto == "l0fa":
        kb.phase_barrier()
        kb.group("SP", lambda e: e.dma_start(out=xout, in_=x2d_t.ap()),
                 writes=["final"], dma="SPQ")
        kb.barrier_on("SP")
        kb.finalize()
        return nc

    kb.begin_phase("l0fm")
    io_l0fm["x"] = xcB
    io_l0fm["x2"] = x2d_t.ap()
    io_l0fm["xo"] = xcA
    emit_L0f_moe(pr, io_l0fm, T=TQ)
    if upto == "l0fm":
        kb.phase_barrier()
        kb.group("SP", lambda e: e.dma_start(out=xout, in_=xcA),
                 writes=["final"], dma="SPQ")
        kb.barrier_on("SP")
        kb.finalize()
        return nc

    kb.begin_phase("l1x")
    _emit_gather(pr, xcA, xg, "g2")
    kb.begin_phase("l1a")
    io_l1a["xb"] = xg
    io_l1a["og"] = ogfm_t.ap()
    emit_L1a(pr, io_l1a, T=TB, fm_out=True)

    kb.begin_phase("l1ax")
    _emit_gather(pr, ogfm_t.ap(), oggf_t.ap(), "g3")
    _emit_colslice(pr, oggf_t, ogsel_t.ap(), 4096, TQ, "g3")

    kb.begin_phase("l1ap")
    io_l1ap["x"] = xcA
    io_l1ap["ogf"] = ogsel_t.ap()
    io_l1ap["xo"] = xcB
    emit_L1a_proj(pr, io_l1ap, T=TQ, DV=4096)
    if upto == "l1ap":
        kb.phase_barrier()
        kb.group("SP", lambda e: e.dma_start(out=xout, in_=xcB),
                 writes=["final"], dma="SPQ")
        kb.barrier_on("SP")
        kb.finalize()
        return nc

    kb.begin_phase("l1f")
    io_l1f["x"] = xcB
    io_l1f["xo"] = xcA
    emit_L1f(pr, io_l1f, T=TQ)
    if upto == "l1f":
        kb.phase_barrier()
        kb.group("SP", lambda e: e.dma_start(out=xout, in_=xcA),
                 writes=["final"], dma="SPQ")
        kb.barrier_on("SP")
        kb.finalize()
        return nc

    kb.begin_phase("l2x")
    _emit_gather(pr, xcA, xg, "g4")
    kb.begin_phase("l2a")
    io_l2a["xb"] = xg
    io_l2a["ctxo"] = ctxd_t.ap()
    emit_attn_atom(pr, io_l2a, "L2a", TB, dump_fm=True)

    kb.begin_phase("l2ax")
    _emit_gather(pr, ctxd_t.ap(), ctxg_t.ap(), "g5")
    _emit_colslice(pr, ctxg_t, ctxsel_t.ap(), 1024, TQ, "g5")

    kb.begin_phase("l2ap")
    io_l2ap["x"] = xcA
    io_l2ap["ctxin"] = ctxsel_t.ap()
    io_l2ap["xo"] = xcB
    emit_L2a_proj(pr, io_l2ap, T=TQ)
    if upto == "l2ap":
        kb.phase_barrier()
        kb.group("SP", lambda e: e.dma_start(out=xout, in_=xcB),
                 writes=["final"], dma="SPQ")
        kb.barrier_on("SP")
        kb.finalize()
        return nc

    kb.begin_phase("l2f")
    io_l2f["x"] = xcB
    io_l2f["xo"] = xcA
    emit_L2f(pr, io_l2f, T=TQ)
    if upto == "l2f":
        kb.phase_barrier()
        kb.group("SP", lambda e: e.dma_start(out=xout, in_=xcA),
                 writes=["final"], dma="SPQ")
        kb.barrier_on("SP")
        kb.finalize()
        return nc

    kb.begin_phase("l3x")
    _emit_gather(pr, xcA, xg, "g6")
    _emit_reorder_batch(pr, xcA, xg_t, xbb, "g6")
    kb.begin_phase("l3a")
    io_l3a["xb"] = xbb
    io_l3a["xo"] = xcB
    emit_attn_atom(pr, io_l3a, "L3a", TB)
    if upto == "l3a":
        kb.phase_barrier()
        kb.group("SP", lambda e: e.dma_start(out=xout, in_=xcB),
                 writes=["final"], dma="SPQ")
        kb.barrier_on("SP")
        kb.finalize()
        return nc

    kb.begin_phase("l3f")
    io_l3f["x"] = xcB
    io_l3f["xo"] = xout
    emit_L3f(pr, io_l3f, T=TQ)

    kb.phase_barrier()
    kb.barrier_on("SP")
    kb.finalize()
    return nc


def _fused_in_maps(ins):
    f32 = np.float32
    x = np.asarray(ins["L0_attn_hidden_states"], dtype=f32)
    E, H = 8, 2048
    common = {
        "l0a.WqkvT": np.ascontiguousarray(ins["L0_attn_W_qkv"].T.astype(f32)),
        "l0a.bqkv_cols": np.ascontiguousarray(
            ins["L0_attn_b_qkv"].astype(f32).reshape(24, 128).T),
        "l0a.bqkv_row": ins["L0_attn_b_qkv"].astype(f32).reshape(1, 3072),
        "l0a.gammat": _bt(ins["L0_attn_ln_gamma"].astype(f32)),
        "l0a.betat": _bt(ins["L0_attn_ln_beta"].astype(f32)),
        "l0a.WoT": np.ascontiguousarray(ins["L0_attn_W_out"].T.astype(f32)),
        "l0a.bout_row": ins["L0_attn_b_out"].astype(f32).reshape(1, 1024),
        "l0fa.WqT": ins["L0_ffn_wq"].astype(f32),
        "l0fa.WkT": ins["L0_ffn_wk"].astype(f32),
        "l0fa.WvT": ins["L0_ffn_wv"].astype(f32),
        "l0fa.normwt": _bt(ins["L0_ffn_norm1_w"].astype(f32)),
        "l0fa.WoT": ins["L0_ffn_wo"].astype(f32),
        "l0fm.norm2t": _bt(ins["L0_ffn_norm2_w"].astype(f32)),
        "l0fm.routerT": ins["L0_ffn_router_w"].astype(f32),
        "l0fm.sh_gate": ins["L0_ffn_sh_gate"].astype(f32),
        "l0fm.sh_up": ins["L0_ffn_sh_up"].astype(f32),
        "l0fm.sh_down": ins["L0_ffn_sh_down"].astype(f32),
        "l0fm.e_gate": ins["L0_ffn_e_gate"].astype(f32),
        "l0fm.e_up": ins["L0_ffn_e_up"].astype(f32),
        "l0fm.e_down": ins["L0_ffn_e_down"].astype(f32),
        "l1a.normwt": _bt(ins["L1_attn_norm_w"].astype(f32)),
        "l1a.onormt": _bt(ins["L1_attn_o_norm_w"].astype(f32)),
        "l1ap.owT": np.ascontiguousarray(ins["L1_attn_o_w"].astype(f32).T),
        "l1f.gateT": ins["L1_ffn_gate_w"].astype(f32),
        "l1f.w1": ins["L1_ffn_w1"].astype(f32),
        "l1f.b1r": ins["L1_ffn_b1"].astype(f32),
        "l1f.ln1gt": np.ascontiguousarray(np.broadcast_to(
            ins["L1_ffn_ln1_g"].astype(f32)[:, None, :], (E, 128, H))),
        "l1f.ln1bt": np.ascontiguousarray(np.broadcast_to(
            ins["L1_ffn_ln1_b"].astype(f32)[:, None, :], (E, 128, H))),
        "l1f.w2": ins["L1_ffn_w2"].astype(f32),
        "l1f.b2r": ins["L1_ffn_b2"].astype(f32),
        "l1f.lngt": _bt(ins["L1_ffn_ln_g"].astype(f32)),
        "l1f.lnbt": _bt(ins["L1_ffn_ln_b"].astype(f32)),
        "l2ap.WoD": ins["L2_attn_Wo"].astype(f32),
        "l2f.gateT": np.ascontiguousarray(ins["L2_ffn_gate_w"].astype(f32).T),
        "l2f.gbias": np.ascontiguousarray(np.broadcast_to(
            ins["L2_ffn_gate_bias"].astype(f32), (128, 32))),
        "l2f.w_gateT": np.ascontiguousarray(
            ins["L2_ffn_w_gate"].astype(f32).transpose(0, 2, 1)),
        "l2f.w_upT": np.ascontiguousarray(
            ins["L2_ffn_w_up"].astype(f32).transpose(0, 2, 1)),
        "l2f.w_downT": np.ascontiguousarray(
            ins["L2_ffn_w_down"].astype(f32).transpose(0, 2, 1)),
        "l2f.sh_gateT": np.ascontiguousarray(ins["L2_ffn_sh_gate"].astype(f32).T),
        "l2f.sh_upT": np.ascontiguousarray(ins["L2_ffn_sh_up"].astype(f32).T),
        "l2f.sh_downT": np.ascontiguousarray(ins["L2_ffn_sh_down"].astype(f32).T),
        "l3a.WqkvT": np.ascontiguousarray(ins["L3_attn_qkv_w"].astype(f32).T),
        "l3a.qnormt": np.ascontiguousarray(np.broadcast_to(
            np.tile(ins["L3_attn_q_norm_w"].astype(f32), 2), (128, 128))),
        "l3a.knormt": np.ascontiguousarray(np.broadcast_to(
            np.tile(ins["L3_attn_k_norm_w"].astype(f32), 2), (128, 128))),
        "l3a.WoT": np.ascontiguousarray(ins["L3_attn_proj_w"].astype(f32).T),
        "l3a.bout_row": ins["L3_attn_proj_b"].astype(f32).reshape(1, 1024),
        "l3f.routerT": np.ascontiguousarray(ins["L3_ffn_w_router"].astype(f32).T),
        "l3f.b_router": ins["L3_ffn_b_router"].astype(f32).reshape(1, 8),
        "l3f.expT": np.ascontiguousarray(
            ins["L3_ffn_w_experts"].astype(f32).transpose(0, 2, 1)),
        "l3f.b_exp": ins["L3_ffn_b_experts"].astype(f32),
    }
    q_w, k_w, v_w = (ins["L1_attn_q_w"].astype(f32), ins["L1_attn_k_w"].astype(f32),
                     ins["L1_attn_v_w"].astype(f32))
    g_w, b_w, a_w = (ins["L1_attn_g_w"].astype(f32), ins["L1_attn_b_w"].astype(f32),
                     ins["L1_attn_a_w"].astype(f32))
    Wq, Wk, Wv = (ins["L2_attn_Wq"].astype(f32), ins["L2_attn_Wk"].astype(f32),
                  ins["L2_attn_Wv"].astype(f32))
    xown = _shard_own(x)
    in_maps = []
    for c in range(8):
        h0 = (c % 2) * 4
        rqk = slice(h0 * 256, (h0 + 4) * 256)
        rv = slice(h0 * 512, (h0 + 4) * 512)
        h2 = (c % 2) * 8
        sl2 = slice(h2 * 64, (h2 + 8) * 64)
        m = dict(common)
        m.update({
            "xin": xown[c],
            "l1a.qwT": np.ascontiguousarray(q_w[rqk].T),
            "l1a.kwT": np.ascontiguousarray(k_w[rqk].T),
            "l1a.vwT": np.ascontiguousarray(v_w[rv].T),
            "l1a.gwT": np.ascontiguousarray(g_w[rv].T),
            "l1a.bwT": np.ascontiguousarray(b_w[h0:h0 + 4].T),
            "l1a.awT": np.ascontiguousarray(a_w[h0:h0 + 4].T),
            "l1a.qconvt": _conv_cols(ins["L1_attn_q_conv"].astype(f32),
                                     np.arange(h0 * 256, (h0 + 4) * 256)),
            "l1a.kconvt": _conv_cols(ins["L1_attn_k_conv"].astype(f32),
                                     np.arange(h0 * 256, (h0 + 4) * 256)),
            "l1a.vconvt": _conv_cols(ins["L1_attn_v_conv"].astype(f32),
                                     np.arange(h0 * 512, (h0 + 4) * 512)),
            "l1a.alog": ins["L1_attn_A_log"].astype(f32)[h0:h0 + 4].reshape(1, 4),
            "l1a.dtb": ins["L1_attn_dt_bias"].astype(f32)[h0:h0 + 4].reshape(1, 4),
            "l2a.WqT": np.ascontiguousarray(Wq[:, sl2]),
            "l2a.WkT": np.ascontiguousarray(Wk[:, sl2]),
            "l2a.WvT": np.ascontiguousarray(Wv[:, sl2]),
        })
        in_maps.append(m)
    return in_maps


def kernel_fused(**ins):
    nc = _prog("FUSED", build_fused)
    in_maps = _fused_in_maps(ins)
    res = run_bass_kernel_spmd(nc, in_maps, core_ids=list(range(8)))
    return _unshard_own([r["xout"] for r in res.results])


# version 07
# speedup vs baseline: 1.3225x; 1.0171x over previous
import contextlib
import numpy as np
import ml_dtypes
import concourse.bass as bass
import concourse.mybir as mybir
from concourse.bass_utils import run_bass_kernel_spmd

F32 = mybir.dt.float32
BF16 = mybir.dt.bfloat16
AF = mybir.ActivationFunctionType
ALU = mybir.AluOpType
BF = ml_dtypes.bfloat16

N_CORES = 8


ENGINES = ("PE", "ACT", "DVE", "POOL", "SP")
DMA_QUEUES = ("SPQ", "ACTQ", "POOLQ")
ISSUER = {"SPQ": "SP", "ACTQ": "ACT", "POOLQ": "POOL"}
ENG_ATTR = {"PE": "tensor", "ACT": "scalar", "DVE": "vector",
            "POOL": "gpsimd", "SP": "sync"}


class KB:
    def __init__(self, nc):
        self.nc = nc
        self.stack = contextlib.ExitStack()
        self.sems = {}
        self.dma_sems = set()
        for name in ENGINES:
            self.sems[name] = self.stack.enter_context(nc.semaphore("sem_" + name))
        for name in DMA_QUEUES:
            self._add_dma_sem(name)
        self.cnt = {name: 0 for name in self.sems}
        self.known = {e: {} for e in ENGINES}
        self.last_write = {}
        self.readers = {}
        self.thunks = {e: [] for e in ENGINES}
        self.tensors = {}

    def _add_dma_sem(self, name):
        self.sems[name] = self.stack.enter_context(
            self.nc.semaphore("sem_" + name.replace(".", "_")))
        self.dma_sems.add(name)

    def dma_queue(self, name):
        if name not in self.sems:
            self._add_dma_sem(name)
            self.cnt[name] = 0
        return name


    def begin_phase(self, name):
        if getattr(self, "phase", None) is not None:
            self.phase_barrier()
        self.phase = name
        if not hasattr(self, "phase_base"):
            self.phase_base = ((self.nc.sbuf_base + 31) // 32) * 32
        self.phase_off = self.phase_base
        self._arena_low = self.nc.sbuf_top

    def phase_barrier(self):
        for eng in ENGINES:
            self.barrier_on(eng)

    def _pname(self, name):
        ph = getattr(self, "phase", None)
        return f"{ph}_{name}" if ph else name

    def t(self, name):
        return self.tensors[self._pname(name)]

    def sbuf(self, name, shape, dtype):
        nbytes = shape[1] * {F32: 4, BF16: 2}.get(dtype, 4)
        name = self._pname(name)
        if getattr(self, "phase", None) is not None:
            off = self.phase_off
            self.phase_off = ((off + nbytes + 31) // 32) * 32
            assert self.phase_off <= self._arena_low, \
                f"phase {self.phase} sbuf overflow at {name}: {self.phase_off}"
            t = self.nc.alloc_sbuf_tensor_at(name, shape, dtype, offset=off)
        else:
            t = self.stack.enter_context(self.nc.sbuf_tensor(name, shape, dtype))
            assert self.nc.sbuf_base <= self.arena_low(), \
                f"sbuf bump ({self.nc.sbuf_base}) hit arena ({self.arena_low()})"
        self.tensors[name] = t
        return t

    def arena_low(self):
        return getattr(self, "_arena_low", self.nc.sbuf_top)

    def sbuf_at(self, name, shape, dtype, offset):
        name = self._pname(name)
        nbytes = shape[1] * {F32: 4, BF16: 2}.get(dtype, 4)
        assert offset + nbytes <= self.nc.sbuf_top, "arena over top"
        base = self.phase_off if getattr(self, "phase", None) is not None \
            else self.nc.sbuf_base
        assert offset >= base, f"arena under base: {offset} < {base}"
        self._arena_low = min(self.arena_low(), offset)
        t = self.nc.alloc_sbuf_tensor_at(name, shape, dtype, offset=offset)
        self.tensors[name] = t
        return t

    def psum(self, name, shape, dtype):
        t = self.stack.enter_context(self.nc.psum_tensor(name, shape, dtype))
        self.tensors[name] = t
        return t

    def dram_in(self, name, shape, dtype):
        return self.nc.dram_tensor(name, shape, dtype, kind="ExternalInput").ap()

    def dram_out(self, name, shape, dtype):
        return self.nc.dram_tensor(name, shape, dtype, kind="ExternalOutput").ap()

    def dram_tmp(self, name, shape, dtype):
        return self.nc.dram_tensor(self._pname(name), shape, dtype).ap()

    def constr(self, value):
        if not hasattr(self, "_constrow"):
            self._constrow = self.sbuf("constrow", [1, 16], F32)
            self._constrmap = {}
        if value not in self._constrmap:
            idx = len(self._constrmap)
            assert idx < 16
            self._constrmap[value] = idx
            ap = self._constrow[0:1, idx:idx + 1]
            self.group("POOL", lambda e, ap=ap, value=value: e.memset(ap, value),
                       writes=[self._constrow])
        idx = self._constrmap[value]
        return self._constrow[0:1, idx:idx + 1]

    def constv(self, value):
        if not hasattr(self, "_constpool"):
            self._constpool = self.sbuf("constpool", [128, 32], F32)
            self._constmap = {}
        if value not in self._constmap:
            idx = len(self._constmap)
            assert idx < 32
            self._constmap[value] = idx
            ap = self._constpool[:, idx:idx + 1]
            self.group("POOL", lambda e, ap=ap, value=value: e.memset(ap, value),
                       writes=[self._constpool])
        idx = self._constmap[value]
        return self._constpool[:, idx:idx + 1]

    @staticmethod
    def _key(x):
        if isinstance(x, (str, tuple)):
            return x
        if hasattr(x, "tensor"):
            return x.tensor.name
        return x.name

    def _wait(self, eng, sem_name, val):
        if self.known[eng].get(sem_name, 0) >= val:
            return
        self.known[eng][sem_name] = val
        sem = self.sems[sem_name]
        e = getattr(self.nc, ENG_ATTR[eng])
        self.thunks[eng].append(lambda e=e, sem=sem, val=val: e.wait_ge(sem, val))

    def group(self, eng, emit, reads=(), writes=(), dma=None):
        sem_name = dma if dma else eng
        step = 16 if dma else 1
        if dma:
            self._wait(eng, sem_name, self.cnt[sem_name])
        rkeys = [self._key(r) for r in reads]
        wkeys = [self._key(w) for w in writes]
        for k in rkeys:
            if k in self.last_write:
                s, c = self.last_write[k]
                self._wait(eng, s, c)
        for k in wkeys:
            if k in self.readers:
                for s, c in self.readers[k].items():
                    self._wait(eng, s, c)
            if k in self.last_write:
                s, c = self.last_write[k]
                self._wait(eng, s, c)
        self.cnt[sem_name] += step
        cnow = self.cnt[sem_name]
        sem = self.sems[sem_name]
        e = getattr(self.nc, ENG_ATTR[eng])

        def thunk(e=e, sem=sem, step=step, emit=emit):
            last = emit(e)
            assert last is not None, "emit must return the last instruction"
            last.then_inc(sem, step)
        self.thunks[eng].append(thunk)
        for k in wkeys:
            self.last_write[k] = (sem_name, cnow)
            self.readers[k] = {}
        for k in rkeys:
            self.readers.setdefault(k, {})
            prev = self.readers[k].get(sem_name, 0)
            self.readers[k][sem_name] = max(prev, cnow)

    def raw(self, eng, emit):
        e = getattr(self.nc, ENG_ATTR[eng])
        self.thunks[eng].append(lambda e=e: emit(e))

    def barrier_on(self, eng):
        for s in list(self.cnt):
            if self.cnt[s] > 0:
                self._wait(eng, s, self.cnt[s])

    def finalize(self):
        with self.nc.Block() as block:
            for eng in ENGINES:
                if not self.thunks[eng]:
                    continue
                dec = getattr(block, ENG_ATTR[eng])

                def body(e, eng=eng):
                    for th in self.thunks[eng]:
                        th()
                dec(body)
        self.stack.close()
        return self.nc



class Prog:

    def __init__(self, nc):
        self.nc = nc
        self.kb = KB(nc)
        kb = self.kb
        self.ident = kb.sbuf("identb", [128, 128], BF16)
        kb.group("POOL", lambda e: e.memset(self.ident[:, :], 0.0), writes=[self.ident])
        kb.group("POOL", lambda e: e.affine_select(
            out=self.ident[:, :], in_=self.ident[:, :], compare_op=ALU.not_equal,
            fill=1.0, base=0, pattern=[[-1, 128]], channel_multiplier=1),
            reads=[self.ident], writes=[self.ident])
        self.identf = kb.sbuf("identf", [128, 128], F32)
        kb.group("POOL", lambda e: e.memset(self.identf[:, :], 0.0), writes=[self.identf])
        kb.group("POOL", lambda e: e.affine_select(
            out=self.identf[:, :], in_=self.identf[:, :], compare_op=ALU.not_equal,
            fill=1.0, base=0, pattern=[[-1, 128]], channel_multiplier=1),
            reads=[self.identf], writes=[self.identf])
        self.ones_row = kb.sbuf("ones_row", [1, 512], BF16)
        kb.group("POOL", lambda e: e.memset(self.ones_row[:, :], 1.0),
                 writes=[self.ones_row])
        self.ones_col = kb.sbuf("ones_col", [128, 1], BF16)
        kb.group("POOL", lambda e: e.memset(self.ones_col[:, :], 1.0),
                 writes=[self.ones_col])
        self.onesf = kb.sbuf("onesf", [128, 128], F32)
        kb.group("POOL", lambda e: e.memset(self.onesf[:, :], 1.0),
                 writes=[self.onesf])
        self.pA = kb.psum("pA", [128, 1024], F32)
        self.pB = kb.psum("pB", [128, 1024], F32)
        self.pC = kb.psum("pC", [128, 512], F32)
        self.pD = kb.psum("pD", [128, 512], F32)
        self.pb0 = kb.psum("pb0", [128, 1024], BF16)
        self.pb1 = kb.psum("pb1", [128, 1024], BF16)
        kb.constv(0.0)
        kb.constr(0.0)
        self._uid = 0

    def uid(self):
        self._uid += 1
        return self._uid


def emit_ln_stats(pr, xsrc, ntiles, D, pfx):
    kb = pr.kb
    nchunk = (D + 511) // 512
    bns = kb.sbuf(f"{pfx}_bns", [128, 6 * nchunk], F32)
    mvb = kb.sbuf(f"{pfx}_mv", [128, 2 * ntiles], F32)
    for t in range(ntiles):
        src = xsrc(t)
        for c in range(nchunk):
            w = min(512, D - c * 512)
            kb.group("DVE", lambda e, src=src, c=c, w=w: e.bn_stats(
                bns[:, c * 6:(c + 1) * 6], src[:, c * 512:c * 512 + w]),
                reads=[src], writes=[bns])
        kb.group("DVE", lambda e, t=t: e.bn_aggr(mvb[:, 2 * t:2 * t + 2], bns[:, :]),
                 reads=[bns], writes=[mvb])
    return mvb


def emit_rstd(pr, var_view, n, eps, pfx, var_reads):
    kb = pr.kb
    rstd = kb.sbuf(f"{pfx}_rstd", [128, n], F32)
    epsap = kb.constv(eps)
    zero = kb.constv(0.0)
    kb.group("ACT", lambda e: e.activation(rstd[:, :], var_view, AF.Ln,
                                           bias=epsap, scale=1.0),
             reads=list(var_reads) + ["constpool"], writes=[rstd])
    kb.group("ACT", lambda e: e.activation(rstd[:, :], rstd[:, :], AF.Exp,
                                           bias=zero, scale=-0.5),
             reads=[rstd, "constpool"], writes=[rstd])
    return rstd


def emit_tm_to_fm(pr, src_fn, ntiles, D, dst, T_dst, col0, pfx,
                  dst_keys=None, src_reads=None):
    kb = pr.kb
    nchunk = D // 128
    assert nchunk <= 8
    pt = [pr.pb0, pr.pb1]
    for t in range(ntiles):
        src = src_fn(t)
        buf = pt[t % 2]
        def tr(e, src=src, buf=buf, nchunk=nchunk):
            last = None
            for c in range(nchunk):
                last = e.transpose(buf[:, c * 128:(c + 1) * 128],
                                   src[:, c * 128:(c + 1) * 128], pr.ident[:, :])
            return last
        rd = [src, pr.ident] + (src_reads(t) if src_reads else [])
        kb.group("PE", tr, reads=rd, writes=[buf])
        def cp(e, buf=buf, t=t, nchunk=nchunk):
            dap = bass.AP(dst, col0 + t * 128,
                          [[dst.shape[1], 128], [T_dst, nchunk], [1, 128]])
            return e.tensor_copy(dap, buf[:, 0:nchunk * 128].rearrange(
                "p (c n) -> p c n", c=nchunk))
        kb.group("DVE", cp, reads=[buf],
                 writes=[(dst.name, "fm", t) if dst_keys is None else dst_keys[t]])


def mm_group(kb, psum_ap, pieces, reads, writes, bias=None):
    def emit(e):
        last = None
        n = len(pieces) + (1 if bias is not None else 0)
        for i, (l, r) in enumerate(pieces):
            last = e.matmul(psum_ap, l, r, start=(i == 0), stop=(i == n - 1))
        if bias is not None:
            l, r = bias
            last = e.matmul(psum_ap, l, r, start=False, stop=True)
        return last
    kb.group("PE", emit, reads=reads, writes=writes)


def load_weight_chunked(kb, wb, dram, din, dout, queue, extra_reads=()):
    nchunk = din // 128
    kb.group("POOL", lambda e: e.dma_start(
        out=wb[:, 0:nchunk * dout].rearrange("p (c n) -> p c n", c=nchunk),
        in_=dram.rearrange("(c p) n -> p c n", p=128)),
        reads=list(extra_reads), writes=[wb], dma=queue)



def emit_attn_pair(pr, p, TQ, TK, qfm, kfm, vtm, ctxfm, scale, cb2, pfx,
                   causal=False, caus_masks=None):
    kb = pr.kb
    nkc = TK // 128
    nqc = TQ // 512
    pbuf = kb.t(pfx + "_pbuf")
    rb = kb.t(pfx + "_rb")
    den = rb
    DW = den.shape[1]
    RW = rb.shape[1]
    for qc in range(nqc):
        if causal:
            qend = (qc + 1) * 512
            kcs = [kc for kc in range(nkc) if kc * 128 < qend]
        else:
            kcs = list(range(nkc))
        n = len(kcs)

        def qk(i):
            kc = kcs[i]
            pX = pr.pA if i % 2 == 0 else pr.pB
            for h01 in range(2):
                hp = slice(h01 * 64, h01 * 64 + 64)
                kb.group("PE", lambda e, kc=kc, qc=qc, pX=pX, hp=hp, h01=h01:
                         e.matmul(pX[:, h01 * 512:(h01 + 1) * 512],
                                  kfm[hp, kc * 128:(kc + 1) * 128],
                                  qfm[hp, qc * 512:(qc + 1) * 512],
                                  start=True, stop=True),
                         reads=[(pfx, "kfm"), (pfx, "qfm")],
                         writes=[(pfx, "sc", i % 2, h01), (pX.name, h01)])

        def ex(i):
            kc = kcs[i]
            pX = pr.pA if i % 2 == 0 else pr.pB
            pt = pbuf[:, (i % 2) * 1024:(i % 2) * 1024 + 1024]
            kb.group("ACT", lambda e, pX=pX, pt=pt: e.activation(
                pt, pX[:, 0:1024], AF.Exp, bias=cb2[:, 2:3], scale=scale),
                reads=[(pfx, "sc", i % 2, 0), (pfx, "sc", i % 2, 1), (pfx, "cb")],
                writes=[(pfx, "pt", i % 2)])
            if causal:
                moff = kc * 128 - qc * 512
                if moff >= 0:
                    m = caus_masks[moff // 128]
                    kb.group("DVE", lambda e, pt=pt, m=m: e.tensor_tensor(
                        pt, pt, m[:, :], ALU.mult),
                        reads=[(pfx, "pt", i % 2), m], writes=[(pfx, "pt", i % 2)])

        def pv(i):
            kc = kcs[i]
            for h01 in range(2):
                pc = pr.pC if h01 == 0 else pr.pD
                kb.group("PE", lambda e, kc=kc, h01=h01, i=i, pc=pc, n=n: e.matmul(
                    pc[0:65, 0:512],
                    vtm[:, (kc * 2 + h01) * 65:(kc * 2 + h01) * 65 + 65],
                    pbuf[:, (i % 2) * 1024 + h01 * 512:
                         (i % 2) * 1024 + (h01 + 1) * 512],
                    start=(i == 0), stop=(i == n - 1)),
                    reads=[(pfx, "pt", i % 2), (pfx, "vtm")],
                    writes=[(pfx, "ctxp", h01)])

        for i in range(n):
            qk(i)
            ex(i)
            if i > 0:
                pv(i - 1)
        pv(n - 1)
        for h01 in range(2):
            pc = pr.pC if h01 == 0 else pr.pD
            kb.group("DVE", lambda e, pc=pc, h01=h01: e.tensor_copy(
                den[64:65, h01 * 512:(h01 + 1) * 512], pc[64:65, 0:512]),
                reads=[(pfx, "ctxp", h01)], writes=[(pfx, "den", h01)])
        kb.group("DVE", lambda e: e.reciprocal(den[64:65, 0:1024],
                                               den[64:65, 0:1024]),
                 reads=[(pfx, "den", 0), (pfx, "den", 1)],
                 writes=[(pfx, "den", 0), (pfx, "den", 1)])
        for h01 in range(2):
            kb.group("SP", lambda e, h01=h01: e.dma_start(
                out=bass.AP(rb, h01 * 512, [[RW, 64], [1, 512]]),
                in_=bass.AP(den, 64 * DW + h01 * 512, [[DW, 1], [0, 64], [1, 512]])),
                reads=[(pfx, "den", 0), (pfx, "den", 1)],
                writes=[(pfx, "rb", h01)], dma=kb.dma_queue(f"SPQ.rb{h01}"))
        for h01 in range(2):
            h = 2 * p + h01
            pc = pr.pC if h01 == 0 else pr.pD
            kb.group("DVE", lambda e, h=h, qc=qc, pc=pc, h01=h01: e.tensor_tensor(
                ctxfm[0:64, h * TQ + qc * 512: h * TQ + (qc + 1) * 512],
                pc[0:64, 0:512], rb[0:64, h01 * 512:(h01 + 1) * 512], ALU.mult),
                reads=[(pfx, "ctxp", h01), (pfx, "rb", h01)],
                writes=[(pfx, "ctx", h, qc)])


def emit_pair_bounds(pr, qfm, kfm, TQ, TK, scale, pfx):
    kb = pr.kb
    mx = kb.t(pfx + "_mx")
    mrow = kb.t(pfx + "_mrow")
    cb2 = kb.t(pfx + "_cb2")
    kb.group("DVE", lambda e: e.tensor_reduce(
        mx[:, 0:1], qfm[:, 0:TQ], axis=mybir.AxisListType.X,
        op=ALU.max, apply_absolute_value=True),
        reads=[(pfx, "qfm")], writes=[mx])
    kb.group("DVE", lambda e: e.tensor_reduce(
        mx[:, 1:2], kfm[:, 0:TK], axis=mybir.AxisListType.X,
        op=ALU.max, apply_absolute_value=True),
        reads=[(pfx, "kfm"), mx], writes=[mx])
    pdkey = (pfx, "ctxp", 1)
    for j in range(2):
        kb.group("PE", lambda e, j=j: e.transpose(
            pr.pD[0:1, j * 128:j * 128 + 128], mx[:, j:j + 1], pr.identf[:, :]),
            reads=[mx, pr.identf], writes=[pdkey])
    for h01 in range(2):
        for j in range(2):
            kb.group("DVE", lambda e, j=j, h01=h01: e.tensor_reduce(
                mrow[0:1, j:j + 1], pr.pD[0:1, j * 128 + h01 * 64:j * 128 + h01 * 64 + 64],
                axis=mybir.AxisListType.X, op=ALU.max),
                reads=[pdkey, mrow], writes=[mrow])
        kb.group("DVE", lambda e, h01=h01: e.tensor_tensor(
            mrow[0:1, 2 + h01:3 + h01], mrow[0:1, 0:1], mrow[0:1, 1:2], ALU.mult),
            reads=[mrow], writes=[mrow])
    kb.group("PE", lambda e: e.matmul(pr.pD[:, 0:2], pr.onesf[0:1, 0:128],
                                      mrow[0:1, 2:4], start=True, stop=True),
             reads=[mrow, pr.onesf], writes=[pdkey])
    kb.group("DVE", lambda e: e.tensor_scalar(
        cb2[:, 0:2], pr.pD[:, 0:2], -scale, None, ALU.mult),
        reads=[pdkey], writes=[(pfx, "cb")])
    kb.group("DVE", lambda e: e.tensor_tensor(
        cb2[:, 2:3], cb2[:, 0:1], cb2[:, 1:2], ALU.min),
        reads=[(pfx, "cb")], writes=[(pfx, "cb")])
    return cb2


def alloc_attn_scratch(pr, pfx):
    kb = pr.kb
    kb.sbuf(pfx + "_pbuf", [128, 2048], BF16)
    kb.sbuf(pfx + "_rb", [128, 1024], F32)
    kb.sbuf(pfx + "_mx", [128, 2], F32)
    kb.sbuf(pfx + "_mrow", [1, 4], F32)
    kb.sbuf(pfx + "_cb2", [128, 4], F32)



ATTN_CFGS = {
    "L0a": dict(npairs=8, wq="combined", qkv_bias=True, prenorm="lnln",
                qk_norm=None, scale=0.125, fixed_bound=None, causal=False,
                out_mode="proj", proj_bias=True, internal_residual=True,
                headsplit=False),
    "L3a": dict(npairs=8, wq="combined", qkv_bias=False, prenorm="ln",
                qk_norm="rms_w", scale=0.125, fixed_bound=None, causal=False,
                out_mode="proj", proj_bias=True, internal_residual=False,
                headsplit=False),
    "L2a": dict(npairs=4, wq="separate", qkv_bias=False, prenorm="ln",
                qk_norm="l2", scale=8.0, fixed_bound=8.0, causal=True,
                out_mode="dump_ctx", proj_bias=False, internal_residual=False,
                headsplit=True),
    "L0f_attn": dict(npairs=8, wq="separate", qkv_bias=False, prenorm="lnrms",
                     qk_norm=None, scale=0.125, fixed_bound=None, causal=False,
                     out_mode="proj", proj_bias=False, internal_residual=True,
                     headsplit=False, sandwich=False),
}


def attn_dram_decls(kb, which, TB=4096, weights_only=False):
    cfg = dict(ATTN_CFGS[which])
    TQ = TB if cfg["headsplit"] else TB // 2
    NP = cfg["npairs"]
    NH = 2 * NP
    D = 1024
    io = {}
    if not weights_only:
        io["xb"] = kb.dram_in("xb", [TB, D], F32)
    if cfg["wq"] == "combined":
        io["WqkvT"] = kb.dram_in("WqkvT", [D, 3 * D], F32)
    else:
        io["WqT"] = kb.dram_in("WqT", [D, NP * 128], F32)
        io["WkT"] = kb.dram_in("WkT", [D, NP * 128], F32)
        io["WvT"] = kb.dram_in("WvT", [D, NP * 128], F32)
    if cfg["qkv_bias"]:
        io["bqkv_cols"] = kb.dram_in("bqkv_cols", [128, 24], F32)
        io["bqkv_row"] = kb.dram_in("bqkv_row", [1, 3 * D], F32)
    if cfg["prenorm"] == "lnln":
        io["gammat"] = kb.dram_in("gammat", [128, D], F32)
        io["betat"] = kb.dram_in("betat", [128, D], F32)
    elif cfg["prenorm"] == "lnrms":
        io["normwt"] = kb.dram_in("normwt", [128, D], F32)
    if cfg["qk_norm"] == "rms_w":
        io["qnormt"] = kb.dram_in("qnormt", [128, 128], F32)
        io["knormt"] = kb.dram_in("knormt", [128, 128], F32)
    if cfg["out_mode"] == "proj":
        io["WoT"] = kb.dram_in("WoT", [D, D], F32)
        if cfg["proj_bias"]:
            io["bout_row"] = kb.dram_in("bout_row", [1, D], F32)
        if not weights_only:
            io["xo"] = kb.dram_out("xo", [TQ, D], F32)
    elif not weights_only:
        io["ctxo"] = kb.dram_out("ctxo", [64, NH * TQ], mybir.dt.bfloat16)
    return io


def emit_attn_atom(pr, io, which, TB=4096, dump_fm=False):
    cfg = dict(ATTN_CFGS[which])
    TQ = TB if cfg["headsplit"] else TB // 2
    NP = cfg["npairs"]
    NH = 2 * NP
    nc = pr.nc
    kb = pr.kb
    D = 1024
    NBT = TB // 128
    NQT = TQ // 128
    xb = io["xb"]
    if cfg["wq"] == "combined":
        WqkvT = io["WqkvT"]
        wsrc = lambda j: WqkvT
        woffs = (0, D, 2 * D)
    else:
        WqT, WkT, WvT = io["WqT"], io["WkT"], io["WvT"]
        wsrc = lambda j: (WqT, WkT, WvT)[j]
        woffs = (0, 0, 0)
    if cfg["qkv_bias"]:
        bqkv_cols = io["bqkv_cols"]
        bqkv_row = io["bqkv_row"]
    if cfg["prenorm"] == "lnln":
        gammat = io["gammat"]
        betat = io["betat"]
    elif cfg["prenorm"] == "lnrms":
        normwt = io["normwt"]
    if cfg["qk_norm"] == "rms_w":
        qnormt = io["qnormt"]
        knormt = io["knormt"]
    if cfg["out_mode"] == "proj":
        WoT = io["WoT"]
        if cfg["proj_bias"]:
            bout_row = io["bout_row"]
        xo = io["xo"]
    else:
        ctxo = io["ctxo"]

    alloc_attn_scratch(pr, "at")
    hT_off = nc.sbuf_top - max(16 * TB, 32 * 1024)
    ctx_off = hT_off - 2 * NH * TQ
    hT = kb.sbuf_at("hT", [128, 8 * TB], BF16, hT_off)
    ctxfm = kb.sbuf_at("ctxfm", [64, NH * TQ], BF16, ctx_off)
    qfm = kb.sbuf("qfm", [128, TQ], BF16)
    kfm = kb.sbuf("kfm", [128, TB], BF16)
    vtm = kb.sbuf("vtm", [128, (TB // 128) * 2 * 65], BF16)
    wpair = [kb.sbuf(f"wpair{i}", [128, 8 * 384], BF16) for i in range(2)]
    if cfg["out_mode"] == "proj":
        wo = kb.sbuf_at("wo", [64, 16 * 1024], BF16, hT_off)
        bor = kb.sbuf("bor", [1, D], BF16)
    if cfg["qkv_bias"]:
        bqc = kb.sbuf("bqc", [128, 24], F32)
        bqr = kb.sbuf("bqr", [1, D], BF16)
    gmt = kb.sbuf("gmt", [128, D], BF16)
    btt = kb.sbuf("btt", [128, D], BF16)
    qnt = kb.sbuf("qnt", [128, 128], BF16)
    knt = kb.sbuf("knt", [128, 128], BF16)
    xrb = [kb.sbuf(f"xrb{i}", [128, D], F32) for i in range(2)]
    xnb = [kb.sbuf(f"xnb{i}", [128, D], BF16) for i in range(2)]
    if cfg["out_mode"] == "proj":
        ybuf = kb.sbuf("ybuf", [128, D], F32)
        obuf = [kb.sbuf(f"obuf{i}", [128, D], F32) for i in range(2)]
    qtmb = kb.sbuf("qtmb", [128, 128], BF16)
    sqb = kb.sbuf("sqb", [128, 128], F32)
    nsum = kb.sbuf("nsum", [128, 4], F32)
    mvball = kb.sbuf("mvball", [128, 2 * NBT], F32)
    r1all = kb.sbuf("r1all", [128, NBT], F32)
    cball = kb.sbuf("cball", [128, NBT], F32)
    sc1 = kb.sbuf("sc1", [128, 4], F32)
    bns = kb.sbuf("bns", [128, 12], F32)
    mv2 = kb.sbuf("mv2", [128, 2], F32)
    cmasks = [kb.sbuf(f"cmask{m}", [128, 1024], BF16) for m in range(4)] \
        if cfg["causal"] else None

    if cfg["qkv_bias"]:
        kb.group("POOL", lambda e: e.dma_start(out=bqc[:, :], in_=bqkv_cols[:, :]),
                 writes=[bqc], dma="POOLQ")
        kb.group("POOL", lambda e: e.dma_start(out=bqr[:, :],
                                               in_=bqkv_row[0:1, 2 * D:3 * D]),
                 writes=[bqr], dma="POOLQ")
    if cfg["out_mode"] == "proj" and cfg["proj_bias"]:
        kb.group("POOL", lambda e: e.dma_start(out=bor[:, :], in_=bout_row[:, :]),
                 writes=[bor], dma="POOLQ")
    elif cfg["out_mode"] == "proj":
        kb.group("POOL", lambda e: e.memset(bor[:, :], 0.0), writes=[bor])
    if cfg["prenorm"] == "lnln":
        kb.group("POOL", lambda e: e.dma_start(out=gmt[:, :], in_=gammat[:, :]),
                 writes=[gmt], dma="POOLQ")
        kb.group("POOL", lambda e: e.dma_start(out=btt[:, :], in_=betat[:, :]),
                 writes=[btt], dma="POOLQ")
    elif cfg["prenorm"] == "lnrms":
        kb.group("POOL", lambda e: e.dma_start(out=gmt[:, :], in_=normwt[:, :]),
                 writes=[gmt], dma="POOLQ")
    if cfg["qk_norm"] == "rms_w":
        kb.group("POOL", lambda e: e.dma_start(out=qnt[:, :], in_=qnormt[:, :]),
                 writes=[qnt], dma="POOLQ")
        kb.group("POOL", lambda e: e.dma_start(out=knt[:, :], in_=knormt[:, :]),
                 writes=[knt], dma="POOLQ")
    kb.group("POOL", lambda e: e.memset(
        bass.AP(vtm, 64, [[vtm.shape[1], 128], [65, (TB // 128) * 2], [1, 1]]), 1.0),
        writes=["vtm_ones"])
    if cfg["causal"]:
        for m in range(4):
            kb.group("POOL", lambda e, m=m: e.memset(cmasks[m][:, :], 1.0),
                     writes=[cmasks[m]])
            for hf in range(2):
                kb.group("POOL", lambda e, m=m, hf=hf: e.affine_select(
                    out=cmasks[m][:, hf * 512:(hf + 1) * 512],
                    in_=cmasks[m][:, hf * 512:(hf + 1) * 512],
                    compare_op=ALU.is_ge,
                    fill=0.0, base=-128 * m, pattern=[[1, 512]],
                    channel_multiplier=-1),
                    reads=[cmasks[m]], writes=[cmasks[m]])
    if cfg["fixed_bound"] is not None:
        cb2t = kb.t("at_cb2")
        kb.group("POOL", lambda e, cb2t=cb2t: e.memset(
            cb2t[:, :], -cfg["fixed_bound"]),
            writes=[("at", "cb")])

    eps1 = kb.constv(1e-5)
    zero = kb.constv(0.0)

    def prep_tile(t):
        xt = xrb[t % 2]
        kb.group("SP", lambda e, t=t, xt=xt: e.dma_start(
            out=xt[:, :], in_=xb[t * 128:(t + 1) * 128, :]),
            writes=[xt], dma="SPQ")
        kb.group("DVE", lambda e, xt=xt: e.bn_stats(bns[:, 0:6], xt[:, 0:512]),
                 reads=[xt], writes=[bns])
        kb.group("DVE", lambda e, xt=xt: e.bn_stats(bns[:, 6:12], xt[:, 512:1024]),
                 reads=[xt, bns], writes=[bns])
        kb.group("DVE", lambda e, t=t: e.bn_aggr(mvball[:, 2 * t:2 * t + 2], bns[:, :]),
                 reads=[bns], writes=[mvball])
        kb.group("ACT", lambda e, t=t: e.activation(
            r1all[:, t:t + 1], mvball[:, 2 * t + 1:2 * t + 2], AF.Ln,
            bias=eps1, scale=1.0), reads=[mvball, "constpool"], writes=[r1all])
        kb.group("ACT", lambda e, t=t: e.activation(
            r1all[:, t:t + 1], r1all[:, t:t + 1], AF.Exp, bias=zero, scale=-0.5),
            reads=[r1all, "constpool"], writes=[r1all])
        if cfg["prenorm"] in ("lnln", "lnrms"):
            eps2 = eps1 if cfg["prenorm"] == "lnln" else kb.constv(1e-6)
            kb.group("DVE", lambda e, t=t: e.tensor_tensor(
                sc1[:, 0:1], mvball[:, 2 * t + 1:2 * t + 2], r1all[:, t:t + 1], ALU.mult),
                reads=[mvball, r1all], writes=[sc1])
            kb.group("DVE", lambda e, t=t: e.tensor_tensor(
                sc1[:, 1:2], sc1[:, 0:1], r1all[:, t:t + 1], ALU.mult),
                reads=[sc1, r1all], writes=[sc1])
            kb.group("ACT", lambda e, eps2=eps2: e.activation(
                sc1[:, 2:3], sc1[:, 1:2], AF.Ln, bias=eps2, scale=1.0),
                reads=[sc1, "constpool"], writes=[sc1])
            kb.group("ACT", lambda e: e.activation(sc1[:, 2:3], sc1[:, 2:3], AF.Exp,
                                                   bias=zero, scale=-0.5),
                     reads=[sc1, "constpool"], writes=[sc1])
            kb.group("DVE", lambda e, t=t: e.tensor_tensor(
                cball[:, t:t + 1], r1all[:, t:t + 1], sc1[:, 2:3], ALU.mult),
                reads=[sc1, r1all], writes=[cball])
            comb = cball
        else:
            comb = r1all
        kb.group("DVE", lambda e, t=t, xt=xt, comb=comb: e.tensor_scalar(
            xt[:, :], xt[:, :], mvball[:, 2 * t:2 * t + 1], comb[:, t:t + 1],
            ALU.subtract, ALU.mult), reads=[xt, mvball, comb], writes=[xt])
        xn = xnb[t % 2]
        if cfg["prenorm"] == "lnln":
            kb.group("DVE", lambda e, xt=xt, xn=xn: e.scalar_tensor_tensor(
                xn[:, :], xt[:, :], 1.0, gmt[:, :], ALU.mult, ALU.mult),
                reads=[xt, gmt], writes=[xn])
            kb.group("DVE", lambda e, xn=xn: e.tensor_tensor(
                xn[:, :], xn[:, :], btt[:, :], ALU.add), reads=[xn, btt], writes=[xn])
        elif cfg["prenorm"] == "lnrms":
            kb.group("DVE", lambda e, xt=xt, xn=xn: e.scalar_tensor_tensor(
                xn[:, :], xt[:, :], 1.0, gmt[:, :], ALU.mult, ALU.mult),
                reads=[xt, gmt], writes=[xn])
        else:
            kb.group("DVE", lambda e, xt=xt, xn=xn: e.tensor_copy(xn[:, :], xt[:, :]),
                     reads=[xt], writes=[xn])
        return xn[:, :]

    emit_tm_to_fm(pr, prep_tile, NBT, D, hT, TB, 0, "tf")

    hT_keys = [(hT.name, "fm", t) for t in range(NBT)]

    def qk_tm_norm(p, wp, woff, wbase, dst_fm, Ttok, wnt, fmkey):
        for tt in range(Ttok // 128):
            slot = tt % 2
            ps = pr.pB[:, slot * 512:slot * 512 + 128]
            pskey = (pr.pB.name, slot)
            mm_group(kb, ps,
                     [(hT[:, c * TB + tt * 128: c * TB + (tt + 1) * 128],
                       wp[:, c * 384 + wbase: c * 384 + wbase + 128])
                      for c in range(8)],
                     reads=[(wp.name, woff)] + hT_keys, writes=[pskey])
            kb.group("ACT", lambda e, ps=ps: e.activation(
                sqb[:, :], ps, AF.Square, bias=zero, scale=1.0),
                reads=[pskey, "constpool"], writes=[sqb])
            kb.group("DVE", lambda e: e.tensor_reduce(
                nsum[:, 0:2], sqb[:, :].rearrange("p (h d) -> p h d", h=2),
                axis=mybir.AxisListType.X, op=ALU.add),
                reads=[sqb], writes=[nsum])
            if cfg["qk_norm"] == "l2":
                kb.group("DVE", lambda e: e.tensor_scalar(
                    nsum[:, 0:2], nsum[:, 0:2], 1e-24, None, ALU.max),
                    reads=[nsum], writes=[nsum])
                lnscale, lnbias = 1.0, zero
            else:
                lnscale, lnbias = 1.0 / 64.0, kb.constv(1e-6)
            kb.group("ACT", lambda e, lnscale=lnscale, lnbias=lnbias: e.activation(
                nsum[:, 2:4], nsum[:, 0:2], AF.Ln, bias=lnbias, scale=lnscale),
                reads=[nsum, "constpool"], writes=[nsum])
            kb.group("ACT", lambda e: e.activation(
                nsum[:, 2:4], nsum[:, 2:4], AF.Exp, bias=zero, scale=-0.5),
                reads=[nsum, "constpool"], writes=[nsum])
            for h01 in range(2):
                kb.group("DVE", lambda e, ps=ps, h01=h01: e.tensor_scalar(
                    qtmb[:, h01 * 64:(h01 + 1) * 64], ps[:, h01 * 64:(h01 + 1) * 64],
                    nsum[:, 2 + h01:3 + h01], None, ALU.mult),
                    reads=[pskey, nsum, qtmb], writes=[qtmb])
            if cfg["qk_norm"] == "rms_w":
                kb.group("DVE", lambda e, wnt=wnt: e.tensor_tensor(
                    qtmb[:, :], qtmb[:, :], wnt[:, :], ALU.mult),
                    reads=[qtmb, wnt], writes=[qtmb])
            kb.group("PE", lambda e: e.transpose(pr.pb0[:, 0:128], qtmb[:, :],
                                                 pr.ident[:, :]),
                     reads=[qtmb, pr.ident], writes=[pr.pb0])
            kb.group("DVE", lambda e, tt=tt, dst_fm=dst_fm: e.tensor_copy(
                dst_fm[:, tt * 128:(tt + 1) * 128], pr.pb0[:, 0:128]),
                reads=[pr.pb0], writes=[fmkey])

    for p in range(NP):
        wp = wpair[p % 2]
        q = kb.dma_queue(f"POOLQ.w{p % 2}")
        for j in range(3):
            kb.group("POOL", lambda e, wp=wp, j=j, p=p: e.dma_start(
                out=bass.AP(wp, j * 128, [[wp.shape[1], 128], [384, 8], [1, 128]]),
                in_=wsrc(j)[:, woffs[j] + p * 128: woffs[j] + (p + 1) * 128].rearrange(
                    "(c p) n -> p c n", p=128)),
                writes=[(wp.name, j)], dma=q)
        if cfg["qk_norm"] is None:
            for qc in range(TQ // 512):
                ps = pr.pB[:, (qc % 2) * 512:(qc % 2) * 512 + 512]
                pskey = (pr.pB.name, qc % 2)
                mm_group(kb, ps,
                         [(wp[:, c * 384: c * 384 + 128],
                           hT[:, c * TB + qc * 512: c * TB + (qc + 1) * 512])
                          for c in range(8)],
                         reads=[(wp.name, 0)] + hT_keys, writes=[pskey])
                bias_ap = bqc[:, p:p + 1] if cfg["qkv_bias"] else zero
                kb.group("ACT", lambda e, qc=qc, ps=ps, bias_ap=bias_ap: e.activation(
                    qfm[:, qc * 512:(qc + 1) * 512], ps, AF.Identity,
                    bias=bias_ap, scale=1.0),
                    reads=[pskey, bqc if cfg["qkv_bias"] else "constpool"],
                    writes=[("at", "qfm")])
            for kc in range(TB // 512):
                ps = pr.pB[:, (kc % 2) * 512:(kc % 2) * 512 + 512]
                pskey = (pr.pB.name, kc % 2)
                mm_group(kb, ps,
                         [(wp[:, c * 384 + 128: c * 384 + 256],
                           hT[:, c * TB + kc * 512: c * TB + (kc + 1) * 512])
                          for c in range(8)],
                         reads=[(wp.name, 1)] + hT_keys, writes=[pskey])
                bias_ap = bqc[:, 8 + p:9 + p] if cfg["qkv_bias"] else zero
                kb.group("ACT", lambda e, kc=kc, ps=ps, bias_ap=bias_ap: e.activation(
                    kfm[:, kc * 512:(kc + 1) * 512], ps, AF.Identity,
                    bias=bias_ap, scale=1.0),
                    reads=[pskey, bqc if cfg["qkv_bias"] else "constpool"],
                    writes=[("at", "kfm")])
        else:
            qk_tm_norm(p, wp, 0, 0, qfm, TQ, qnt, ("at", "qfm"))
            qk_tm_norm(p, wp, 1, 128, kfm, TB, knt, ("at", "kfm"))
        for kt in range(TB // 128):
            slot = kt % 2
            ps = pr.pB[:, slot * 512:slot * 512 + 128]
            pskey = (pr.pB.name, slot)
            vbias = (pr.ones_row[0:1, 0:128],
                     bqr[0:1, p * 128: (p + 1) * 128]) \
                if cfg["qkv_bias"] else None
            vb_reads = [bqr, pr.ones_row] if cfg["qkv_bias"] else []
            mm_group(kb, ps,
                     [(hT[:, c * TB + kt * 128: c * TB + (kt + 1) * 128],
                       wp[:, c * 384 + 256: c * 384 + 384]) for c in range(8)],
                     reads=[(wp.name, 2)] + hT_keys + vb_reads,
                     writes=[pskey], bias=vbias)
            def vev(e, kt=kt, ps=ps):
                dap = bass.AP(vtm, kt * 2 * 65,
                              [[vtm.shape[1], 128], [65, 2], [1, 64]])
                return e.tensor_copy(dap, ps.rearrange("p (h n) -> p h n", h=2))
            kb.group("DVE", vev, reads=[pskey, "vtm_ones"], writes=[("at", "vtm")])
        if cfg["fixed_bound"] is None:
            cb2 = emit_pair_bounds(pr, qfm, kfm, TQ, TB, cfg["scale"], "at")
        else:
            cb2 = cb2t
        emit_attn_pair(pr, p, TQ, TB, qfm, kfm, vtm, ctxfm, cfg["scale"], cb2, "at",
                       causal=cfg["causal"], caus_masks=cmasks)

    ctx_keys = [("at", "ctx", h, qc) for h in range(NH) for qc in range(TQ // 512)]
    if cfg["out_mode"] == "dump_ctx":
        if not dump_fm:
            kb.group("SP", lambda e: e.dma_start(out=ctxo, in_=ctxfm[0:64, :]),
                     reads=ctx_keys, writes=["ctxo"], dma="SPQ")
        else:
            def cdma(e):
                ct = ctxo.tensor if hasattr(ctxo, "tensor") else ctxo
                dap = bass.AP(ct, 0, [[TQ, 64], [64 * TQ, NH], [1, TQ]])
                return e.dma_start(out=dap, in_=ctxfm[0:64, :].rearrange(
                    "p (h n) -> p h n", h=NH))
            kb.group("SP", cdma, reads=ctx_keys, writes=["ctxo"], dma="SPQ")
        return

    hT_dead = hT_keys + ["wo"]
    kb.group("POOL", lambda e: e.dma_start(
        out=wo[0:64, 0:NH * 1024].rearrange("p (c n) -> p c n", c=NH),
        in_=WoT[0:64 * NH, :].rearrange("(c p) n -> p c n", p=64)),
        reads=[], writes=hT_dead, dma="POOLQ")
    for t in range(NQT):
        ob = obuf[t % 2]
        for half in range(2):
            ps = pr.pA[:, half * 512:half * 512 + 512]
            pskey = (pr.pA.name, half)
            mm_group(kb, ps,
                     [(ctxfm[0:64, c * TQ + t * 128: c * TQ + (t + 1) * 128],
                       wo[0:64, c * 1024 + half * 512: c * 1024 + (half + 1) * 512])
                      for c in range(NH)],
                     reads=["wo", bor, pr.ones_row] + ctx_keys, writes=[pskey],
                     bias=(pr.ones_row[0:1, 0:128],
                           bor[0:1, half * 512:(half + 1) * 512]))
            kb.group("DVE", lambda e, half=half, ps=ps, ob=ob: e.tensor_copy(
                ob[:, half * 512:(half + 1) * 512], ps),
                reads=[pskey], writes=[ob])
        xt = xrb[t % 2]
        kb.group("SP", lambda e, t=t, xt=xt: e.dma_start(
            out=xt[:, :], in_=xb[t * 128:(t + 1) * 128, :]),
            writes=[xt], dma="SPQ")
        if cfg["internal_residual"]:
            kb.group("DVE", lambda e, t=t, xt=xt: e.tensor_scalar(
                ybuf[:, :], xt[:, :], mvball[:, 2 * t:2 * t + 1],
                r1all[:, t:t + 1], ALU.subtract, ALU.mult),
                reads=[xt, mvball, r1all], writes=[ybuf])
            kb.group("DVE", lambda e, ob=ob: e.tensor_tensor(
                ob[:, :], ob[:, :], ybuf[:, :], ALU.add),
                reads=[ob, ybuf], writes=[ob])
        if not cfg.get("sandwich", True):
            kb.group("SP", lambda e, t=t, ob=ob: e.dma_start(
                out=xo[t * 128:(t + 1) * 128, :], in_=ob[:, :]),
                reads=[ob], writes=[("xo", t)], dma="SPQ")
            continue
        for c in range(2):
            kb.group("DVE", lambda e, c=c, ob=ob: e.bn_stats(
                bns[:, c * 6:(c + 1) * 6], ob[:, c * 512:(c + 1) * 512]),
                reads=[ob, bns], writes=[bns])
        kb.group("DVE", lambda e: e.bn_aggr(mv2[:, :], bns[:, 0:12]),
                 reads=[bns], writes=[mv2])
        kb.group("ACT", lambda e: e.activation(mv2[:, 1:2], mv2[:, 1:2],
                                               AF.Ln, bias=eps1, scale=1.0),
                 reads=[mv2, "constpool"], writes=[mv2])
        kb.group("ACT", lambda e: e.activation(mv2[:, 1:2], mv2[:, 1:2],
                                               AF.Exp, bias=zero, scale=-0.5),
                 reads=[mv2, "constpool"], writes=[mv2])
        kb.group("DVE", lambda e, ob=ob: e.tensor_scalar(
            ob[:, :], ob[:, :], mv2[:, 0:1], mv2[:, 1:2],
            ALU.subtract, ALU.mult), reads=[ob, mv2], writes=[ob])
        kb.group("DVE", lambda e, ob=ob, xt=xt: e.tensor_tensor(
            ob[:, :], ob[:, :], xt[:, :], ALU.add),
            reads=[ob, xt], writes=[ob])
        kb.group("SP", lambda e, t=t, ob=ob: e.dma_start(
            out=xo[t * 128:(t + 1) * 128, :], in_=ob[:, :]),
            reads=[ob], writes=[("xo", t)], dma="SPQ")
    return


def build_attn_atom(which, TB=4096):
    nc = bass.Bass("TRN2", target_bir_lowering=False)
    pr = Prog(nc)
    io = attn_dram_decls(pr.kb, which, TB)
    emit_attn_atom(pr, io, which, TB)
    pr.kb.barrier_on("SP")
    pr.kb.finalize()
    return nc



def build_L0a(TB=4096, prenorm="lnln"):
    return build_attn_atom("L0a", TB=TB)


def emit_L2a_proj(pr, io, T=2048):
    nc = pr.nc
    kb = pr.kb
    D = 1024
    NT = T // 128
    x = io["x"]
    ctxin = io["ctxin"]
    WoD = io["WoD"]
    xo = io["xo"]
    cfm = kb.sbuf("cfm", [128, 8 * T], BF16)
    wob = kb.sbuf("wob", [128, 8 * D], BF16)
    xrb = [kb.sbuf(f"xrb{i}", [128, D], F32) for i in range(2)]
    obuf = [kb.sbuf(f"obuf{i}", [128, D], F32) for i in range(2)]
    bns = kb.sbuf("bns", [128, 12], F32)
    mv2 = kb.sbuf("mv2", [128, 2], F32)
    eps1 = kb.constv(1e-5)
    zero = kb.constv(0.0)
    kb.group("SP", lambda e: e.dma_start(
        out=cfm[:, :].rearrange("p (c n) -> p c n", c=8),
        in_=ctxin.rearrange("(c p) n -> p c n", p=128)), writes=[cfm], dma="SPQ")
    load_weight_chunked(kb, wob, WoD, D, D, "POOLQ")
    for t in range(NT):
        ob = obuf[t % 2]
        xt = xrb[t % 2]
        kb.group("SP", lambda e, t=t, xt=xt: e.dma_start(
            out=xt[:, :], in_=x[t * 128:(t + 1) * 128, :]),
            writes=[xt], dma="SPQ")
        for half in range(2):
            ps = pr.pA[:, half * 512:half * 512 + 512]
            pskey = (pr.pA.name, half)
            mm_group(kb, ps,
                     [(cfm[:, c * T + t * 128: c * T + (t + 1) * 128],
                       wob[:, c * D + half * 512: c * D + (half + 1) * 512])
                      for c in range(8)],
                     reads=[cfm, wob], writes=[pskey])
            kb.group("DVE", lambda e, half=half, ps=ps, ob=ob: e.tensor_copy(
                ob[:, half * 512:(half + 1) * 512], ps),
                reads=[pskey], writes=[ob])
        for c in range(2):
            kb.group("DVE", lambda e, c=c, ob=ob: e.bn_stats(
                bns[:, c * 6:(c + 1) * 6], ob[:, c * 512:(c + 1) * 512]),
                reads=[ob, bns], writes=[bns])
        kb.group("DVE", lambda e: e.bn_aggr(mv2[:, :], bns[:, 0:12]),
                 reads=[bns], writes=[mv2])
        kb.group("ACT", lambda e: e.activation(mv2[:, 1:2], mv2[:, 1:2],
                                               AF.Ln, bias=eps1, scale=1.0),
                 reads=[mv2, "constpool"], writes=[mv2])
        kb.group("ACT", lambda e: e.activation(mv2[:, 1:2], mv2[:, 1:2],
                                               AF.Exp, bias=zero, scale=-0.5),
                 reads=[mv2, "constpool"], writes=[mv2])
        kb.group("DVE", lambda e, ob=ob: e.tensor_scalar(
            ob[:, :], ob[:, :], mv2[:, 0:1], mv2[:, 1:2],
            ALU.subtract, ALU.mult), reads=[ob, mv2], writes=[ob])
        kb.group("DVE", lambda e, ob=ob, xt=xt: e.tensor_tensor(
            ob[:, :], ob[:, :], xt[:, :], ALU.add),
            reads=[ob, xt], writes=[ob])
        kb.group("SP", lambda e, t=t, ob=ob: e.dma_start(
            out=xo[t * 128:(t + 1) * 128, :], in_=ob[:, :]),
            reads=[ob], writes=[("xo", t)], dma="SPQ")
    return





def build_L2a_proj(T=2048):
    D = 1024
    nc = bass.Bass("TRN2", target_bir_lowering=False)
    pr = Prog(nc)
    kb = pr.kb
    io = {}
    io["x"] = kb.dram_in("x", [T, D], F32)
    io["ctxin"] = kb.dram_in("ctxin", [D, T], mybir.dt.bfloat16)
    io["WoD"] = kb.dram_in("WoD", [D, D], F32)
    io["xo"] = kb.dram_out("xo", [T, D], F32)
    emit_L2a_proj(pr, io, T=T)
    kb.barrier_on("SP")
    kb.finalize()
    return nc

def emit_L1a(pr, io, T=4096, fm_out=False):
    HK, HV, NHL = 256, 512, 4
    C = 128
    NCH = T // C
    NG = NCH // 8
    D = 1024
    nc = pr.nc
    kb = pr.kb
    xb = io["xb"]
    normwt = io["normwt"]
    qwT = io["qwT"]
    kwT = io["kwT"]
    vwT = io["vwT"]
    gwT = io["gwT"]
    bwT = io["bwT"]
    awT = io["awT"]
    qconvt = io["qconvt"]
    kconvt = io["kconvt"]
    vconvt = io["vconvt"]
    alog = io["alog"]
    dtb = io["dtb"]
    onormt = io["onormt"]
    og = io["og"]

    hT = kb.sbuf_at("hT", [128, 8 * T], BF16, nc.sbuf_top - 16 * T)
    rhsb2 = kb.sbuf_at("rhsb2", [128, 2 * HV], BF16, nc.sbuf_top - 16 * T - 2 * 2 * HV) \
        if False else None
    nwt = kb.sbuf("nwt", [128, D], BF16)
    onb = kb.sbuf("onb", [128, HV], F32)
    wqh = kb.sbuf("wqh", [128, 8 * HK], BF16)
    wkh = kb.sbuf("wkh", [128, 8 * HK], BF16)
    wvh = kb.sbuf("wvh", [128, 8 * HV], BF16)
    wgh = kb.sbuf("wgh", [128, 8 * HV], BF16)
    wba = kb.sbuf("wba", [128, 8 * 2 * NHL], BF16)
    cvt = kb.sbuf("cvt", [128, NHL * 16 * 4], F32)
    TRW = 9 * 128
    qraw = kb.sbuf("qraw", [128, 2 * TRW], BF16)
    kraw = kb.sbuf("kraw", [128, 2 * TRW], BF16)
    vraw = kb.sbuf("vraw", [128, 4 * TRW], BF16)
    TG = 1024
    qc = kb.sbuf("qcb", [128, 2 * TG], F32)
    kcb = qc
    vcb = kb.sbuf("vcb", [128, 2 * TG], F32)
    qcf = kb.sbuf("qcf", [128, 2 * TG], BF16)
    kcf = kb.sbuf("kcf", [128, 2 * TG], BF16)
    vcf = kb.sbuf("vcf", [128, 4 * TG], BF16)
    ktm = kb.sbuf("ktm", [128, 8 * HK], BF16)
    vtmg = kb.sbuf("vtmg", [128, 8 * HV], BF16)
    rrow = kb.sbuf("rrow", [1, TG], F32)
    g_all = kb.sbuf("g_all", [128, NHL * NCH], F32)
    beta_all = kb.sbuf("beta_all", [128, NHL * NCH], F32)
    G_all = kb.sbuf("G_all", [128, NHL * NCH], F32)
    egt_all = kb.sbuf("egt_all", [128, NHL * NCH], F32)
    nbeg_all = kb.sbuf("nbeg_all", [128, NHL * NCH], F32)
    egc_b = kb.sbuf("egc_b", [128, NCH], F32)
    gc_b = kb.sbuf("gc_b", [128, NCH], F32)
    growb = kb.sbuf("growb", [1, 2 * C], F32)
    dtl = kb.sbuf("dtl", [128, 8 * C], BF16)
    dst = kb.sbuf("dst", [128, 8 * C], BF16)
    qkd = kb.sbuf("qkd", [128, 8 * C], BF16)
    Xb = kb.sbuf("Xb", [128, 8 * C], BF16)
    Atb = kb.sbuf("Atb", [128, 8 * C], BF16)
    Wb2 = kb.sbuf("Wb2", [128, 8 * C], BF16)
    xtb = kb.sbuf("xtb", [128, C], BF16)
    Ttb = kb.sbuf("Ttb", [128, 8 * C], BF16)
    ident2 = kb.sbuf("ident2", [128, C], BF16)
    S32 = kb.sbuf("S32", [128, 2 * HV], F32)
    Sbf = kb.sbuf("Sbf", [128, 2 * HV], BF16)
    rhsb = kb.sbuf("rhsb", [128, HV], BF16)
    ubf = kb.sbuf("ubf", [128, HV], BF16)
    obuf = kb.sbuf("obuf", [128, HV], F32)
    o2b = kb.sbuf("o2b", [128, HV], F32)
    sgb = kb.sbuf("sgb", [128, HV], F32)
    ogb = kb.sbuf("ogb", [128, 2 * HV], BF16)
    ogfmb = kb.sbuf("ogfmb", [128, 512], BF16)
    kdec = kb.sbuf("kdec", [128, HK], BF16)
    scc = kb.sbuf("scc", [128, 8], F32)
    narow = kb.sbuf("narow", [1, 2 * NHL], F32)
    nab = kb.sbuf("nab", [128, NHL], F32)
    spb = [kb.sbuf(f"spb{i}", [128, NHL], F32) for i in range(3)]
    xrb = [kb.sbuf("xrb0", [128, D], F32)] * 2
    xnb = [kb.sbuf("xnb0", [128, D], BF16)] * 2
    mvball = kb.sbuf("mvball", [128, 2 * NCH], F32)
    r1all = kb.sbuf("r1all", [128, NCH], F32)
    cball = kb.sbuf("cball", [128, NCH], F32)
    sc1 = kb.sbuf("sc1", [128, 4], F32)
    bns = kb.sbuf("bns", [128, 12], F32)
    eps1 = kb.constv(1e-5)
    eps6 = kb.constv(1e-6)
    one1 = kb.constv(1.0)
    zero = kb.constv(0.0)

    kb.group("POOL", lambda e: e.dma_start(out=nwt[:, :], in_=normwt[:, :]),
             writes=[nwt], dma="POOLQ")
    kb.group("POOL", lambda e: e.dma_start(out=onb[:, :], in_=onormt[:, :]),
             writes=[onb], dma="POOLQ")
    kb.group("POOL", lambda e: e.dma_start(out=cvt[:, 0:32], in_=qconvt[:, :]),
             writes=["cv_q"], dma="POOLQ")
    kb.group("POOL", lambda e: e.dma_start(out=cvt[:, 32:64], in_=kconvt[:, :]),
             writes=["cv_k"], dma="POOLQ")
    kb.group("POOL", lambda e: e.dma_start(out=cvt[:, 64:128], in_=vconvt[:, :]),
             writes=["cv_v"], dma="POOLQ")
    kb.group("POOL", lambda e: e.dma_start(
        out=wba[:, 0:8 * NHL].rearrange("p (c n) -> p c n", c=8),
        in_=bwT.rearrange("(c p) n -> p c n", p=128)), writes=["wba_b"], dma="POOLQ")
    kb.group("POOL", lambda e: e.dma_start(
        out=wba[:, 8 * NHL:16 * NHL].rearrange("p (c n) -> p c n", c=8),
        in_=awT.rearrange("(c p) n -> p c n", p=128)), writes=["wba_a"], dma="POOLQ")
    kb.group("POOL", lambda e: e.dma_start(out=narow[0:1, 0:NHL], in_=alog[:, :]),
             writes=[narow], dma="POOLQ")
    kb.group("POOL", lambda e: e.dma_start(out=narow[0:1, NHL:2 * NHL], in_=dtb[:, :]),
             writes=[narow], dma="POOLQ")
    kb.group("POOL", lambda e: e.memset(ident2[:, :], 0.0), writes=[ident2])
    kb.group("POOL", lambda e: e.affine_select(
        out=ident2[:, :], in_=ident2[:, :], compare_op=ALU.not_equal, fill=2.0,
        base=0, pattern=[[-1, C]], channel_multiplier=1),
        reads=[ident2], writes=[ident2])
    zeror = kb.constr(0.0)
    kb.group("ACT", lambda e, zeror=zeror: e.activation(
        narow[0:1, 0:NHL], narow[0:1, 0:NHL], AF.Exp, bias=zeror, scale=1.0),
        reads=[narow, "constrow"], writes=[narow])
    kb.group("PE", lambda e: e.matmul(pr.pD[:, 0:2 * NHL], pr.onesf[0:1, 0:128],
                                      narow[0:1, 0:2 * NHL], start=True, stop=True),
             reads=[narow, pr.onesf], writes=[pr.pD])
    kb.group("DVE", lambda e: e.tensor_scalar(nab[:, :], pr.pD[:, 0:NHL], -1.0,
                                              None, ALU.mult),
             reads=[pr.pD], writes=[nab])
    kb.group("DVE", lambda e: e.tensor_copy(spb[2][:, :], pr.pD[:, NHL:2 * NHL]),
             reads=[pr.pD], writes=["dtb_b"])

    def prep_tile(t):
        xt = xrb[t % 2]
        kb.group("SP", lambda e, t=t, xt=xt: e.dma_start(
            out=xt[:, :], in_=xb[t * 128:(t + 1) * 128, :]),
            writes=[xt], dma="SPQ")
        kb.group("DVE", lambda e, xt=xt: e.bn_stats(bns[:, 0:6], xt[:, 0:512]),
                 reads=[xt], writes=[bns])
        kb.group("DVE", lambda e, xt=xt: e.bn_stats(bns[:, 6:12], xt[:, 512:1024]),
                 reads=[xt, bns], writes=[bns])
        kb.group("DVE", lambda e, t=t: e.bn_aggr(mvball[:, 2 * t:2 * t + 2], bns[:, :]),
                 reads=[bns], writes=[mvball])
        kb.group("ACT", lambda e, t=t: e.activation(
            r1all[:, t:t + 1], mvball[:, 2 * t + 1:2 * t + 2], AF.Ln,
            bias=eps1, scale=1.0), reads=[mvball, "constpool"], writes=[r1all])
        kb.group("ACT", lambda e, t=t: e.activation(
            r1all[:, t:t + 1], r1all[:, t:t + 1], AF.Exp, bias=zero, scale=-0.5),
            reads=[r1all, "constpool"], writes=[r1all])
        kb.group("DVE", lambda e, t=t: e.tensor_tensor(
            sc1[:, 0:1], mvball[:, 2 * t + 1:2 * t + 2], r1all[:, t:t + 1], ALU.mult),
            reads=[mvball, r1all], writes=[sc1])
        kb.group("DVE", lambda e, t=t: e.tensor_tensor(
            sc1[:, 1:2], sc1[:, 0:1], r1all[:, t:t + 1], ALU.mult),
            reads=[sc1, r1all], writes=[sc1])
        kb.group("ACT", lambda e, e6=eps6: e.activation(
            sc1[:, 2:3], sc1[:, 1:2], AF.Ln, bias=e6, scale=1.0),
            reads=[sc1, "constpool"], writes=[sc1])
        kb.group("ACT", lambda e: e.activation(sc1[:, 2:3], sc1[:, 2:3], AF.Exp,
                                               bias=zero, scale=-0.5),
                 reads=[sc1, "constpool"], writes=[sc1])
        kb.group("DVE", lambda e, t=t: e.tensor_tensor(
            cball[:, t:t + 1], r1all[:, t:t + 1], sc1[:, 2:3], ALU.mult),
            reads=[sc1, r1all], writes=[cball])
        kb.group("DVE", lambda e, t=t, xt=xt: e.tensor_scalar(
            xt[:, :], xt[:, :], mvball[:, 2 * t:2 * t + 1], cball[:, t:t + 1],
            ALU.subtract, ALU.mult), reads=[xt, mvball, cball], writes=[xt])
        xn = xnb[t % 2]
        kb.group("DVE", lambda e, xt=xt, xn=xn: e.scalar_tensor_tensor(
            xn[:, :], xt[:, :], 1.0, nwt[:, :], ALU.mult, ALU.mult),
            reads=[xt, nwt], writes=[xn])
        return xn[:, :]

    emit_tm_to_fm(pr, prep_tile, NCH, D, hT, T, 0, "tf")
    hT_keys = [(hT.name, "fm", t) for t in range(NCH)]

    for t in range(NCH):
        ps = pr.pC[:, 0:2 * NHL]
        mm_group(kb, pr.pC[:, 0:NHL],
                 [(hT[:, c * T + t * 128: c * T + (t + 1) * 128],
                   wba[:, c * NHL:(c + 1) * NHL]) for c in range(8)],
                 reads=["wba_b"] + hT_keys, writes=[pr.pC])
        mm_group(kb, pr.pC[:, NHL:2 * NHL],
                 [(hT[:, c * T + t * 128: c * T + (t + 1) * 128],
                   wba[:, 8 * NHL + c * NHL:8 * NHL + (c + 1) * NHL])
                  for c in range(8)],
                 reads=["wba_a"] + hT_keys, writes=[pr.pC])
        def bev(e, t=t, ps=ps):
            dap = bass.AP(beta_all, t, [[beta_all.shape[1], 128], [NCH, NHL]])
            return e.activation(dap, ps[:, 0:NHL], AF.Sigmoid, bias=zero, scale=1.0)
        kb.group("ACT", bev, reads=[pr.pC, "constpool"], writes=[beta_all])
        kb.group("DVE", lambda e, ps=ps: e.tensor_tensor(
            spb[0][:, :], ps[:, NHL:2 * NHL], spb[2][:, :], ALU.add),
            reads=[pr.pC, "dtb_b"], writes=[spb[0]])
        kb.group("ACT", lambda e: e.activation(spb[1][:, :], spb[0][:, :], AF.Abs,
                                               bias=zero, scale=1.0),
                 reads=[spb[0], "constpool"], writes=[spb[1]])
        kb.group("ACT", lambda e: e.activation(spb[1][:, :], spb[1][:, :], AF.Exp,
                                               bias=zero, scale=-1.0),
                 reads=[spb[1], "constpool"], writes=[spb[1]])
        kb.group("ACT", lambda e, one1=one1: e.activation(
            spb[1][:, :], spb[1][:, :], AF.Ln, bias=one1, scale=1.0),
            reads=[spb[1], "constpool"], writes=[spb[1]])
        kb.group("DVE", lambda e: e.tensor_scalar(spb[0][:, :], spb[0][:, :], 0.0,
                                                  None, ALU.max),
                 reads=[spb[0]], writes=[spb[0]])
        kb.group("DVE", lambda e: e.tensor_tensor(spb[0][:, :], spb[0][:, :],
                                                  spb[1][:, :], ALU.add),
                 reads=[spb[0], spb[1]], writes=[spb[0]])
        def gev(e, t=t):
            dap = bass.AP(g_all, t, [[g_all.shape[1], 128], [NCH, NHL]])
            return e.tensor_tensor(dap, spb[0][:, :], nab[:, :], ALU.mult)
        kb.group("DVE", gev, reads=[spb[0], nab], writes=[g_all])
    triu_f = kb.sbuf("triu_f", [128, C], F32)
    kb.group("POOL", lambda e: e.memset(triu_f[:, :], 1.0), writes=[triu_f])
    kb.group("POOL", lambda e: e.affine_select(
        out=triu_f[:, :], in_=triu_f[:, :], compare_op=ALU.is_ge, fill=0.0,
        base=0, pattern=[[1, C]], channel_multiplier=-1),
        reads=[triu_f], writes=[triu_f])
    for h in range(NHL):
        kb.group("PE", lambda e, h=h: e.matmul(
            pr.pA[:, 0:NCH], triu_f[:, :], g_all[:, h * NCH:(h + 1) * NCH],
            start=True, stop=True),
            reads=[triu_f, g_all], writes=[(pr.pA.name, 0)])
        kb.group("DVE", lambda e, h=h: e.tensor_copy(
            G_all[:, h * NCH:(h + 1) * NCH], pr.pA[:, 0:NCH]),
            reads=[(pr.pA.name, 0)], writes=[G_all])
    kb.group("ACT", lambda e: e.activation(egt_all[:, :], G_all[:, :], AF.Exp,
                                           bias=zero, scale=1.0),
             reads=[G_all, "constpool"], writes=[egt_all])
    kb.group("DVE", lambda e: e.tensor_scalar(nbeg_all[:, :], beta_all[:, :],
                                              -1.0, None, ALU.mult),
             reads=[beta_all], writes=[nbeg_all])
    kb.group("DVE", lambda e: e.tensor_tensor(nbeg_all[:, :], nbeg_all[:, :],
                                              egt_all[:, :], ALU.mult),
             reads=[nbeg_all, egt_all], writes=[nbeg_all])

    for h in range(NHL):
        q1 = kb.dma_queue("POOLQ.wh")
        kb.group("POOL", lambda e, h=h: e.dma_start(
            out=wqh[:, :].rearrange("p (c n) -> p c n", c=8),
            in_=qwT[:, h * HK:(h + 1) * HK].rearrange("(c p) n -> p c n", p=128)),
            writes=[wqh], dma=q1)
        kb.group("POOL", lambda e, h=h: e.dma_start(
            out=wkh[:, :].rearrange("p (c n) -> p c n", c=8),
            in_=kwT[:, h * HK:(h + 1) * HK].rearrange("(c p) n -> p c n", p=128)),
            writes=[wkh], dma=q1)
        kb.group("POOL", lambda e, h=h: e.dma_start(
            out=wvh[:, :].rearrange("p (c n) -> p c n", c=8),
            in_=vwT[:, h * HV:(h + 1) * HV].rearrange("(c p) n -> p c n", p=128)),
            writes=[wvh], dma=q1)
        kb.group("POOL", lambda e, h=h: e.dma_start(
            out=wgh[:, :].rearrange("p (c n) -> p c n", c=8),
            in_=gwT[:, h * HV:(h + 1) * HV].rearrange("(c p) n -> p c n", p=128)),
            writes=[wgh], dma=q1)
        kb.group("PE", lambda e, h=h: e.matmul(
            pr.pA[:, 0:NCH] if False else pr.pD[0:1, 0:NCH],
            pr.onesf[:, 0:1], g_all[:, h * NCH:(h + 1) * NCH],
            start=True, stop=True),
            reads=[g_all, pr.onesf], writes=[pr.pD])
        kb.group("DVE", lambda e: e.tensor_copy(growb[0:1, 0:NCH], pr.pD[0:1, 0:NCH]),
                 reads=[pr.pD], writes=[growb])
        kb.group("PE", lambda e: e.matmul(pr.pD[:, 0:NCH], pr.onesf[0:1, 0:128],
                                          growb[0:1, 0:NCH], start=True, stop=True),
                 reads=[growb, pr.onesf], writes=[pr.pD])
        kb.group("DVE", lambda e: e.tensor_copy(gc_b[:, :], pr.pD[:, 0:NCH]),
                 reads=[pr.pD], writes=[gc_b])
        kb.group("ACT", lambda e: e.activation(egc_b[:, :], gc_b[:, :], AF.Exp,
                                               bias=zero, scale=1.0),
                 reads=[gc_b, "constpool"], writes=[egc_b])
        kb.group("POOL", lambda e: e.memset(S32[:, :], 0.0), writes=[S32])
        kb.group("POOL", lambda e: e.memset(Sbf[:, :], 0.0), writes=[Sbf])

        for g in range(NG):
            t0 = g * 8
            for j, (wbuf, wname, nch, raw) in enumerate((
                    (wqh, "wqh", 2, qraw), (wkh, "wkh", 2, kraw),
                    (wvh, "wvh", 4, vraw))):
                if g == 0:
                    kb.group("POOL", lambda e, raw=raw, nch=nch: e.memset(
                        bass.AP(raw, 0,
                                [[raw.shape[1], 128], [TRW, nch], [1, 128]]), 0.0),
                        writes=[(raw.name, "z")])
                    blocks = [(128, 512), (640, 512)]
                else:
                    blocks = [(0, 512), (512, 512), (1024, 128)]
                for cc in range(nch):
                    for bi, (j0, w_) in enumerate(blocks):
                        slot = (cc + bi) % 2
                        ps = pr.pB[:, slot * 512:slot * 512 + w_]
                        pskey = (pr.pB.name, slot)
                        tok0 = t0 * 128 - 128 + j0
                        mm_group(kb, ps,
                                 [(wbuf[:, c * nch * 128 + cc * 128:
                                        c * nch * 128 + (cc + 1) * 128],
                                   hT[:, c * T + tok0: c * T + tok0 + w_])
                                  for c in range(8)],
                                 reads=[wbuf.name] + hT_keys, writes=[pskey])
                        kb.group("DVE", lambda e, raw=raw, cc=cc, j0=j0, w_=w_, ps=ps:
                                 e.tensor_copy(raw[:, cc * TRW + j0:
                                                   cc * TRW + j0 + w_], ps),
                                 reads=[pskey], writes=[(raw.name, cc, j0)])
            for (raw, cb, cf, nch, cvoff) in ((qraw, qc, qcf, 2, 0),
                                              (kraw, kcb, kcf, 2, 32),
                                              (vraw, vcb, vcf, 4, 64)):
                rawkeys = ([(raw.name, "z")] if g == 0 else []) + \
                    [(raw.name, cc, j0) for cc in range(nch)
                     for j0 in ([128, 640] if g == 0 else [0, 512, 1024])]
                for ccg in range(0, nch, 2):
                    for cc in (ccg, ccg + 1):
                        sl = cc % 2
                        for i in range(4):
                            tap = cvt[:, cvoff + (h * nch + cc) * 4 + i:
                                      cvoff + (h * nch + cc) * 4 + i + 1]
                            src2 = raw[:, cc * TRW + 125 + i: cc * TRW + 125 + i + TG]
                            dst = cb[:, sl * TG:(sl + 1) * TG]
                            if i == 0:
                                kb.group("DVE", lambda e, src2=src2, dst=dst, tap=tap:
                                         e.tensor_scalar(dst, src2, tap, None, ALU.mult),
                                         reads=rawkeys + ["cvt", cb.name], writes=[cb.name])
                            else:
                                kb.group("DVE", lambda e, src2=src2, dst=dst, tap=tap:
                                         e.scalar_tensor_tensor(dst, src2, tap, dst,
                                                                ALU.mult, ALU.add),
                                         reads=rawkeys + ["cvt", cb.name], writes=[cb.name])
                    for cc in (ccg, ccg + 1):
                        sl = cc % 2
                        kb.group("ACT", lambda e, cb=cb, sl=sl: e.activation(
                            cb[:, sl * TG:(sl + 1) * TG], cb[:, sl * TG:(sl + 1) * TG],
                            AF.Silu, bias=zero, scale=1.0),
                            reads=[cb.name, "constpool"], writes=[cb.name])
                    if nch == 4:
                        for cc in (ccg, ccg + 1):
                            sl = cc % 2
                            kb.group("DVE", lambda e, cb=cb, cf=cf, cc=cc, sl=sl:
                                     e.tensor_copy(cf[:, cc * TG:(cc + 1) * TG],
                                                   cb[:, sl * TG:(sl + 1) * TG]),
                                     reads=[cb.name], writes=[cf.name])
                if nch == 2:
                    for cc in range(2):
                        kb.group("ACT", lambda e, cb=cb, cc=cc: e.activation(
                            vcb[:, cc * TG:(cc + 1) * TG],
                            cb[:, cc * TG:(cc + 1) * TG],
                            AF.Square, bias=zero, scale=1.0),
                            reads=[cb.name, "constpool"], writes=[vcb.name])
                    for tc in range(TG // 512):
                        def sm(e, tc=tc):
                            last = None
                            for cc in range(2):
                                last = e.matmul(
                                    pr.pA[0:1, tc * 512:(tc + 1) * 512],
                                    pr.onesf[:, 0:1],
                                    vcb[:, cc * TG + tc * 512:
                                        cc * TG + (tc + 1) * 512],
                                    start=(cc == 0), stop=(cc == 1))
                            return last
                        kb.group("PE", sm, reads=[vcb.name, pr.onesf],
                                 writes=[(pr.pA.name, 0)])
                        kb.group("DVE", lambda e, tc=tc: e.tensor_copy(
                            rrow[0:1, tc * 512:(tc + 1) * 512],
                            pr.pA[0:1, tc * 512:(tc + 1) * 512]),
                            reads=[(pr.pA.name, 0)], writes=[rrow])
                    epsr = kb.constr(1e-6)
                    kb.group("ACT", lambda e, epsr=epsr: e.activation(
                        rrow[0:1, :], rrow[0:1, :], AF.Ln, bias=epsr,
                        scale=1.0), reads=[rrow, "constrow"], writes=[rrow])
                    qsc = kb.constr(-2.7725887222397811) if cb is qc and cf is qcf \
                        else kb.constr(0.0)
                    kb.group("ACT", lambda e, qsc=qsc: e.activation(
                        rrow[0:1, :], rrow[0:1, :], AF.Exp, bias=qsc, scale=-0.5),
                        reads=[rrow, "constrow"], writes=[rrow])
                    for tc in range(TG // 512):
                        kb.group("PE", lambda e, tc=tc: e.matmul(
                            pr.pD[:, 0:512], pr.onesf[0:1, 0:128],
                            rrow[0:1, tc * 512:(tc + 1) * 512],
                            start=True, stop=True),
                            reads=[rrow, pr.onesf], writes=[pr.pD])
                        for cc in range(2):
                            kb.group("DVE", lambda e, cb=cb, cf=cf, cc=cc, tc=tc:
                                     e.tensor_tensor(
                                         cf[:, cc * TG + tc * 512:cc * TG + (tc + 1) * 512],
                                         cb[:, cc * TG + tc * 512:cc * TG + (tc + 1) * 512],
                                         pr.pD[:, 0:512], ALU.mult),
                                     reads=[cb.name, pr.pD], writes=[cf.name])
            for c8 in range(8):
                def ktr(e, c8=c8):
                    last = None
                    for cc in range(2):
                        last = e.transpose(
                            pr.pb0[:, cc * 128:(cc + 1) * 128],
                            kcf[:, cc * TG + c8 * 128:cc * TG + (c8 + 1) * 128],
                            pr.ident[:, :])
                    return last
                kb.group("PE", ktr, reads=[kcf.name, pr.ident], writes=[pr.pb0])
                kb.group("DVE", lambda e, c8=c8: e.tensor_copy(
                    ktm[:, c8 * HK:(c8 + 1) * HK], pr.pb0[:, 0:HK]),
                    reads=[pr.pb0], writes=[ktm])
                def vtr(e, c8=c8):
                    last = None
                    for cc in range(4):
                        last = e.transpose(
                            pr.pb1[:, cc * 128:(cc + 1) * 128],
                            vcf[:, cc * TG + c8 * 128:cc * TG + (c8 + 1) * 128],
                            pr.ident[:, :])
                    return last
                kb.group("PE", vtr, reads=[vcf.name, pr.ident], writes=[pr.pb1])
                kb.group("DVE", lambda e, c8=c8: e.tensor_copy(
                    vtmg[:, c8 * HV:(c8 + 1) * HV], pr.pb1[:, 0:HV]),
                    reads=[pr.pb1], writes=[vtmg])
            for c8 in range(8):
                t = t0 + c8
                kb.group("PE", lambda e, t=t, h=h: e.transpose(
                    pr.pD[0:1, 0:128], G_all[:, h * NCH + t: h * NCH + t + 1],
                    pr.identf[:, :]), reads=[G_all, pr.identf], writes=[pr.pD])
                kb.group("DVE", lambda e: e.tensor_copy(growb[0:1, 0:C],
                                                        pr.pD[0:1, 0:128]),
                         reads=[pr.pD], writes=[growb])
                kb.group("DVE", lambda e: e.tensor_scalar(
                    growb[0:1, C:2 * C], growb[0:1, 0:C], -1.0, None, ALU.mult),
                    reads=[growb], writes=[growb])
                def dtm(e):
                    e.matmul(pr.pA[:, 0:C], pr.onesf[0:1, 0:128],
                             growb[0:1, 0:C], start=True, stop=False)
                    return e.matmul(pr.pA[:, 0:C], growb[0:1, C:2 * C],
                                    pr.onesf[0:1, 0:128], start=False, stop=True)
                kb.group("PE", dtm, reads=[growb, pr.onesf], writes=[(pr.pA.name, 0)])
                kb.group("DVE", lambda e: e.tensor_scalar(
                    pr.pA[:, 0:C], pr.pA[:, 0:C], 0.0, None, ALU.min),
                    reads=[(pr.pA.name, 0)], writes=[(pr.pA.name, 0)])
                kb.group("ACT", lambda e, c8=c8: e.activation(
                    dtl[:, c8 * C:(c8 + 1) * C], pr.pA[:, 0:C], AF.Exp,
                    bias=zero, scale=1.0),
                    reads=[(pr.pA.name, 0), "constpool"], writes=[(dtl.name, c8)])
                kb.group("POOL", lambda e, c8=c8: e.affine_select(
                    out=dtl[:, c8 * C:(c8 + 1) * C], in_=dtl[:, c8 * C:(c8 + 1) * C],
                    compare_op=ALU.is_ge, fill=0.0, base=0,
                    pattern=[[1, C]], channel_multiplier=-1),
                    reads=[(dtl.name, c8)], writes=[(dtl.name, c8)])
                def dsm(e):
                    e.matmul(pr.pA[:, 0:C], growb[0:1, 0:C],
                             pr.onesf[0:1, 0:128], start=True, stop=False)
                    return e.matmul(pr.pA[:, 0:C], pr.onesf[0:1, 0:128],
                                    growb[0:1, C:2 * C], start=False, stop=True)
                kb.group("PE", dsm, reads=[growb, pr.onesf], writes=[(pr.pA.name, 0)])
                kb.group("DVE", lambda e: e.tensor_scalar(
                    pr.pA[:, 0:C], pr.pA[:, 0:C], 0.0, None, ALU.min),
                    reads=[(pr.pA.name, 0)], writes=[(pr.pA.name, 0)])
                kb.group("ACT", lambda e, c8=c8: e.activation(
                    dst[:, c8 * C:(c8 + 1) * C], pr.pA[:, 0:C], AF.Exp,
                    bias=zero, scale=1.0),
                    reads=[(pr.pA.name, 0), "constpool"], writes=[(dst.name, c8)])
                kb.group("POOL", lambda e, c8=c8: e.affine_select(
                    out=dst[:, c8 * C:(c8 + 1) * C], in_=dst[:, c8 * C:(c8 + 1) * C],
                    compare_op=ALU.is_ge, fill=0.0, base=-1,
                    pattern=[[-1, C]], channel_multiplier=1),
                    reads=[(dst.name, c8)], writes=[(dst.name, c8)])
                def qkm(e, c8=c8):
                    last = None
                    for cc in range(2):
                        last = e.matmul(pr.pA[:, 0:C],
                                        kcf[:, cc * TG + c8 * 128:cc * TG + (c8 + 1) * 128],
                                        qcf[:, cc * TG + c8 * 128:cc * TG + (c8 + 1) * 128],
                                        start=(cc == 0), stop=(cc == 1))
                    return last
                kb.group("PE", qkm, reads=[kcf.name, qcf.name], writes=[(pr.pA.name, 0)])
                kb.group("DVE", lambda e, c8=c8: e.tensor_tensor(
                    qkd[:, c8 * C:(c8 + 1) * C], pr.pA[:, 0:C],
                    dtl[:, c8 * C:(c8 + 1) * C], ALU.mult),
                    reads=[(pr.pA.name, 0), (dtl.name, c8)], writes=[(qkd.name, c8)])
                def kkm(e, c8=c8):
                    last = None
                    for cc in range(2):
                        last = e.matmul(pr.pA[:, 0:C],
                                        kcf[:, cc * TG + c8 * 128:cc * TG + (c8 + 1) * 128],
                                        kcf[:, cc * TG + c8 * 128:cc * TG + (c8 + 1) * 128],
                                        start=(cc == 0), stop=(cc == 1))
                    return last
                kb.group("PE", kkm, reads=[kcf.name], writes=[(pr.pA.name, 0)])
                bcol = beta_all[:, h * NCH + t: h * NCH + t + 1]
                kb.group("DVE", lambda e, c8=c8, bcol=bcol: e.scalar_tensor_tensor(
                    Wb2[:, c8 * C:(c8 + 1) * C], pr.pA[:, 0:C], bcol,
                    dst[:, c8 * C:(c8 + 1) * C], ALU.mult, ALU.mult),
                    reads=[(pr.pA.name, 0), beta_all, (dst.name, c8)],
                    writes=[(Wb2.name, c8)])
                kb.group("DVE", lambda e, c8=c8: e.tensor_tensor(
                    Atb[:, c8 * C:(c8 + 1) * C], Wb2[:, c8 * C:(c8 + 1) * C],
                    pr.ident[:, :], ALU.add),
                    reads=[(Wb2.name, c8), pr.ident], writes=["preA", (Atb.name, c8)])
                kb.group("DVE", lambda e, c8=c8: e.tensor_tensor(
                    Xb[:, c8 * C:(c8 + 1) * C], pr.ident[:, :],
                    Wb2[:, c8 * C:(c8 + 1) * C], ALU.subtract),
                    reads=[(Wb2.name, c8), pr.ident], writes=[(Xb.name, c8)])
                kb.group("PE", lambda e, c8=c8: e.transpose(
                    pr.pb0[:, 0:C], Atb[:, c8 * C:(c8 + 1) * C], pr.ident[:, :]),
                    reads=[(Atb.name, c8), pr.ident], writes=[pr.pb0])
                kb.group("DVE", lambda e, c8=c8: e.tensor_copy(
                    Atb[:, c8 * C:(c8 + 1) * C], pr.pb0[:, 0:C]),
                    reads=[pr.pb0], writes=[(Atb.name, c8)])
            for it in range(6):
                for c8 in range(8):
                    kb.group("PE", lambda e, c8=c8: e.matmul(
                        pr.pA[:, 0:C], Atb[:, c8 * C:(c8 + 1) * C],
                        Xb[:, c8 * C:(c8 + 1) * C], start=True, stop=True),
                        reads=[(Atb.name, c8), (Xb.name, c8)], writes=[(pr.pA.name, 0)])
                    kb.group("DVE", lambda e, c8=c8: e.scalar_tensor_tensor(
                        Wb2[:, c8 * C:(c8 + 1) * C], pr.pA[:, 0:C], -1.0,
                        ident2[:, :], ALU.mult, ALU.add),
                        reads=[(pr.pA.name, 0), ident2], writes=[(Wb2.name, c8)])
                    kb.group("PE", lambda e, c8=c8: e.transpose(
                        pr.pb0[:, 0:C], Xb[:, c8 * C:(c8 + 1) * C], pr.ident[:, :]),
                        reads=[(Xb.name, c8), pr.ident], writes=[pr.pb0])
                    kb.group("DVE", lambda e: e.tensor_copy(xtb[:, :], pr.pb0[:, 0:C]),
                             reads=[pr.pb0], writes=[xtb])
                    kb.group("PE", lambda e, c8=c8: e.matmul(
                        pr.pB[:, 0:C], xtb[:, :], Wb2[:, c8 * C:(c8 + 1) * C],
                        start=True, stop=True),
                        reads=[xtb, (Wb2.name, c8)], writes=[(pr.pB.name, 0)])
                    kb.group("DVE", lambda e, c8=c8: e.tensor_copy(
                        Xb[:, c8 * C:(c8 + 1) * C], pr.pB[:, 0:C]),
                        reads=[(pr.pB.name, 0)], writes=[(Xb.name, c8)])
            for c8 in range(8):
                kb.group("PE", lambda e, c8=c8: e.transpose(
                    pr.pb0[:, 0:C], Xb[:, c8 * C:(c8 + 1) * C], pr.ident[:, :]),
                    reads=[(Xb.name, c8), pr.ident], writes=[pr.pb0])
                kb.group("DVE", lambda e, c8=c8: e.tensor_copy(
                    Ttb[:, c8 * C:(c8 + 1) * C], pr.pb0[:, 0:C]),
                    reads=[pr.pb0], writes=[(Ttb.name, c8)])
            for c8 in range(8):
                t = t0 + c8
                egtc = egt_all[:, h * NCH + t: h * NCH + t + 1]
                nbegc = nbeg_all[:, h * NCH + t: h * NCH + t + 1]
                bc = beta_all[:, h * NCH + t: h * NCH + t + 1]
                egcc = egc_b[:, t:t + 1]
                def ks0(e, c8=c8):
                    last = None
                    for cc in range(2):
                        last = e.matmul(pr.pA[:, 0:512],
                                        kcf[:, cc * TG + c8 * 128:cc * TG + (c8 + 1) * 128],
                                        Sbf[:, cc * HV:(cc + 1) * HV],
                                        start=(cc == 0), stop=(cc == 1))
                    return last
                kb.group("PE", ks0, reads=[kcf.name, Sbf], writes=[(pr.pA.name, 0)])
                kb.group("DVE", lambda e, c8=c8, bc=bc: e.tensor_scalar(
                    sgb[:, :], vtmg[:, c8 * HV:(c8 + 1) * HV], bc, None, ALU.mult),
                    reads=[vtmg, beta_all], writes=[sgb])
                kb.group("DVE", lambda e, nbegc=nbegc: e.scalar_tensor_tensor(
                    rhsb[:, :], pr.pA[:, 0:512], nbegc, sgb[:, :],
                    ALU.mult, ALU.add),
                    reads=[(pr.pA.name, 0), nbeg_all, sgb], writes=[rhsb])
                kb.group("PE", lambda e, c8=c8: e.matmul(
                    pr.pB[:, 0:512], Ttb[:, c8 * C:(c8 + 1) * C], rhsb[:, :],
                    start=True, stop=True),
                    reads=[(Ttb.name, c8), rhsb], writes=[(pr.pB.name, 0)])
                kb.group("DVE", lambda e: e.tensor_copy(ubf[:, :], pr.pB[:, 0:512]),
                         reads=[(pr.pB.name, 0)], writes=[ubf])
                def qs0(e, c8=c8):
                    last = None
                    for cc in range(2):
                        last = e.matmul(pr.pA[:, 0:512],
                                        qcf[:, cc * TG + c8 * 128:cc * TG + (c8 + 1) * 128],
                                        Sbf[:, cc * HV:(cc + 1) * HV],
                                        start=(cc == 0), stop=(cc == 1))
                    return last
                kb.group("PE", qs0, reads=[qcf.name, Sbf], writes=[(pr.pA.name, 0)])
                kb.group("PE", lambda e, c8=c8: e.matmul(
                    pr.pB[:, 0:512], qkd[:, c8 * C:(c8 + 1) * C], ubf[:, :],
                    start=True, stop=True),
                    reads=[(qkd.name, c8), ubf], writes=[(pr.pB.name, 0)])
                kb.group("DVE", lambda e: e.tensor_copy(o2b[:, :], pr.pB[:, 0:512]),
                         reads=[(pr.pB.name, 0)], writes=[o2b])
                kb.group("DVE", lambda e, egtc=egtc: e.scalar_tensor_tensor(
                    obuf[:, :], pr.pA[:, 0:512], egtc, o2b[:, :], ALU.mult, ALU.add),
                    reads=[(pr.pA.name, 0), egt_all, o2b], writes=[obuf])
                kb.group("DVE", lambda e, h=h, t=t: e.tensor_tensor(
                    scc[:, 2:3], gc_b[:, t:t + 1],
                    G_all[:, h * NCH + t:h * NCH + t + 1], ALU.subtract),
                    reads=[G_all, gc_b, scc], writes=[scc])
                kb.group("ACT", lambda e: e.activation(
                    scc[:, 2:3], scc[:, 2:3], AF.Exp, bias=zero, scale=1.0),
                    reads=[scc, "constpool"], writes=[scc])
                kb.group("DVE", lambda e, c8=c8: e.tensor_scalar(
                    kdec[:, :], ktm[:, c8 * HK:(c8 + 1) * HK],
                    scc[:, 2:3], None, ALU.mult),
                    reads=[ktm, scc], writes=[kdec])
                for cc in range(2):
                    kb.group("PE", lambda e, cc=cc: e.matmul(
                        pr.pA[:, 0:512], kdec[:, cc * 128:(cc + 1) * 128],
                        ubf[:, :], start=True, stop=True),
                        reads=[kdec, ubf], writes=[(pr.pA.name, 0)])
                    kb.group("DVE", lambda e, cc=cc, egcc=egcc: e.scalar_tensor_tensor(
                        S32[:, cc * HV:(cc + 1) * HV], S32[:, cc * HV:(cc + 1) * HV],
                        egcc, pr.pA[:, 0:512], ALU.mult, ALU.add),
                        reads=[S32, egc_b, (pr.pA.name, 0)], writes=[S32])
                kb.group("DVE", lambda e: e.tensor_copy(Sbf[:, :], S32[:, :]),
                         reads=[S32], writes=[Sbf])
                def gpm(e, t=t):
                    last = None
                    for c in range(8):
                        last = e.matmul(pr.pB[:, 0:512],
                                        hT[:, c * T + t * 128: c * T + (t + 1) * 128],
                                        wgh[:, c * HV:(c + 1) * HV],
                                        start=(c == 0), stop=(c == 7))
                    return last
                kb.group("PE", gpm, reads=[wgh.name] + hT_keys, writes=[(pr.pB.name, 0)])
                kb.group("ACT", lambda e: e.activation(sgb[:, :], pr.pB[:, 0:512],
                                                       AF.Silu, bias=zero, scale=1.0),
                         reads=[(pr.pB.name, 0), "constpool"], writes=[sgb])
                kb.group("DVE", lambda e: e.scalar_tensor_tensor(
                    o2b[:, :], obuf[:, :], 1.0, obuf[:, :], ALU.mult, ALU.mult,
                    accum_out=scc[:, 0:1]),
                    reads=[obuf], writes=[o2b, scc])
                kb.group("ACT", lambda e: e.activation(scc[:, 1:2], scc[:, 0:1],
                                                       AF.Ln, bias=eps1,
                                                       scale=1.0 / HV),
                         reads=[scc, "constpool"], writes=[scc])
                kb.group("ACT", lambda e: e.activation(scc[:, 1:2], scc[:, 1:2],
                                                       AF.Exp, bias=zero, scale=-0.5),
                         reads=[scc, "constpool"], writes=[scc])
                kb.group("DVE", lambda e: e.scalar_tensor_tensor(
                    o2b[:, :], obuf[:, :], scc[:, 1:2], sgb[:, :],
                    ALU.mult, ALU.mult), reads=[obuf, scc, sgb], writes=[o2b])
                kb.group("DVE", lambda e, c8=c8: e.tensor_tensor(
                    ogb[:, (c8 % 2) * HV:(c8 % 2 + 1) * HV], o2b[:, :],
                    onb[:, :], ALU.mult),
                    reads=[o2b, onb], writes=[(ogb.name, c8 % 2)])
                if not fm_out:
                    kb.group("SP", lambda e, g=g, h=h, c8=c8, t=t: e.dma_start(
                        out=og[t * 128:(t + 1) * 128, h * HV:(h + 1) * HV],
                        in_=ogb[:, (c8 % 2) * HV:(c8 % 2 + 1) * HV]),
                        reads=[(ogb.name, c8 % 2)], writes=["ogout"],
                        dma=kb.dma_queue(f"SPQ.og{c8 % 2}"))
                else:
                    def ogtr(e, c8=c8):
                        last = None
                        for cc in range(4):
                            last = e.transpose(
                                pr.pb0[:, cc * 128:(cc + 1) * 128],
                                ogb[:, (c8 % 2) * HV + cc * 128:
                                    (c8 % 2) * HV + (cc + 1) * 128],
                                pr.ident[:, :])
                        return last
                    kb.group("PE", ogtr, reads=[(ogb.name, c8 % 2), pr.ident],
                             writes=[pr.pb0])
                    kb.group("DVE", lambda e: e.tensor_copy(
                        ogfmb[:, :], pr.pb0[:, 0:512]),
                        reads=[pr.pb0], writes=["ogfmb"])
                    def ogdma(e, h=h, t=t):
                        ogt = og.tensor if hasattr(og, "tensor") else og
                        dap = bass.AP(ogt, (h * HV) * T + t * 128,
                                      [[T, 128], [128 * T, 4], [1, 128]])
                        return e.dma_start(
                            out=dap, in_=ogfmb[:, :].rearrange(
                                "p (c n) -> p c n", c=4))
                    kb.group("SP", ogdma, reads=["ogfmb"], writes=["ogout"],
                             dma=kb.dma_queue(f"SPQ.og{c8 % 2}"))
    return




def build_L1a(T=4096):
    D, HK, HV, NHL = 1024, 256, 512, 4
    nc = bass.Bass("TRN2", target_bir_lowering=False)
    pr = Prog(nc)
    kb = pr.kb
    io = {}
    io["xb"] = kb.dram_in("xb", [T, D], F32)
    io["normwt"] = kb.dram_in("normwt", [128, D], F32)
    io["qwT"] = kb.dram_in("qwT", [D, NHL * HK], F32)
    io["kwT"] = kb.dram_in("kwT", [D, NHL * HK], F32)
    io["vwT"] = kb.dram_in("vwT", [D, NHL * HV], F32)
    io["gwT"] = kb.dram_in("gwT", [D, NHL * HV], F32)
    io["bwT"] = kb.dram_in("bwT", [D, NHL], F32)
    io["awT"] = kb.dram_in("awT", [D, NHL], F32)
    io["qconvt"] = kb.dram_in("qconvt", [128, NHL * 2 * 4], F32)
    io["kconvt"] = kb.dram_in("kconvt", [128, NHL * 2 * 4], F32)
    io["vconvt"] = kb.dram_in("vconvt", [128, NHL * 4 * 4], F32)
    io["alog"] = kb.dram_in("alog", [1, NHL], F32)
    io["dtb"] = kb.dram_in("dtb", [1, NHL], F32)
    io["onormt"] = kb.dram_in("onormt", [128, HV], F32)
    io["og"] = kb.dram_out("og", [T, NHL * HV], mybir.dt.bfloat16)
    emit_L1a(pr, io, T=T)
    kb.barrier_on("SP")
    kb.finalize()
    return nc

def emit_L1a_proj(pr, io, T=2048, DV=4096):
    nc = pr.nc
    kb = pr.kb
    D = 1024
    NT = T // 128
    NPC = DV // 1024
    x = io["x"]
    ogf = io["ogf"]
    owT = io["owT"]
    xo = io["xo"]
    cfm = kb.sbuf("cfm", [128, 8 * T], BF16)
    wob = kb.sbuf("wob", [128, 8 * D], BF16)
    acc = kb.sbuf_at("acc", [128, NT * D * 4 // 4], F32, nc.sbuf_top - 4 * NT * D)
    xrb = [kb.sbuf(f"xrb{i}", [128, D], F32) for i in range(2)]
    obuf = [kb.sbuf(f"obuf{i}", [128, D], F32) for i in range(2)]
    ybuf = kb.sbuf("ybuf", [128, D], F32)
    mvball = kb.sbuf("mvball", [128, 2 * NT], F32)
    r1all = kb.sbuf("r1all", [128, NT], F32)
    bns = kb.sbuf("bns", [128, 12], F32)
    mv2 = kb.sbuf("mv2", [128, 2], F32)
    eps1 = kb.constv(1e-5)
    zero = kb.constv(0.0)
    for pc in range(NPC):
        kb.group("SP", lambda e, pc=pc: e.dma_start(
            out=cfm[:, :].rearrange("p (c n) -> p c n", c=8),
            in_=ogf[pc * 1024:(pc + 1) * 1024, :].rearrange("(c p) n -> p c n", p=128)),
            writes=[cfm], dma="SPQ")
        kb.group("POOL", lambda e, pc=pc: e.dma_start(
            out=wob[:, :].rearrange("p (c n) -> p c n", c=8),
            in_=owT[pc * 1024:(pc + 1) * 1024, :].rearrange("(c p) n -> p c n", p=128)),
            writes=[wob], dma="POOLQ")
        for t in range(NT):
            for half in range(2):
                ps = pr.pA[:, half * 512:half * 512 + 512]
                pskey = (pr.pA.name, half)
                mm_group(kb, ps,
                         [(cfm[:, c * T + t * 128: c * T + (t + 1) * 128],
                           wob[:, c * D + half * 512: c * D + (half + 1) * 512])
                          for c in range(8)],
                         reads=[cfm, wob], writes=[pskey])
                aslice = acc[:, t * D + half * 512: t * D + (half + 1) * 512]
                if pc == 0:
                    kb.group("DVE", lambda e, ps=ps, aslice=aslice: e.tensor_copy(
                        aslice, ps), reads=[pskey], writes=[(acc.name, t, half)])
                else:
                    kb.group("DVE", lambda e, ps=ps, aslice=aslice: e.tensor_tensor(
                        aslice, aslice, ps, ALU.add),
                        reads=[pskey, (acc.name, t, half)],
                        writes=[(acc.name, t, half)])
    for t in range(NT):
        xt = xrb[t % 2]
        ob = obuf[t % 2]
        kb.group("SP", lambda e, t=t, xt=xt: e.dma_start(
            out=xt[:, :], in_=x[t * 128:(t + 1) * 128, :]),
            writes=[xt], dma="SPQ")
        kb.group("DVE", lambda e, xt=xt: e.bn_stats(bns[:, 0:6], xt[:, 0:512]),
                 reads=[xt], writes=[bns])
        kb.group("DVE", lambda e, xt=xt: e.bn_stats(bns[:, 6:12], xt[:, 512:1024]),
                 reads=[xt, bns], writes=[bns])
        kb.group("DVE", lambda e: e.bn_aggr(mv2[:, :], bns[:, 0:12]),
                 reads=[bns], writes=[mv2])
        kb.group("ACT", lambda e: e.activation(mv2[:, 1:2], mv2[:, 1:2], AF.Ln,
                                               bias=eps1, scale=1.0),
                 reads=[mv2, "constpool"], writes=[mv2])
        kb.group("ACT", lambda e: e.activation(mv2[:, 1:2], mv2[:, 1:2], AF.Exp,
                                               bias=zero, scale=-0.5),
                 reads=[mv2, "constpool"], writes=[mv2])
        kb.group("DVE", lambda e, t=t, xt=xt: e.tensor_scalar(
            ybuf[:, :], xt[:, :], mv2[:, 0:1], mv2[:, 1:2], ALU.subtract, ALU.mult),
            reads=[xt, mv2], writes=[ybuf])
        kb.group("DVE", lambda e, t=t, ob=ob: e.tensor_tensor(
            ob[:, :], ybuf[:, :], acc[:, t * D:(t + 1) * D], ALU.add),
            reads=[ybuf, (acc.name, t, 0), (acc.name, t, 1)], writes=[ob])
        for c in range(2):
            kb.group("DVE", lambda e, c=c, ob=ob: e.bn_stats(
                bns[:, c * 6:(c + 1) * 6], ob[:, c * 512:(c + 1) * 512]),
                reads=[ob, bns], writes=[bns])
        kb.group("DVE", lambda e: e.bn_aggr(mv2[:, :], bns[:, 0:12]),
                 reads=[bns], writes=[mv2])
        kb.group("ACT", lambda e: e.activation(mv2[:, 1:2], mv2[:, 1:2], AF.Ln,
                                               bias=eps1, scale=1.0),
                 reads=[mv2, "constpool"], writes=[mv2])
        kb.group("ACT", lambda e: e.activation(mv2[:, 1:2], mv2[:, 1:2], AF.Exp,
                                               bias=zero, scale=-0.5),
                 reads=[mv2, "constpool"], writes=[mv2])
        kb.group("DVE", lambda e, ob=ob: e.tensor_scalar(
            ob[:, :], ob[:, :], mv2[:, 0:1], mv2[:, 1:2], ALU.subtract, ALU.mult),
            reads=[ob, mv2], writes=[ob])
        kb.group("DVE", lambda e, ob=ob, xt=xt: e.tensor_tensor(
            ob[:, :], ob[:, :], xt[:, :], ALU.add), reads=[ob, xt], writes=[ob])
        kb.group("SP", lambda e, t=t, ob=ob: e.dma_start(
            out=xo[t * 128:(t + 1) * 128, :], in_=ob[:, :]),
            reads=[ob], writes=[("xo", t)], dma="SPQ")
    return




def build_L1a_proj(T=2048, DV=4096):
    D = 1024
    nc = bass.Bass("TRN2", target_bir_lowering=False)
    pr = Prog(nc)
    kb = pr.kb
    io = {}
    io["x"] = kb.dram_in("x", [T, D], F32)
    io["ogf"] = kb.dram_in("ogf", [DV, T], mybir.dt.bfloat16)
    io["owT"] = kb.dram_in("owT", [DV, D], F32)
    io["xo"] = kb.dram_out("xo", [T, D], F32)
    emit_L1a_proj(pr, io, T=T, DV=DV)
    kb.barrier_on("SP")
    kb.finalize()
    return nc

def emit_sandwich_epilogue_tile(pr, t, ob, xt, bns, mv2, eps1, zero, xo,
                                extra_ln=None):
    kb = pr.kb
    if extra_ln is not None:
        lng, lnb = extra_ln
        for c in range(2):
            kb.group("DVE", lambda e, c=c, ob=ob: e.bn_stats(
                bns[:, c * 6:(c + 1) * 6], ob[:, c * 512:(c + 1) * 512]),
                reads=[ob, bns], writes=[bns])
        kb.group("DVE", lambda e: e.bn_aggr(mv2[:, :], bns[:, 0:12]),
                 reads=[bns], writes=[mv2])
        kb.group("ACT", lambda e: e.activation(mv2[:, 1:2], mv2[:, 1:2], AF.Ln,
                                               bias=eps1, scale=1.0),
                 reads=[mv2, "constpool"], writes=[mv2])
        kb.group("ACT", lambda e: e.activation(mv2[:, 1:2], mv2[:, 1:2], AF.Exp,
                                               bias=zero, scale=-0.5),
                 reads=[mv2, "constpool"], writes=[mv2])
        kb.group("DVE", lambda e, ob=ob: e.tensor_scalar(
            ob[:, :], ob[:, :], mv2[:, 0:1], mv2[:, 1:2], ALU.subtract, ALU.mult),
            reads=[ob, mv2], writes=[ob])
        kb.group("DVE", lambda e, ob=ob, lng=lng: e.tensor_tensor(
            ob[:, :], ob[:, :], lng[:, :], ALU.mult), reads=[ob, lng], writes=[ob])
        kb.group("DVE", lambda e, ob=ob, lnb=lnb: e.tensor_tensor(
            ob[:, :], ob[:, :], lnb[:, :], ALU.add), reads=[ob, lnb], writes=[ob])
    for c in range(2):
        kb.group("DVE", lambda e, c=c, ob=ob: e.bn_stats(
            bns[:, c * 6:(c + 1) * 6], ob[:, c * 512:(c + 1) * 512]),
            reads=[ob, bns], writes=[bns])
    kb.group("DVE", lambda e: e.bn_aggr(mv2[:, :], bns[:, 0:12]),
             reads=[bns], writes=[mv2])
    kb.group("ACT", lambda e: e.activation(mv2[:, 1:2], mv2[:, 1:2], AF.Ln,
                                           bias=eps1, scale=1.0),
             reads=[mv2, "constpool"], writes=[mv2])
    kb.group("ACT", lambda e: e.activation(mv2[:, 1:2], mv2[:, 1:2], AF.Exp,
                                           bias=zero, scale=-0.5),
             reads=[mv2, "constpool"], writes=[mv2])
    kb.group("DVE", lambda e, ob=ob: e.tensor_scalar(
        ob[:, :], ob[:, :], mv2[:, 0:1], mv2[:, 1:2], ALU.subtract, ALU.mult),
        reads=[ob, mv2], writes=[ob])
    kb.group("DVE", lambda e, ob=ob, xt=xt: e.tensor_tensor(
        ob[:, :], ob[:, :], xt[:, :], ALU.add), reads=[ob, xt], writes=[ob])
    kb.group("SP", lambda e, t=t, ob=ob: e.dma_start(
        out=xo[t * 128:(t + 1) * 128, :], in_=ob[:, :]),
        reads=[ob], writes=[("xo", t)], dma="SPQ")



def emit_L1f(pr, io, T=2048):
    E, H = 8, 2048
    D = 1024
    NT = T // 128
    nc = pr.nc
    kb = pr.kb
    x = io["x"]
    gateT = io["gateT"]
    w1 = io["w1"]
    b1r = io["b1r"]
    ln1gt = io["ln1gt"]
    ln1bt = io["ln1bt"]
    w2 = io["w2"]
    b2r = io["b2r"]
    lngt = io["lngt"]
    lnbt = io["lnbt"]
    xo = io["xo"]

    accd = kb.dram_tmp("accd", [T, D], F32)
    gbscr = kb.sbuf_at("gbscr", [128, H], F32, nc.sbuf_top - 4 * H)
    hnT = kb.sbuf_at("hnT", [128, 16 * 128], BF16,
                     nc.sbuf_top - 4 * H - 2 * 16 * 128)
    stg = kb.sbuf("stg", [128, D], F32)
    xT = kb.sbuf("xT", [128, 8 * T], BF16)
    w1b2 = [kb.sbuf(f"w1b{i}", [128, 8 * H], BF16) for i in range(2)]
    w2b = kb.sbuf("w2b", [128, 16 * D], BF16)
    gtb = kb.sbuf("gtb", [128, 8 * E], BF16)
    b1b = kb.sbuf("b1b", [1, H], BF16)
    b2b = kb.sbuf("b2b", [1, D], BF16)
    gbc = kb.sbuf("gbc", [128, H], BF16)
    bbc = kb.sbuf("bbc", [128, H], BF16)
    lngb = kb.sbuf("lngb", [128, D], BF16)
    lnbb = kb.sbuf("lnbb", [128, D], BF16)
    cmb = kb.sbuf("cmb", [128, NT * E], F32)
    hnb2 = [kb.sbuf(f"hnb{i}", [128, H], BF16) for i in range(2)]
    logit = kb.sbuf("logit", [128, E], F32)
    rtmp = kb.sbuf("rtmp", [128, E], F32)
    lmax = kb.sbuf("lmax", [128, 4], F32)
    xrb = [kb.sbuf("xrb0", [128, D], F32)] * 2
    xnb = [kb.sbuf("xnb0", [128, D], BF16)] * 2
    ybuf = kb.sbuf("ybuf", [128, D], F32)
    obuf = [kb.sbuf("obuf0", [128, D], F32)] * 2
    mvball = kb.sbuf("mvball", [128, 2 * NT], F32)
    r1all = kb.sbuf("r1all", [128, NT], F32)
    mvh = kb.sbuf("mvh", [128, 2], F32)
    bns = kb.sbuf("bns", [128, 24], F32)
    mv2 = kb.sbuf("mv2", [128, 2], F32)
    eps1 = kb.constv(1e-5)
    zero = kb.constv(0.0)

    load_weight_chunked(kb, gtb, gateT, D, E, "POOLQ")
    kb.group("POOL", lambda e: e.dma_start(out=lngb[:, :], in_=lngt[:, :]),
             writes=[lngb], dma="POOLQ")
    kb.group("POOL", lambda e: e.dma_start(out=lnbb[:, :], in_=lnbt[:, :]),
             writes=[lnbb], dma="POOLQ")

    def prep_tile(t):
        xt = xrb[t % 2]
        kb.group("SP", lambda e, t=t, xt=xt: e.dma_start(
            out=xt[:, :], in_=x[t * 128:(t + 1) * 128, :]),
            writes=[xt], dma="SPQ")
        kb.group("DVE", lambda e, xt=xt: e.bn_stats(bns[:, 0:6], xt[:, 0:512]),
                 reads=[xt], writes=[bns])
        kb.group("DVE", lambda e, xt=xt: e.bn_stats(bns[:, 6:12], xt[:, 512:1024]),
                 reads=[xt, bns], writes=[bns])
        kb.group("DVE", lambda e, t=t: e.bn_aggr(mvball[:, 2 * t:2 * t + 2],
                                                 bns[:, 0:12]),
                 reads=[bns], writes=[mvball])
        kb.group("ACT", lambda e, t=t: e.activation(
            r1all[:, t:t + 1], mvball[:, 2 * t + 1:2 * t + 2], AF.Ln,
            bias=eps1, scale=1.0), reads=[mvball, "constpool"], writes=[r1all])
        kb.group("ACT", lambda e, t=t: e.activation(
            r1all[:, t:t + 1], r1all[:, t:t + 1], AF.Exp, bias=zero, scale=-0.5),
            reads=[r1all, "constpool"], writes=[r1all])
        xn = xnb[t % 2]
        kb.group("DVE", lambda e, t=t, xt=xt, xn=xn: e.tensor_scalar(
            xn[:, :], xt[:, :], mvball[:, 2 * t:2 * t + 1], r1all[:, t:t + 1],
            ALU.subtract, ALU.mult), reads=[xt, mvball, r1all], writes=[xn])
        return xn[:, :]

    emit_tm_to_fm(pr, prep_tile, NT, D, xT, T, 0, "tf")
    xT_keys = [(xT.name, "fm", t) for t in range(NT)]

    for t in range(NT):
        pc = pr.pC if t % 2 == 0 else pr.pD
        mm_group(kb, pc[:, 0:E],
                 [(xT[:, c * T + t * 128: c * T + (t + 1) * 128],
                   gtb[:, c * E:(c + 1) * E]) for c in range(8)],
                 reads=[gtb, (xT.name, "fm", t)], writes=[pc])
        kb.group("DVE", lambda e, pc=pc: e.tensor_copy(logit[:, :], pc[:, 0:E]),
                 reads=[pc], writes=[logit])
        kb.group("DVE", lambda e: e.reduce_max(lmax[:, 0:1], logit[:, :],
                                               axis=mybir.AxisListType.X),
                 reads=[logit], writes=[lmax])
        kb.group("DVE", lambda e: e.tensor_scalar(rtmp[:, :], logit[:, :],
                                                  lmax[:, 0:1], None, ALU.is_ge),
                 reads=[logit, lmax], writes=[rtmp])
        kb.group("DVE", lambda e: e.scalar_tensor_tensor(
            rtmp[:, :], rtmp[:, :], -1e30, logit[:, :], ALU.mult, ALU.add),
            reads=[rtmp, logit], writes=[rtmp])
        kb.group("DVE", lambda e: e.reduce_max(lmax[:, 1:2], rtmp[:, :],
                                               axis=mybir.AxisListType.X),
                 reads=[rtmp], writes=[lmax])
        kb.group("DVE", lambda e: e.tensor_scalar(rtmp[:, :], logit[:, :],
                                                  lmax[:, 1:2], None, ALU.is_ge),
                 reads=[logit, lmax], writes=[rtmp])
        kb.group("DVE", lambda e: e.tensor_scalar(logit[:, :], logit[:, :],
                                                  lmax[:, 0:1], None, ALU.subtract),
                 reads=[logit, lmax], writes=[logit])
        kb.group("ACT", lambda e: e.activation(logit[:, :], logit[:, :], AF.Exp,
                                               bias=zero, scale=1.0),
                 reads=[logit, "constpool"], writes=[logit])
        kb.group("DVE", lambda e: e.tensor_tensor(logit[:, :], logit[:, :],
                                                  rtmp[:, :], ALU.mult),
                 reads=[logit, rtmp], writes=[logit])
        kb.group("DVE", lambda e: e.tensor_reduce(lmax[:, 2:3], logit[:, :],
                                                  axis=mybir.AxisListType.X,
                                                  op=ALU.add),
                 reads=[logit, lmax], writes=[lmax])
        kb.group("ACT", lambda e: e.activation(lmax[:, 3:4], lmax[:, 2:3], AF.Ln,
                                               bias=zero, scale=1.0),
                 reads=[lmax, "constpool"], writes=[lmax])
        kb.group("ACT", lambda e: e.activation(lmax[:, 3:4], lmax[:, 3:4], AF.Exp,
                                               bias=zero, scale=-1.0),
                 reads=[lmax, "constpool"], writes=[lmax])
        kb.group("DVE", lambda e, t=t: e.tensor_scalar(
            cmb[:, t * E:(t + 1) * E], logit[:, :], lmax[:, 3:4], None, ALU.mult),
            reads=[logit, lmax], writes=[(cmb.name, t)])

    pkeys = [(pr.pA.name, 0), (pr.pA.name, 1), (pr.pB.name, 0), (pr.pB.name, 1)]

    def emit_h(t, w1b):
        for hh in range(4):
            ps = (pr.pA if hh < 2 else pr.pB)[:, (hh % 2) * 512:(hh % 2) * 512 + 512]
            pk = (pr.pA.name if hh < 2 else pr.pB.name, hh % 2)
            mm_group(kb, ps,
                     [(xT[:, c * T + t * 128: c * T + (t + 1) * 128],
                       w1b[:, c * H + hh * 512: c * H + (hh + 1) * 512])
                      for c in range(8)],
                     reads=[w1b, b1b, pr.ones_row, (xT.name, "fm", t)],
                     writes=[pk],
                     bias=(pr.ones_row[0:1, 0:128],
                           b1b[0:1, hh * 512:(hh + 1) * 512]))

    def emit_chain(t):
        hnb = hnb2[t % 2]
        for hh in range(4):
            psrc = (pr.pA if hh < 2 else pr.pB)[:, (hh % 2) * 512:(hh % 2) * 512 + 512]
            kb.group("DVE", lambda e, psrc=psrc, hh=hh: e.bn_stats(
                bns[:, hh * 6:(hh + 1) * 6], psrc),
                reads=pkeys + [bns], writes=[bns])
        kb.group("DVE", lambda e: e.bn_aggr(mvh[:, :], bns[:, :]),
                 reads=[bns], writes=[mvh])
        for half in range(2):
            psrc = (pr.pA if half == 0 else pr.pB)[:, 0:1024]
            kb.group("DVE", lambda e, psrc=psrc, half=half: e.tensor_copy(
                gbscr[:, half * 1024:(half + 1) * 1024], psrc),
                reads=pkeys, writes=[("gbscr", half)])
        kb.group("ACT", lambda e: e.activation(mvh[:, 1:2], mvh[:, 1:2], AF.Ln,
                                               bias=eps1, scale=1.0),
                 reads=[mvh, "constpool"], writes=[mvh])
        kb.group("ACT", lambda e: e.activation(mvh[:, 1:2], mvh[:, 1:2], AF.Exp,
                                               bias=zero, scale=-0.5),
                 reads=[mvh, "constpool"], writes=[mvh])
        for hh in range(4):
            sl = slice(hh * 512, (hh + 1) * 512)
            kb.group("DVE", lambda e, sl=sl: e.scalar_tensor_tensor(
                gbscr[:, sl], gbscr[:, sl],
                mvh[:, 0:1], gbc[:, sl], ALU.subtract, ALU.mult),
                reads=[("gbscr", 0), ("gbscr", 1), mvh, gbc],
                writes=[("gbscr", 0), ("gbscr", 1)])
            kb.group("DVE", lambda e, sl=sl: e.scalar_tensor_tensor(
                gbscr[:, sl], gbscr[:, sl],
                mvh[:, 1:2], bbc[:, sl], ALU.mult, ALU.add),
                reads=[("gbscr", 0), ("gbscr", 1), mvh, bbc],
                writes=[("gbscr", 0), ("gbscr", 1)])
            kb.group("ACT", lambda e, sl=sl, hnb=hnb: e.activation(
                hnb[:, sl], gbscr[:, sl], AF.Gelu,
                bias=zero, scale=1.0),
                reads=[("gbscr", 0), ("gbscr", 1), "constpool"],
                writes=[(hnb.name,)])

    def emit_ty(t, e_i):
        hnb = hnb2[t % 2]
        for g8 in range(2):
            buf = pr.pb0 if g8 == 0 else pr.pb1
            def tr(e, g8=g8, buf=buf, hnb=hnb):
                last = None
                for j in range(8):
                    last = e.transpose(buf[:, j * 128:(j + 1) * 128],
                                       hnb[:, (g8 * 8 + j) * 128:
                                           (g8 * 8 + j + 1) * 128],
                                       pr.ident[:, :])
                return last
            kb.group("PE", tr, reads=[(hnb.name,), pr.ident], writes=[buf])
            kb.group("DVE", lambda e, g8=g8, buf=buf: e.tensor_copy(
                hnT[:, g8 * 1024:(g8 + 1) * 1024], buf[:, 0:1024]),
                reads=[buf], writes=[(hnT.name, g8)])
        for half in range(2):
            ps = pr.pC[:, :] if half == 0 else pr.pD[:, :]
            pk = pr.pC if half == 0 else pr.pD
            mm_group(kb, ps,
                     [(hnT[:, c * 128:(c + 1) * 128],
                       w2b[:, c * D + half * 512: c * D + (half + 1) * 512])
                      for c in range(16)],
                     reads=[(hnT.name, 0), (hnT.name, 1), w2b, b2b, pr.ones_row],
                     writes=[pk],
                     bias=(pr.ones_row[0:1, 0:128],
                           b2b[0:1, half * 512:(half + 1) * 512]))
            sc = cmb[:, t * E + e_i: t * E + e_i + 1]
            kb.group("DVE", lambda e, ps=ps, sc=sc, half=half, t=t:
                     e.tensor_scalar(stg[:, half * 512:(half + 1) * 512],
                                     ps, sc, None, ALU.mult),
                     reads=[pk, (cmb.name, t)], writes=[(stg.name, half)])
        kb.group("POOL", lambda e, t=t, e_i=e_i: e.dma_start(
            out=accd[t * 128:(t + 1) * 128, :], in_=stg[:, :],
            accum_op=mybir.AluOpType.add)
            if e_i > 0 else
            e.dma_start(out=accd[t * 128:(t + 1) * 128, :], in_=stg[:, :]),
            reads=[(stg.name, 0), (stg.name, 1)], writes=[("accd", t)],
            dma=kb.dma_queue(f"POOLQ.acc{t % 2}"))

    def load_w1(e_i):
        w1b = w1b2[e_i % 2]
        kb.group("POOL", lambda e, e_i=e_i, w1b=w1b: e.dma_start(
            out=w1b[:, :].rearrange("p (c n) -> p c n", c=8),
            in_=w1[e_i].rearrange("(c p) n -> p c n", p=128)),
            writes=[w1b], dma=kb.dma_queue(f"POOLQ.w1{e_i % 2}"))

    pend = None
    for e_i in range(E):
        if pend is not None and pend[1] != e_i:
            emit_ty(*pend)
            pend = None
        if e_i == 0:
            load_w1(0)
        if e_i + 1 < E:
            load_w1(e_i + 1)
        w1b = w1b2[e_i % 2]
        kb.group("POOL", lambda e, e_i=e_i: e.dma_start(
            out=w2b[:, :].rearrange("p (c n) -> p c n", c=16),
            in_=w2[e_i].rearrange("(c p) n -> p c n", p=128)),
            writes=[w2b], dma=kb.dma_queue("POOLQ.w2"))
        kb.group("POOL", lambda e, e_i=e_i: e.dma_start(
            out=b1b[:, :], in_=b1r[e_i:e_i + 1, :]),
            writes=[b1b], dma=kb.dma_queue("POOLQ.w3"))
        kb.group("POOL", lambda e, e_i=e_i: e.dma_start(
            out=b2b[:, :], in_=b2r[e_i:e_i + 1, :]),
            writes=[b2b], dma=kb.dma_queue("POOLQ.w3"))
        kb.group("POOL", lambda e, e_i=e_i: e.dma_start(
            out=gbc[:, :], in_=ln1gt[e_i]), writes=[gbc],
            dma=kb.dma_queue("POOLQ.w4"))
        kb.group("POOL", lambda e, e_i=e_i: e.dma_start(
            out=bbc[:, :], in_=ln1bt[e_i]), writes=[bbc],
            dma=kb.dma_queue("POOLQ.w4"))
        for t in range(NT):
            emit_h(t, w1b)
            emit_chain(t)
            if pend is not None:
                emit_ty(*pend)
            pend = (t, e_i)
    emit_ty(*pend)
    for t in range(NT):
        xt = xrb[t % 2]
        ob = obuf[t % 2]
        kb.group("SP", lambda e, t=t, xt=xt: e.dma_start(
            out=xt[:, :], in_=x[t * 128:(t + 1) * 128, :]),
            writes=[xt], dma="SPQ")
        kb.group("SP", lambda e, t=t: e.dma_start(
            out=stg[:, :], in_=accd[t * 128:(t + 1) * 128, :]),
            reads=[("accd", t)], writes=[(stg.name, 0), (stg.name, 1)],
            dma=kb.dma_queue("SPQ.rd"))
        kb.group("DVE", lambda e, t=t, xt=xt: e.tensor_scalar(
            ybuf[:, :], xt[:, :], mvball[:, 2 * t:2 * t + 1], r1all[:, t:t + 1],
            ALU.subtract, ALU.mult), reads=[xt, mvball, r1all], writes=[ybuf])
        kb.group("DVE", lambda e, t=t, ob=ob: e.tensor_tensor(
            ob[:, :], ybuf[:, :], stg[:, :], ALU.add),
            reads=[ybuf, (stg.name, 0), (stg.name, 1)], writes=[ob])
        emit_sandwich_epilogue_tile(pr, t, ob, xt, bns, mv2, eps1, zero, xo,
                                    extra_ln=(lngb, lnbb))
    return





def build_L1f(T=2048):
    D, E, H = 1024, 8, 2048
    nc = bass.Bass("TRN2", target_bir_lowering=False)
    pr = Prog(nc)
    kb = pr.kb
    io = {}
    io["x"] = kb.dram_in("x", [T, D], F32)
    io["gateT"] = kb.dram_in("gateT", [D, E], F32)
    io["w1"] = kb.dram_in("w1", [E, D, H], F32)
    io["b1r"] = kb.dram_in("b1r", [E, H], F32)
    io["ln1gt"] = kb.dram_in("ln1gt", [E, 128, H], F32)
    io["ln1bt"] = kb.dram_in("ln1bt", [E, 128, H], F32)
    io["w2"] = kb.dram_in("w2", [E, H, D], F32)
    io["b2r"] = kb.dram_in("b2r", [E, D], F32)
    io["lngt"] = kb.dram_in("lngt", [128, D], F32)
    io["lnbt"] = kb.dram_in("lnbt", [128, D], F32)
    io["xo"] = kb.dram_out("xo", [T, D], F32)
    emit_L1f(pr, io, T=T)
    kb.barrier_on("SP")
    kb.finalize()
    return nc

def emit_L2f(pr, io, T=2048):
    E, I, G = 32, 512, 8
    D = 1024
    NT = T // 128
    nc = pr.nc
    kb = pr.kb
    x = io["x"]
    gateT = io["gateT"]
    gbias = io["gbias"]
    w_gateT = io["w_gateT"]
    w_upT = io["w_upT"]
    w_downT = io["w_downT"]
    sh_gateT = io["sh_gateT"]
    sh_upT = io["sh_upT"]
    sh_downT = io["sh_downT"]
    xo = io["xo"]

    accd = kb.dram_tmp("accd", [T, D], F32)
    stg = kb.sbuf("stg", [128, D], F32)
    xT = kb.sbuf("xT", [128, 8 * T], BF16)
    wgb2 = [kb.sbuf(f"wgb{i}", [128, 8 * 2 * I], BF16) for i in range(2)]
    wub2 = [kb.sbuf(f"wub{i}", [128, 8 * 2 * I], BF16) for i in range(2)]
    wdb2 = [kb.sbuf(f"wdb{i}", [128, 8 * D], BF16) for i in range(2)]
    gtb = kb.sbuf("gtb", [128, 8 * E], BF16)
    gbb = kb.sbuf("gbb", [128, E], F32)
    prodb2 = [kb.sbuf(f"prodb{i}", [128, 8 * 512], BF16) for i in range(2)]
    gsb2 = [kb.sbuf(f"gsb{i}", [128, 512], BF16) for i in range(2)]
    cmb = kb.sbuf("cmb", [128, NT * E], F32)
    logit = kb.sbuf("logit", [128, E], F32)
    rtmp = kb.sbuf("rtmp", [128, E], F32)
    rtm2 = kb.sbuf("rtm2", [128, E], F32)
    gsum = kb.sbuf("gsum", [128, G], F32)
    gsum2 = kb.sbuf("gsum2", [128, G], F32)
    gtmp = kb.sbuf("gtmp", [128, E], F32)
    mtmp = kb.sbuf("mtmp", [128, E], F32)
    lmax = kb.sbuf("lmax", [128, 4], F32)
    xrb = [kb.sbuf(f"xrb{i}", [128, D], F32) for i in range(2)]
    xnb = [kb.sbuf(f"xnb{i}", [128, D], BF16) for i in range(2)]
    obuf = [kb.sbuf(f"obuf{i}", [128, D], F32) for i in range(2)]
    mvball = kb.sbuf("mvball", [128, 2 * NT], F32)
    r1all = kb.sbuf("r1all", [128, NT], F32)
    bns = kb.sbuf("bns", [128, 12], F32)
    mv2 = kb.sbuf("mv2", [128, 2], F32)
    eps1 = kb.constv(1e-5)
    eps20 = kb.constv(1e-20)
    zero = kb.constv(0.0)

    load_weight_chunked(kb, gtb, gateT, D, E, "POOLQ")
    kb.group("POOL", lambda e: e.dma_start(out=gbb[:, :], in_=gbias[:, :]),
             writes=[gbb], dma="POOLQ")

    def prep_tile(t):
        xt = xrb[t % 2]
        kb.group("SP", lambda e, t=t, xt=xt: e.dma_start(
            out=xt[:, :], in_=x[t * 128:(t + 1) * 128, :]),
            writes=[xt], dma="SPQ")
        kb.group("DVE", lambda e, xt=xt: e.bn_stats(bns[:, 0:6], xt[:, 0:512]),
                 reads=[xt], writes=[bns])
        kb.group("DVE", lambda e, xt=xt: e.bn_stats(bns[:, 6:12], xt[:, 512:1024]),
                 reads=[xt, bns], writes=[bns])
        kb.group("DVE", lambda e, t=t: e.bn_aggr(mvball[:, 2 * t:2 * t + 2],
                                                 bns[:, 0:12]),
                 reads=[bns], writes=[mvball])
        kb.group("ACT", lambda e, t=t: e.activation(
            r1all[:, t:t + 1], mvball[:, 2 * t + 1:2 * t + 2], AF.Ln,
            bias=eps1, scale=1.0), reads=[mvball, "constpool"], writes=[r1all])
        kb.group("ACT", lambda e, t=t: e.activation(
            r1all[:, t:t + 1], r1all[:, t:t + 1], AF.Exp, bias=zero, scale=-0.5),
            reads=[r1all, "constpool"], writes=[r1all])
        xn = xnb[t % 2]
        kb.group("DVE", lambda e, t=t, xt=xt, xn=xn: e.tensor_scalar(
            xn[:, :], xt[:, :], mvball[:, 2 * t:2 * t + 1], r1all[:, t:t + 1],
            ALU.subtract, ALU.mult), reads=[xt, mvball, r1all], writes=[xn])
        return xn[:, :]

    emit_tm_to_fm(pr, prep_tile, NT, D, xT, T, 0, "tf")
    xT_keys = [(xT.name, "fm", t) for t in range(NT)]

    for t in range(NT):
        pc = pr.pC if t % 2 == 0 else pr.pD
        mm_group(kb, pc[:, 0:E],
                 [(xT[:, c * T + t * 128: c * T + (t + 1) * 128],
                   gtb[:, c * E:(c + 1) * E]) for c in range(8)],
                 reads=[gtb, (xT.name, "fm", t)], writes=[pc])
        kb.group("ACT", lambda e, pc=pc: e.activation(logit[:, :], pc[:, 0:E],
                                               AF.Sigmoid, bias=zero, scale=1.0),
                 reads=[pc, "constpool"], writes=[logit])
        kb.group("DVE", lambda e: e.tensor_tensor(rtmp[:, :], logit[:, :],
                                                  gbb[:, :], ALU.add),
                 reads=[logit, gbb], writes=[rtmp])
        kb.group("DVE", lambda e: e.tensor_reduce(
            gsum[:, :], rtmp[:, :].rearrange("p (g k) -> p g k", g=G),
            axis=mybir.AxisListType.X, op=ALU.max),
            reads=[rtmp], writes=[gsum])
        for gg in range(G):
            kb.group("DVE", lambda e, gg=gg: e.tensor_scalar(
                rtm2[:, gg * 4:(gg + 1) * 4], rtmp[:, gg * 4:(gg + 1) * 4],
                gsum[:, gg:gg + 1], None, ALU.is_ge),
                reads=[rtmp, gsum, rtm2], writes=[rtm2])
        kb.group("DVE", lambda e: e.scalar_tensor_tensor(
            rtm2[:, :], rtm2[:, :], -1e30, rtmp[:, :], ALU.mult, ALU.add),
            reads=[rtm2, rtmp], writes=[rtm2])
        kb.group("DVE", lambda e: e.tensor_reduce(
            gsum2[:, :], rtm2[:, :].rearrange("p (g k) -> p g k", g=G),
            axis=mybir.AxisListType.X, op=ALU.max),
            reads=[rtm2], writes=[gsum2])
        kb.group("DVE", lambda e: e.tensor_tensor(gsum[:, :], gsum[:, :],
                                                  gsum2[:, :], ALU.add),
                 reads=[gsum, gsum2], writes=[gsum])
        kb.group("DVE", lambda e: e.tensor_copy(gsum2[:, :], gsum[:, :]),
                 reads=[gsum], writes=[gsum2])
        for _ in range(3):
            kb.group("DVE", lambda e: e.reduce_max(lmax[:, 0:1], gsum2[:, :],
                                                   axis=mybir.AxisListType.X),
                     reads=[gsum2], writes=[lmax])
            kb.group("DVE", lambda e: e.tensor_scalar(
                gtmp[:, 0:G], gsum2[:, :], lmax[:, 0:1], None,
                ALU.is_ge), reads=[gsum2, lmax], writes=["gtmp"])
            kb.group("DVE", lambda e: e.scalar_tensor_tensor(
                gsum2[:, :], gtmp[:, 0:G], -1e30, gsum2[:, :],
                ALU.mult, ALU.add), reads=["gtmp", gsum2], writes=[gsum2])
        kb.group("DVE", lambda e: e.reduce_max(lmax[:, 0:1], gsum2[:, :],
                                               axis=mybir.AxisListType.X),
                 reads=[gsum2], writes=[lmax])
        kb.group("DVE", lambda e: e.tensor_scalar(gsum2[:, :], gsum[:, :],
                                                  lmax[:, 0:1], None, ALU.is_ge),
                 reads=[gsum, lmax], writes=[gsum2])
        for gg in range(G):
            kb.group("DVE", lambda e, gg=gg: e.tensor_scalar(
                rtm2[:, gg * 4:(gg + 1) * 4], rtmp[:, gg * 4:(gg + 1) * 4],
                gsum2[:, gg:gg + 1], None, ALU.mult),
                reads=[rtmp, gsum2, rtm2], writes=[rtm2])
        kb.group("DVE", lambda e: e.tensor_copy(mtmp[:, :],
                                                rtm2[:, :]),
                 reads=[rtm2], writes=["mtmp"])
        for _ in range(7):
            kb.group("DVE", lambda e: e.reduce_max(lmax[:, 1:2],
                                                   mtmp[:, :],
                                                   axis=mybir.AxisListType.X),
                     reads=["mtmp"], writes=[lmax])
            kb.group("DVE", lambda e: e.tensor_scalar(
                gtmp[:, 0:E], mtmp[:, :],
                lmax[:, 1:2], None, ALU.is_ge), reads=["mtmp", lmax],
                writes=["gtmp"])
            kb.group("DVE", lambda e: e.scalar_tensor_tensor(
                mtmp[:, :], gtmp[:, 0:E], -1e30,
                mtmp[:, :], ALU.mult, ALU.add),
                reads=["gtmp", "mtmp"], writes=["mtmp"])
        kb.group("DVE", lambda e: e.reduce_max(lmax[:, 1:2],
                                               mtmp[:, :],
                                               axis=mybir.AxisListType.X),
                 reads=["mtmp"], writes=[lmax])
        kb.group("DVE", lambda e: e.tensor_scalar(rtm2[:, :], rtm2[:, :],
                                                  lmax[:, 1:2], None, ALU.is_ge),
                 reads=[rtm2, lmax], writes=[rtm2])
        kb.group("DVE", lambda e: e.tensor_tensor(logit[:, :], logit[:, :],
                                                  rtm2[:, :], ALU.mult),
                 reads=[logit, rtm2], writes=[logit])
        kb.group("DVE", lambda e: e.tensor_reduce(lmax[:, 2:3], logit[:, :],
                                                  axis=mybir.AxisListType.X,
                                                  op=ALU.add),
                 reads=[logit, lmax], writes=[lmax])
        kb.group("ACT", lambda e: e.activation(lmax[:, 3:4], lmax[:, 2:3], AF.Ln,
                                               bias=eps20, scale=1.0),
                 reads=[lmax, "constpool"], writes=[lmax])
        kb.group("ACT", lambda e: e.activation(lmax[:, 3:4], lmax[:, 3:4], AF.Exp,
                                               bias=zero, scale=-1.0),
                 reads=[lmax, "constpool"], writes=[lmax])
        kb.group("DVE", lambda e, t=t: e.tensor_scalar(
            cmb[:, t * E:(t + 1) * E], logit[:, :], lmax[:, 3:4], 2.5,
            ALU.mult, ALU.mult),
            reads=[logit, lmax], writes=[(cmb.name, t)])

    def emit_down(e_i, tb, wdb, prodb, nffc):
        for tt4 in range(4):
            t = tb * 4 + tt4
            for half in range(2):
                ps = pr.pC[:, :] if half == 0 else pr.pD[:, :]
                pk = pr.pC if half == 0 else pr.pD
                mm_group(kb, ps,
                         [(prodb[:, fc * 512 + tt4 * 128: fc * 512 + tt4 * 128 + 128],
                           wdb[:, fc * D + half * 512: fc * D + (half + 1) * 512])
                          for fc in range(nffc)],
                         reads=[prodb, wdb], writes=[pk])
                if e_i < 0:
                    kb.group("DVE", lambda e, ps=ps, half=half: e.tensor_copy(
                        stg[:, half * 512:(half + 1) * 512], ps),
                        reads=[pk], writes=[(stg.name, half)])
                else:
                    sc = cmb[:, t * E + e_i: t * E + e_i + 1]
                    kb.group("DVE", lambda e, ps=ps, sc=sc, half=half:
                             e.tensor_scalar(stg[:, half * 512:(half + 1) * 512],
                                             ps, sc, None, ALU.mult),
                             reads=[pk, (cmb.name, t)], writes=[(stg.name, half)])
            kb.group("POOL", lambda e, t=t, e_i=e_i: e.dma_start(
                out=accd[t * 128:(t + 1) * 128, :], in_=stg[:, :],
                accum_op=mybir.AluOpType.add)
                if e_i >= 0 else
                e.dma_start(out=accd[t * 128:(t + 1) * 128, :], in_=stg[:, :]),
                reads=[(stg.name, 0), (stg.name, 1)], writes=[("accd", t)],
                dma=kb.dma_queue(f"POOLQ.acc{t % 2}"))

    def load_gu(e_i):
        ff = 2 * I if e_i < 0 else I
        wgb = wgb2[e_i % 2]
        wub = wub2[e_i % 2]
        gsrc = sh_gateT if e_i < 0 else w_gateT[e_i]
        usrc = sh_upT if e_i < 0 else w_upT[e_i]
        kb.group("POOL", lambda e, ff=ff, wgb=wgb, gsrc=gsrc: e.dma_start(
            out=wgb[:, 0:8 * ff].rearrange("p (c n) -> p c n", c=8),
            in_=gsrc.rearrange("(c p) n -> p c n", p=128)),
            writes=[wgb], dma=kb.dma_queue(f"POOLQ.w1{e_i % 2}"))
        kb.group("POOL", lambda e, ff=ff, wub=wub, usrc=usrc: e.dma_start(
            out=wub[:, 0:8 * ff].rearrange("p (c n) -> p c n", c=8),
            in_=usrc.rearrange("(c p) n -> p c n", p=128)),
            writes=[wub], dma=kb.dma_queue(f"POOLQ.w2{e_i % 2}"))

    pend = None
    blk = 0
    for e_i in range(-1, E):
        ff = 2 * I if e_i < 0 else I
        nffc = ff // 128
        wgb = wgb2[e_i % 2]
        wub = wub2[e_i % 2]
        wdb = wdb2[e_i % 2]
        if e_i < 0:
            load_gu(-1)
        if e_i + 1 < E:
            load_gu(e_i + 1)
        dsrc = sh_downT if e_i < 0 else w_downT[e_i]
        kb.group("POOL", lambda e, nffc=nffc, wdb=wdb, dsrc=dsrc: e.dma_start(
            out=wdb[:, 0:nffc * D].rearrange("p (c n) -> p c n", c=nffc),
            in_=dsrc.rearrange("(c p) n -> p c n", p=128)),
            writes=[wdb], dma=kb.dma_queue(f"POOLQ.w3{e_i % 2}"))
        for tb in range(T // 512):
            prodb = prodb2[blk % 2]
            blk += 1
            for fc in range(nffc):
                psg = pr.pA[:, (fc % 2) * 512:(fc % 2) * 512 + 512]
                pkg = (pr.pA.name, fc % 2)
                mm_group(kb, psg,
                         [(wgb[:, c * ff + fc * 128: c * ff + (fc + 1) * 128],
                           xT[:, c * T + tb * 512: c * T + (tb + 1) * 512])
                          for c in range(8)],
                         reads=[wgb] + [(xT.name, "fm", tb * 4 + j)
                                        for j in range(4)], writes=[pkg])
                gsb = gsb2[fc % 2]
                kb.group("ACT", lambda e, psg=psg, gsb=gsb: e.activation(
                    gsb[:, :], psg, AF.Silu, bias=zero, scale=1.0),
                    reads=[pkg, "constpool"], writes=[gsb])
                psu = pr.pB[:, (fc % 2) * 512:(fc % 2) * 512 + 512]
                pku = (pr.pB.name, fc % 2)
                mm_group(kb, psu,
                         [(wub[:, c * ff + fc * 128: c * ff + (fc + 1) * 128],
                           xT[:, c * T + tb * 512: c * T + (tb + 1) * 512])
                          for c in range(8)],
                         reads=[wub] + [(xT.name, "fm", tb * 4 + j)
                                        for j in range(4)], writes=[pku])
                kb.group("DVE", lambda e, psu=psu, fc=fc, gsb=gsb, prodb=prodb:
                         e.tensor_tensor(
                    prodb[:, fc * 512:(fc + 1) * 512], gsb[:, :], psu, ALU.mult),
                    reads=[gsb, pku, prodb], writes=[prodb])
            if pend is not None:
                emit_down(*pend)
            pend = (e_i, tb, wdb, prodb, nffc)
    if pend is not None:
        emit_down(*pend)
    for t in range(NT):
        xt = xrb[t % 2]
        ob = obuf[t % 2]
        kb.group("SP", lambda e, t=t, xt=xt: e.dma_start(
            out=xt[:, :], in_=x[t * 128:(t + 1) * 128, :]),
            writes=[xt], dma="SPQ")
        kb.group("SP", lambda e, t=t, ob=ob: e.dma_start(
            out=ob[:, :], in_=accd[t * 128:(t + 1) * 128, :]),
            reads=[("accd", t)], writes=[ob], dma=kb.dma_queue("SPQ.rd"))
        emit_sandwich_epilogue_tile(pr, t, ob, xt, bns, mv2, eps1, zero, xo)
    return





def build_L2f(T=2048):
    D, E, I, G = 1024, 32, 512, 8
    nc = bass.Bass("TRN2", target_bir_lowering=False)
    pr = Prog(nc)
    kb = pr.kb
    io = {}
    io["x"] = kb.dram_in("x", [T, D], F32)
    io["gateT"] = kb.dram_in("gateT", [D, E], F32)
    io["gbias"] = kb.dram_in("gbias", [128, E], F32)
    io["w_gateT"] = kb.dram_in("w_gateT", [E, D, I], F32)
    io["w_upT"] = kb.dram_in("w_upT", [E, D, I], F32)
    io["w_downT"] = kb.dram_in("w_downT", [E, I, D], F32)
    io["sh_gateT"] = kb.dram_in("sh_gateT", [D, 2 * I], F32)
    io["sh_upT"] = kb.dram_in("sh_upT", [D, 2 * I], F32)
    io["sh_downT"] = kb.dram_in("sh_downT", [2 * I, D], F32)
    io["xo"] = kb.dram_out("xo", [T, D], F32)
    emit_L2f(pr, io, T=T)
    kb.barrier_on("SP")
    kb.finalize()
    return nc

def emit_L0f_moe(pr, io, T=2048):
    E, H = 8, 2048
    D = 1024
    NT = T // 128
    nc = pr.nc
    kb = pr.kb
    x = io["x"]
    x2 = io["x2"]
    norm2t = io["norm2t"]
    routerT = io["routerT"]
    sh_gate = io["sh_gate"]
    sh_up = io["sh_up"]
    sh_down = io["sh_down"]
    e_gate = io["e_gate"]
    e_up = io["e_up"]
    e_down = io["e_down"]
    xo = io["xo"]

    accd = kb.dram_tmp("accd", [T, D], F32)
    stg = kb.sbuf("stg", [128, D], F32)
    xT = kb.sbuf("xT", [128, 8 * T], BF16)
    wgb = kb.sbuf("wgb", [128, 8 * H], BF16)
    wub = kb.sbuf("wub", [128, 8 * H], BF16)
    wdb = kb.sbuf("wdb", [128, 16 * D], BF16)
    gtb = kb.sbuf("gtb", [128, 8 * E], BF16)
    prodb2 = [kb.sbuf(f"prodb{i}", [128, 16 * 512], BF16) for i in range(2)]
    gsb2 = [kb.sbuf(f"gsb{i}", [128, 512], BF16) for i in range(2)]
    nwt = kb.sbuf("nwt", [128, D], BF16)
    cmb = kb.sbuf("cmb", [128, NT * E], F32)
    logit = kb.sbuf("logit", [128, E], F32)
    rtmp = kb.sbuf("rtmp", [128, E], F32)
    lmax = kb.sbuf("lmax", [128, 4], F32)
    xrb = [kb.sbuf(f"xrb{i}", [128, D], F32) for i in range(2)]
    xnb = [kb.sbuf("xnb0", [128, D], BF16)] * 2
    obuf = [kb.sbuf(f"obuf{i}", [128, D], F32) for i in range(2)]
    mvball = kb.sbuf("mvball", [128, 2 * NT], F32)
    r1all = kb.sbuf("r1all", [128, NT], F32)
    sc1 = kb.sbuf("sc1", [128, 4], F32)
    bns = kb.sbuf("bns", [128, 12], F32)
    mv2 = kb.sbuf("mv2", [128, 2], F32)
    eps1 = kb.constv(1e-5)
    eps6 = kb.constv(1e-6)
    zero = kb.constv(0.0)

    load_weight_chunked(kb, gtb, routerT, D, E, "POOLQ")
    kb.group("POOL", lambda e: e.dma_start(out=nwt[:, :], in_=norm2t[:, :]),
             writes=[nwt], dma="POOLQ")

    def prep_tile(t):
        xt = xrb[t % 2]
        kb.group("SP", lambda e, t=t, xt=xt: e.dma_start(
            out=xt[:, :], in_=x2[t * 128:(t + 1) * 128, :]),
            writes=[xt], dma="SPQ")
        kb.group("DVE", lambda e, xt=xt: e.bn_stats(bns[:, 0:6], xt[:, 0:512]),
                 reads=[xt], writes=[bns])
        kb.group("DVE", lambda e, xt=xt: e.bn_stats(bns[:, 6:12], xt[:, 512:1024]),
                 reads=[xt, bns], writes=[bns])
        kb.group("DVE", lambda e, t=t: e.bn_aggr(mvball[:, 2 * t:2 * t + 2],
                                                 bns[:, 0:12]),
                 reads=[bns], writes=[mvball])
        kb.group("DVE", lambda e, t=t: e.tensor_tensor(
            sc1[:, 0:1], mvball[:, 2 * t:2 * t + 1], mvball[:, 2 * t:2 * t + 1],
            ALU.mult), reads=[mvball], writes=[sc1])
        kb.group("DVE", lambda e, t=t: e.tensor_tensor(
            sc1[:, 0:1], sc1[:, 0:1], mvball[:, 2 * t + 1:2 * t + 2], ALU.add),
            reads=[sc1, mvball], writes=[sc1])
        kb.group("ACT", lambda e: e.activation(sc1[:, 1:2], sc1[:, 0:1], AF.Ln,
                                               bias=eps6, scale=1.0),
                 reads=[sc1, "constpool"], writes=[sc1])
        kb.group("ACT", lambda e, t=t: e.activation(
            r1all[:, t:t + 1], sc1[:, 1:2], AF.Exp, bias=zero, scale=-0.5),
            reads=[sc1, "constpool"], writes=[r1all])
        xn = xnb[t % 2]
        kb.group("DVE", lambda e, t=t, xt=xt, xn=xn: e.tensor_scalar(
            xn[:, :], xt[:, :], r1all[:, t:t + 1], None, ALU.mult),
            reads=[xt, r1all], writes=[xn])
        kb.group("DVE", lambda e, xn=xn: e.tensor_tensor(
            xn[:, :], xn[:, :], nwt[:, :], ALU.mult), reads=[xn, nwt], writes=[xn])
        return xn[:, :]

    emit_tm_to_fm(pr, prep_tile, NT, D, xT, T, 0, "tf")
    xT_keys = [(xT.name, "fm", t) for t in range(NT)]

    for t in range(NT):
        pc = pr.pC if t % 2 == 0 else pr.pD
        mm_group(kb, pc[:, 0:E],
                 [(xT[:, c * T + t * 128: c * T + (t + 1) * 128],
                   gtb[:, c * E:(c + 1) * E]) for c in range(8)],
                 reads=[gtb, (xT.name, "fm", t)], writes=[pc])
        kb.group("DVE", lambda e, pc=pc: e.tensor_copy(logit[:, :], pc[:, 0:E]),
                 reads=[pc], writes=[logit])
        kb.group("DVE", lambda e: e.reduce_max(lmax[:, 0:1], logit[:, :],
                                               axis=mybir.AxisListType.X),
                 reads=[logit], writes=[lmax])
        kb.group("DVE", lambda e: e.tensor_scalar(logit[:, :], logit[:, :],
                                                  lmax[:, 0:1], None, ALU.subtract),
                 reads=[logit, lmax], writes=[logit])
        kb.group("ACT", lambda e: e.activation(logit[:, :], logit[:, :], AF.Exp,
                                               bias=zero, scale=1.0),
                 reads=[logit, "constpool"], writes=[logit])
        kb.group("DVE", lambda e: e.tensor_reduce(lmax[:, 1:2], logit[:, :],
                                                  axis=mybir.AxisListType.X,
                                                  op=ALU.add),
                 reads=[logit, lmax], writes=[lmax])
        kb.group("ACT", lambda e: e.activation(lmax[:, 1:2], lmax[:, 1:2], AF.Ln,
                                               bias=zero, scale=1.0),
                 reads=[lmax, "constpool"], writes=[lmax])
        kb.group("ACT", lambda e: e.activation(lmax[:, 1:2], lmax[:, 1:2], AF.Exp,
                                               bias=zero, scale=-1.0),
                 reads=[lmax, "constpool"], writes=[lmax])
        kb.group("DVE", lambda e: e.tensor_scalar(logit[:, :], logit[:, :],
                                                  lmax[:, 1:2], None, ALU.mult),
                 reads=[logit, lmax], writes=[logit])
        kb.group("DVE", lambda e: e.reduce_max(lmax[:, 0:1], logit[:, :],
                                               axis=mybir.AxisListType.X),
                 reads=[logit], writes=[lmax])
        kb.group("DVE", lambda e: e.tensor_scalar(rtmp[:, :], logit[:, :],
                                                  lmax[:, 0:1], None, ALU.is_ge),
                 reads=[logit, lmax], writes=[rtmp])
        kb.group("DVE", lambda e: e.scalar_tensor_tensor(
            rtmp[:, :], rtmp[:, :], -1e30, logit[:, :], ALU.mult, ALU.add),
            reads=[rtmp, logit], writes=[rtmp])
        kb.group("DVE", lambda e: e.reduce_max(lmax[:, 1:2], rtmp[:, :],
                                               axis=mybir.AxisListType.X),
                 reads=[rtmp], writes=[lmax])
        kb.group("DVE", lambda e: e.tensor_scalar(rtmp[:, :], logit[:, :],
                                                  lmax[:, 1:2], None, ALU.is_ge),
                 reads=[logit, lmax], writes=[rtmp])
        kb.group("ACT", lambda e: e.activation(logit[:, :], logit[:, :], AF.Exp,
                                               bias=zero, scale=1.0),
                 reads=[logit, "constpool"], writes=[logit])
        kb.group("DVE", lambda e: e.tensor_tensor(logit[:, :], logit[:, :],
                                                  rtmp[:, :], ALU.mult),
                 reads=[logit, rtmp], writes=[logit])
        kb.group("DVE", lambda e: e.tensor_reduce(lmax[:, 2:3], logit[:, :],
                                                  axis=mybir.AxisListType.X,
                                                  op=ALU.add),
                 reads=[logit, lmax], writes=[lmax])
        kb.group("ACT", lambda e: e.activation(lmax[:, 3:4], lmax[:, 2:3], AF.Ln,
                                               bias=zero, scale=1.0),
                 reads=[lmax, "constpool"], writes=[lmax])
        kb.group("ACT", lambda e: e.activation(lmax[:, 3:4], lmax[:, 3:4], AF.Exp,
                                               bias=zero, scale=-1.0),
                 reads=[lmax, "constpool"], writes=[lmax])
        kb.group("DVE", lambda e, t=t: e.tensor_scalar(
            cmb[:, t * E:(t + 1) * E], logit[:, :], lmax[:, 3:4], None, ALU.mult),
            reads=[logit, lmax], writes=[(cmb.name, t)])

    def emit_down(e_i, tb, prodb):
        for tt4 in range(4):
            t = tb * 4 + tt4
            for half in range(2):
                ps = pr.pC[:, :] if half == 0 else pr.pD[:, :]
                pk = pr.pC if half == 0 else pr.pD
                mm_group(kb, ps,
                         [(prodb[:, fc * 512 + tt4 * 128: fc * 512 + tt4 * 128 + 128],
                           wdb[:, fc * D + half * 512: fc * D + (half + 1) * 512])
                          for fc in range(16)],
                         reads=[prodb, wdb], writes=[pk])
                if e_i < 0:
                    kb.group("DVE", lambda e, ps=ps, half=half: e.tensor_copy(
                        stg[:, half * 512:(half + 1) * 512], ps),
                        reads=[pk], writes=[(stg.name, half)])
                else:
                    sc = cmb[:, t * E + e_i: t * E + e_i + 1]
                    kb.group("DVE", lambda e, ps=ps, sc=sc, half=half:
                             e.tensor_scalar(stg[:, half * 512:(half + 1) * 512],
                                             ps, sc, None, ALU.mult),
                             reads=[pk, (cmb.name, t)], writes=[(stg.name, half)])
            kb.group("POOL", lambda e, t=t, e_i=e_i: e.dma_start(
                out=accd[t * 128:(t + 1) * 128, :], in_=stg[:, :],
                accum_op=mybir.AluOpType.add)
                if e_i >= 0 else
                e.dma_start(out=accd[t * 128:(t + 1) * 128, :], in_=stg[:, :]),
                reads=[(stg.name, 0), (stg.name, 1)], writes=[("accd", t)],
                dma=kb.dma_queue(f"POOLQ.acc{t % 2}"))

    pend = None
    blk = 0
    for e_i in range(-1, E):
        if pend is not None:
            emit_down(*pend)
            pend = None
        srcs = (sh_gate, sh_up, sh_down) if e_i < 0 else \
            (e_gate[e_i], e_up[e_i], e_down[e_i])
        kb.group("POOL", lambda e, s=srcs[0]: e.dma_start(
            out=wgb[:, :].rearrange("p (c n) -> p c n", c=8),
            in_=s.rearrange("(c p) n -> p c n", p=128)),
            writes=[wgb], dma=kb.dma_queue("POOLQ.w1"))
        kb.group("POOL", lambda e, s=srcs[1]: e.dma_start(
            out=wub[:, :].rearrange("p (c n) -> p c n", c=8),
            in_=s.rearrange("(c p) n -> p c n", p=128)),
            writes=[wub], dma=kb.dma_queue("POOLQ.w2"))
        kb.group("POOL", lambda e, s=srcs[2]: e.dma_start(
            out=wdb[:, :].rearrange("p (c n) -> p c n", c=16),
            in_=s.rearrange("(c p) n -> p c n", p=128)),
            writes=[wdb], dma=kb.dma_queue("POOLQ.w3"))
        for tb in range(T // 512):
            prodb = prodb2[blk % 2]
            blk += 1
            for fc in range(16):
                psg = pr.pA[:, (fc % 2) * 512:(fc % 2) * 512 + 512]
                pkg = (pr.pA.name, fc % 2)
                mm_group(kb, psg,
                         [(wgb[:, c * H + fc * 128: c * H + (fc + 1) * 128],
                           xT[:, c * T + tb * 512: c * T + (tb + 1) * 512])
                          for c in range(8)],
                         reads=[wgb] + [(xT.name, "fm", tb * 4 + j)
                                        for j in range(4)], writes=[pkg])
                gsb = gsb2[fc % 2]
                kb.group("ACT", lambda e, psg=psg, gsb=gsb: e.activation(
                    gsb[:, :], psg, AF.Silu, bias=zero, scale=1.0),
                    reads=[pkg, "constpool"], writes=[gsb])
                psu = pr.pB[:, (fc % 2) * 512:(fc % 2) * 512 + 512]
                pku = (pr.pB.name, fc % 2)
                mm_group(kb, psu,
                         [(wub[:, c * H + fc * 128: c * H + (fc + 1) * 128],
                           xT[:, c * T + tb * 512: c * T + (tb + 1) * 512])
                          for c in range(8)],
                         reads=[wub] + [(xT.name, "fm", tb * 4 + j)
                                        for j in range(4)], writes=[pku])
                kb.group("DVE", lambda e, psu=psu, fc=fc, gsb=gsb, prodb=prodb:
                         e.tensor_tensor(
                    prodb[:, fc * 512:(fc + 1) * 512], gsb[:, :], psu, ALU.mult),
                    reads=[gsb, pku, prodb], writes=[prodb])
            if pend is not None:
                emit_down(*pend)
            pend = (e_i, tb, prodb)
    if pend is not None:
        emit_down(*pend)
    for t in range(NT):
        xt = xrb[t % 2]
        ob = obuf[t % 2]
        kb.group("SP", lambda e, t=t, xt=xt: e.dma_start(
            out=xt[:, :], in_=x[t * 128:(t + 1) * 128, :]),
            writes=[xt], dma="SPQ")
        kb.group("SP", lambda e, t=t, ob=ob: e.dma_start(
            out=ob[:, :], in_=x2[t * 128:(t + 1) * 128, :]),
            reads=[ob], writes=[ob], dma=kb.dma_queue("SPQ.x2"))
        kb.group("SP", lambda e, t=t: e.dma_start(
            out=stg[:, :], in_=accd[t * 128:(t + 1) * 128, :]),
            reads=[("accd", t)], writes=[(stg.name, 0), (stg.name, 1)],
            dma=kb.dma_queue("SPQ.rd"))
        kb.group("DVE", lambda e, ob=ob: e.tensor_tensor(
            ob[:, :], ob[:, :], stg[:, :], ALU.add),
            reads=[ob, (stg.name, 0), (stg.name, 1)], writes=[ob])
        emit_sandwich_epilogue_tile(pr, t, ob, xt, bns, mv2, eps1, zero, xo)
    return





def build_L0f_moe(T=2048):
    D, E, H = 1024, 8, 2048
    nc = bass.Bass("TRN2", target_bir_lowering=False)
    pr = Prog(nc)
    kb = pr.kb
    io = {}
    io["x"] = kb.dram_in("x", [T, D], F32)
    io["x2"] = kb.dram_in("x2", [T, D], F32)
    io["norm2t"] = kb.dram_in("norm2t", [128, D], F32)
    io["routerT"] = kb.dram_in("routerT", [D, E], F32)
    io["sh_gate"] = kb.dram_in("sh_gate", [D, H], F32)
    io["sh_up"] = kb.dram_in("sh_up", [D, H], F32)
    io["sh_down"] = kb.dram_in("sh_down", [H, D], F32)
    io["e_gate"] = kb.dram_in("e_gate", [E, D, H], F32)
    io["e_up"] = kb.dram_in("e_up", [E, D, H], F32)
    io["e_down"] = kb.dram_in("e_down", [E, H, D], F32)
    io["xo"] = kb.dram_out("xo", [T, D], F32)
    emit_L0f_moe(pr, io, T=T)
    kb.barrier_on("SP")
    kb.finalize()
    return nc

def emit_L3f(pr, io, T=2048):
    nc = pr.nc
    kb = pr.kb
    NT = T // 128
    D = 1024
    E = 8
    x = io["x"]
    routerT = io["routerT"]
    b_router = io["b_router"]
    expT = io["expT"]
    b_exp = io["b_exp"]
    xo = io["xo"]

    xr = kb.sbuf("xr", [128, NT * D], F32)
    xT = kb.sbuf("xT", [128, 8 * T], BF16)
    xnb = [kb.sbuf(f"xnb{i}", [128, D], BF16) for i in range(2)]
    routerTb = kb.sbuf("routerTb", [128, 8 * E], BF16)
    brow = kb.sbuf("brow", [1, E], BF16)
    bexpb = kb.sbuf("bexpb", [8, D], BF16)
    mselb = kb.sbuf("mselb", [128, NT * E], F32)
    mselT = kb.sbuf("mselT", [8, NT * 128], BF16)
    webuf = [kb.sbuf(f"webuf{i}", [128, 8 * D], BF16) for i in range(2)]
    acc = kb.sbuf("acc", [128, 8 * D], F32)
    logit = kb.sbuf("logit", [128, E], F32)
    rtmp = kb.sbuf("rtmp", [128, E], F32)
    lmax = kb.sbuf("lmax", [128, 2], F32)
    ytmp = kb.sbuf("ytmp", [128, D], F32)
    ps_r = pr.pC
    ps_t = pr.pD
    ps_e = pr.pA
    ps_b = pr.pB

    kb.group("SP", lambda e: e.dma_start(
        out=xr[:, :].rearrange("p (t n) -> p t n", t=NT),
        in_=x.rearrange("(t p) n -> p t n", p=128)), writes=[xr], dma="SPQ")
    load_weight_chunked(kb, routerTb, routerT, D, E, "POOLQ")
    kb.group("POOL", lambda e: e.dma_start(out=brow[:, :], in_=b_router[:, :]),
             writes=[brow], dma="POOLQ")
    kb.group("POOL", lambda e: e.dma_start(out=bexpb[:, :], in_=b_exp[:, :]),
             writes=[bexpb], dma="POOLQ")

    eps5c = kb.constv(1e-5)
    zeroc = kb.constv(0.0)
    mvb = emit_ln_stats(pr, lambda t: xr[:, t * D:(t + 1) * D], NT, D, "ln1")
    bns3 = kb.t("ln1_bns")
    rstd1 = emit_rstd(pr, mvb[:, 1::2], NT, 1e-5, "ln1", [mvb])

    def xn_src(t):
        buf = xnb[t % 2]
        kb.group("DVE", lambda e, t=t, buf=buf: e.tensor_scalar(
            buf[:, :], xr[:, t * D:(t + 1) * D], mvb[:, 2 * t:2 * t + 1],
            rstd1[:, t:t + 1], ALU.subtract, ALU.mult),
            reads=[xr, mvb, rstd1], writes=[buf])
        return buf[:, :]
    emit_tm_to_fm(pr, xn_src, NT, D, xT, T, 0, "t1")

    for t in range(NT):
        mm_group(kb, ps_r[:, 0:E],
                 [(xT[:, c * T + t * 128: c * T + (t + 1) * 128],
                   routerTb[:, c * E:(c + 1) * E]) for c in range(8)],
                 reads=[(xT.name, "fm", t), routerTb, pr.ones_row, brow],
                 writes=[ps_r],
                 bias=(pr.ones_row[0:1, 0:128], brow[0:1, :]))
        kb.group("DVE", lambda e: e.tensor_copy(logit[:, :], ps_r[:, 0:E]),
                 reads=[ps_r], writes=[logit])
        kb.group("DVE", lambda e: e.reduce_max(lmax[:, 0:1], logit[:, :], axis=mybir.AxisListType.X),
                 reads=[logit], writes=[lmax])
        kb.group("DVE", lambda e: e.tensor_scalar(rtmp[:, :], logit[:, :], lmax[:, 0:1],
                                                  None, ALU.is_ge),
                 reads=[logit, lmax], writes=[rtmp])
        kb.group("DVE", lambda e: e.scalar_tensor_tensor(rtmp[:, :], rtmp[:, :],
                                                         -1e30, logit[:, :],
                                                         ALU.mult, ALU.add),
                 reads=[rtmp, logit], writes=[rtmp])
        kb.group("DVE", lambda e: e.reduce_max(lmax[:, 1:2], rtmp[:, :], axis=mybir.AxisListType.X),
                 reads=[rtmp], writes=[lmax])
        kb.group("DVE", lambda e, t=t: e.tensor_scalar(mselb[:, t * E:(t + 1) * E],
                                                       logit[:, :], lmax[:, 1:2],
                                                       None, ALU.is_ge),
                 reads=[logit, lmax], writes=[(mselb.name, t)])
        kb.group("PE", lambda e, t=t: e.transpose(
            ps_t[0:8, 0:128], mselb[:, t * E:(t + 1) * E], pr.identf[:, :]),
            reads=[(mselb.name, t), pr.identf], writes=[ps_t])
        kb.group("DVE", lambda e, t=t: e.tensor_copy(
            mselT[0:8, t * 128:(t + 1) * 128], ps_t[0:8, 0:128]),
            reads=[ps_t], writes=[mselT])

    for tb in range((NT + 7) // 8):
        tiles = range(tb * 8, min(tb * 8 + 8, NT))
        for e_i in range(E):
            buf = webuf[e_i % 2]
            q = kb.dma_queue(f"POOLQ.w{e_i % 2}")
            load_weight_chunked(kb, buf, expT[e_i], D, D, q)
            for t in tiles:
                for half in range(2):
                    mm_group(kb, ps_e[:, half * 512:(half + 1) * 512],
                             [(xT[:, c * T + t * 128: c * T + (t + 1) * 128],
                               buf[:, c * D + half * 512: c * D + (half + 1) * 512])
                              for c in range(8)],
                             reads=[(xT.name, "fm", t), buf], writes=[(ps_e.name, half)])
                sc = mselb[:, t * E + e_i: t * E + e_i + 1]
                aslice = acc[:, (t - tb * 8) * D:(t - tb * 8 + 1) * D]
                if e_i == 0:
                    kb.group("DVE", lambda e, sc=sc, aslice=aslice: e.tensor_scalar(
                        aslice, ps_e[:, :], sc, None, ALU.mult),
                        reads=[(ps_e.name, 0), (ps_e.name, 1), (mselb.name, t)],
                        writes=[(acc.name, t)])
                else:
                    kb.group("DVE", lambda e, sc=sc, aslice=aslice: e.scalar_tensor_tensor(
                        aslice, ps_e[:, :], sc, aslice, ALU.mult, ALU.add),
                        reads=[(ps_e.name, 0), (ps_e.name, 1), (mselb.name, t), (acc.name, t)],
                        writes=[(acc.name, t)])
        for t in tiles:
            mm_group(kb, ps_b[:, 0:512],
                     [(mselT[0:8, t * 128:(t + 1) * 128], bexpb[0:8, 0:512])],
                     reads=[mselT, bexpb], writes=[(ps_b.name, 0)])
            mm_group(kb, ps_b[:, 512:1024],
                     [(mselT[0:8, t * 128:(t + 1) * 128], bexpb[0:8, 512:1024])],
                     reads=[mselT, bexpb], writes=[(ps_b.name, 1)])
            kb.group("DVE", lambda e, t=t, tb=tb: e.tensor_tensor(
                ytmp[:, :], acc[:, (t - tb * 8) * D:(t - tb * 8 + 1) * D],
                ps_b[:, :], ALU.add),
                reads=[(acc.name, t), (ps_b.name, 0), (ps_b.name, 1)], writes=[ytmp])
            for c in range(2):
                kb.group("DVE", lambda e, c=c: e.bn_stats(
                    bns3[:, c * 6:(c + 1) * 6], ytmp[:, c * 512:(c + 1) * 512]),
                    reads=[ytmp], writes=["ln1_bns"])
            kb.group("DVE", lambda e: e.bn_aggr(lmax[:, 0:2], bns3[:, 0:12]),
                     reads=["ln1_bns"], writes=[lmax])
            kb.group("ACT", lambda e: e.activation(lmax[:, 1:2], lmax[:, 1:2], AF.Ln,
                                                   bias=eps5c, scale=1.0),
                     reads=[lmax, "constpool"], writes=[lmax])
            kb.group("ACT", lambda e: e.activation(lmax[:, 1:2], lmax[:, 1:2], AF.Exp,
                                                   bias=zeroc, scale=-0.5),
                     reads=[lmax, "constpool"], writes=[lmax])
            kb.group("DVE", lambda e: e.tensor_scalar(ytmp[:, :], ytmp[:, :],
                                                      lmax[:, 0:1], lmax[:, 1:2],
                                                      ALU.subtract, ALU.mult),
                     reads=[ytmp, lmax], writes=[ytmp])
            kb.group("DVE", lambda e, t=t: e.tensor_tensor(
                xr[:, t * D:(t + 1) * D], xr[:, t * D:(t + 1) * D], ytmp[:, :], ALU.add),
                reads=[xr, ytmp], writes=[(xr.name, "out", t)])

    kb.group("SP", lambda e: e.dma_start(
        out=xo.rearrange("(t p) n -> p t n", p=128),
        in_=xr[:, :].rearrange("p (t n) -> p t n", t=NT)),
        reads=[(xr.name, "out", t) for t in range(NT)] + [xr], writes=["xo"], dma="SPQ")
    return




def build_L3f(T=2048):
    D, E = 1024, 8
    nc = bass.Bass("TRN2", target_bir_lowering=False)
    pr = Prog(nc)
    kb = pr.kb
    io = {}
    io["x"] = kb.dram_in("x", [T, D], F32)
    io["routerT"] = kb.dram_in("routerT", [D, E], F32)
    io["b_router"] = kb.dram_in("b_router", [1, E], F32)
    io["expT"] = kb.dram_in("expT", [E, D, D], F32)
    io["b_exp"] = kb.dram_in("b_exp", [E, D], F32)
    io["xo"] = kb.dram_out("xo", [T, D], F32)
    emit_L3f(pr, io, T=T)
    kb.barrier_on("SP")
    kb.finalize()
    return nc


def _shard_own(x):
    return [np.ascontiguousarray(x[c // 2, (c % 2) * 2048:(c % 2 + 1) * 2048])
            for c in range(8)]


def _unshard_own(shards):
    out = np.empty((4, 4096, 1024), dtype=shards[0].dtype)
    for c in range(8):
        out[c // 2, (c % 2) * 2048:(c % 2 + 1) * 2048] = shards[c]
    return out


def _shard_batch_reordered(x):
    outs = []
    for c in range(8):
        b, h = c // 2, c % 2
        outs.append(np.ascontiguousarray(np.concatenate(
            [x[b, h * 2048:(h + 1) * 2048], x[b, (1 - h) * 2048:(2 - h) * 2048]],
            axis=0)))
    return outs


def _shard_batch_natural(x):
    return [np.ascontiguousarray(x[c // 2]) for c in range(8)]


def _bt(a):
    return np.ascontiguousarray(np.broadcast_to(a, (128, a.shape[-1])))


_PROGRAMS = {}


def _prog(name, fn):
    if name not in _PROGRAMS:
        _PROGRAMS[name] = fn()
    return _PROGRAMS[name]


def _conv_cols(cw, rows):
    r = cw[rows].reshape(-1, 128, 4)
    out = np.zeros((128, r.shape[0] * 4), np.float32)
    for c in range(r.shape[0]):
        for i in range(4):
            out[:, c * 4 + i] = r[c, :, i]
    return out


def _launch(nc, in_maps):
    res = run_bass_kernel_spmd(nc, in_maps, core_ids=list(range(8)))
    return res.results


USE_FUSED = True


def kernel(**ins):
    if USE_FUSED:
        return kernel_fused(**ins)
    return kernel_multilaunch(**ins)


def kernel_multilaunch(**ins):
    f32 = np.float32
    x = np.asarray(ins["L0_attn_hidden_states"], dtype=f32)

    nc = _prog("L0a", lambda: build_attn_atom("L0a"))
    w = {
        "WqkvT": np.ascontiguousarray(ins["L0_attn_W_qkv"].T.astype(f32)),
        "bqkv_cols": np.ascontiguousarray(
            ins["L0_attn_b_qkv"].astype(f32).reshape(24, 128).T),
        "bqkv_row": ins["L0_attn_b_qkv"].astype(f32).reshape(1, 3072),
        "gammat": _bt(ins["L0_attn_ln_gamma"].astype(f32)),
        "betat": _bt(ins["L0_attn_ln_beta"].astype(f32)),
        "WoT": np.ascontiguousarray(ins["L0_attn_W_out"].T.astype(f32)),
        "bout_row": ins["L0_attn_b_out"].astype(f32).reshape(1, 1024),
    }
    shards = _shard_batch_reordered(x)
    res = _launch(nc, [dict(w, xb=s) for s in shards])
    x = _unshard_own([r["xo"] for r in res])

    nc = _prog("L0fa", lambda: build_attn_atom("L0f_attn"))
    w = {
        "WqT": ins["L0_ffn_wq"].astype(f32),
        "WkT": ins["L0_ffn_wk"].astype(f32),
        "WvT": ins["L0_ffn_wv"].astype(f32),
        "normwt": _bt(ins["L0_ffn_norm1_w"].astype(f32)),
        "WoT": ins["L0_ffn_wo"].astype(f32),
    }
    shards = _shard_batch_reordered(x)
    res = _launch(nc, [dict(w, xb=s) for s in shards])
    x2s = [r["xo"] for r in res]

    nc = _prog("L0fm", lambda: build_L0f_moe())
    w = {
        "norm2t": _bt(ins["L0_ffn_norm2_w"].astype(f32)),
        "routerT": ins["L0_ffn_router_w"].astype(f32),
        "sh_gate": ins["L0_ffn_sh_gate"].astype(f32),
        "sh_up": ins["L0_ffn_sh_up"].astype(f32),
        "sh_down": ins["L0_ffn_sh_down"].astype(f32),
        "e_gate": ins["L0_ffn_e_gate"].astype(f32),
        "e_up": ins["L0_ffn_e_up"].astype(f32),
        "e_down": ins["L0_ffn_e_down"].astype(f32),
    }
    xo = _shard_own(x)
    res = _launch(nc, [dict(w, x=xo[c], x2=x2s[c]) for c in range(8)])
    x = _unshard_own([r["xo"] for r in res])

    nc = _prog("L1a", lambda: build_L1a())
    q_w, k_w, v_w = (ins["L1_attn_q_w"].astype(f32), ins["L1_attn_k_w"].astype(f32),
                     ins["L1_attn_v_w"].astype(f32))
    g_w, b_w, a_w = (ins["L1_attn_g_w"].astype(f32), ins["L1_attn_b_w"].astype(f32),
                     ins["L1_attn_a_w"].astype(f32))
    in_maps = []
    shards = _shard_batch_natural(x)
    for c in range(8):
        h0 = (c % 2) * 4
        rqk = slice(h0 * 256, (h0 + 4) * 256)
        rv = slice(h0 * 512, (h0 + 4) * 512)
        in_maps.append({
            "xb": shards[c],
            "normwt": _bt(ins["L1_attn_norm_w"].astype(f32)),
            "qwT": np.ascontiguousarray(q_w[rqk].T),
            "kwT": np.ascontiguousarray(k_w[rqk].T),
            "vwT": np.ascontiguousarray(v_w[rv].T),
            "gwT": np.ascontiguousarray(g_w[rv].T),
            "bwT": np.ascontiguousarray(b_w[h0:h0 + 4].T),
            "awT": np.ascontiguousarray(a_w[h0:h0 + 4].T),
            "qconvt": _conv_cols(ins["L1_attn_q_conv"].astype(f32),
                                 np.arange(h0 * 256, (h0 + 4) * 256)),
            "kconvt": _conv_cols(ins["L1_attn_k_conv"].astype(f32),
                                 np.arange(h0 * 256, (h0 + 4) * 256)),
            "vconvt": _conv_cols(ins["L1_attn_v_conv"].astype(f32),
                                 np.arange(h0 * 512, (h0 + 4) * 512)),
            "alog": ins["L1_attn_A_log"].astype(f32)[h0:h0 + 4].reshape(1, 4),
            "dtb": ins["L1_attn_dt_bias"].astype(f32)[h0:h0 + 4].reshape(1, 4),
            "onormt": _bt(ins["L1_attn_o_norm_w"].astype(f32)),
        })
    res = _launch(nc, in_maps)
    ogs = [r["og"] for r in res]
    nc = _prog("L1ap", lambda: build_L1a_proj())
    owT = np.ascontiguousarray(ins["L1_attn_o_w"].astype(f32).T)
    in_maps = []
    xo = _shard_own(x)
    for c in range(8):
        b, hh = c // 2, c % 2
        og_full = np.concatenate([ogs[2 * b], ogs[2 * b + 1]], axis=1)
        ogf = np.ascontiguousarray(og_full[hh * 2048:(hh + 1) * 2048].T)
        in_maps.append({"x": xo[c], "ogf": ogf, "owT": owT})
    res = _launch(nc, in_maps)
    x = _unshard_own([r["xo"] for r in res])

    nc = _prog("L1f", lambda: build_L1f())
    E, H = 8, 2048
    w = {
        "gateT": ins["L1_ffn_gate_w"].astype(f32),
        "w1": ins["L1_ffn_w1"].astype(f32),
        "b1r": ins["L1_ffn_b1"].astype(f32),
        "ln1gt": np.ascontiguousarray(np.broadcast_to(
            ins["L1_ffn_ln1_g"].astype(f32)[:, None, :], (E, 128, H))),
        "ln1bt": np.ascontiguousarray(np.broadcast_to(
            ins["L1_ffn_ln1_b"].astype(f32)[:, None, :], (E, 128, H))),
        "w2": ins["L1_ffn_w2"].astype(f32),
        "b2r": ins["L1_ffn_b2"].astype(f32),
        "lngt": _bt(ins["L1_ffn_ln_g"].astype(f32)),
        "lnbt": _bt(ins["L1_ffn_ln_b"].astype(f32)),
    }
    res = _launch(nc, [dict(w, x=s) for s in _shard_own(x)])
    x = _unshard_own([r["xo"] for r in res])

    nc = _prog("L2a", lambda: build_attn_atom("L2a"))
    Wq, Wk, Wv = (ins["L2_attn_Wq"].astype(f32), ins["L2_attn_Wk"].astype(f32),
                  ins["L2_attn_Wv"].astype(f32))
    shards = _shard_batch_natural(x)
    in_maps = []
    for c in range(8):
        h0 = (c % 2) * 8
        sl = slice(h0 * 64, (h0 + 8) * 64)
        in_maps.append({
            "xb": shards[c],
            "WqT": np.ascontiguousarray(Wq[:, sl]),
            "WkT": np.ascontiguousarray(Wk[:, sl]),
            "WvT": np.ascontiguousarray(Wv[:, sl]),
        })
    res = _launch(nc, in_maps)
    ctxs = [r["ctxo"] for r in res]
    nc = _prog("L2ap", lambda: build_L2a_proj())
    in_maps = []
    xo = _shard_own(x)
    for c in range(8):
        b, hh = c // 2, c % 2
        cf = np.zeros((1024, 4096), dtype=ctxs[0].dtype)
        for i in range(2):
            cc = ctxs[2 * b + i]
            for hl in range(8):
                cf[(i * 8 + hl) * 64:(i * 8 + hl + 1) * 64] = \
                    cc[:, hl * 4096:(hl + 1) * 4096]
        ctxin = np.ascontiguousarray(cf[:, hh * 2048:(hh + 1) * 2048])
        in_maps.append({"x": xo[c], "ctxin": ctxin,
                        "WoD": ins["L2_attn_Wo"].astype(f32)})
    res = _launch(nc, in_maps)
    x = _unshard_own([r["xo"] for r in res])

    nc = _prog("L2f", lambda: build_L2f())
    E2 = 32
    w = {
        "gateT": np.ascontiguousarray(ins["L2_ffn_gate_w"].astype(f32).T),
        "gbias": np.ascontiguousarray(np.broadcast_to(
            ins["L2_ffn_gate_bias"].astype(f32), (128, E2))),
        "w_gateT": np.ascontiguousarray(
            ins["L2_ffn_w_gate"].astype(f32).transpose(0, 2, 1)),
        "w_upT": np.ascontiguousarray(
            ins["L2_ffn_w_up"].astype(f32).transpose(0, 2, 1)),
        "w_downT": np.ascontiguousarray(
            ins["L2_ffn_w_down"].astype(f32).transpose(0, 2, 1)),
        "sh_gateT": np.ascontiguousarray(ins["L2_ffn_sh_gate"].astype(f32).T),
        "sh_upT": np.ascontiguousarray(ins["L2_ffn_sh_up"].astype(f32).T),
        "sh_downT": np.ascontiguousarray(ins["L2_ffn_sh_down"].astype(f32).T),
    }
    res = _launch(nc, [dict(w, x=s) for s in _shard_own(x)])
    x = _unshard_own([r["xo"] for r in res])

    nc = _prog("L3a", lambda: build_attn_atom("L3a"))
    qn = ins["L3_attn_q_norm_w"].astype(f32)
    kn = ins["L3_attn_k_norm_w"].astype(f32)
    w = {
        "WqkvT": np.ascontiguousarray(ins["L3_attn_qkv_w"].astype(f32).T),
        "qnormt": np.ascontiguousarray(np.broadcast_to(np.tile(qn, 2), (128, 128))),
        "knormt": np.ascontiguousarray(np.broadcast_to(np.tile(kn, 2), (128, 128))),
        "WoT": np.ascontiguousarray(ins["L3_attn_proj_w"].astype(f32).T),
        "bout_row": ins["L3_attn_proj_b"].astype(f32).reshape(1, 1024),
    }
    shards = _shard_batch_reordered(x)
    res = _launch(nc, [dict(w, xb=s) for s in shards])
    x = _unshard_own([r["xo"] for r in res])

    nc = _prog("L3f", lambda: build_L3f())
    w = {
        "routerT": np.ascontiguousarray(ins["L3_ffn_w_router"].astype(f32).T),
        "b_router": ins["L3_ffn_b_router"].astype(f32).reshape(1, 8),
        "expT": np.ascontiguousarray(
            ins["L3_ffn_w_experts"].astype(f32).transpose(0, 2, 1)),
        "b_exp": ins["L3_ffn_b_experts"].astype(f32),
    }
    res = _launch(nc, [dict(w, x=s) for s in _shard_own(x)])
    x = _unshard_own([r["xo"] for r in res])
    return x



PAIRS = [[0, 1], [2, 3], [4, 5], [6, 7]]


def _emit_gather(pr, src_ap, dst_ap, tag):
    kb = pr.kb
    kb.group("POOL", lambda e: e.collective_compute(
        "AllGather", ALU.bypass, replica_groups=PAIRS,
        ins=[src_ap.opt()], outs=[dst_ap.opt()]),
        reads=[tag + "_src"], writes=[tag + "_dst"])


def _emit_reorder_batch(pr, xc_ap, xg_t, xbb_ap, tag, T=2048, D=1024):
    kb = pr.kb
    kb.group("POOL", lambda e: e.dma_start(out=xbb_ap[0:T, :], in_=xc_ap),
             reads=[tag + "_xc"], writes=[tag + "_own"],
             dma=kb.dma_queue("POOLQ.ro0"))

    def emit(e):
        pid = pr.nc.partition_id()
        off = (1 - (pid % 2)) * (T * D)
        return e.dma_start(out=xbb_ap[T:2 * T, :],
                           in_=bass.AP(xg_t, off, [[D, T], [1, D]]))
    kb.group("POOL", emit, reads=[tag + "_dst"], writes=[tag + "_peer"],
             dma=kb.dma_queue("POOLQ.ro1"))


def _emit_colslice(pr, src_t, dst_ap, R, TQ, tag):
    kb = pr.kb

    def emit(e):
        pid = pr.nc.partition_id()
        off = (pid % 2) * TQ
        return e.dma_start(out=dst_ap,
                           in_=bass.AP(src_t, off, [[2 * TQ, R], [1, TQ]]))
    kb.group("POOL", emit, reads=[tag + "_dst"], writes=[tag + "_sel"],
             dma=kb.dma_queue("POOLQ.ro0"))


def build_fused(upto=None):
    D, TQ, TB = 1024, 2048, 4096
    nc = bass.Bass("TRN2", target_bir_lowering=False)
    pr = Prog(nc)
    kb = pr.kb

    class NS:
        def __init__(self, p):
            self.p = p

        def dram_in(self, name, shape, dtype):
            return kb.dram_in(self.p + name, shape, dtype)

        def dram_out(self, name, shape, dtype):
            return nc.dram_tensor(self.p + name, shape, dtype).ap()

    xin = kb.dram_in("xin", [TQ, D], F32)
    xout = kb.dram_out("xout", [TQ, D], F32)
    xcA_t = nc.dram_tensor("xcA", [TQ, D], F32)
    xcB_t = nc.dram_tensor("xcB", [TQ, D], F32)
    xg_t = nc.dram_tensor("xg", [TB, D], F32)
    xbb_t = nc.dram_tensor("xbb", [TB, D], F32)
    x2d_t = nc.dram_tensor("x2d", [TQ, D], F32)
    ogfm_t = nc.dram_tensor("ogfm", [2048, TB], BF16)
    oggf_t = nc.dram_tensor("oggf", [4096, TB], BF16)
    ogsel_t = nc.dram_tensor("ogsel", [4096, TQ], BF16)
    ctxd_t = nc.dram_tensor("ctxd", [512, TB], BF16)
    ctxg_t = nc.dram_tensor("ctxg", [1024, TB], BF16)
    ctxsel_t = nc.dram_tensor("ctxsel", [1024, TQ], BF16)
    xcA, xcB = xcA_t.ap(), xcB_t.ap()
    xg, xbb = xg_t.ap(), xbb_t.ap()

    io_l0a = attn_dram_decls(NS("l0a."), "L0a", TB, weights_only=True)
    io_l0fa = attn_dram_decls(NS("l0fa."), "L0f_attn", TB, weights_only=True)
    ns = NS("l0fm.")
    io_l0fm = {
        "norm2t": ns.dram_in("norm2t", [128, D], F32),
        "routerT": ns.dram_in("routerT", [D, 8], F32),
        "sh_gate": ns.dram_in("sh_gate", [D, 2048], F32),
        "sh_up": ns.dram_in("sh_up", [D, 2048], F32),
        "sh_down": ns.dram_in("sh_down", [2048, D], F32),
        "e_gate": ns.dram_in("e_gate", [8, D, 2048], F32),
        "e_up": ns.dram_in("e_up", [8, D, 2048], F32),
        "e_down": ns.dram_in("e_down", [8, 2048, D], F32),
    }
    ns = NS("l1a.")
    io_l1a = {
        "normwt": ns.dram_in("normwt", [128, D], F32),
        "qwT": ns.dram_in("qwT", [D, 1024], F32),
        "kwT": ns.dram_in("kwT", [D, 1024], F32),
        "vwT": ns.dram_in("vwT", [D, 2048], F32),
        "gwT": ns.dram_in("gwT", [D, 2048], F32),
        "bwT": ns.dram_in("bwT", [D, 4], F32),
        "awT": ns.dram_in("awT", [D, 4], F32),
        "qconvt": ns.dram_in("qconvt", [128, 32], F32),
        "kconvt": ns.dram_in("kconvt", [128, 32], F32),
        "vconvt": ns.dram_in("vconvt", [128, 64], F32),
        "alog": ns.dram_in("alog", [1, 4], F32),
        "dtb": ns.dram_in("dtb", [1, 4], F32),
        "onormt": ns.dram_in("onormt", [128, 512], F32),
    }
    io_l1ap = {"owT": kb.dram_in("l1ap.owT", [4096, D], F32)}
    ns = NS("l1f.")
    io_l1f = {
        "gateT": ns.dram_in("gateT", [D, 8], F32),
        "w1": ns.dram_in("w1", [8, D, 2048], F32),
        "b1r": ns.dram_in("b1r", [8, 2048], F32),
        "ln1gt": ns.dram_in("ln1gt", [8, 128, 2048], F32),
        "ln1bt": ns.dram_in("ln1bt", [8, 128, 2048], F32),
        "w2": ns.dram_in("w2", [8, 2048, D], F32),
        "b2r": ns.dram_in("b2r", [8, D], F32),
        "lngt": ns.dram_in("lngt", [128, D], F32),
        "lnbt": ns.dram_in("lnbt", [128, D], F32),
    }
    io_l2a = attn_dram_decls(NS("l2a."), "L2a", TB, weights_only=True)
    io_l2ap = {"WoD": kb.dram_in("l2ap.WoD", [D, D], F32)}
    ns = NS("l2f.")
    io_l2f = {
        "gateT": ns.dram_in("gateT", [D, 32], F32),
        "gbias": ns.dram_in("gbias", [128, 32], F32),
        "w_gateT": ns.dram_in("w_gateT", [32, D, 512], F32),
        "w_upT": ns.dram_in("w_upT", [32, D, 512], F32),
        "w_downT": ns.dram_in("w_downT", [32, 512, D], F32),
        "sh_gateT": ns.dram_in("sh_gateT", [D, 1024], F32),
        "sh_upT": ns.dram_in("sh_upT", [D, 1024], F32),
        "sh_downT": ns.dram_in("sh_downT", [1024, D], F32),
    }
    io_l3a = attn_dram_decls(NS("l3a."), "L3a", TB, weights_only=True)
    ns = NS("l3f.")
    io_l3f = {
        "routerT": ns.dram_in("routerT", [D, 8], F32),
        "b_router": ns.dram_in("b_router", [1, 8], F32),
        "expT": ns.dram_in("expT", [8, D, D], F32),
        "b_exp": ns.dram_in("b_exp", [8, D], F32),
    }

    kb.begin_phase("pre")
    kb.group("SP", lambda e: e.dma_start(out=xcA, in_=xin),
             writes=["l0a_xc"], dma="SPQ")

    if upto == "l0fa_only":
        kb.begin_phase("l0fx0")
        _emit_gather(pr, xcA, xg, "gX")
        _emit_reorder_batch(pr, xcA, xg_t, xbb, "gX")
        kb.begin_phase("l0fa")
        io_l0fa["xb"] = xbb
        io_l0fa["xo"] = x2d_t.ap()
        emit_attn_atom(pr, io_l0fa, "L0f_attn", TB)
        kb.phase_barrier()
        kb.group("SP", lambda e: e.dma_start(out=xout, in_=x2d_t.ap()),
                 writes=["final"], dma="SPQ")
        kb.barrier_on("SP")
        kb.finalize()
        return nc
    kb.begin_phase("l0ax")
    _emit_gather(pr, xcA, xg, "g0")
    _emit_reorder_batch(pr, xcA, xg_t, xbb, "g0")
    kb.begin_phase("l0a")
    io_l0a["xb"] = xbb
    io_l0a["xo"] = xcB
    emit_attn_atom(pr, io_l0a, "L0a", TB)
    if upto == "l0a":
        kb.phase_barrier()
        kb.group("SP", lambda e: e.dma_start(out=xout, in_=xcB),
                 writes=["final"], dma="SPQ")
        kb.barrier_on("SP")
        kb.finalize()
        return nc

    kb.begin_phase("l0fx")
    _emit_gather(pr, xcB, xg, "g1")
    _emit_reorder_batch(pr, xcB, xg_t, xbb, "g1")
    kb.begin_phase("l0fa")
    if upto == "g1":
        kb.phase_barrier()
        kb.group("SP", lambda e: e.dma_start(out=xout, in_=xbb[TQ:2 * TQ, :]),
                 writes=["final"], dma="SPQ")
        kb.barrier_on("SP")
        kb.finalize()
        return nc
    io_l0fa["xb"] = xbb
    io_l0fa["xo"] = x2d_t.ap()
    emit_attn_atom(pr, io_l0fa, "L0f_attn", TB)
    if upto == "l0fa":
        kb.phase_barrier()
        kb.group("SP", lambda e: e.dma_start(out=xout, in_=x2d_t.ap()),
                 writes=["final"], dma="SPQ")
        kb.barrier_on("SP")
        kb.finalize()
        return nc

    kb.begin_phase("l0fm")
    io_l0fm["x"] = xcB
    io_l0fm["x2"] = x2d_t.ap()
    io_l0fm["xo"] = xcA
    emit_L0f_moe(pr, io_l0fm, T=TQ)
    if upto == "l0fm":
        kb.phase_barrier()
        kb.group("SP", lambda e: e.dma_start(out=xout, in_=xcA),
                 writes=["final"], dma="SPQ")
        kb.barrier_on("SP")
        kb.finalize()
        return nc

    kb.begin_phase("l1x")
    _emit_gather(pr, xcA, xg, "g2")
    kb.begin_phase("l1a")
    io_l1a["xb"] = xg
    io_l1a["og"] = ogfm_t.ap()
    emit_L1a(pr, io_l1a, T=TB, fm_out=True)

    kb.begin_phase("l1ax")
    _emit_gather(pr, ogfm_t.ap(), oggf_t.ap(), "g3")
    _emit_colslice(pr, oggf_t, ogsel_t.ap(), 4096, TQ, "g3")

    kb.begin_phase("l1ap")
    io_l1ap["x"] = xcA
    io_l1ap["ogf"] = ogsel_t.ap()
    io_l1ap["xo"] = xcB
    emit_L1a_proj(pr, io_l1ap, T=TQ, DV=4096)
    if upto == "l1ap":
        kb.phase_barrier()
        kb.group("SP", lambda e: e.dma_start(out=xout, in_=xcB),
                 writes=["final"], dma="SPQ")
        kb.barrier_on("SP")
        kb.finalize()
        return nc

    kb.begin_phase("l1f")
    io_l1f["x"] = xcB
    io_l1f["xo"] = xcA
    emit_L1f(pr, io_l1f, T=TQ)
    if upto == "l1f":
        kb.phase_barrier()
        kb.group("SP", lambda e: e.dma_start(out=xout, in_=xcA),
                 writes=["final"], dma="SPQ")
        kb.barrier_on("SP")
        kb.finalize()
        return nc

    kb.begin_phase("l2x")
    _emit_gather(pr, xcA, xg, "g4")
    kb.begin_phase("l2a")
    io_l2a["xb"] = xg
    io_l2a["ctxo"] = ctxd_t.ap()
    emit_attn_atom(pr, io_l2a, "L2a", TB, dump_fm=True)

    kb.begin_phase("l2ax")
    _emit_gather(pr, ctxd_t.ap(), ctxg_t.ap(), "g5")
    _emit_colslice(pr, ctxg_t, ctxsel_t.ap(), 1024, TQ, "g5")

    kb.begin_phase("l2ap")
    io_l2ap["x"] = xcA
    io_l2ap["ctxin"] = ctxsel_t.ap()
    io_l2ap["xo"] = xcB
    emit_L2a_proj(pr, io_l2ap, T=TQ)
    if upto == "l2ap":
        kb.phase_barrier()
        kb.group("SP", lambda e: e.dma_start(out=xout, in_=xcB),
                 writes=["final"], dma="SPQ")
        kb.barrier_on("SP")
        kb.finalize()
        return nc

    kb.begin_phase("l2f")
    io_l2f["x"] = xcB
    io_l2f["xo"] = xcA
    emit_L2f(pr, io_l2f, T=TQ)
    if upto == "l2f":
        kb.phase_barrier()
        kb.group("SP", lambda e: e.dma_start(out=xout, in_=xcA),
                 writes=["final"], dma="SPQ")
        kb.barrier_on("SP")
        kb.finalize()
        return nc

    kb.begin_phase("l3x")
    _emit_gather(pr, xcA, xg, "g6")
    _emit_reorder_batch(pr, xcA, xg_t, xbb, "g6")
    kb.begin_phase("l3a")
    io_l3a["xb"] = xbb
    io_l3a["xo"] = xcB
    emit_attn_atom(pr, io_l3a, "L3a", TB)
    if upto == "l3a":
        kb.phase_barrier()
        kb.group("SP", lambda e: e.dma_start(out=xout, in_=xcB),
                 writes=["final"], dma="SPQ")
        kb.barrier_on("SP")
        kb.finalize()
        return nc

    kb.begin_phase("l3f")
    io_l3f["x"] = xcB
    io_l3f["xo"] = xout
    emit_L3f(pr, io_l3f, T=TQ)

    kb.phase_barrier()
    kb.barrier_on("SP")
    kb.finalize()
    return nc


def _fused_in_maps(ins):
    f32 = np.float32
    x = np.asarray(ins["L0_attn_hidden_states"], dtype=f32)
    E, H = 8, 2048
    common = {
        "l0a.WqkvT": np.ascontiguousarray(ins["L0_attn_W_qkv"].T.astype(f32)),
        "l0a.bqkv_cols": np.ascontiguousarray(
            ins["L0_attn_b_qkv"].astype(f32).reshape(24, 128).T),
        "l0a.bqkv_row": ins["L0_attn_b_qkv"].astype(f32).reshape(1, 3072),
        "l0a.gammat": _bt(ins["L0_attn_ln_gamma"].astype(f32)),
        "l0a.betat": _bt(ins["L0_attn_ln_beta"].astype(f32)),
        "l0a.WoT": np.ascontiguousarray(ins["L0_attn_W_out"].T.astype(f32)),
        "l0a.bout_row": ins["L0_attn_b_out"].astype(f32).reshape(1, 1024),
        "l0fa.WqT": ins["L0_ffn_wq"].astype(f32),
        "l0fa.WkT": ins["L0_ffn_wk"].astype(f32),
        "l0fa.WvT": ins["L0_ffn_wv"].astype(f32),
        "l0fa.normwt": _bt(ins["L0_ffn_norm1_w"].astype(f32)),
        "l0fa.WoT": ins["L0_ffn_wo"].astype(f32),
        "l0fm.norm2t": _bt(ins["L0_ffn_norm2_w"].astype(f32)),
        "l0fm.routerT": ins["L0_ffn_router_w"].astype(f32),
        "l0fm.sh_gate": ins["L0_ffn_sh_gate"].astype(f32),
        "l0fm.sh_up": ins["L0_ffn_sh_up"].astype(f32),
        "l0fm.sh_down": ins["L0_ffn_sh_down"].astype(f32),
        "l0fm.e_gate": ins["L0_ffn_e_gate"].astype(f32),
        "l0fm.e_up": ins["L0_ffn_e_up"].astype(f32),
        "l0fm.e_down": ins["L0_ffn_e_down"].astype(f32),
        "l1a.normwt": _bt(ins["L1_attn_norm_w"].astype(f32)),
        "l1a.onormt": _bt(ins["L1_attn_o_norm_w"].astype(f32)),
        "l1ap.owT": np.ascontiguousarray(ins["L1_attn_o_w"].astype(f32).T),
        "l1f.gateT": ins["L1_ffn_gate_w"].astype(f32),
        "l1f.w1": ins["L1_ffn_w1"].astype(f32),
        "l1f.b1r": ins["L1_ffn_b1"].astype(f32),
        "l1f.ln1gt": np.ascontiguousarray(np.broadcast_to(
            ins["L1_ffn_ln1_g"].astype(f32)[:, None, :], (E, 128, H))),
        "l1f.ln1bt": np.ascontiguousarray(np.broadcast_to(
            ins["L1_ffn_ln1_b"].astype(f32)[:, None, :], (E, 128, H))),
        "l1f.w2": ins["L1_ffn_w2"].astype(f32),
        "l1f.b2r": ins["L1_ffn_b2"].astype(f32),
        "l1f.lngt": _bt(ins["L1_ffn_ln_g"].astype(f32)),
        "l1f.lnbt": _bt(ins["L1_ffn_ln_b"].astype(f32)),
        "l2ap.WoD": ins["L2_attn_Wo"].astype(f32),
        "l2f.gateT": np.ascontiguousarray(ins["L2_ffn_gate_w"].astype(f32).T),
        "l2f.gbias": np.ascontiguousarray(np.broadcast_to(
            ins["L2_ffn_gate_bias"].astype(f32), (128, 32))),
        "l2f.w_gateT": np.ascontiguousarray(
            ins["L2_ffn_w_gate"].astype(f32).transpose(0, 2, 1)),
        "l2f.w_upT": np.ascontiguousarray(
            ins["L2_ffn_w_up"].astype(f32).transpose(0, 2, 1)),
        "l2f.w_downT": np.ascontiguousarray(
            ins["L2_ffn_w_down"].astype(f32).transpose(0, 2, 1)),
        "l2f.sh_gateT": np.ascontiguousarray(ins["L2_ffn_sh_gate"].astype(f32).T),
        "l2f.sh_upT": np.ascontiguousarray(ins["L2_ffn_sh_up"].astype(f32).T),
        "l2f.sh_downT": np.ascontiguousarray(ins["L2_ffn_sh_down"].astype(f32).T),
        "l3a.WqkvT": np.ascontiguousarray(ins["L3_attn_qkv_w"].astype(f32).T),
        "l3a.qnormt": np.ascontiguousarray(np.broadcast_to(
            np.tile(ins["L3_attn_q_norm_w"].astype(f32), 2), (128, 128))),
        "l3a.knormt": np.ascontiguousarray(np.broadcast_to(
            np.tile(ins["L3_attn_k_norm_w"].astype(f32), 2), (128, 128))),
        "l3a.WoT": np.ascontiguousarray(ins["L3_attn_proj_w"].astype(f32).T),
        "l3a.bout_row": ins["L3_attn_proj_b"].astype(f32).reshape(1, 1024),
        "l3f.routerT": np.ascontiguousarray(ins["L3_ffn_w_router"].astype(f32).T),
        "l3f.b_router": ins["L3_ffn_b_router"].astype(f32).reshape(1, 8),
        "l3f.expT": np.ascontiguousarray(
            ins["L3_ffn_w_experts"].astype(f32).transpose(0, 2, 1)),
        "l3f.b_exp": ins["L3_ffn_b_experts"].astype(f32),
    }
    q_w, k_w, v_w = (ins["L1_attn_q_w"].astype(f32), ins["L1_attn_k_w"].astype(f32),
                     ins["L1_attn_v_w"].astype(f32))
    g_w, b_w, a_w = (ins["L1_attn_g_w"].astype(f32), ins["L1_attn_b_w"].astype(f32),
                     ins["L1_attn_a_w"].astype(f32))
    Wq, Wk, Wv = (ins["L2_attn_Wq"].astype(f32), ins["L2_attn_Wk"].astype(f32),
                  ins["L2_attn_Wv"].astype(f32))
    xown = _shard_own(x)
    in_maps = []
    for c in range(8):
        h0 = (c % 2) * 4
        rqk = slice(h0 * 256, (h0 + 4) * 256)
        rv = slice(h0 * 512, (h0 + 4) * 512)
        h2 = (c % 2) * 8
        sl2 = slice(h2 * 64, (h2 + 8) * 64)
        m = dict(common)
        m.update({
            "xin": xown[c],
            "l1a.qwT": np.ascontiguousarray(q_w[rqk].T),
            "l1a.kwT": np.ascontiguousarray(k_w[rqk].T),
            "l1a.vwT": np.ascontiguousarray(v_w[rv].T),
            "l1a.gwT": np.ascontiguousarray(g_w[rv].T),
            "l1a.bwT": np.ascontiguousarray(b_w[h0:h0 + 4].T),
            "l1a.awT": np.ascontiguousarray(a_w[h0:h0 + 4].T),
            "l1a.qconvt": _conv_cols(ins["L1_attn_q_conv"].astype(f32),
                                     np.arange(h0 * 256, (h0 + 4) * 256)),
            "l1a.kconvt": _conv_cols(ins["L1_attn_k_conv"].astype(f32),
                                     np.arange(h0 * 256, (h0 + 4) * 256)),
            "l1a.vconvt": _conv_cols(ins["L1_attn_v_conv"].astype(f32),
                                     np.arange(h0 * 512, (h0 + 4) * 512)),
            "l1a.alog": ins["L1_attn_A_log"].astype(f32)[h0:h0 + 4].reshape(1, 4),
            "l1a.dtb": ins["L1_attn_dt_bias"].astype(f32)[h0:h0 + 4].reshape(1, 4),
            "l2a.WqT": np.ascontiguousarray(Wq[:, sl2]),
            "l2a.WkT": np.ascontiguousarray(Wk[:, sl2]),
            "l2a.WvT": np.ascontiguousarray(Wv[:, sl2]),
        })
        in_maps.append(m)
    return in_maps


def kernel_fused(**ins):
    nc = _prog("FUSED", build_fused)
    in_maps = _fused_in_maps(ins)
    res = run_bass_kernel_spmd(nc, in_maps, core_ids=list(range(8)))
    return _unshard_own([r["xout"] for r in res.results])
